# version 03
# baseline (speedup 1.0000x reference)
import numpy as np
from contextlib import ExitStack

import concourse.bass as bass
import concourse.mybir as mybir
from concourse.bass_utils import run_bass_kernel_spmd

F32 = mybir.dt.float32
BF16 = mybir.dt.bfloat16
AF = mybir.ActivationFunctionType
ALU = mybir.AluOpType
AX = mybir.AxisListType

D = 1024
B = 4
T = 4096
NCORE = 8
TPC = 2048
LN_EPS = 1e-5


class Sem:
    def __init__(self, handle):
        self.h = handle
        self.n = 0

    def reserve(self, k=1):
        self.n += k
        return self.n


class Arena:

    def __init__(self, nc, size_b):
        self.nc = nc
        self.size_b = size_b
        self.cursor = 0
        self.peak = 0
        self.handle = None
        self.base = 0

    def allocate(self, stack):
        import concourse.mybir as _mb
        self.handle = stack.enter_context(
            self.nc.sbuf_tensor("mega_arena", [128, self.size_b], _mb.dt.uint8))
        self.base = self.nc.lookup_mloc(self.handle).addr

    def alloc(self, name, shape, dt):
        per = 1
        for s in shape[1:]:
            per *= int(s)
        per *= mybir.dt.size(dt)
        addr = self.cursor
        self.cursor = (addr + per + 31) // 32 * 32
        self.peak = max(self.peak, self.cursor)
        assert self.cursor <= self.size_b, f"arena overflow at {name}: {self.cursor} > {self.size_b}"
        h = self.nc.virtual_sbuf_tensor(name, shape, dt)
        self.nc._tensor(h.name, h.shape, h.dtype, type="SB")
        mloc = self.nc.lookup_mloc(h)
        mloc.allocated = True
        mloc.addr = self.base + addr
        return h


class Prog:

    def __init__(self, nc):
        self.nc = nc
        self.stack = ExitStack()
        self.queues = {"sync": [], "tensor": [], "vector": [], "scalar": [], "gpsimd": []}
        self.esem = {k: Sem(self.stack.enter_context(nc.semaphore("sem_" + k)))
                     for k in self.queues}
        self.arena = None
        self.psum_pool = None
        self.psum_ix = 0

    def begin_stage(self):
        if self.arena is not None:
            self.arena.cursor = 0
        self.psum_ix = 0

    def dsem(self, name):
        key = "dma_" + name
        if key not in self.esem:
            self.esem[key] = Sem(self.stack.enter_context(self.nc.semaphore("sem_" + key)))
        return key

    def sbuf(self, name, shape, dt):
        if self.arena is not None:
            return self.arena.alloc(name, shape, dt)
        return self.stack.enter_context(self.nc.sbuf_tensor(name, shape, dt))

    def psum(self, name, shape, dt=F32):
        if self.psum_pool is not None:
            assert dt == F32, name
            bank = self.psum_pool[self.psum_ix]
            self.psum_ix += 1
            return bank[0:shape[0], 0:shape[1]]
        return self.stack.enter_context(self.nc.psum_tensor(name, shape, dt))

    def op(self, eng, makeop, inc=False, inck=1, waits=(), chain=None, dma=None):
        if chain is None:
            chain = eng != "tensor"
        if dma is not None:
            key = self.dsem(dma)
            inc = True
            inck = 16
        else:
            key = eng
        wl = [(en, v) for (en, v) in waits if v > 0]
        if chain:
            prev_c = self.esem[eng].n
            if dma is None:
                inc = True
            if prev_c > 0:
                wl.append((eng, prev_c))
        tgt = self.esem[key].reserve(inck) if inc else None
        wl = [(self.esem[en], v) for (en, v) in wl]

        def f(e, makeop=makeop, inc=inc, inck=inck, wl=wl, key=key):
            for s, v in wl:
                e.wait_ge(s.h, v)
            o = makeop(e)
            if inc:
                o.then_inc(self.esem[key].h, inck)

        self.queues[eng].append(f)
        return tgt

    def val(self, eng):
        return self.esem[eng].n

    def finish(self):
        block = self.stack.enter_context(self.nc.Block())
        for eng_name in ("sync", "tensor", "vector", "scalar", "gpsimd"):
            fns = self.queues[eng_name]
            if not fns:
                continue

            def section(eng, fns=fns):
                for f in fns:
                    f(eng)

            getattr(block, eng_name)(section)
        self.stack.close()


def emit_ln_tile(p, bufs, xt_ap, yt_ap, pre_a=0, pre_v=0, scr=None):
    ln_sum, ln_ss = bufs["ln_sum"], bufs["ln_ss"]
    ln_scr = scr if scr is not None else bufs["ln_scr"]
    ln_t1, ln_t2 = bufs["ln_t1"], bufs["ln_t2"]

    p.op("vector", lambda e: e.tensor_reduce(out=ln_sum[:], in_=xt_ap, axis=AX.X, op=ALU.add),
         inc=True, waits=[("scalar", pre_a), ("vector", pre_v)])
    a0 = p.op("scalar", lambda e: e.activation(out=ln_scr[:], in_=xt_ap, func=AF.Square, accum_out=ln_ss[:]),
              inc=True, waits=[("vector", pre_v)])
    p.op("vector", lambda e: e.tensor_scalar(out=ln_t1[:], in0=ln_ss[:], scalar1=1.0 / D, scalar2=LN_EPS,
                                             op0=ALU.mult, op1=ALU.add),
         waits=[("scalar", a0)], inc=True)
    p.op("vector", lambda e: e.tensor_scalar_mul(out=ln_sum[:], in0=ln_sum[:], scalar1=1.0 / D), inc=True)
    p.op("vector", lambda e: e.tensor_mul(out=ln_t2[:], in0=ln_sum[:], in1=ln_sum[:]), inc=True)
    v0 = p.op("vector", lambda e: e.tensor_sub(out=ln_t1[:], in0=ln_t1[:], in1=ln_t2[:]), inc=True)
    a1 = p.op("scalar", lambda e: e.activation(out=ln_t2[:], in_=ln_t1[:], func=AF.Sqrt),
              waits=[("vector", v0)], inc=True)
    v1 = p.op("vector", lambda e: e.reciprocal(out=ln_t1[:], in_=ln_t2[:]), waits=[("scalar", a1)], inc=True)
    p.op("vector", lambda e: e.tensor_mul(out=ln_t2[:], in0=ln_sum[:], in1=ln_t1[:]),
         waits=[("vector", v1)], inc=True)
    v2 = p.op("vector", lambda e: e.tensor_scalar_mul(out=ln_t2[:], in0=ln_t2[:], scalar1=-1.0), inc=True)
    a2 = p.op("scalar", lambda e: e.activation(out=yt_ap, in_=xt_ap, func=AF.Identity, bias=ln_t2[:], scale=ln_t1[:]),
              waits=[("vector", v2)], inc=True)
    return a2



def emit_l0ffn(p, xs, gate_w, gate_b, up_w, up_bT, down_w, down_b, xo, tpc=TPC):
    NE, HCH, ICH, HALF = 9, 8, 32, tpc // 2
    ntile = tpc // 128
    NGR = max(1, HALF // 512)
    NDS = HALF // 256
    NSUPA = 8
    NSUPB = 4

    xt = p.sbuf("xt", [128, D], F32)
    yt = p.sbuf("yt", [128, D], F32)
    ln_bufs = {
        "ln_sum": p.sbuf("ln_sum", [128, 1], F32),
        "ln_ss": p.sbuf("ln_ss", [128, 1], F32),
        "ln_scr": p.sbuf("ln_scr", [128, D], F32),
        "ln_t1": p.sbuf("ln_t1", [128, 1], F32),
        "ln_t2": p.sbuf("ln_t2", [128, 1], F32),
    }
    ident = p.sbuf("ident", [128, 128], F32)
    fones = p.sbuf("fones", [1, 128], F32)
    bones = p.sbuf("bones", [1, 128], BF16)
    yf32 = p.sbuf("yf32", [128, HCH * 128], F32)
    ybf = p.sbuf("ybf", [128, HCH * tpc], BF16)
    gbias = p.sbuf("gbias", [1, 8], F32)
    gwt = p.sbuf("gwt", [128, HCH * 8], F32)
    gatel = p.sbuf("gatel", [128, 8 * ntile], F32)
    w01 = p.sbuf("w01", [128, 8 * ntile], F32)
    top8 = p.sbuf("top8", [128, 8], F32)
    gsc = p.sbuf("gsc", [128, 8], F32)
    mcol = [p.sbuf(f"mcol{i}", [128, 1], F32) for i in range(3)]
    upblk = [p.sbuf(f"upblk{i}", [128, HCH * 4 * 128], BF16) for i in range(2)]
    dwblk = [p.sbuf(f"dwblk{i}", [128, 8 * D], BF16) for i in range(2)]
    dbias = [p.sbuf(f"dbias{i}", [1, D], BF16) for i in range(2)]
    upb = p.sbuf("upb", [128, NE * 32], F32)
    hact = p.sbuf("hact", [128, ICH * HALF], BF16)
    acc = p.sbuf("acc", [128, (HALF // 128) * D], F32)
    xt2 = xt
    outt = yt

    ps_up = [p.psum(f"ps_up{i}", [128, 512], F32) for i in range(2)]
    ps_dn = [p.psum(f"ps_dn{i}", [128, 512], F32) for i in range(4)]
    ps_tr = [ps_up[0][:, 0:128], ps_up[1][:, 0:128]]
    ps_g = ps_dn[0][:, 0:8]

    g0 = p.op("gpsimd", lambda e: e.memset(ident[:], 0.0), inc=True)
    p.op("gpsimd", lambda e: e.affine_select(out=ident[:], in_=ident[:], compare_op=ALU.not_equal,
                                             fill=1.0, base=0, pattern=[[-1, 128]], channel_multiplier=1),
         inc=True, waits=[("gpsimd", g0)])
    p.op("gpsimd", lambda e: e.memset(fones[:], 1.0), inc=True)
    g_const = p.op("gpsimd", lambda e: e.memset(bones[:], 1.0), inc=True)
    for eix in range(NE):
        p.op("sync", lambda e, eix=eix: e.dma_start(out=upb[:, eix * 32:(eix + 1) * 32], in_=up_bT[eix]),
             dma="const")
    p.op("sync", lambda e: e.dma_start(out=gbias[:], in_=gate_b), dma="const")
    for hk in range(HCH):
        p.op("sync", lambda e, hk=hk: e.dma_start(out=gwt[:, hk * 8:(hk + 1) * 8],
                                                  in_=gate_w[hk * 128:(hk + 1) * 128, :]), dma="const")
    const_lds = p.esem[p.dsem("const")].n

    prev_ln = 0
    prev_lnv = 0
    prev_w01 = 0
    prev_gate = 0
    prev_sig = 0
    tr_evac = [0, 0]
    for t in range(ntile):
        ld = p.op("sync", lambda e, t=t: e.dma_start(out=xt[:], in_=xs[t * 128:(t + 1) * 128, :]),
                  dma="ldx", waits=[("scalar", prev_ln), ("vector", prev_lnv)])
        p.op("vector", lambda e: e.nop(), waits=[("dma_ldx", ld)])
        p.op("scalar", lambda e: e.nop(), waits=[("dma_ldx", ld)])
        prev_ln = emit_ln_tile(p, ln_bufs, xt[:], yt[:])
        prev_lnv = p.val("vector")
        for hk in range(HCH):
            pt = ps_tr[hk % 2]
            t_tr = p.op("tensor", lambda e, hk=hk, pt=pt: e.transpose(pt, yt[:, hk * 128:(hk + 1) * 128], ident[:]),
                        waits=[("scalar", prev_ln), ("vector", tr_evac[hk % 2]), ("gpsimd", g_const)], inc=True)
            p.op("vector", lambda e, hk=hk, pt=pt: e.tensor_copy(out=yf32[:, hk * 128:(hk + 1) * 128], in_=pt),
                 waits=[("tensor", t_tr), ("tensor", prev_gate)], inc=True)
            tr_evac[hk % 2] = p.op("vector", lambda e, hk=hk, t=t: e.tensor_copy(
                out=ybf[:, hk * tpc + t * 128: hk * tpc + (t + 1) * 128],
                in_=yf32[:, hk * 128:(hk + 1) * 128]), inc=True)
        v_tr = p.val("vector")
        for hk in range(HCH):
            p.op("tensor", lambda e, hk=hk: e.matmul(ps_g, yf32[:, hk * 128:(hk + 1) * 128],
                                                     gwt[:, hk * 8:(hk + 1) * 8], start=(hk == 0), stop=False),
                 waits=([("vector", v_tr), ("dma_const", const_lds), ("scalar", prev_sig)] if hk == 0 else []))
        t_g = p.op("tensor", lambda e: e.matmul(ps_g, fones[:], gbias[:], start=False, stop=True), inc=True)
        prev_gate = t_g
        prev_sig = p.op("scalar", lambda e, t=t: e.activation(out=gatel[:, t * 8:(t + 1) * 8], in_=ps_g, func=AF.Sigmoid),
                        waits=[("tensor", t_g)], inc=True)
        v0 = p.op("vector", lambda e, t=t: e.max(out=top8[:], in_=gatel[:, t * 8:(t + 1) * 8]),
                  waits=[("scalar", prev_sig)], inc=True)
        v1 = p.op("vector", lambda e, t=t: e.tensor_scalar(out=gsc[:], in0=gatel[:, t * 8:(t + 1) * 8],
                                                           scalar1=top8[:, 1:2], scalar2=None, op0=ALU.is_ge),
                  waits=[("vector", v0)], inc=True)
        p.op("vector", lambda e: e.tensor_tensor(out=mcol[0][:], in0=top8[:, 0:1], in1=top8[:, 1:2], op=ALU.add),
             inc=True)
        v2 = p.op("vector", lambda e: e.tensor_scalar_add(out=mcol[0][:], in0=mcol[0][:], scalar1=1e-6), inc=True)
        v3 = p.op("vector", lambda e: e.reciprocal(out=mcol[1][:], in_=mcol[0][:]), waits=[("vector", v2)], inc=True)
        v4 = p.op("vector", lambda e: e.tensor_scalar_mul(out=mcol[1][:], in0=mcol[1][:], scalar1=0.1),
                  waits=[("vector", v3)], inc=True)
        v5 = p.op("vector", lambda e, t=t: e.tensor_mul(out=gsc[:], in0=gsc[:], in1=gatel[:, t * 8:(t + 1) * 8]),
                  inc=True)
        prev_w01 = p.op("vector", lambda e, t=t: e.tensor_scalar(out=w01[:, t * 8:(t + 1) * 8], in0=gsc[:],
                                                                 scalar1=mcol[1][:], scalar2=None, op0=ALU.mult),
                        waits=[("vector", v4), ("vector", v5)], inc=True)
        prev_lnv = max(prev_lnv, prev_w01)
    pass1_v = p.val("vector")
    pass1_a = p.val("scalar")

    up_last_use = [0, 0]
    dw_last_use = [0, 0]
    gelu_done = [0, 0]
    fma_done = [0, 0, 0, 0]
    hact_free = 0
    dbias_free = [0, 0]
    store_guard = 0
    prev_fin = 0

    for half in range(2):
        tok0 = half * HALF
        for eix in range(NE):
            dbpar = (half * NE + eix) % 2
            wdb = p.op("gpsimd", lambda e, eix=eix, dbpar=dbpar: e.dma_start(out=dbias[dbpar][:], in_=down_b[eix]),
                       dma=f"wdb{dbpar}", waits=[("tensor", dbias_free[dbpar])])
            for sup in range(NSUPA):
                blk = upblk[sup % 2]
                src4 = up_w[eix].rearrange("p (hc ii i) -> p hc ii i", hc=8, ii=32, i=128)
                wv = p.op("gpsimd", lambda e, src4=src4, sup=sup, blk=blk: e.dma_start(
                    out=blk[:].rearrange("p (hc ii i) -> p hc ii i", hc=8, ii=4, i=128),
                    in_=src4[:, :, sup * 4:(sup + 1) * 4, :]),
                    dma=f"wup{sup % 2}", waits=[("tensor", up_last_use[sup % 2])])
                t_up = 0
                for iil in range(4):
                    ii = sup * 4 + iil
                    for gr in range(NGR):
                        bank = ps_up[gr % 2]
                        for hk in range(HCH):
                            wl = []
                            if hk == 0:
                                wl = [(f"dma_wup{sup % 2}", wv), ("scalar", gelu_done[gr % 2]), ("vector", pass1_v)]
                            t_up = p.op("tensor", lambda e, hk=hk, iil=iil, gr=gr, bank=bank, blk=blk, tok0=tok0:
                                        e.matmul(bank[:], blk[:, (hk * 4 + iil) * 128:(hk * 4 + iil + 1) * 128],
                                                 ybf[:, hk * tpc + tok0 + gr * 512: hk * tpc + tok0 + (gr + 1) * 512],
                                                 start=(hk == 0), stop=(hk == HCH - 1)), waits=wl,
                                        inc=(hk == HCH - 1))
                        gelu_done[gr % 2] = p.op("scalar", lambda e, ii=ii, gr=gr, bank=bank, eix=eix:
                                                 e.activation(out=hact[:, ii * HALF + gr * 512: ii * HALF + (gr + 1) * 512],
                                                              in_=bank[:], func=AF.Gelu,
                                                              bias=upb[:, eix * 32 + ii: eix * 32 + ii + 1]),
                                                 waits=[("tensor", t_up), ("tensor", hact_free)], inc=True)
                up_last_use[sup % 2] = t_up
            a_phaseA = p.val("scalar")
            for sup in range(NSUPB):
                blk = dwblk[sup % 2]
                src3 = down_w[eix].rearrange("p (ii f) -> p ii f", ii=32, f=D)
                wv = p.op("gpsimd", lambda e, src3=src3, sup=sup, blk=blk: e.dma_start(
                    out=blk[:].rearrange("p (ii f) -> p ii f", ii=8, f=D),
                    in_=src3[:, sup * 8:(sup + 1) * 8, :]),
                    dma=f"wdn{sup % 2}", waits=[("tensor", dw_last_use[sup % 2])])
                for ds in range(NDS):
                    grp_t = [0, 0, 0, 0]
                    for t2 in range(2):
                        for f2 in range(2):
                            bk = t2 * 2 + f2
                            bank = ps_dn[bk]
                            tt_local = ds * 2 + t2
                            first = True
                            if sup == 0:
                                p.op("tensor", lambda e, bank=bank, dbpar=dbpar, f2=f2:
                                     e.matmul(bank[:], bones[:], dbias[dbpar][0:1, f2 * 512:(f2 + 1) * 512],
                                              start=True, stop=False),
                                     waits=[(f"dma_wdn{sup % 2}", wv), (f"dma_wdb{dbpar}", wdb), ("vector", fma_done[bk]),
                                            ("scalar", a_phaseA)])
                                first = False
                            for iil in range(8):
                                ii = sup * 8 + iil
                                wl = []
                                if first:
                                    wl = [(f"dma_wdn{sup % 2}", wv), ("vector", fma_done[bk]), ("scalar", a_phaseA)]
                                    first = False
                                grp_t[bk] = p.op("tensor", lambda e, bank=bank, ii=ii, iil=iil, tt_local=tt_local, f2=f2,
                                                 blk=blk, st=(sup != 0 and iil == 0):
                                                 e.matmul(bank[:], hact[:, ii * HALF + tt_local * 128: ii * HALF + (tt_local + 1) * 128],
                                                          blk[:, iil * D + f2 * 512: iil * D + (f2 + 1) * 512],
                                                          start=st, stop=(iil == 7)), waits=wl, inc=(iil == 7))
                    for t2 in range(2):
                        for f2 in range(2):
                            bk = t2 * 2 + f2
                            bank = ps_dn[bk]
                            tt_local = ds * 2 + t2
                            tglob = (tok0 + tt_local * 128) // 128
                            acc_ap = acc[:, tt_local * D + f2 * 512: tt_local * D + (f2 + 1) * 512]
                            if eix == 0 and sup == 0:
                                fma_done[bk] = p.op("vector", lambda e, bank=bank, acc_ap=acc_ap, tglob=tglob:
                                                    e.tensor_scalar(out=acc_ap, in0=bank[:],
                                                                    scalar1=w01[:, tglob * 8: tglob * 8 + 1],
                                                                    scalar2=None, op0=ALU.mult),
                                                    waits=[("tensor", grp_t[bk])], inc=True)
                            elif eix == 8:
                                fma_done[bk] = p.op("vector", lambda e, bank=bank, acc_ap=acc_ap:
                                                    e.scalar_tensor_tensor(out=acc_ap, in0=bank[:], scalar=0.1,
                                                                           in1=acc_ap, op0=ALU.mult, op1=ALU.add),
                                                    waits=[("tensor", grp_t[bk])], inc=True)
                            else:
                                fma_done[bk] = p.op("vector", lambda e, bank=bank, acc_ap=acc_ap, tglob=tglob, eix=eix:
                                                    e.scalar_tensor_tensor(out=acc_ap, in0=bank[:],
                                                                           scalar=w01[:, tglob * 8 + eix: tglob * 8 + eix + 1],
                                                                           in1=acc_ap, op0=ALU.mult, op1=ALU.add),
                                                    waits=[("tensor", grp_t[bk])], inc=True)
                dw_last_use[sup % 2] = max(grp_t)
            t_phaseB = dw_last_use[(NSUPB - 1) % 2]
            hact_free = t_phaseB
            dbias_free[dbpar] = t_phaseB
        v_acc = p.val("vector")
        for tt_local in range(HALF // 128):
            tglob = tok0 // 128 + tt_local
            acc_ap = acc[:, tt_local * D: (tt_local + 1) * D]
            ld = p.op("sync", lambda e, tglob=tglob: e.dma_start(out=xt2[:], in_=xs[tglob * 128:(tglob + 1) * 128, :]),
                      dma="ldx", waits=[("vector", max(prev_fin, v_acc)), ("scalar", pass1_a)])
            v0 = p.op("vector", lambda e, acc_ap=acc_ap: e.tensor_reduce(out=mcol[2][:], in_=acc_ap, axis=AX.X,
                                                                         op=ALU.max, apply_absolute_value=True),
                      waits=[("vector", v_acc)], inc=True)
            v1 = p.op("vector", lambda e: e.tensor_scalar_add(out=mcol[2][:], in0=mcol[2][:], scalar1=1e-6),
                      waits=[("vector", v0)], inc=True)
            v2 = p.op("vector", lambda e: e.reciprocal(out=mcol[2][:], in_=mcol[2][:]), waits=[("vector", v1)], inc=True)
            v3 = p.op("vector", lambda e: e.tensor_scalar_mul(out=mcol[2][:], in0=mcol[2][:], scalar1=0.5),
                      waits=[("vector", v2)], inc=True)
            v4 = p.op("vector", lambda e, acc_ap=acc_ap: e.scalar_tensor_tensor(out=outt[:], in0=acc_ap,
                                                                                scalar=mcol[2][:], in1=xt2[:],
                                                                                op0=ALU.mult, op1=ALU.add),
                      waits=[("vector", v3), ("dma_ldx", ld), ("dma_stout", store_guard)], inc=True)
            store_guard = p.op("sync", lambda e, tglob=tglob: e.dma_start(out=xo[tglob * 128:(tglob + 1) * 128, :],
                                                                          in_=outt[:]),
                               dma="stout", waits=[("vector", v4)])
            prev_fin = v4
    p.op("sync", lambda e: e.nop(), waits=[("dma_stout", p.esem[p.dsem("stout")].n)])


def build_l0ffn(tpc=TPC):
    nc = bass.Bass("TRN2", target_bir_lowering=False, debug=False, name="l0ffn")
    xs = nc.dram_tensor("xs", [tpc, D], F32, kind="ExternalInput").ap()
    gate_w = nc.dram_tensor("gate_w", [D, 8], F32, kind="ExternalInput").ap()
    gate_b = nc.dram_tensor("gate_b", [1, 8], F32, kind="ExternalInput").ap()
    up_w = nc.dram_tensor("up_w", [9, 128, 8 * 32 * 128], F32, kind="ExternalInput").ap()
    up_bT = nc.dram_tensor("up_bT", [9, 128, 32], F32, kind="ExternalInput").ap()
    down_w = nc.dram_tensor("down_w", [9, 128, 32 * D], F32, kind="ExternalInput").ap()
    down_b = nc.dram_tensor("down_b", [9, 1, D], F32, kind="ExternalInput").ap()
    xo = nc.dram_tensor("xo", [tpc, D], F32, kind="ExternalOutput").ap()

    p = Prog(nc)
    emit_l0ffn(p, xs, gate_w, gate_b, up_w, up_bT, down_w, down_b, xo, tpc)
    p.finish()
    return nc



def prep_l0ffn_weights(I):
    up_all = np.concatenate([I["L0_ffn_up_w"], I["L0_ffn_sw_up"][None]], 0)
    down_all = np.concatenate([I["L0_ffn_down_w"], I["L0_ffn_sw_down"][None]], 0)
    upb_all = np.concatenate([I["L0_ffn_up_b"], I["L0_ffn_sb_up"][None]], 0)
    dnb_all = np.concatenate([I["L0_ffn_down_b"], I["L0_ffn_sb_down"][None]], 0)
    up_dev = np.ascontiguousarray(
        up_all.reshape(9, 8, 128, 32, 128).transpose(0, 2, 1, 3, 4).reshape(9, 128, 8 * 32 * 128))
    down_dev = np.ascontiguousarray(
        down_all.reshape(9, 32, 128, 1024).transpose(0, 2, 1, 3).reshape(9, 128, 32 * 1024))
    up_bT = np.ascontiguousarray(upb_all.reshape(9, 32, 128).transpose(0, 2, 1))
    down_b = np.ascontiguousarray(dnb_all.reshape(9, 1, 1024))
    return {
        "gate_w": np.ascontiguousarray(I["L0_ffn_gate_w"]),
        "gate_b": np.ascontiguousarray(I["L0_ffn_bias"].reshape(1, 8)),
        "up_w": up_dev, "up_bT": up_bT, "down_w": down_dev, "down_b": down_b,
    }


def shard_tokens(x):
    xb = x.reshape(B, T // 128, 128, D)
    return [np.ascontiguousarray(xb[c // 2, c % 2::2].reshape(TPC, D)) for c in range(NCORE)]


def unshard_tokens(shards):
    out = np.zeros((B, T // 128, 128, D), np.float32)
    for c in range(NCORE):
        out[c // 2, c % 2::2] = np.asarray(shards[c]).reshape(TPC // 128, 128, D)
    return out.reshape(B, T, D)


_L0FFN_NC = None


def run_l0ffn(x, I):
    global _L0FFN_NC
    if _L0FFN_NC is None:
        _L0FFN_NC = build_l0ffn()
    w = prep_l0ffn_weights(I)
    shards = shard_tokens(x)
    in_maps = [dict(w, xs=s) for s in shards]
    res = run_bass_kernel_spmd(_L0FFN_NC, in_maps, core_ids=list(range(NCORE)))
    return unshard_tokens([res.results[c]["xo"] for c in range(NCORE)])


def kernel(**inputs):
    return kernel_mega(**inputs)


if __name__ == "__main__":
    print("kernel module")



def emit_l0attn(p, xf, xq, mq, mqT, w_attn, b_qk, b_v, w_proj, b_proj, xo, tq=TPC, tf=T, newton=7):
    HCH = 8
    nqt = tq // 128
    nft = tf // 128
    NH = 16

    xt = p.sbuf("xt", [128, D], F32)
    yt = p.sbuf("yt", [128, D], F32)
    cols = p.sbuf("cols", [128, 32], F32)
    ln_bufs = {
        "ln_sum": cols[:, 0:1],
        "ln_ss": cols[:, 1:2],
        "ln_t1": cols[:, 2:3],
        "ln_t2": cols[:, 3:4],
    }
    ident = p.sbuf("ident", [128, 128], F32)
    fones = p.sbuf("fones", [1, 128], F32)
    bones = p.sbuf("bones", [1, 128], BF16)
    ybg = p.sbuf("ybg", [128, HCH * 512], BF16)
    watq = p.sbuf("watq", [128, HCH * D], BF16)
    watk = p.sbuf("watk", [128, HCH * D], BF16)
    watv = p.sbuf("watv", [128, HCH * D], BF16)
    bqk = cols[:, 16:32]
    bvrow = p.sbuf("bvrow", [1, D], BF16)
    bprow = bvrow
    kbf = p.sbuf("kbf", [128, HCH * tf], BF16)
    vbf = p.sbuf("vbf", [128, (tf // 128) * D], BF16)
    qtile = p.sbuf("qtile", [128, HCH * 128], BF16)
    mqs = p.sbuf("mqs", [128, 256], BF16)
    mqTs = p.sbuf("mqTs", [128, 256], BF16)
    zbuf = watv
    abuf = [p.sbuf(f"abuf{i}", [128, 128], BF16) for i in range(2)]
    ascr = p.sbuf("ascr", [128, 256], BF16)
    outf = p.sbuf("outf", [128, HCH * 128], BF16)
    ntrow = p.sbuf("ntrow", [1, 128], F32)
    m_run = cols[:, 4:5]
    tau = cols[:, 5:6]
    ntau = cols[:, 6:7]
    fcol = cols[:, 7:8]
    ccol = cols[:, 8:9]
    fcol2 = cols[:, 9:10]
    ccol2 = cols[:, 10:11]
    rcol = cols[:, 11:12]
    nscr = p.sbuf("nscr", [128, max(tf // 2, D)], BF16)
    nscr2 = ybg
    outt = yt

    ps_kv = [p.psum(f"ps_kv{i}", [128, 512], F32) for i in range(2)]
    ps_z = [p.psum(f"ps_z{i}", [128, 512], F32) for i in range(2)]
    ps_zT = [p.psum(f"ps_zT{i}", [128, 128], F32) for i in range(2)]
    ps_o = p.psum("ps_o", [64, 128], F32)
    ps_r = p.psum("ps_r", [1, 128], F32)

    g0 = p.op("gpsimd", lambda e: e.memset(ident[:], 0.0), inc=True)
    p.op("gpsimd", lambda e: e.affine_select(out=ident[:], in_=ident[:], compare_op=ALU.not_equal,
                                             fill=1.0, base=0, pattern=[[-1, 128]], channel_multiplier=1),
         inc=True, waits=[("gpsimd", g0)])
    p.op("gpsimd", lambda e: e.memset(fones[:], 1.0), inc=True)
    g_const = p.op("gpsimd", lambda e: e.memset(bones[:], 1.0), inc=True)
    p.op("sync", lambda e: e.dma_start(out=bqk[:], in_=b_qk), dma="const")
    p.op("gpsimd", lambda e: e.dma_start(out=bvrow[:], in_=b_v), dma="gconst")
    p.op("gpsimd", lambda e: e.dma_start(out=mqs[:], in_=mq), dma="gconst")
    p.op("gpsimd", lambda e: e.dma_start(out=mqTs[:], in_=mqT), dma="gconst")
    for hk in range(HCH):
        p.op("gpsimd", lambda e, hk=hk: e.dma_start(out=watq[:, hk * D:(hk + 1) * D],
                                                    in_=w_attn[hk * 128:(hk + 1) * 128, 0:D]), dma="gconst")
        p.op("gpsimd", lambda e, hk=hk: e.dma_start(out=watk[:, hk * D:(hk + 1) * D],
                                                    in_=w_attn[hk * 128:(hk + 1) * 128, D:2 * D]), dma="gconst")
        p.op("gpsimd", lambda e, hk=hk: e.dma_start(out=watv[:, hk * D:(hk + 1) * D],
                                                    in_=w_attn[hk * 128:(hk + 1) * 128, 2 * D:3 * D]), dma="gconst")
    const_lds = p.esem[p.dsem("const")].n
    gconst_lds = p.esem[p.dsem("gconst")].n
    bq_scaled = p.op("vector", lambda e: e.tensor_scalar_mul(out=bqk[:, 0:8], in0=bqk[:, 0:8], scalar1=0.125),
                     waits=[("dma_const", const_lds)], inc=True)

    def wslice(hk, f0, f1):
        if f1 <= D:
            return watq[:, hk * D + f0: hk * D + f1]
        if f1 <= 2 * D:
            return watk[:, hk * D + f0 - D: hk * D + f1 - D]
        return watv[:, hk * D + f0 - 2 * D: hk * D + f1 - 2 * D]

    prev_ln = 0
    prev_lnv = 0
    tr_evac = [0, 0]
    prev_kv = 0
    kv_evac_a = 0
    kv_evac_v = 0
    for grp in range(nft // 4):
        for tl in range(4):
            t = grp * 4 + tl
            ld = p.op("sync", lambda e, t=t: e.dma_start(out=xt[:], in_=xf[t * 128:(t + 1) * 128, :]),
                      dma="ldx", waits=[("scalar", prev_ln), ("vector", prev_lnv)])
            p.op("vector", lambda e: e.nop(), waits=[("dma_ldx", ld)])
            p.op("scalar", lambda e: e.nop(), waits=[("dma_ldx", ld)])
            prev_ln = emit_ln_tile(p, ln_bufs, xt[:], yt[:], scr=nscr[:, 0:D])
            for hk in range(HCH):
                pt = ps_z[hk % 2][:, 0:128]
                t_tr = p.op("tensor", lambda e, hk=hk, pt=pt: e.transpose(pt, yt[:, hk * 128:(hk + 1) * 128], ident[:]),
                            waits=[("scalar", prev_ln), ("vector", tr_evac[hk % 2]), ("gpsimd", g_const),
                                   ("tensor", prev_kv)], inc=True)
                tr_evac[hk % 2] = p.op("vector", lambda e, hk=hk, tl=tl, pt=pt: e.tensor_copy(
                    out=ybg[:, hk * 512 + tl * 128: hk * 512 + (tl + 1) * 128], in_=pt),
                    waits=[("tensor", t_tr)], inc=True)
            prev_lnv = p.val("vector")
        v_grp = p.val("vector")
        for fc in range(HCH):
            bank = ps_kv[fc % 2]
            for hk in range(HCH):
                wl = [("vector", max(v_grp, kv_evac_v)), ("dma_const", const_lds),
                      ("dma_gconst", gconst_lds)] if hk == 0 else []
                if hk == 0 and fc >= 2:
                    wl.append(("scalar", kv_evac_a))
                p.op("tensor", lambda e, fc=fc, hk=hk, bank=bank: e.matmul(
                    bank[:], wslice(hk, D + fc * 128, D + (fc + 1) * 128),
                    ybg[:, hk * 512:(hk + 1) * 512], start=(hk == 0), stop=(hk == HCH - 1)),
                     waits=wl, inc=(hk == HCH - 1))
            t_k = p.val("tensor")
            kv_evac_a = p.op("scalar", lambda e, fc=fc, grp=grp, bank=bank: e.activation(
                out=kbf[:, fc * tf + grp * 512: fc * tf + (grp + 1) * 512], in_=bank[:],
                func=AF.Identity, bias=bqk[:, 8 + fc: 9 + fc], scale=1.0),
                waits=[("tensor", t_k)], inc=True)
        for tl in range(4):
            for ft in range(2):
                bank = ps_kv[(tl * 2 + ft) % 2]
                for hk in range(HCH):
                    wl = []
                    if hk == 0:
                        wl = [("scalar", kv_evac_a), ("vector", kv_evac_v)]
                    p.op("tensor", lambda e, tl=tl, ft=ft, hk=hk, bank=bank: e.matmul(
                        bank[:], ybg[:, hk * 512 + tl * 128: hk * 512 + (tl + 1) * 128],
                        wslice(hk, 2048 + ft * 512, 2048 + (ft + 1) * 512),
                        start=(hk == 0), stop=False), waits=wl)
                t_v = p.op("tensor", lambda e, ft=ft, bank=bank: e.matmul(
                    bank[:], bones[0:1, :], bvrow[0:1, ft * 512:(ft + 1) * 512],
                    start=False, stop=True), inc=True)
                kv_evac_v = p.op("vector", lambda e, tl=tl, ft=ft, grp=grp, bank=bank: e.tensor_copy(
                    out=vbf[:, (grp * 4 + tl) * D + ft * 512: (grp * 4 + tl) * D + (ft + 1) * 512], in_=bank[:]),
                    waits=[("tensor", t_v)], inc=True)
        prev_kv = p.val("tensor")
    pass1_a = p.val("scalar")
    pass1_v = p.val("vector")

    wv_p = p.op("gpsimd", lambda e: e.dma_start(out=bprow[:], in_=b_proj),
                dma="wproj", waits=[("tensor", prev_kv)])
    for hk in range(HCH):
        wv_p = p.op("gpsimd", lambda e, hk=hk: e.dma_start(out=watk[:, hk * D:(hk + 1) * D],
                                                           in_=w_proj[hk * 128:(hk + 1) * 128, :]),
                    dma="wproj", waits=[("tensor", prev_kv)])

    ctx = dict(locals())
    emit_l0attn_tiles(p, ctx)
    p.op("sync", lambda e: e.nop(), waits=[("dma_stout", p.esem[p.dsem("stout")].n)])


def build_l0attn(tq=TPC, tf=T, newton=7):
    nc = bass.Bass("TRN2", target_bir_lowering=False, debug=False, name="l0attn")
    xf = nc.dram_tensor("xf", [tf, D], F32, kind="ExternalInput").ap()
    xq = nc.dram_tensor("xq", [tq, D], F32, kind="ExternalInput").ap()
    mq = nc.dram_tensor("mq", [128, 256], F32, kind="ExternalInput").ap()
    mqT = nc.dram_tensor("mqT", [128, 256], F32, kind="ExternalInput").ap()
    w_attn = nc.dram_tensor("w_attn", [D, 3072], F32, kind="ExternalInput").ap()
    b_qk = nc.dram_tensor("b_qk", [128, 16], F32, kind="ExternalInput").ap()
    b_v = nc.dram_tensor("b_v", [1, D], F32, kind="ExternalInput").ap()
    w_proj = nc.dram_tensor("w_proj", [D, D], F32, kind="ExternalInput").ap()
    b_proj = nc.dram_tensor("b_proj", [1, D], F32, kind="ExternalInput").ap()
    xo = nc.dram_tensor("xo", [tq, D], F32, kind="ExternalOutput").ap()

    p = Prog(nc)
    emit_l0attn(p, xf, xq, mq, mqT, w_attn, b_qk, b_v, w_proj, b_proj, xo, tq, tf, newton)
    p.finish()
    return nc


GP_DIV = True


def emit_l0attn_tiles(p, c):
    nqt = c["nqt"]; tf = c["tf"]; NH = c["NH"]; HCH = c["HCH"]
    xq = c["xq"]; xo = c["xo"]
    xt = c["xt"]; yt = c["yt"]; ln_bufs = c["ln_bufs"]
    ident = c["ident"]; fones = c["fones"]; bones = c["bones"]
    watq = c["watq"]; watk = c["watk"]; bqk = c["bqk"]; bprow = c["bprow"]
    kbf = c["kbf"]; vbf = c["vbf"]; qtile = c["qtile"]
    mqs = c["mqs"]; mqTs = c["mqTs"]
    zbuf = c["zbuf"]; abuf = c["abuf"]; ascr = c["ascr"]; outf = c["outf"]
    ntrow = c["ntrow"]; m_run = c["m_run"]; nscr = c["nscr"]; nscr2 = c["nscr2"]; outt = c["outt"]
    tau = c["tau"]; ntau = c["ntau"]; fcol = c["fcol"]; ccol = c["ccol"]
    fcol2 = c["fcol2"]; ccol2 = c["ccol2"]; rcol = c["rcol"]
    ps_kv = c["ps_kv"]; ps_z = c["ps_z"]; ps_zT = c["ps_zT"]; ps_o = c["ps_o"]; ps_r = c["ps_r"]
    wv_p = c["wv_p"]; pass1_a = c["pass1_a"]; pass1_v = c["pass1_v"]
    prev_kv = c["prev_kv"]

    def wslice(hk, f0, f1):
        if f1 <= D:
            return watq[:, hk * D + f0: hk * D + f1]
        return watk[:, hk * D + f0 - D: hk * D + f1 - D]

    prev_q = 0
    qev = [0, 0]
    m_guard = 0
    prev_lna = pass1_a
    prev_outp = pass1_v
    zevac_guard = 0
    ab_relu = [0, 0]
    ntrow_ev = 0
    outf_ev = 0
    prev_store = 0
    tr2_evac = [0, 0]
    last_o_ev = 0

    for i in range(nqt):
        nkb = 2 * i + 2
        W = nkb * 128
        ld = p.op("sync", lambda e, i=i: e.dma_start(out=xt[:], in_=xq[i * 128:(i + 1) * 128, :]),
                  dma="ldq", waits=[("vector", prev_outp), ("scalar", prev_lna), ("dma_stout", prev_store)])
        p.op("vector", lambda e: e.nop(), waits=[("dma_ldq", ld)])
        p.op("scalar", lambda e: e.nop(), waits=[("dma_ldq", ld)])
        ln_a = emit_ln_tile(p, ln_bufs, xt[:], yt[:], scr=nscr[:, 0:D])
        prev_lna = ln_a
        ybq = outf
        for hk in range(HCH):
            pt = ps_z[hk % 2][:, 0:128]
            t_tr = p.op("tensor", lambda e, hk=hk, pt=pt: e.transpose(pt, yt[:, hk * 128:(hk + 1) * 128], ident[:]),
                        waits=[("scalar", ln_a), ("vector", tr2_evac[hk % 2]), ("tensor", prev_kv)], inc=True)
            tr2_evac[hk % 2] = p.op("vector", lambda e, hk=hk, pt=pt: e.tensor_copy(
                out=ybq[:, hk * 128:(hk + 1) * 128], in_=pt), waits=[("tensor", t_tr)], inc=True)
        v_trq = p.val("vector")
        for fc in range(HCH):
            bank = ps_zT[fc % 2]
            for hk in range(HCH):
                wl = []
                if hk == 0:
                    wl = [("vector", v_trq), ("scalar", max(qev[fc % 2], ab_relu[0], ab_relu[1]))]
                p.op("tensor", lambda e, fc=fc, hk=hk, bank=bank, ybq=ybq: e.matmul(
                    bank[:], wslice(hk, fc * 128, (fc + 1) * 128), ybq[:, hk * 128:(hk + 1) * 128],
                    start=(hk == 0), stop=(hk == HCH - 1)), waits=wl, inc=(hk == HCH - 1))
            t_q = p.val("tensor")
            qev[fc % 2] = prev_q = p.op("scalar", lambda e, fc=fc, bank=bank: e.activation(
                out=qtile[:, fc * 128:(fc + 1) * 128], in_=bank[:], func=AF.Identity,
                bias=bqk[:, fc: fc + 1], scale=0.125),
                waits=[("tensor", t_q), ("vector", c["bq_scaled"])], inc=True)
        for h in range(NH):
            qs = qtile[(h % 2) * 64:(h % 2) * 64 + 64, (h // 2) * 128:(h // 2) * 128 + 128]
            nchunk = nkb // 2
            for ch in range(nchunk):
                bank = ps_z[ch % 2]
                t_z = p.op("tensor", lambda e, ch=ch, h=h, bank=bank, qs=qs: e.matmul(
                    bank[:, 0:256], qs,
                    kbf[(h % 2) * 64:(h % 2) * 64 + 64, (h // 2) * tf + ch * 256:(h // 2) * tf + (ch + 1) * 256],
                    start=True, stop=True),
                    waits=[("vector", tr2_evac[ch % 2]), ("scalar", max(prev_q, zevac_guard))], inc=True)
                if ch == nchunk - 1:
                    v1 = p.op("vector", lambda e, bank=bank: e.tensor_tensor(out=ascr[:, 0:256], in0=bank[:, 0:256], in1=mqs[:], op=ALU.add),
                              waits=[("tensor", t_z), ("scalar", zevac_guard)], inc=True)
                    tr2_evac[ch % 2] = p.op("vector", lambda e, ch=ch: e.tensor_scalar(
                        out=zbuf[:, ch * 256:(ch + 1) * 256], in0=ascr[:, 0:256], scalar1=0.0,
                        scalar2=(None if ch == 0 else m_run[:]), op0=ALU.add, op1=ALU.max, accum_out=m_run[:]),
                        waits=[("gpsimd", m_guard)], inc=True)
                else:
                    tr2_evac[ch % 2] = p.op("vector", lambda e, ch=ch, bank=bank: e.tensor_scalar(
                        out=zbuf[:, ch * 256:(ch + 1) * 256], in0=bank[:, 0:256], scalar1=0.0,
                        scalar2=(None if ch == 0 else m_run[:]), op0=ALU.add, op1=ALU.max, accum_out=m_run[:]),
                        waits=[("tensor", t_z)] + ([("gpsimd", m_guard)] if ch == 0 else []), inc=True)
            v_z = p.val("vector")
            g0 = p.op("gpsimd", lambda e: e.tensor_scalar(out=ntau[:], in0=m_run[:], scalar1=-1.0, scalar2=1.1,
                                                          op0=ALU.mult, op1=ALU.add),
                      waits=[("vector", v_z)], inc=True)
            g_nt = p.op("gpsimd", lambda e: e.tensor_scalar_mul(out=tau[:], in0=ntau[:], scalar1=-1.0), inc=True)
            m_guard = g_nt
            a_last = 0
            half_w = min(W, tf // 2) if tf >= 2 * D else W
            pieces = [(0, min(W, half_w))]
            if W > half_w:
                pieces.append((half_w, W))
            for it in range(7):
                for pi, (w0, w1) in enumerate(pieces):
                    a_f = p.op("scalar", lambda e, w0=w0, w1=w1, pi=pi: e.activation(
                        out=nscr[:, 0:w1 - w0], in_=zbuf[:, w0:w1], func=AF.Relu,
                        bias=ntau[:], accum_out=(fcol[:] if pi == 0 else fcol2[:])),
                        waits=[("gpsimd", g_nt)], inc=True)
                    v_c = p.op("vector", lambda e, w0=w0, w1=w1, pi=pi: e.tensor_scalar(
                        out=nscr2[:, w0:w1], in0=zbuf[:, w0:w1],
                        scalar1=tau[:], scalar2=0.0,
                        op0=ALU.is_gt, op1=ALU.add, accum_out=(ccol[:] if pi == 0 else ccol2[:])),
                        waits=[("gpsimd", g_nt)], inc=True)
                g_cmb = 0
                if len(pieces) > 1:
                    p.op("gpsimd", lambda e: e.tensor_tensor(out=fcol[:], in0=fcol[:], in1=fcol2[:], op=ALU.add),
                         waits=[("scalar", a_f), ("vector", v_c)], inc=True)
                    p.op("gpsimd", lambda e: e.tensor_tensor(out=ccol[:], in0=ccol[:], in1=ccol2[:], op=ALU.add),
                         inc=True)
                g_mx = p.op("gpsimd", lambda e: e.tensor_scalar_max(out=ccol[:], in0=ccol[:], scalar1=1.0),
                            waits=[("scalar", a_f), ("vector", v_c)], inc=True)
                v_r = p.op("vector", lambda e: e.reciprocal(out=rcol[:], in_=ccol[:]),
                           waits=[("gpsimd", g_mx)], inc=True)
                p.op("gpsimd", lambda e: e.tensor_scalar_add(out=fcol[:], in0=fcol[:], scalar1=-1.0),
                     waits=[("vector", v_r)], inc=True)
                p.op("gpsimd", lambda e: e.tensor_tensor(out=fcol[:], in0=fcol[:], in1=rcol[:], op=ALU.mult), inc=True)
                p.op("gpsimd", lambda e: e.tensor_tensor(out=tau[:], in0=tau[:], in1=fcol[:], op=ALU.add), inc=True)
                g_nt = p.op("gpsimd", lambda e: e.tensor_scalar_mul(out=ntau[:], in0=tau[:], scalar1=-1.0), inc=True)
                a_last = a_f
            g_fin = p.val("gpsimd")
            zevac_guard = a_last
            zevac_guard_v = p.val("vector")
            t_r = p.op("tensor", lambda e: e.matmul(ps_r[:], ntau[:], ident[:], start=True, stop=True),
                       waits=[("gpsimd", g_fin), ("scalar", ntrow_ev)], inc=True)
            ntrow_ev = p.op("scalar", lambda e: e.activation(out=ntrow[:], in_=ps_r[:], func=AF.Identity),
                            waits=[("tensor", t_r)], inc=True)
            for kb in range(nkb):
                bank = ps_zT[kb % 2]
                wl = [("scalar", max(ab_relu[kb % 2], ntrow_ev))]
                p.op("tensor", lambda e, kb=kb, h=h, bank=bank, qs=qs: e.matmul(
                    bank[:], kbf[(h % 2) * 64:(h % 2) * 64 + 64, (h // 2) * tf + kb * 128:(h // 2) * tf + (kb + 1) * 128],
                    qs, start=True, stop=False), waits=wl)
                t_zt = p.op("tensor", lambda e, bank=bank: e.matmul(
                    bank[:], fones[0:1, :], ntrow[0:1, :], start=False, stop=True), inc=True)
                if kb >= nkb - 2:
                    moff = (kb - (nkb - 2)) * 128
                    v_m = p.op("vector", lambda e, bank=bank, moff=moff: e.tensor_tensor(
                        out=ascr[:, 0:128], in0=bank[:], in1=mqTs[:, moff:moff + 128], op=ALU.add),
                        waits=[("tensor", t_zt)], inc=True)
                    ab_relu[kb % 2] = p.op("scalar", lambda e, kb=kb: e.activation(
                        out=abuf[kb % 2][:], in_=ascr[:, 0:128], func=AF.Relu), waits=[("vector", v_m)], inc=True)
                else:
                    ab_relu[kb % 2] = p.op("scalar", lambda e, kb=kb, bank=bank: e.activation(
                        out=abuf[kb % 2][:], in_=bank[:], func=AF.Relu), waits=[("tensor", t_zt)], inc=True)
                p.op("tensor", lambda e, kb=kb, h=h, nkb=nkb: e.matmul(
                    ps_o[:], vbf[:, kb * D + h * 64: kb * D + (h + 1) * 64], abuf[kb % 2][:],
                    start=(kb == 0), stop=(kb == nkb - 1)),
                     waits=[("scalar", ab_relu[kb % 2])] + ([("scalar", last_o_ev)] if kb == 0 else []),
                     inc=(kb == nkb - 1))
            t_av = p.val("tensor")
            last_o_ev = p.op("scalar", lambda e, h=h: e.activation(
                out=outf[(h % 2) * 64:(h % 2) * 64 + 64, (h // 2) * 128:(h // 2) * 128 + 128],
                in_=ps_o[:], func=AF.Identity), waits=[("tensor", t_av)], inc=True)
        outf_ev = last_o_ev
        for ft in range(2):
            bank = ps_z[ft]
            for hk in range(HCH):
                wl = []
                if hk == 0:
                    wl = [("scalar", outf_ev), ("vector", tr2_evac[0]), ("vector", tr2_evac[1]),
                          ("dma_wproj", wv_p)]
                p.op("tensor", lambda e, ft=ft, hk=hk, bank=bank: e.matmul(
                    bank[:], outf[:, hk * 128:(hk + 1) * 128],
                    wslice(hk, D + ft * 512, D + (ft + 1) * 512), start=(hk == 0), stop=False), waits=wl)
            t_op = p.op("tensor", lambda e, ft=ft, bank=bank: e.matmul(
                bank[:], bones[0:1, :], bprow[0:1, ft * 512:(ft + 1) * 512], start=False, stop=True), inc=True)
            prev_outp = p.op("vector", lambda e, ft=ft, bank=bank: e.tensor_tensor(
                out=outt[:, ft * 512:(ft + 1) * 512], in0=bank[:], in1=xt[:, ft * 512:(ft + 1) * 512], op=ALU.add),
                waits=[("tensor", t_op), ("dma_stout", prev_store)], inc=True)
        prev_store = p.op("sync", lambda e, i=i: e.dma_start(out=xo[i * 128:(i + 1) * 128, :], in_=outt[:]),
                          dma="stout", waits=[("vector", prev_outp)])


def prep_l0attn_weights(I):
    b_attn = I["L0_attn_b_attn"].astype(np.float32)
    b_qk = np.concatenate([
        b_attn[0:1024].reshape(8, 128).T,
        b_attn[1024:2048].reshape(8, 128).T], axis=1)
    return {
        "w_attn": np.ascontiguousarray(I["L0_attn_w_attn"]),
        "b_qk": np.ascontiguousarray(b_qk),
        "b_v": np.ascontiguousarray(b_attn[2048:3072].reshape(1, 1024)),
        "w_proj": np.ascontiguousarray(I["L0_attn_w_proj"]),
        "b_proj": np.ascontiguousarray(I["L0_attn_b_proj"].reshape(1, 1024)),
    }


def l0attn_masks(par):
    q = np.arange(128)
    tri = np.where(q[:, None] >= np.arange(128)[None, :], 0.0, -1e9).astype(np.float32)
    full = np.full((128, 128), -1e9, np.float32)
    zero = np.zeros((128, 128), np.float32)
    if par == 0:
        mq = np.concatenate([tri, full], axis=1)
        mqT = np.concatenate([tri.T, full], axis=1)
    else:
        mq = np.concatenate([zero, tri], axis=1)
        mqT = np.concatenate([zero, tri.T], axis=1)
    return np.ascontiguousarray(mq), np.ascontiguousarray(mqT)


_L0ATTN_NC = None


def run_l0attn(x, I):
    global _L0ATTN_NC
    if _L0ATTN_NC is None:
        _L0ATTN_NC = build_l0attn()
    w = prep_l0attn_weights(I)
    shards = shard_tokens(x)
    in_maps = []
    for c in range(NCORE):
        b, par = c // 2, c % 2
        mqa, mqTa = l0attn_masks(par)
        in_maps.append(dict(w, xf=np.ascontiguousarray(x[b].astype(np.float32)),
                            xq=shards[c], mq=mqa, mqT=mqTa))
    res = run_bass_kernel_spmd(_L0ATTN_NC, in_maps, core_ids=list(range(NCORE)))
    return unshard_tokens([res.results[c]["xo"] for c in range(NCORE)])



def emit_l1attn(p, xf, filt, gamma, beta, xo, tf=T):
    HCH = 8
    nft = tf // 128
    n1, n2 = tf // 2, tf // 4

    xt = p.sbuf("xt", [128, D], F32)
    yt = p.sbuf("yt", [128, D], F32)
    cols = p.sbuf("cols", [128, 16], F32)
    ln_bufs = {"ln_sum": cols[:, 0:1], "ln_ss": cols[:, 1:2], "ln_t1": cols[:, 2:3], "ln_t2": cols[:, 3:4]}
    ln_scr = p.sbuf("ln_scr", [128, D], F32)
    ident = p.sbuf("ident", [128, 128], F32)
    identb = p.sbuf("identb", [128, 128], BF16)
    fones = p.sbuf("fones", [1, 128], F32)
    filts = p.sbuf("filts", [128, 128], F32)
    grow = p.sbuf("grow", [1, D], F32)
    brow = p.sbuf("brow", [1, D], F32)
    gbc = p.sbuf("gbc", [128, D], F32)
    bbc = p.sbuf("bbc", [128, D], F32)
    ybf = p.sbuf("ybf", [128, HCH * tf], BF16)
    ypad = p.sbuf("ypad", [128, tf + 4], F32)
    l1pad = p.sbuf("l1pad", [128, n1 + 4], F32)
    h1 = p.sbuf("h1", [128, n1], F32)
    l2 = p.sbuf("l2", [128, n2], F32)
    h2 = p.sbuf("h2", [128, n2], F32)
    qef = p.sbuf("qef", [128, n2], F32)
    qet = p.sbuf("qet", [128, n2], F32)
    vt = p.sbuf("vt", [128, n2], F32)
    kvz = p.sbuf("kvz", [128, 65], F32)
    lowg = p.sbuf("lowg", [128, n2], F32)
    reppad = p.sbuf("reppad", [128, tf + 4], F32)
    low1pad = p.sbuf("low1pad", [128, n1 + 4], F32)
    low0 = p.sbuf("low0", [128, tf], F32)
    den = p.sbuf("den", [1, n2], F32)
    lowt = p.sbuf("lowt", [128, D], F32)
    ps_a = [p.psum(f"ps_a{i}", [128, 512], F32) for i in range(2)]
    ps_kv = p.psum("ps_kv", [64, 64], F32)
    ps_z = p.psum("ps_z", [64, 1], F32)
    ps_n = p.psum("ps_n", [65, 512], F32)
    ps_t = p.psum("ps_t", [128, 128], F32)
    ocol = cols[:, 15:16]

    g0 = p.op("gpsimd", lambda e: e.memset(ident[:], 0.0), inc=True)
    p.op("gpsimd", lambda e: e.affine_select(out=ident[:], in_=ident[:], compare_op=ALU.not_equal,
                                             fill=1.0, base=0, pattern=[[-1, 128]], channel_multiplier=1),
         inc=True, waits=[("gpsimd", g0)])
    p.op("gpsimd", lambda e: e.memset(fones[:], 1.0), inc=True)
    gi = p.op("gpsimd", lambda e: e.tensor_copy(out=identb[:], in_=ident[:]), inc=True)
    p.op("gpsimd", lambda e: e.memset(ocol[:], 1.0), inc=True)
    p.op("gpsimd", lambda e: e.memset(ypad[:], 0.0), inc=True)
    p.op("gpsimd", lambda e: e.memset(l1pad[:], 0.0), inc=True)
    p.op("gpsimd", lambda e: e.memset(reppad[:], 0.0), inc=True)
    g_const = p.op("gpsimd", lambda e: e.memset(low1pad[:], 0.0), inc=True)
    ld_c = p.op("sync", lambda e: e.dma_start(out=filts[:], in_=filt), dma="const")
    p.op("sync", lambda e: e.dma_start(out=grow[:], in_=gamma), dma="const")
    ld_c = p.op("sync", lambda e: e.dma_start(out=brow[:], in_=beta), dma="const")
    for half in range(2):
        bank = ps_a[half]
        t_g = p.op("tensor", lambda e, half=half, bank=bank: e.matmul(
            bank[:], fones[:], grow[0:1, half * 512:(half + 1) * 512], start=True, stop=True),
            waits=[("dma_const", ld_c), ("gpsimd", g_const)], inc=True)
        v_g = p.op("vector", lambda e, half=half, bank=bank: e.tensor_copy(out=gbc[:, half * 512:(half + 1) * 512], in_=bank[:]),
                   waits=[("tensor", t_g)], inc=True)
        t_b = p.op("tensor", lambda e, half=half, bank=bank: e.matmul(
            bank[:], fones[:], brow[0:1, half * 512:(half + 1) * 512], start=True, stop=True),
            waits=[("vector", v_g)], inc=True)
        v_b = p.op("vector", lambda e, half=half, bank=bank: e.tensor_copy(out=bbc[:, half * 512:(half + 1) * 512], in_=bank[:]),
                   waits=[("tensor", t_b)], inc=True)
    const_v = p.val("vector")

    prev_ln = 0
    tr_evac = [0, 0]
    for t in range(nft):
        ld = p.op("sync", lambda e, t=t: e.dma_start(out=xt[:], in_=xf[t * 128:(t + 1) * 128, :]),
                  dma="ldx", waits=[("scalar", prev_ln)])
        p.op("vector", lambda e: e.nop(), waits=[("dma_ldx", ld)])
        p.op("scalar", lambda e: e.nop(), waits=[("dma_ldx", ld)])
        prev_ln = emit_ln_tile(p, ln_bufs, xt[:], yt[:], scr=ln_scr[:])
        for hk in range(HCH):
            pt = ps_a[hk % 2][:, 0:128]
            t_tr = p.op("tensor", lambda e, pt=pt, hk=hk: e.transpose(pt, yt[:, hk * 128:(hk + 1) * 128], ident[:]),
                        waits=[("scalar", prev_ln), ("vector", max(tr_evac[hk % 2], const_v))], inc=True)
            tr_evac[hk % 2] = p.op("vector", lambda e, pt=pt, hk=hk, t=t: e.tensor_copy(
                out=ybf[:, hk * tf + t * 128: hk * tf + (t + 1) * 128], in_=pt), waits=[("tensor", t_tr)], inc=True)
    passA_v = p.val("vector")

    def fcol_ap(f, g, k):
        return filts[:, f * 32 + g * 4 + k: f * 32 + g * 4 + k + 1]

    def conv_strided(dst_ap, src, n_out, f, g, first_waits=()):
        for k in range(4):
            if k == 0:
                p.op("vector", lambda e, k=k: e.tensor_scalar(
                    out=dst_ap, in0=src[:, k: k + 2 * n_out: 2], scalar1=fcol_ap(f, g, k), scalar2=None,
                    op0=ALU.mult), waits=list(first_waits), inc=True)
            else:
                p.op("vector", lambda e, k=k: e.scalar_tensor_tensor(
                    out=dst_ap, in0=src[:, k: k + 2 * n_out: 2], scalar=fcol_ap(f, g, k), in1=dst_ap,
                    op0=ALU.mult, op1=ALU.add), inc=True)
        return p.val("vector")

    def conv_unit(dst_ap, src, n_out, f, g, first_waits=(), accumulate=False):
        for k in range(4):
            if k == 0 and not accumulate:
                p.op("vector", lambda e, k=k: e.tensor_scalar(
                    out=dst_ap, in0=src[:, k: k + n_out], scalar1=fcol_ap(f, g, k), scalar2=None,
                    op0=ALU.mult), waits=list(first_waits), inc=True)
            else:
                p.op("vector", lambda e, k=k: e.scalar_tensor_tensor(
                    out=dst_ap, in0=src[:, k: k + n_out], scalar=fcol_ap(f, g, k), in1=dst_ap,
                    op0=ALU.mult, op1=ALU.add), inc=True)
        return p.val("vector")

    low0_done = []
    ps_n_free = 0
    for g in range(HCH):
        v_y = p.op("vector", lambda e, g=g: e.tensor_copy(out=ypad[:, 1: 1 + tf], in_=ybf[:, g * tf:(g + 1) * tf]),
                   waits=[("vector", passA_v), ("dma_const", ld_c)], inc=True)
        conv_strided(l1pad[:, 1: 1 + n1], ypad, n1, 0, g)
        v_c0 = conv_strided(h1[:, 0:n1], ypad, n1, 1, g)
        conv_strided(l2[:, 0:n2], l1pad, n2, 0, g)
        v_c1 = conv_strided(h2[:, 0:n2], l1pad, n2, 1, g)
        v_m = p.op("vector", lambda e: e.tensor_scalar_min(out=qef[:], in0=l2[:], scalar1=0.0), inc=True)
        a_e = p.op("scalar", lambda e: e.activation(out=qef[:], in_=qef[:], func=AF.Exp),
                   waits=[("vector", v_m)], inc=True)
        v_r = p.op("vector", lambda e: e.tensor_scalar_max(out=ln_scr[:, 0:n2], in0=l2[:], scalar1=0.0),
                   waits=[("scalar", a_e)], inc=True)
        v_q = p.op("vector", lambda e: e.tensor_add(out=qef[:], in0=qef[:], in1=ln_scr[:, 0:n2]), inc=True)
        tr_ev = 0
        for chk in range(n2 // 128):
            pt = ps_a[chk % 2][:, 0:128]
            t_t1 = p.op("tensor", lambda e, chk=chk, pt=pt: e.transpose(pt, qef[:, chk * 128:(chk + 1) * 128], ident[:]),
                        waits=[("vector", v_q), ("vector", tr_ev)], inc=True)
            tr_ev = p.op("vector", lambda e, chk=chk, pt=pt: e.tensor_copy(out=qet[:, chk * 128:(chk + 1) * 128], in_=pt),
                         waits=[("tensor", t_t1)], inc=True)
            t_t2 = p.op("tensor", lambda e, chk=chk, pt=pt: e.transpose(pt, l2[:, chk * 128:(chk + 1) * 128], ident[:]),
                        waits=[("vector", tr_ev)], inc=True)
            tr_ev = p.op("vector", lambda e, chk=chk, pt=pt: e.tensor_copy(out=vt[:, chk * 128:(chk + 1) * 128], in_=pt),
                         waits=[("tensor", t_t2)], inc=True)
        for h in range(2):
            t_kv = t_z = 0
            for chk in range(n2 // 128):
                wl = [("vector", tr_ev)] if chk == 0 else []
                last = chk == n2 // 128 - 1
                t_kv = p.op("tensor", lambda e, chk=chk, h=h, last=last: e.matmul(
                    ps_kv[:], qet[0:128, chk * 128 + h * 64: chk * 128 + h * 64 + 64],
                    vt[0:128, chk * 128 + h * 64: chk * 128 + h * 64 + 64],
                    start=(chk == 0), stop=last), waits=wl, inc=last)
                t_z = p.op("tensor", lambda e, chk=chk, h=h, last=last: e.matmul(
                    ps_z[:], qet[0:128, chk * 128 + h * 64: chk * 128 + h * 64 + 64], ocol[:],
                    start=(chk == 0), stop=last), inc=last)
            v_kv = p.op("vector", lambda e, h=h: e.tensor_copy(out=kvz[h * 64:h * 64 + 64, 0:64], in_=ps_kv[:]),
                        waits=[("tensor", t_kv)], inc=True)
            v_kz = p.op("vector", lambda e, h=h: e.tensor_copy(out=kvz[h * 64:h * 64 + 64, 64:65], in_=ps_z[:]),
                        waits=[("tensor", t_z)], inc=True)
            for nh in range(n2 // 512):
                t_n = p.op("tensor", lambda e, nh=nh, h=h: e.matmul(
                    ps_n[:], kvz[h * 64:h * 64 + 64, 0:65], qef[h * 64:(h + 1) * 64, nh * 512:(nh + 1) * 512],
                    start=True, stop=True), waits=[("vector", v_kz), ("vector", ps_n_free)], inc=True)
                v_d = p.op("vector", lambda e, nh=nh: e.tensor_scalar_add(out=den[0:1, nh * 512:(nh + 1) * 512],
                                                                          in0=ps_n[64:65, :], scalar1=1e-6),
                           waits=[("tensor", t_n)], inc=True)
                v_dr = p.op("vector", lambda e, nh=nh: e.reciprocal(out=den[0:1, nh * 512:(nh + 1) * 512],
                                                                    in_=den[0:1, nh * 512:(nh + 1) * 512]),
                            waits=[("vector", v_d)], inc=True)
                t_bc = p.op("tensor", lambda e, nh=nh: e.matmul(ps_a[0][:, 0:512], fones[:],
                                                                den[0:1, nh * 512:(nh + 1) * 512], start=True, stop=True),
                            waits=[("vector", v_dr)], inc=True)
                v_rcb = p.op("vector", lambda e: e.tensor_copy(out=ln_scr[0:64, 0:512], in_=ps_a[0][0:64, 0:512]),
                             waits=[("tensor", t_bc)], inc=True)
                ps_n_free = v_low = p.op("vector", lambda e, nh=nh, h=h: e.tensor_tensor(
                    out=lowg[h * 64:(h + 1) * 64, nh * 512:(nh + 1) * 512], in0=ps_n[0:64, :],
                    in1=ln_scr[0:64, 0:512], op=ALU.mult), waits=[("vector", v_rcb)], inc=True)
        v_attn = p.val("vector")
        p.op("vector", lambda e: e.memset(reppad[:, 1 + n1: 4 + n1], 0.0), waits=[("vector", v_attn)], inc=True)
        v_rp1 = p.op("vector", lambda e: e.tensor_copy(out=reppad[:, 1: 1 + n1: 2], in_=lowg[:, 0:n2]),
                     inc=True)
        v_rp2 = p.op("vector", lambda e: e.tensor_copy(out=reppad[:, 2: 2 + n1: 2], in_=lowg[:, 0:n2]), inc=True)
        conv_unit(low1pad[:, 1: 1 + n1], reppad, n1, 2, g, first_waits=[("vector", v_rp2)])
        v_rp3 = p.op("vector", lambda e: e.tensor_copy(out=reppad[:, 1: 1 + n1: 2], in_=h2[:, 0:n2]), inc=True)
        v_rp4 = p.op("vector", lambda e: e.tensor_copy(out=reppad[:, 2: 2 + n1: 2], in_=h2[:, 0:n2]), inc=True)
        v_l1 = conv_unit(low1pad[:, 1: 1 + n1], reppad, n1, 3, g, first_waits=[("vector", v_rp4)], accumulate=True)
        v_rp5 = p.op("vector", lambda e: e.tensor_copy(out=reppad[:, 1: 1 + tf: 2], in_=low1pad[:, 1: 1 + n1]),
                     waits=[("vector", v_l1)], inc=True)
        v_rp6 = p.op("vector", lambda e: e.tensor_copy(out=reppad[:, 2: 2 + tf: 2], in_=low1pad[:, 1: 1 + n1]), inc=True)
        conv_unit(low0[:, 0:tf], reppad, tf, 2, g, first_waits=[("vector", v_rp6)])
        v_rp7 = p.op("vector", lambda e: e.tensor_copy(out=reppad[:, 1: 1 + tf: 2], in_=h1[:, 0:n1]), inc=True)
        v_rp8 = p.op("vector", lambda e: e.tensor_copy(out=reppad[:, 2: 2 + tf: 2], in_=h1[:, 0:n1]), inc=True)
        v_l0 = conv_unit(low0[:, 0:tf], reppad, tf, 3, g, first_waits=[("vector", v_rp8)], accumulate=True)
        v_ov = p.op("vector", lambda e, g=g: e.tensor_copy(out=ybf[:, g * tf:(g + 1) * tf], in_=low0[:, 0:tf]),
                    waits=[("vector", v_l0)], inc=True)
        low0_done.append(v_ov)
    passB_v = p.val("vector")

    prev_st = 0
    tr_ev2 = [0, 0]
    prev_fin = passB_v
    for t in range(nft):
        ld = p.op("sync", lambda e, t=t: e.dma_start(out=xt[:], in_=xf[t * 128:(t + 1) * 128, :]),
                  dma="ldx", waits=[("scalar", prev_ln), ("vector", prev_fin), ("dma_stout", prev_st)])
        p.op("vector", lambda e: e.nop(), waits=[("dma_ldx", ld)])
        p.op("scalar", lambda e: e.nop(), waits=[("dma_ldx", ld)])
        prev_ln = emit_ln_tile(p, ln_bufs, xt[:], yt[:], scr=ln_scr[:])
        for hk in range(HCH):
            t_tr = p.op("tensor", lambda e, hk=hk, t=t: e.matmul(
                ps_t[:], ybf[:, hk * tf + t * 128: hk * tf + (t + 1) * 128], identb[:],
                start=True, stop=True),
                waits=[("vector", max(tr_ev2[hk % 2], passB_v)), ("scalar", prev_ln),
                       ("vector", tr_ev2[(hk + 1) % 2])], inc=True)
            tr_ev2[hk % 2] = p.op("vector", lambda e, hk=hk: e.tensor_copy(
                out=lowt[:, hk * 128:(hk + 1) * 128], in_=ps_t[:]), waits=[("tensor", t_tr)], inc=True)
        v_xl = p.op("vector", lambda e: e.tensor_add(out=yt[:], in0=yt[:], in1=lowt[:]),
                    waits=[("vector", tr_ev2[0]), ("vector", tr_ev2[1])], inc=True)
        ln2 = emit_ln_tile(p, ln_bufs, yt[:], lowt[:], scr=ln_scr[:], pre_v=v_xl)
        v_f1 = p.op("vector", lambda e: e.tensor_mul(out=lowt[:], in0=lowt[:], in1=gbc[:]),
                    waits=[("scalar", ln2)], inc=True)
        v_f2 = p.op("vector", lambda e: e.tensor_add(out=lowt[:], in0=lowt[:], in1=bbc[:]), inc=True)
        prev_fin = p.op("vector", lambda e: e.tensor_add(out=yt[:], in0=lowt[:], in1=xt[:]), inc=True)
        prev_st = p.op("sync", lambda e, t=t: e.dma_start(out=xo[t * 128:(t + 1) * 128, :], in_=yt[:]),
                       dma="stout", waits=[("vector", prev_fin)])
    p.op("sync", lambda e: e.nop(), waits=[("dma_stout", p.esem[p.dsem("stout")].n)])


def build_l1attn(tf=T):
    nc = bass.Bass("TRN2", target_bir_lowering=False, debug=False, name="l1attn")
    xf = nc.dram_tensor("xf", [tf, D], F32, kind="ExternalInput").ap()
    filt = nc.dram_tensor("filt", [128, 128], F32, kind="ExternalInput").ap()
    gamma = nc.dram_tensor("gamma", [1, D], F32, kind="ExternalInput").ap()
    beta = nc.dram_tensor("beta", [1, D], F32, kind="ExternalInput").ap()
    xo = nc.dram_tensor("xo", [tf, D], F32, kind="ExternalOutput").ap()

    p = Prog(nc)
    emit_l1attn(p, xf, filt, gamma, beta, xo, tf)
    p.finish()
    return nc


def prep_l1attn_weights(I):
    fs = []
    for nm in ("L1_attn_h_filter", "L1_attn_g_filter", "L1_attn_h_synth", "L1_attn_g_synth"):
        f = I[nm].astype(np.float32)
        fs.append(f.reshape(4, 8, 128).transpose(2, 1, 0).reshape(128, 32))
    filt = np.ascontiguousarray(np.concatenate(fs, axis=1))
    return {
        "filt": filt,
        "gamma": np.ascontiguousarray(I["L1_attn_norm_gamma"].reshape(1, D)),
        "beta": np.ascontiguousarray(I["L1_attn_norm_beta"].reshape(1, D)),
    }


_L1ATTN_NC = None


def run_l1attn(x, I):
    global _L1ATTN_NC
    if _L1ATTN_NC is None:
        _L1ATTN_NC = build_l1attn()
    w = prep_l1attn_weights(I)
    in_maps = [dict(w, xf=np.ascontiguousarray(x[c // 2].astype(np.float32))) for c in range(NCORE)]
    res = run_bass_kernel_spmd(_L1ATTN_NC, in_maps, core_ids=list(range(NCORE)))
    out = np.empty((B, T, D), np.float32)
    for b in range(B):
        out[b] = res.results[2 * b]["xo"]
    return out


def emit_l1ffn(p, xf, wqkv, lncol, rt4, qcn, wq_in, wkv_in, wout_in, wlo_in, nks, nvs,
               rotm, cossin, wmaskin, mlocQ_in, mlocKT_in, xo,
               s_hbm, yf_hbm, yr_hbm, ql_hbm, kl_hbm, vl_hbm, qg_hbm, kg_hbm, vg_hbm, tf=T):
    import math as _math
    HCH = 8
    nft = tf // 128
    NW = tf // 512
    NWIN = NW - 1
    NCTX = tf - 512
    NKB = NCTX // 128
    K_Q, K_KV = 256, 1024
    logk_q = float(_math.log(min(256 * 9.0 / 8.0, 512.0)))
    logk_kv = float(_math.log(min(1024 * 9.0 / 8.0, float(NCTX))))
    EPSL = [max(4.0 * 0.7 ** i, 0.03) for i in range(20)]
    FE = 0.03

    xt = p.sbuf("xt", [128, D], F32)
    yt = p.sbuf("yt", [128, D], F32)
    yr = p.sbuf("yr", [128, D], F32)
    cols = p.sbuf("cols", [128, 32], F32)
    ln_bufs = {"ln_sum": cols[:, 0:1], "ln_ss": cols[:, 1:2], "ln_t1": cols[:, 2:3], "ln_t2": cols[:, 3:4]}
    fac = cols[:, 4:5]
    ifac = cols[:, 5:6]
    ccol = cols[:, 6:7]
    ocol = cols[:, 8:9]
    ln_scr = p.sbuf("ln_scr", [128, D], F32)
    ident = p.sbuf("ident", [128, 128], F32)
    identb = p.sbuf("identb", [128, 128], BF16)
    fones = p.sbuf("fones", [1, 128], F32)
    bones = p.sbuf("bones", [1, 128], BF16)
    yf32 = p.sbuf("yf32", [128, HCH * 128], F32)
    lnc = p.sbuf("lnc", [128, 16], F32)
    lnbb = p.sbuf("lnbb", [128, 8], BF16)
    rts = p.sbuf("rts", [128, 32], F32)
    qcns = p.sbuf("qcns", [128, 32], F32)
    nkss = p.sbuf("nkss", [128, 32], BF16)
    rotms = p.sbuf("rotms", [128, 128], BF16)
    selbT = p.sbuf("selbT", [128, NKB * 16], F32)
    qselT = p.sbuf("qselT", [128, 4 * 16], F32)
    accw = p.sbuf("accw", [128, 4 * 1024], F32)
    qkvb = p.sbuf("qkvb", [128, 24], F32)
    nmrow = p.sbuf("nmrow", [1, 512], F32)
    enull = p.sbuf("enull", [1, 512], F32)
    denr = p.sbuf("denr", [1, 512], F32)
    mvals = p.sbuf("mvals", [128, 16 * 4], F32)
    mlocQs = p.sbuf("mlocQs", [128, 11 * 128], F32)
    mlocKTs = p.sbuf("mlocKTs", [128, 11 * 128], F32)
    bocol = p.sbuf("bocol", [128, 1], BF16)
    AF32 = p.sbuf("AF32", [128, 14 * 1024], F32)
    AB16 = p.sbuf("AB16", [128, 45 * 1024], BF16)

    ps_t = [p.psum(f"ps_t{i}", [128, 512], F32) for i in range(2)]
    ps_u = [p.psum(f"ps_u{i}", [128, 512], F32) for i in range(2)]
    ps_o = p.psum("ps_o", [64, 512], F32)
    ps_d = p.psum("ps_d", [1, 512], F32)
    ps_r = p.psum("ps_r", [16, 512], F32)
    ps_s = p.psum("ps_s", [128, 512], F32)

    def barrier(extra_dma=()):
        vals = [(en, p.val(en)) for en in ("sync", "tensor", "vector", "scalar", "gpsimd")]
        dvals = [(p.dsem(nm), p.esem[p.dsem(nm)].n) for nm in extra_dma]
        dvals = [(k, v) for k, v in dvals]
        for en in ("sync", "tensor", "vector", "scalar", "gpsimd"):
            wl = [(e2, v) for e2, v in vals if e2 != en] + [(k, v) for k, v in dvals]
            p.op(en, lambda e: e.nop(), waits=wl, inc=True, chain=True)

    g0 = p.op("gpsimd", lambda e: e.memset(ident[:], 0.0), inc=True)
    p.op("gpsimd", lambda e: e.affine_select(out=ident[:], in_=ident[:], compare_op=ALU.not_equal,
                                             fill=1.0, base=0, pattern=[[-1, 128]], channel_multiplier=1),
         inc=True, waits=[("gpsimd", g0)])
    p.op("gpsimd", lambda e: e.tensor_copy(out=identb[:], in_=ident[:]), inc=True)
    p.op("gpsimd", lambda e: e.memset(fones[:], 1.0), inc=True)
    p.op("gpsimd", lambda e: e.memset(bones[:], 1.0), inc=True)
    p.op("gpsimd", lambda e: e.memset(ocol[:], 1.0), inc=True)
    p.op("gpsimd", lambda e: e.memset(bocol[:], 1.0), inc=True)
    g_const = p.op("gpsimd", lambda e: e.memset(cols[:, 20:21], -20.0), inc=True)
    ldc = 0
    for nm, dst in ((lncol, lnc), (rt4, rts), (qcn, qcns),
                    (mlocQ_in, mlocQs), (mlocKT_in, mlocKTs)):
        ldc = p.op("sync", lambda e, nm=nm, dst=dst: e.dma_start(out=dst[:], in_=nm), dma="const")
    p.op("gpsimd", lambda e: e.dma_start(out=rotms[:], in_=rotm), dma="gconst")
    gldc = p.op("gpsimd", lambda e: e.dma_start(out=nkss[:], in_=nks), dma="gconst")
    p.op("gpsimd", lambda e: e.tensor_copy(out=lnbb[:], in_=lnc[:, 8:16]),
         waits=[("dma_const", ldc)], inc=True)

    pbuf = AB16[:, 0:2048]
    srow = AF32[:, 0:4096]
    prev_ln = 0
    tr_evac = [0, 0]
    st_g = 0
    st_g2 = 0
    prev_srow = 0
    for t in range(nft):
        ld = p.op("sync", lambda e, t=t: e.dma_start(out=xt[:], in_=xf[t * 128:(t + 1) * 128, :]),
                  dma="ldx", waits=[("scalar", prev_ln)])
        p.op("vector", lambda e: e.nop(), waits=[("dma_ldx", ld)])
        p.op("scalar", lambda e: e.nop(), waits=[("dma_ldx", ld)])
        prev_ln = emit_ln_tile(p, ln_bufs, xt[:], yt[:], scr=ln_scr[:])
        v1 = p.op("vector", lambda e: e.tensor_mul(out=fac[:], in0=ln_bufs["ln_t1"][:], in1=ln_bufs["ln_t1"][:]),
                  waits=[("scalar", prev_ln)], inc=True)
        v2 = p.op("vector", lambda e: e.tensor_scalar(out=fac[:], in0=fac[:], scalar1=-LN_EPS * D, scalar2=float(D),
                                                      op0=ALU.mult, op1=ALU.add), inc=True)
        a_s = p.op("scalar", lambda e: e.activation(out=fac[:], in_=fac[:], func=AF.Sqrt), waits=[("vector", v2)], inc=True)
        p.op("vector", lambda e: e.tensor_scalar_max(out=fac[:], in0=fac[:], scalar1=1e-12),
             waits=[("scalar", a_s)], inc=True)
        p.op("vector", lambda e: e.tensor_scalar_mul(out=ifac[:], in0=fac[:], scalar1=1.0 / 32.0), inc=True)
        p.op("vector", lambda e: e.reciprocal(out=fac[:], in_=fac[:]), inc=True)
        v6 = p.op("vector", lambda e: e.tensor_scalar_mul(out=fac[:], in0=fac[:], scalar1=32.0), inc=True)
        a_yr = p.op("scalar", lambda e: e.activation(out=yr[:], in_=yt[:], func=AF.Identity, scale=fac[:]),
                    waits=[("vector", v6)], inc=True)
        for hk in range(HCH):
            pt = ps_t[hk % 2][:, 0:128]
            t_tr = p.op("tensor", lambda e, hk=hk, pt=pt: e.transpose(pt, yr[:, hk * 128:(hk + 1) * 128], ident[:]),
                        waits=[("scalar", a_yr), ("vector", tr_evac[hk % 2]), ("gpsimd", g_const)], inc=True)
            p.op("vector", lambda e, hk=hk, pt=pt: e.tensor_copy(out=yf32[:, hk * 128:(hk + 1) * 128], in_=pt),
                 waits=[("tensor", t_tr), ("dma_styr", st_g)], inc=True)
            tr_evac[hk % 2] = p.op("vector", lambda e, hk=hk: e.tensor_copy(
                out=pbuf[:, hk * 128:(hk + 1) * 128], in_=yf32[:, hk * 128:(hk + 1) * 128]), inc=True)
        v_tr = p.val("vector")
        st_g = p.op("gpsimd", lambda e, t=t: e.dma_start(
            out=yr_hbm.ap().rearrange("p (hc tt) -> p hc tt", hc=8, tt=tf)[:, :, t * 128:(t + 1) * 128],
            in_=pbuf[:, 0:1024].rearrange("p (hc c) -> p hc c", hc=8, c=128)), dma="styr",
            waits=[("vector", v_tr)])
        for hk in range(HCH):
            p.op("tensor", lambda e, hk=hk: e.matmul(ps_s[:, 0:4],
                                                     yf32[:, hk * 128:(hk + 1) * 128], rts[:, hk * 4:(hk + 1) * 4],
                                                     start=(hk == 0), stop=(hk == HCH - 1)),
                 waits=([("vector", v_tr), ("dma_const", ldc), ("vector", prev_srow)] if hk == 0 else []),
                 inc=(hk == HCH - 1))
        t_s = p.val("tensor")
        v_s = p.op("vector", lambda e: e.tensor_scalar(out=cols[:, 12:16], in0=ps_s[:, 0:4], scalar1=ifac[:],
                                                       scalar2=None, op0=ALU.mult), waits=[("tensor", t_s)], inc=True)
        t_st = p.op("tensor", lambda e: e.matmul(ps_r[0:4, 0:128], cols[:, 12:16], ident[:], start=True, stop=True),
                    waits=[("vector", v_s)], inc=True)
        prev_srow = p.op("vector", lambda e, t=t: e.tensor_copy(out=srow[0:4, t * 128:(t + 1) * 128], in_=ps_r[0:4, 0:128]),
                         waits=[("tensor", t_st)], inc=True)
        for hk in range(HCH):
            pt = ps_t[hk % 2][:, 0:128]
            t_tr = p.op("tensor", lambda e, hk=hk, pt=pt: e.transpose(pt, yt[:, hk * 128:(hk + 1) * 128], ident[:]),
                        waits=[("vector", tr_evac[hk % 2])], inc=True)
            tr_evac[hk % 2] = p.op("vector", lambda e, hk=hk, pt=pt: e.tensor_copy(
                out=pbuf[:, 1024 + hk * 128: 1024 + (hk + 1) * 128], in_=pt),
                waits=[("tensor", t_tr), ("dma_styf", st_g2)], inc=True)
        v_tr2 = p.val("vector")
        st_g2 = p.op("gpsimd", lambda e, t=t: e.dma_start(
            out=yf_hbm.ap().rearrange("p (hc tt) -> p hc tt", hc=8, tt=tf)[:, :, t * 128:(t + 1) * 128],
            in_=pbuf[:, 1024:2048].rearrange("p (hc c) -> p hc c", hc=8, c=128)), dma="styf",
            waits=[("vector", v_tr2)])
    st_s = p.op("sync", lambda e: e.dma_start(out=s_hbm[:, :], in_=srow[0:4, 0:tf]), dma="sts",
                waits=[("vector", p.val("vector"))])
    barrier(extra_dma=("styr", "styf", "sts", "const", "gconst"))

    emit_l1ffn_routers(p, locals())
    barrier()
    emit_l1ffn_proj(p, locals())
    barrier(extra_dma=("stql", "stkl", "stvl", "stqg", "stkg", "stvg", "ldyf0", "ldyf1", "ldcs"))
    emit_l1ffn_attn(p, locals())
    p.op("sync", lambda e: e.nop(), waits=[("dma_stout", p.esem[p.dsem("stout")].n)])


def build_l1ffn(tf=T):
    import math as _math
    nc = bass.Bass("TRN2", target_bir_lowering=False, debug=False, name="l1ffn")
    API = lambda n, s: nc.dram_tensor(n, s, F32, kind="ExternalInput").ap()
    xf = API("xf", [tf, D])
    wqkv = API("wqkv", [D, 3072])
    lncol = API("lncol", [128, 16])
    rt4 = API("rt4", [128, 32])
    qcn = API("qcn", [128, 32])
    wq_in = API("wq", [2, D, 1024])
    wkv_in = API("wkv", [2, D, 2048])
    wout_in = API("wout", [2, D, 1024])
    wlo_in = API("wlo", [D, 1024])
    nks = API("nks", [128, 32])
    nvs = API("nvs", [1, 2048])
    rotm = API("rotm", [128, 128])
    cossin = API("cossin", [128, 2 * tf])
    wmaskin = API("wmaskin", [16, tf - 512])
    mlocQ_in = API("mlocQ", [128, 11 * 128])
    mlocKT_in = API("mlocKT", [128, 11 * 128])
    xo = nc.dram_tensor("xo", [tf, D], F32, kind="ExternalOutput").ap()
    s_hbm = nc.dram_tensor("s_hbm", [4, tf], F32)
    yf_hbm = nc.dram_tensor("yf_hbm", [128, 8 * tf], BF16)
    yr_hbm = nc.dram_tensor("yr_hbm", [128, 8 * tf], BF16)
    ql_hbm = nc.dram_tensor("ql_hbm", [128, 8 * tf], BF16)
    kl_hbm = nc.dram_tensor("kl_hbm", [128, 8 * tf], BF16)
    vl_hbm = nc.dram_tensor("vl_hbm", [tf, D], BF16)
    qg_hbm = nc.dram_tensor("qg_hbm", [2, 128, 8 * tf], BF16)
    kg_hbm = nc.dram_tensor("kg_hbm", [2, 128, 8 * tf], BF16)
    vg_hbm = nc.dram_tensor("vg_hbm", [2, tf, D], BF16)

    p = Prog(nc)
    emit_l1ffn(p, xf, wqkv, lncol, rt4, qcn, wq_in, wkv_in, wout_in, wlo_in, nks, nvs,
               rotm, cossin, wmaskin, mlocQ_in, mlocKT_in, xo,
               s_hbm, yf_hbm, yr_hbm, ql_hbm, kl_hbm, vl_hbm, qg_hbm, kg_hbm, vg_hbm, tf)
    p.finish()
    return nc


def emit_l1ffn_routers(p, c):
    NCTX = c["NCTX"]; NKB = c["NKB"]; NWIN = c["NWIN"]
    AF32 = c["AF32"]; r16 = None
    cols = c["cols"]; ident = c["ident"]; s_hbm = c["s_hbm"]
    wmaskin = c["wmaskin"]; selbT = c["selbT"]; qselT = c["qselT"]
    ps_s = c["ps_s"]
    K_Q, K_KV = c["K_Q"], c["K_KV"]
    logk_q, logk_kv = c["logk_q"], c["logk_kv"]
    EPSL = c["EPSL"]
    ALUx = ALU

    r16b = p.sbuf("r16b", [16, 16], F32)

    def run_router(N, K, logk, src_rows, masked, sel_out_fn, start_t=0, start_v=0):
        S = AF32[0:16, 0:N]
        BP = AF32[0:16, N:2 * N]
        CU = AF32[0:16, 2 * N:3 * N]
        ONE = AF32[0:16, 3 * N:4 * N]
        a_col = r16b[:, 0:1]; m_col = r16b[:, 1:2]; nm_col = r16b[:, 2:3]
        ssum = r16b[:, 3:4]; lo = r16b[:, 4:5]; hi = r16b[:, 5:6]
        mid = r16b[:, 6:7]; cnt = r16b[:, 7:8]; cge = r16b[:, 8:9]
        dcol = r16b[:, 9:10]; room = r16b[:, 10:11]
        ldr = 0
        if masked:
            ldr = p.op("sync", lambda e: e.dma_start(out=BP[:], in_=wmaskin), dma="ldr",
                       waits=[("tensor", start_t), ("vector", start_v)])
        for r in range(16):
            ldr = p.op("sync", lambda e, r=r, ap=src_rows[r]: e.dma_start(out=S[r:r + 1, :], in_=ap),
                       dma="ldr", waits=[("tensor", start_t), ("vector", start_v)])
        v0 = p.op("vector", lambda e: e.memset(ONE[:], 1.0), waits=[("dma_ldr", ldr)], inc=True)
        if masked:
            v0 = p.op("vector", lambda e: e.tensor_add(out=S[:], in0=S[:], in1=BP[:]), inc=True)
        p.op("vector", lambda e: e.tensor_copy(out=BP[:], in_=S[:]), inc=True)
        a_last = 0
        for i, ce in enumerate(EPSL):
            p.op("vector", lambda e: e.tensor_sub(out=CU[:], in0=S[:], in1=BP[:]),
                 waits=[("scalar", a_last)], inc=True)
            p.op("vector", lambda e, ce=ce: e.tensor_scalar_mul(out=CU[:], in0=CU[:], scalar1=1.0 / ce), inc=True)
            p.op("vector", lambda e: e.tensor_reduce(out=m_col[:], in_=CU[:], axis=AX.X, op=ALUx.max), inc=True)
            v_m = p.op("vector", lambda e: e.tensor_scalar_mul(out=nm_col[:], in0=m_col[:], scalar1=-1.0), inc=True)
            a_e = p.op("scalar", lambda e: e.activation(out=CU[:], in_=CU[:], func=AF.Exp, bias=nm_col[:],
                                                        accum_out=ssum[:]), waits=[("vector", v_m)], inc=True)
            a_l = p.op("scalar", lambda e: e.activation(out=ssum[:], in_=ssum[:], func=AF.Ln), inc=True)
            v_a = p.op("vector", lambda e: e.tensor_add(out=a_col[:], in0=ssum[:], in1=m_col[:]),
                       waits=[("scalar", a_l)], inc=True)
            v_a2 = p.op("vector", lambda e, ce=ce, logk=logk: e.tensor_scalar(
                out=a_col[:], in0=a_col[:], scalar1=-ce, scalar2=ce * logk, op0=ALUx.mult, op1=ALUx.add), inc=True)
            a_last = p.op("scalar", lambda e: e.activation(out=BP[:], in_=S[:], func=AF.Relu, bias=a_col[:]),
                          waits=[("vector", v_a2)], inc=True)
        v_u = p.op("vector", lambda e: e.tensor_scalar(out=CU[:], in0=S[:], scalar1=a_col[:], scalar2=0.0,
                                                       op0=ALUx.add, op1=ALUx.min),
                   waits=[("scalar", a_last)], inc=True)
        p.op("vector", lambda e: e.tensor_scalar(out=BP[:], in0=CU[:], scalar1=-3.1192, scalar2=0.0,
                                                 op0=ALUx.is_gt, op1=ALUx.add), inc=True)
        p.op("vector", lambda e: e.tensor_scalar_add(out=CU[:], in0=CU[:], scalar1=10.0), inc=True)
        p.op("vector", lambda e: e.tensor_mul(out=CU[:], in0=CU[:], in1=BP[:]), inc=True)
        p.op("vector", lambda e: e.tensor_scalar_add(out=CU[:], in0=CU[:], scalar1=-10.0), inc=True)
        p.op("vector", lambda e: e.memset(lo[:], -11.0), inc=True)
        p.op("vector", lambda e: e.memset(hi[:], 1.0), inc=True)
        for it in range(50):
            p.op("vector", lambda e: e.tensor_add(out=mid[:], in0=lo[:], in1=hi[:]), inc=True)
            vm = p.op("vector", lambda e: e.tensor_scalar_mul(out=mid[:], in0=mid[:], scalar1=0.5), inc=True)
            vc = p.op("vector", lambda e: e.tensor_scalar(out=BP[:], in0=CU[:], scalar1=mid[:], scalar2=0.0,
                                                          op0=ALUx.is_gt, op1=ALUx.add, accum_out=cnt[:]),
                      waits=[("vector", vm)], inc=True)
            p.op("vector", lambda e, K=K: e.tensor_scalar(out=cge[:], in0=cnt[:], scalar1=float(K), scalar2=None,
                                                          op0=ALUx.is_ge), waits=[("vector", vc)], inc=True)
            p.op("vector", lambda e: e.tensor_sub(out=dcol[:], in0=mid[:], in1=lo[:]), inc=True)
            p.op("vector", lambda e: e.tensor_mul(out=dcol[:], in0=dcol[:], in1=cge[:]), inc=True)
            p.op("vector", lambda e: e.tensor_add(out=lo[:], in0=lo[:], in1=dcol[:]), inc=True)
            p.op("vector", lambda e: e.tensor_sub(out=dcol[:], in0=mid[:], in1=hi[:]), inc=True)
            p.op("vector", lambda e: e.tensor_add(out=hi[:], in0=hi[:], in1=dcol[:]), inc=True)
            p.op("vector", lambda e: e.tensor_mul(out=dcol[:], in0=dcol[:], in1=cge[:]), inc=True)
            p.op("vector", lambda e: e.tensor_sub(out=hi[:], in0=hi[:], in1=dcol[:]), inc=True)
        vd = p.op("vector", lambda e: e.tensor_scalar(out=S[:], in0=CU[:], scalar1=hi[:], scalar2=0.0,
                                                      op0=ALUx.is_gt, op1=ALUx.add, accum_out=cnt[:]), inc=True)
        p.op("vector", lambda e, K=K: e.tensor_scalar(out=room[:], in0=cnt[:], scalar1=-1.0, scalar2=float(K),
                                                      op0=ALUx.mult, op1=ALUx.add), inc=True)
        p.op("vector", lambda e: e.tensor_scalar(out=BP[:], in0=CU[:], scalar1=lo[:], scalar2=None,
                                                 op0=ALUx.is_gt), inc=True)
        p.op("vector", lambda e: e.tensor_sub(out=BP[:], in0=BP[:], in1=S[:]), inc=True)
        p.op("vector", lambda e: e.tensor_tensor_scan(out=CU[:], data0=ONE[:], data1=BP[:], initial=0.0,
                                                      op0=ALUx.mult, op1=ALUx.add), inc=True)
        p.op("vector", lambda e: e.tensor_scalar(out=CU[:], in0=CU[:], scalar1=room[:], scalar2=None,
                                                 op0=ALUx.is_le), inc=True)
        p.op("vector", lambda e: e.tensor_mul(out=CU[:], in0=CU[:], in1=BP[:]), inc=True)
        v_sel = p.op("vector", lambda e: e.tensor_add(out=S[:], in0=S[:], in1=CU[:]), inc=True)
        sel_out_fn(S, v_sel)

    kv_rows = []
    for r in range(16):
        g = r // NWIN if r < 2 * NWIN else 0
        kv_rows.append(s_hbm[2 + g: 3 + g, 0:NCTX])

    def kv_out(S, v_sel):
        p.op("vector", lambda e: e.tensor_scalar_add(out=S[:], in0=S[:], scalar1=-1.0), inc=True)
        p.op("vector", lambda e: e.tensor_scalar_mul(out=S[:], in0=S[:], scalar1=1e9), inc=True)
        vs = p.op("vector", lambda e: e.tensor_scalar_add(out=S[:], in0=S[:], scalar1=-20.0), inc=True)
        ev = 0
        for kb in range(NKB):
            t_t = p.op("tensor", lambda e, kb=kb, S=S: e.matmul(ps_s[:, 0:16], S[0:16, kb * 128:(kb + 1) * 128],
                                                                ident[0:16, 0:16], start=True, stop=True),
                       waits=[("vector", vs if kb == 0 else ev)], inc=True)
            ev = p.op("vector", lambda e, kb=kb: e.tensor_copy(out=selbT[:, kb * 16:(kb + 1) * 16], in_=ps_s[:, 0:16]),
                      waits=[("tensor", t_t)], inc=True)

    run_router(NCTX, K_KV, logk_kv, kv_rows, True, kv_out)

    q_rows = []
    for r in range(16):
        if r < 2 * NWIN:
            g, wd = r // NWIN, r % NWIN
            q_rows.append(s_hbm[g: g + 1, (wd + 1) * 512: (wd + 2) * 512])
        else:
            q_rows.append(s_hbm[0:1, 0:512])

    def q_out(S, v_sel):
        ev = 0
        for qb in range(4):
            t_t = p.op("tensor", lambda e, qb=qb, S=S: e.matmul(ps_s[:, 0:16], S[0:16, qb * 128:(qb + 1) * 128],
                                                                ident[0:16, 0:16], start=True, stop=True),
                       waits=[("vector", v_sel if qb == 0 else ev)], inc=True)
            ev = p.op("vector", lambda e, qb=qb: e.tensor_copy(out=qselT[:, qb * 16:(qb + 1) * 16], in_=ps_s[:, 0:16]),
                      waits=[("tensor", t_t)], inc=True)

    run_router(512, K_Q, logk_q, q_rows, False, q_out,
               start_t=p.val("tensor"), start_v=p.val("vector"))


def emit_l1ffn_proj(p, c):
    tf = c["tf"]; HCH = c["HCH"]; NW = c["NW"]
    AB16 = c["AB16"]; AF32 = c["AF32"]
    wqkv = c["wqkv"]; wq_in = c["wq_in"]; wkv_in = c["wkv_in"]
    lnc = c["lnc"]; lnbb = c["lnbb"]; qcns = c["qcns"]; qkvb = c["qkvb"]
    yf_hbm = c["yf_hbm"]; yr_hbm = c["yr_hbm"]
    ql_hbm = c["ql_hbm"]; kl_hbm = c["kl_hbm"]; vl_hbm = c["vl_hbm"]
    qg_hbm = c["qg_hbm"]; kg_hbm = c["kg_hbm"]; vg_hbm = c["vg_hbm"]
    rotms = c["rotms"]; cossin = c["cossin"] if "cossin" in c else None
    fones = c["fones"]; bones = c["bones"]
    ps_t = c["ps_t"]; ps_u = c["ps_u"]; ps_d = c["ps_d"]
    D_ = D

    WQKV = AB16[:, 0:24576]
    YB = [AB16[:, 24576 + i * 4096: 24576 + (i + 1) * 4096] for i in range(2)]
    PB = AB16[:, 32768 + 0: 32768 + 4096]
    RB = AB16[:, 36864: 36864 + 4096]
    WKVG = AB16[:, 0: 16384]
    WQG = AB16[:, 16384: 24576]
    CSB = AB16[:, 40960: 40960 + 1024]
    RSC = AB16[:, 41984: 41984 + 512]
    qb_row = AF32[0:1, 0:3072]

    wld = 0
    for hk in range(HCH):
        wld = p.op("gpsimd", lambda e, hk=hk: e.dma_start(out=WQKV[:, hk * 3072:(hk + 1) * 3072],
                                                          in_=wqkv[hk * 128:(hk + 1) * 128, :]), dma="wld")
    t_b = 0
    v_bev = 0
    for ftile in range(6):
        for hk in range(HCH):
            wl = []
            if hk == 0:
                wl = ([("dma_wld", wld)] if ftile == 0 else []) + [("vector", v_bev)]
            p.op("tensor", lambda e, ftile=ftile, hk=hk: e.matmul(
                ps_d[:], lnbb[:, hk: hk + 1], WQKV[:, hk * 3072 + ftile * 512: hk * 3072 + (ftile + 1) * 512],
                start=(hk == 0), stop=(hk == HCH - 1)),
                 waits=wl, inc=(hk == HCH - 1))
        t_b = p.val("tensor")
        v_bev = p.op("vector", lambda e, ftile=ftile: e.tensor_copy(out=qb_row[0:1, ftile * 512:(ftile + 1) * 512], in_=ps_d[:]),
                     waits=[("tensor", t_b)], inc=True)
    ev = 0
    for j in range(24):
        t_t = p.op("tensor", lambda e, j=j: e.matmul(ps_u[0][:, 0:1], qb_row[0:1, j * 128:(j + 1) * 128],
                                                     fones[0:1, 0:1], start=True, stop=True),
                   waits=[("vector", ev if j > 0 else p.val("vector"))], inc=True)
        ev = p.op("vector", lambda e, j=j: e.tensor_copy(out=qkvb[:, j:j + 1], in_=ps_u[0][:, 0:1]),
                  waits=[("tensor", t_t)], inc=True)
    v_qb = p.op("vector", lambda e: e.tensor_scalar_mul(out=qkvb[:, 0:8], in0=qkvb[:, 0:8], scalar1=0.125), inc=True)
    v_w = 0
    for hk in range(HCH):
        v_w = p.op("vector", lambda e, hk=hk: e.tensor_scalar(out=WQKV[:, hk * 3072:(hk + 1) * 3072],
                                                              in0=WQKV[:, hk * 3072:(hk + 1) * 3072],
                                                              scalar1=lnc[:, hk:hk + 1], scalar2=None, op0=ALU.mult),
                   waits=[("tensor", t_b)], inc=True)

    st_q = st_k = st_v = 0
    ld_y = 0
    prev_rope_v = 0
    prev_ev = [0, 0]
    for w in range(NW):
        yb = YB[w % 2]
        ld_y = p.op("gpsimd", lambda e, w=w, yb=yb: e.dma_start(
            out=yb.rearrange("p (hc t) -> p hc t", hc=8, t=512),
            in_=yf_hbm.ap().rearrange("p (hc tt) -> p hc tt", hc=8, tt=tf)[:, :, w * 512:(w + 1) * 512]),
            dma="ldyf%d" % (w % 2), waits=[("tensor", prev_ev[w % 2])])
        ldy_nm = "dma_ldyf%d" % (w % 2)
        ld_cs = p.op("gpsimd", lambda e, w=w: e.dma_start(
            out=CSB.rearrange("p (a t) -> p a t", a=2, t=512),
            in_=cossin.rearrange("p (a tt) -> p a tt", a=2, tt=tf)[:, :, w * 512:(w + 1) * 512]),
            dma="ldcs", waits=[("vector", prev_rope_v)])
        for fc in range(16):
            bank = ps_t[fc % 2]
            for hk in range(HCH):
                wl = []
                if hk == 0:
                    wl = [(ldy_nm, ld_y), ("vector", v_w if fc < 2 else p.val("vector")),
                          ("scalar", p.val("scalar"))]
                p.op("tensor", lambda e, fc=fc, hk=hk, bank=bank, yb=yb: e.matmul(
                    bank[:], WQKV[:, hk * 3072 + fc * 128: hk * 3072 + (fc + 1) * 128],
                    yb[:, hk * 512:(hk + 1) * 512], start=(hk == 0), stop=(hk == HCH - 1)), waits=wl,
                    inc=(hk == HCH - 1))
            t_p = p.val("tensor")
            if fc < 8:
                a_e = p.op("scalar", lambda e, fc=fc, bank=bank: e.activation(
                    out=PB[:, fc * 512:(fc + 1) * 512], in_=bank[:], func=AF.Identity,
                    bias=qkvb[:, fc:fc + 1], scale=0.125),
                    waits=[("tensor", t_p)] + ([("dma_stql", st_q), ("dma_stvl", st_v)] if fc == 0 else []), inc=True)
            else:
                a_e = p.op("scalar", lambda e, fc=fc, bank=bank: e.activation(
                    out=RB[:, (fc - 8) * 512:(fc - 7) * 512], in_=bank[:], func=AF.Identity,
                    bias=qkvb[:, fc:fc + 1], scale=1.0),
                    waits=[("tensor", t_p)] + ([("dma_stkl", st_k)] if fc == 8 else []), inc=True)
        a_qk = p.val("scalar")
        for src_buf, dst_hbm, st_prev, nm in ((PB, ql_hbm, st_q, "stql"), (RB, kl_hbm, st_k, "stkl")):
            for fc in range(HCH):
                bank = ps_u[fc % 2]
                t_r = p.op("tensor", lambda e, fc=fc, bank=bank, src_buf=src_buf: e.matmul(
                    bank[:], rotms[:], src_buf[:, fc * 512:(fc + 1) * 512], start=True, stop=True),
                    waits=[("scalar", a_qk), ("vector", p.val("vector"))], inc=True)
                v_0 = p.op("vector", lambda e, bank=bank: e.tensor_copy(out=RSC[:], in_=bank[:]),
                           waits=[("tensor", t_r)], inc=True)
                v_1 = p.op("vector", lambda e, fc=fc, src_buf=src_buf: e.tensor_tensor(
                    out=src_buf[:, fc * 512:(fc + 1) * 512], in0=src_buf[:, fc * 512:(fc + 1) * 512],
                    in1=CSB[:, 0:512], op=ALU.mult), waits=[("dma_ldcs", ld_cs)], inc=True)
                v_2 = p.op("vector", lambda e: e.tensor_tensor(
                    out=RSC[:], in0=RSC[:], in1=CSB[:, 512:1024], op=ALU.mult), inc=True)
                v_3 = p.op("vector", lambda e, fc=fc, src_buf=src_buf: e.tensor_add(
                    out=src_buf[:, fc * 512:(fc + 1) * 512], in0=src_buf[:, fc * 512:(fc + 1) * 512],
                    in1=RSC[:]), inc=True)
            v_rope = prev_rope_v = p.val("vector")
            stv = p.op("gpsimd", lambda e, w=w, src_buf=src_buf, dst_hbm=dst_hbm: e.dma_start(
                out=dst_hbm.ap().rearrange("p (hc tt) -> p hc tt", hc=8, tt=tf)[:, :, w * 512:(w + 1) * 512],
                in_=src_buf.rearrange("p (hc t) -> p hc t", hc=8, t=512)), dma=nm,
                waits=[("vector", v_rope)])
            if nm == "stql":
                st_q = stv
            else:
                st_k = stv
        for sub in range(4):
            for ft in range(2):
                bank = ps_t[(sub * 2 + ft) % 2]
                for hk in range(HCH):
                    wl = [("vector", p.val("vector"))] if hk == 0 else []
                    p.op("tensor", lambda e, sub=sub, ft=ft, hk=hk, bank=bank, yb=yb: e.matmul(
                        bank[:], yb[:, hk * 512 + sub * 128: hk * 512 + (sub + 1) * 128],
                        WQKV[:, hk * 3072 + 2048 + ft * 512: hk * 3072 + 2048 + (ft + 1) * 512],
                        start=(hk == 0), stop=False), waits=wl)
                t_v = p.op("tensor", lambda e, ft=ft, bank=bank: e.matmul(
                    bank[:], fones[0:1, :], qb_row[0:1, 2048 + ft * 512: 2048 + (ft + 1) * 512],
                    start=False, stop=True), inc=True)
                v_e = p.op("vector", lambda e, sub=sub, ft=ft, bank=bank: e.tensor_copy(
                    out=PB[:, sub * 1024 + ft * 512: sub * 1024 + (ft + 1) * 512], in_=bank[:]),
                    waits=[("tensor", t_v), ("dma_stvl", st_v), ("dma_stql", st_q)], inc=True)
        v_vv = p.val("vector")
        st_v = p.op("gpsimd", lambda e, w=w: e.dma_start(
            out=vl_hbm[w * 512:(w + 1) * 512, :].rearrange("(s q) d -> q s d", q=128, s=4),
            in_=PB[:, 0:4096].rearrange("p (s d) -> p s d", s=4, d=1024)), dma="stvl",
            waits=[("vector", v_vv)])
        prev_ev[w % 2] = p.val("tensor")

    c["barrier"](extra_dma=("stql", "stkl", "stvl", "ldcs", "ldyf0", "ldyf1"))
    st_qg = st_kg = st_vg = 0
    for g in range(2):
        wld2 = 0
        for hk in range(HCH):
            wld2 = p.op("gpsimd", lambda e, g=g, hk=hk: e.dma_start(
                out=WKVG[:, hk * 2048:(hk + 1) * 2048], in_=wkv_in[g, hk * 128:(hk + 1) * 128, :]),
                dma="wld2", waits=([("tensor", p.val("tensor"))] if (g > 0 and hk == 0) else []))
        for hk in range(HCH):
            wld2 = p.op("gpsimd", lambda e, g=g, hk=hk: e.dma_start(
                out=WQG[:, hk * 1024:(hk + 1) * 1024], in_=wq_in[g, hk * 128:(hk + 1) * 128, :]), dma="wld2")
        v_w2 = 0
        for hk in range(HCH):
            p.op("vector", lambda e, g=g, hk=hk: e.tensor_scalar(
                out=WKVG[:, hk * 2048:(hk + 1) * 2048], in0=WKVG[:, hk * 2048:(hk + 1) * 2048],
                scalar1=qcns[:, hk * 4 + 2 + g: hk * 4 + 3 + g], scalar2=None, op0=ALU.mult),
                 waits=[("dma_wld2", wld2)], inc=True)
            v_w2 = p.op("vector", lambda e, g=g, hk=hk: e.tensor_scalar(
                out=WQG[:, hk * 1024:(hk + 1) * 1024], in0=WQG[:, hk * 1024:(hk + 1) * 1024],
                scalar1=qcns[:, hk * 4 + g: hk * 4 + 1 + g], scalar2=None, op0=ALU.mult), inc=True)
        for w in range(NW):
            yb = YB[w % 2]
            ld_y = p.op("gpsimd", lambda e, w=w, g=g, yb=yb: e.dma_start(
                out=yb.rearrange("p (hc t) -> p hc t", hc=8, t=512),
                in_=yr_hbm.ap().rearrange("p (hc tt) -> p hc tt", hc=8, tt=tf)[:, :, w * 512:(w + 1) * 512]),
                dma="ldyf%d" % (w % 2), waits=[("tensor", prev_ev[w % 2])])
            ldy_nm = "dma_ldyf%d" % (w % 2)
            for fc in range(16):
                bank = ps_t[fc % 2]
                wsrc = WQG if fc < 8 else WKVG
                fcc = fc if fc < 8 else fc - 8
                wslc = (lambda hk, fcc=fcc, wsrc=wsrc: wsrc[:, hk * (1024 if wsrc is WQG else 2048) + fcc * 128:
                                                           hk * (1024 if wsrc is WQG else 2048) + (fcc + 1) * 128])
                for hk in range(HCH):
                    wl = []
                    if hk == 0:
                        wl = [(ldy_nm, ld_y), ("vector", v_w2 if fc < 2 else p.val("vector")),
                              ("scalar", p.val("scalar"))]
                    p.op("tensor", lambda e, fc=fc, hk=hk, bank=bank, yb=yb, wslc=wslc: e.matmul(
                        bank[:], wslc(hk), yb[:, hk * 512:(hk + 1) * 512],
                        start=(hk == 0), stop=(hk == HCH - 1)), waits=wl, inc=(hk == HCH - 1))
                t_p = p.val("tensor")
                wl2 = [("tensor", t_p)]
                if fc == 0:
                    wl2.append(("dma_stqg", st_qg))
                    wl2.append(("dma_stkg", st_kg))
                if fc == 8:
                    wl2.append(("dma_stkg", st_kg))
                    wl2.append(("dma_stqg", st_qg))
                a_e = p.op("scalar", lambda e, fc=fc, bank=bank: e.activation(
                    out=PB[:, (fc % 8) * 512:((fc % 8) + 1) * 512], in_=bank[:], func=AF.Identity,
                    scale=(0.125 if fc < 8 else 1.0)), waits=wl2, inc=True)
                if fc == 7 or fc == 15:
                    a_d = p.val("scalar")
                    hbm = qg_hbm if fc == 7 else kg_hbm
                    nm = "stqg" if fc == 7 else "stkg"
                    stx = p.op("gpsimd", lambda e, w=w, g=g, hbm=hbm: e.dma_start(
                        out=hbm.ap()[g].rearrange("p (hc tt) -> p hc tt", hc=8, tt=tf)[:, :, w * 512:(w + 1) * 512],
                        in_=PB[:, 0:4096].rearrange("p (hc t) -> p hc t", hc=8, t=512)), dma=nm,
                        waits=[("scalar", a_d)])
                    if fc == 7:
                        st_qg = stx
                    else:
                        st_kg = stx
            for sub in range(4):
                for ft in range(2):
                    bank = ps_t[(sub * 2 + ft) % 2]
                    for hk in range(HCH):
                        wl = [("vector", p.val("vector")), ("scalar", p.val("scalar"))] if hk == 0 else []
                        p.op("tensor", lambda e, sub=sub, ft=ft, hk=hk, bank=bank, yb=yb: e.matmul(
                            bank[:], yb[:, hk * 512 + sub * 128: hk * 512 + (sub + 1) * 128],
                            WKVG[:, hk * 2048 + 1024 + ft * 512: hk * 2048 + 1024 + (ft + 1) * 512],
                            start=(hk == 0), stop=(hk == HCH - 1)), waits=wl, inc=(hk == HCH - 1))
                    t_v = p.val("tensor")
                    v_e = p.op("vector", lambda e, sub=sub, ft=ft, bank=bank: e.tensor_copy(
                        out=RB[:, sub * 1024 + ft * 512: sub * 1024 + (ft + 1) * 512], in_=bank[:]),
                        waits=[("tensor", t_v), ("dma_stvg", st_vg)], inc=True)
            v_vv = p.val("vector")
            st_vg = p.op("gpsimd", lambda e, w=w, g=g: e.dma_start(
                out=vg_hbm.ap()[g][w * 512:(w + 1) * 512, :].rearrange("(s q) d -> q s d", q=128, s=4),
                in_=RB[:, 0:4096].rearrange("p (s d) -> p s d", s=4, d=1024)), dma="stvg",
                waits=[("vector", v_vv)])
            prev_ev[w % 2] = p.val("tensor")


def emit_l1ffn_attn(p, c):
    tf = c["tf"]; NW = c["NW"]; NWIN = c["NWIN"]
    AB16 = c["AB16"]; AF32 = c["AF32"]
    xf = c["xf"]; xo = c["xo"]
    ql_hbm = c["ql_hbm"]; kl_hbm = c["kl_hbm"]; vl_hbm = c["vl_hbm"]
    qg_hbm = c["qg_hbm"]; kg_hbm = c["kg_hbm"]; vg_hbm = c["vg_hbm"]
    wlo_in = c["wlo_in"]; wout_in = c["wout_in"]
    fones = c["fones"]; ident = c["ident"]; bocol = c["bocol"]
    mlocQs = c["mlocQs"]; mlocKTs = c["mlocKTs"]
    selbT = c["selbT"]; qselT = c["qselT"]; accw = c["accw"]
    nkss = c["nkss"]; nvs_in = c["nvs"]
    nmrow = c["nmrow"]; enull = c["enull"]; denr = c["denr"]; mvals = c["mvals"]
    cols = c["cols"]; xt = c["xt"]; yr = c["yr"]; yf32 = c["yf32"]; ln_scr = c["ln_scr"]
    ps_t = c["ps_t"]; ps_u = c["ps_u"]; ps_o = c["ps_o"]; ps_d = c["ps_d"]
    ps_r = c["ps_r"]; ps_s = c["ps_s"]

    KW = [AB16[:, 0:4096], AB16[:, 4096:8192]]
    VW = [AB16[:, 8192:12288], AB16[:, 12288:16384]]
    QWL = AB16[:, 16384:20480]
    QWG = AB16[:, 20480:24576]
    WPR = AB16[:, 24576:32768]
    KCH = [AB16[:, 24576:28672], AB16[:, 28672:32768]]
    VCH = [AB16[:, 32768:36864], AB16[:, 36864:40960]]
    OT = AB16[:, 40960:45056]
    ABF = [AB16[:, 45056:45568], AB16[:, 45568:46080]]
    oacc = AF32[:, 0:4096]
    dacc = AF32[0:1, 4096:12288]
    nvss = AF32[0:1, 12288:14336]
    ascr = [yf32[:, 0:512], ln_scr[:, 0:512]]
    recb = yf32[:, 512:1024]
    stg2 = ln_scr[:, 512:1024]
    cntc = cols[:, 18:19]
    rc05 = cols[:, 19:20]
    m20 = cols[0:1, 20:21]

    def fm_load(dst, hbm_ap, wdx):
        return lambda e: e.dma_start(
            out=dst.rearrange("p (hc t) -> p hc t", hc=8, t=512),
            in_=hbm_ap.rearrange("p (hc tt) -> p hc tt", hc=8, tt=tf)[:, :, wdx * 512:(wdx + 1) * 512])

    def tm_load(dst, hbm_rows):
        return lambda e: e.dma_start(
            out=dst.rearrange("p (s d) -> p s d", s=4, d=1024),
            in_=hbm_rows.rearrange("(s q) d -> q s d", q=128, s=4))

    p.op("sync", lambda e: e.dma_start(out=nvss, in_=nvs_in), dma="lnv")
    st_out = 0
    wpr_read = 0
    ot_read = 0
    qwl_read = 0
    qwg_read = 0
    kwread = [0, 0]
    vwread = [0, 0]
    kchread = [0, 0]
    vchread = [0, 0]
    mval_tread = 0
    abf_g = [0, 0]
    pu_g = [("vector", 0), ("vector", 0)]
    pt_g = [0, 0]
    asc_g = [0, 0]
    a_oc_prev = 0
    v_de_prev = 0
    v_rb_prev = 0
    t_br_prev = 0
    t_nv_prev = 0
    v_dn_prev = 0
    a_fin_prev = 0
    psr_guard = ("vector", 0)
    vfin_g = [0, 0]
    v_comb = 0

    for w in range(NW):
        cur = w % 2
        prv = 1 - cur
        lq_t = p.op("sync", fm_load(QWL, ql_hbm.ap(), w), dma="lq", waits=[("tensor", qwl_read)])
        lk_t = p.op("sync", fm_load(KW[cur], kl_hbm.ap(), w), dma="lk%d" % cur,
                    waits=[("tensor", kwread[cur])])
        lv_t = p.op("sync", tm_load(VW[cur], vl_hbm[w * 512:(w + 1) * 512, :]), dma="lv%d" % cur,
                    waits=[("tensor", vwread[cur])])
        ld_waits = [("dma_lq", lq_t), ("dma_lk%d" % cur, lk_t), ("dma_lv%d" % cur, lv_t)]
        wld = 0
        for j in range(8):
            wld = p.op("gpsimd", lambda e, j=j: e.dma_start(
                out=WPR[:, j * 1024:(j + 1) * 1024], in_=wlo_in[j * 128:(j + 1) * 128, :]),
                dma="wst", waits=([("tensor", wpr_read)] if j == 0 else []))
        wld_local = wld

        nch = 2 if w > 0 else 1
        zi = 0
        first_acc = True
        for h in range(16):
            b = (h % 2) * 64
            hk = h // 2
            for qt in range(4):
                lhs = QWL[b:b + 64, hk * 512 + qt * 128: hk * 512 + (qt + 1) * 128]
                mcol = mvals[:, h * 4 + qt: h * 4 + qt + 1]
                for ci in range(nch):
                    iscur = 1 if (w == 0 or ci == 1) else 0
                    kbuf = KW[cur] if iscur else KW[prv]
                    bank = ps_t[zi % 2]
                    t_z = p.op("tensor", lambda e, lhs=lhs, kbuf=kbuf, bank=bank, b=b, hk=hk: e.matmul(
                        bank[:], lhs, kbuf[b:b + 64, hk * 512:(hk + 1) * 512], start=True, stop=True),
                        waits=[("vector", pt_g[zi % 2])] + (ld_waits if zi < 2 else []), inc=True)
                    m0 = 3 + 4 * iscur - qt
                    p.op("vector", lambda e, bank=bank, m0=m0: e.tensor_tensor(
                        out=stg2[:], in0=bank[:], in1=mlocQs[:, m0 * 128: m0 * 128 + 512], op=ALU.add),
                        waits=[("tensor", t_z)] + ([("scalar", p.val("scalar"))] if first_acc else []),
                        inc=True)
                    p.op("vector", lambda e, mcol=mcol, ci=ci: e.tensor_scalar(
                        out=stg2[:], in0=stg2[:], scalar1=0.0,
                        scalar2=(None if ci == 0 else mcol), op0=ALU.add, op1=ALU.max, accum_out=mcol),
                        waits=([("tensor", mval_tread)] if first_acc else []), inc=True)
                    first_acc = False
                    pt_g[zi % 2] = p.val("vector")
                    zi += 1
        v_neg = p.op("vector", lambda e: e.tensor_scalar_mul(out=mvals[:, 0:64], in0=mvals[:, 0:64],
                                                             scalar1=-1.0), inc=True)
        vfin_g = [v_neg, v_neg]

        kbs = list(range(4, 8)) if w == 0 else list(range(8))
        fkb, lkb = kbs[0], kbs[-1]
        k1_last = 0
        nm_ev = 0
        for h in range(16):
            b = (h % 2) * 64
            hk = h // 2
            qsl = QWL[b:b + 64, hk * 512:(hk + 1) * 512]
            for qt in range(4):
                wl = [("vector", v_neg if (h == 0 and qt == 0) else nm_ev)]
                if h == 0 and qt == 0:
                    wl.append(psr_guard)
                t_t = p.op("tensor", lambda e, h=h, qt=qt: e.matmul(
                    ps_r[0:1, 0:128], mvals[:, h * 4 + qt: h * 4 + qt + 1], ident[:],
                    start=True, stop=True), waits=wl, inc=True)
                nm_ev = p.op("vector", lambda e, qt=qt: e.tensor_copy(
                    out=nmrow[0:1, qt * 128:(qt + 1) * 128], in_=ps_r[0:1, 0:128]),
                    waits=[("tensor", t_t)] + ([("tensor", k1_last)] if qt == 0 else []), inc=True)
            mval_tread = p.val("tensor")
            psr_guard = ("vector", nm_ev)
            for kb in kbs:
                iscur = kb >= 4
                kbuf = KW[cur] if iscur else KW[prv]
                vbuf = VW[cur] if iscur else VW[prv]
                kbr = kb % 4
                bank = ps_u[kb % 2]
                wl = [pu_g[kb % 2]]
                if kb == fkb:
                    wl.append(("vector", nm_ev))
                p.op("tensor", lambda e, kbuf=kbuf, b=b, hk=hk, kbr=kbr, bank=bank, qsl=qsl: e.matmul(
                    bank[:], kbuf[b:b + 64, hk * 512 + kbr * 128: hk * 512 + (kbr + 1) * 128],
                    qsl, start=True, stop=False), waits=wl)
                k1_last = p.op("tensor", lambda e, bank=bank: e.matmul(
                    bank[:], fones[0:1, 0:128], nmrow[0:1, 0:512], start=False, stop=True), inc=True)
                n0 = 7 - kb
                v_mk = p.op("vector", lambda e, bank=bank, n0=n0, kb=kb: e.tensor_tensor(
                    out=ascr[kb % 2], in0=bank[:], in1=mlocKTs[:, n0 * 128: n0 * 128 + 512], op=ALU.add),
                    waits=[("tensor", k1_last), ("scalar", asc_g[kb % 2])], inc=True)
                pu_g[kb % 2] = ("vector", v_mk)
                a_ex = p.op("scalar", lambda e, kb=kb: e.activation(
                    out=ABF[kb % 2][:], in_=ascr[kb % 2], func=AF.Exp),
                    waits=[("vector", v_mk), ("tensor", abf_g[kb % 2])], inc=True)
                asc_g[kb % 2] = a_ex
                p.op("tensor", lambda e, vbuf=vbuf, kbr=kbr, h=h, kb=kb, fkb=fkb, lkb=lkb: e.matmul(
                    ps_o[:], vbuf[:, kbr * 1024 + h * 64: kbr * 1024 + (h + 1) * 64], ABF[kb % 2][:],
                    start=(kb == fkb), stop=(kb == lkb)),
                    waits=[("scalar", a_ex)] + ([("scalar", a_oc_prev)] if kb == fkb else []), inc=True)
                abf_g[kb % 2] = p.op("tensor", lambda e, kb=kb, fkb=fkb, lkb=lkb: e.matmul(
                    ps_d[:], bocol[:, 0:1], ABF[kb % 2][:], start=(kb == fkb), stop=(kb == lkb)),
                    waits=([("vector", v_de_prev)] if kb == fkb else []), inc=True)
            t_avd = p.val("tensor")
            v_de_prev = p.op("vector", lambda e: e.tensor_copy(out=denr[0:1, 0:512], in_=ps_d[0:1, 0:512]),
                             waits=[("tensor", t_avd), ("tensor", t_br_prev)], inc=True)
            p.op("vector", lambda e: e.reciprocal(out=denr[0:1, 0:512], in_=denr[0:1, 0:512]), inc=True)
            t_br_prev = p.op("tensor", lambda e: e.matmul(
                ps_s[:, 0:512], fones[0:1, 0:128], denr[0:1, 0:512], start=True, stop=True),
                waits=[("vector", p.val("vector"))], inc=True)
            v_rb_prev = p.op("vector", lambda e: e.tensor_copy(out=recb[:], in_=ps_s[:, 0:512]),
                             waits=[("tensor", t_br_prev)], inc=True)
            a_oc_prev = a_oc = p.op("scalar", lambda e, b=b: e.activation(
                out=stg2[b:b + 64, :], in_=ps_o[:], func=AF.Identity),
                waits=[("tensor", t_avd), ("vector", vfin_g[h % 2])], inc=True)
            v_fin = p.op("vector", lambda e, b=b: e.tensor_tensor(
                out=stg2[b:b + 64, :], in0=stg2[b:b + 64, :], in1=recb[b:b + 64, :], op=ALU.mult),
                waits=[("scalar", a_oc)], inc=True)
            vfin_g[h % 2] = v_fin
            a_fin_prev = p.op("scalar", lambda e, b=b, hk=hk: e.activation(
                out=OT[b:b + 64, hk * 512:(hk + 1) * 512], in_=stg2[b:b + 64, :], func=AF.Identity),
                waits=[("vector", v_fin)] + ([("tensor", ot_read)] if h == 0 else []), inc=True)
        qwl_read = p.val("tensor")
        kwread[0] = kwread[1] = p.val("tensor")
        vwread[0] = vwread[1] = p.val("tensor")

        for qt in range(4):
            for df in range(2):
                bi = df
                bank = ps_t[bi]
                for j in range(8):
                    wl = []
                    if j == 0:
                        wl = [("vector", pt_g[bi])]
                        if qt == 0 and df == 0:
                            wl += [("scalar", a_fin_prev), ("dma_wst", wld_local)]
                    p.op("tensor", lambda e, qt=qt, df=df, j=j, bank=bank: e.matmul(
                        bank[:], OT[:, j * 512 + qt * 128: j * 512 + (qt + 1) * 128],
                        WPR[:, j * 1024 + df * 512: j * 1024 + (df + 1) * 512],
                        start=(j == 0), stop=(j == 7)), waits=wl, inc=(j == 7))
                t_pj = p.val("tensor")
                p.op("vector", lambda e, qt=qt, df=df, bank=bank: e.tensor_copy(
                    out=accw[:, qt * 1024 + df * 512: qt * 1024 + (df + 1) * 512], in_=bank[:]),
                    waits=[("tensor", t_pj)], inc=True)
                pt_g[bi] = p.val("vector")
        ot_read = p.val("tensor")
        wpr_read = p.val("tensor")

        if w > 0:
            for g in range(2):
                r = g * NWIN + (w - 1)
                gq_t = p.op("sync", fm_load(QWG, qg_hbm.ap()[g], w), dma="gq",
                            waits=[("tensor", qwg_read)])
                for ch in range(w):
                    cb = ch % 2
                    gk_t = p.op("sync", fm_load(KCH[cb], kg_hbm.ap()[g], ch), dma="gk%d" % cb,
                                waits=[("tensor", max(wpr_read, kchread[cb]))])
                    gv_t = p.op("sync", tm_load(VCH[cb], vg_hbm.ap()[g][ch * 512:(ch + 1) * 512, :]),
                                dma="gv%d" % cb, waits=[("tensor", vchread[cb])])
                    gld_waits = [("dma_gk%d" % cb, gk_t), ("dma_gv%d" % cb, gv_t)] + \
                                ([("dma_gq", gq_t)] if ch == 0 else [])
                    for h in range(16):
                        b = (h % 2) * 64
                        hk = h // 2
                        for kb in range(4):
                            bank = ps_u[kb % 2]
                            kbglob = ch * 4 + kb
                            t_s = p.op("tensor", lambda e, cb=cb, b=b, hk=hk, kb=kb, bank=bank: e.matmul(
                                bank[:], KCH[cb][b:b + 64, hk * 512 + kb * 128: hk * 512 + (kb + 1) * 128],
                                QWG[b:b + 64, hk * 512:(hk + 1) * 512], start=True, stop=True),
                                waits=[pu_g[kb % 2]] + (gld_waits if (h == 0 and kb == 0) else []),
                                inc=True)
                            a_ex = p.op("scalar", lambda e, kb=kb, kbglob=kbglob, r=r, bank=bank: e.activation(
                                out=ABF[kb % 2][:], in_=bank[:], func=AF.Exp,
                                bias=selbT[:, kbglob * 16 + r: kbglob * 16 + r + 1]),
                                waits=[("tensor", t_s), ("tensor", abf_g[kb % 2])], inc=True)
                            pu_g[kb % 2] = ("scalar", a_ex)
                            p.op("tensor", lambda e, cb=cb, kb=kb, h=h: e.matmul(
                                ps_o[:], VCH[cb][:, kb * 1024 + h * 64: kb * 1024 + (h + 1) * 64],
                                ABF[kb % 2][:], start=(kb == 0), stop=(kb == 3)),
                                waits=[("scalar", a_ex)] + ([("scalar", a_oc_prev)] if kb == 0 else []),
                                inc=True)
                            abf_g[kb % 2] = p.op("tensor", lambda e, kb=kb: e.matmul(
                                ps_d[:], bocol[:, 0:1], ABF[kb % 2][:], start=(kb == 0), stop=(kb == 3)),
                                waits=([("vector", v_de_prev)] if kb == 0 else []), inc=True)
                        t_avd = p.val("tensor")
                        a_oc_prev = a_oc = p.op("scalar", lambda e, b=b: e.activation(
                            out=stg2[b:b + 64, :], in_=ps_o[:], func=AF.Identity),
                            waits=[("tensor", t_avd), ("vector", vfin_g[h % 2])], inc=True)
                        if ch == 0:
                            p.op("vector", lambda e, b=b, hk=hk: e.tensor_copy(
                                out=oacc[b:b + 64, hk * 512:(hk + 1) * 512], in_=stg2[b:b + 64, :]),
                                waits=[("scalar", a_oc)], inc=True)
                        else:
                            p.op("vector", lambda e, b=b, hk=hk: e.tensor_tensor(
                                out=oacc[b:b + 64, hk * 512:(hk + 1) * 512],
                                in0=oacc[b:b + 64, hk * 512:(hk + 1) * 512], in1=stg2[b:b + 64, :],
                                op=ALU.add), waits=[("scalar", a_oc)], inc=True)
                        vfin_g[h % 2] = p.val("vector")
                        if ch == 0:
                            v_de_prev = p.op("vector", lambda e, h=h: e.tensor_copy(
                                out=dacc[0:1, h * 512:(h + 1) * 512], in_=ps_d[0:1, 0:512]),
                                waits=[("tensor", t_avd), ("tensor", t_br_prev)], inc=True)
                        else:
                            v_de_prev = p.op("vector", lambda e, h=h: e.tensor_tensor(
                                out=dacc[0:1, h * 512:(h + 1) * 512],
                                in0=dacc[0:1, h * 512:(h + 1) * 512], in1=ps_d[0:1, 0:512], op=ALU.add),
                                waits=[("tensor", t_avd), ("tensor", t_br_prev)], inc=True)
                    kchread[cb] = p.val("tensor")
                    vchread[cb] = p.val("tensor")
                for h in range(16):
                    b = (h % 2) * 64
                    hk = h // 2
                    t_n = p.op("tensor", lambda e, b=b, hk=hk, h=h, g=g: e.matmul(
                        ps_r[0:1, 0:512], nkss[b:b + 64, 2 * h + g: 2 * h + g + 1],
                        QWG[b:b + 64, hk * 512:(hk + 1) * 512], start=True, stop=True),
                        waits=[psr_guard], inc=True)
                    a_en = p.op("scalar", lambda e: e.activation(
                        out=enull[0:1, 0:512], in_=ps_r[0:1, 0:512], func=AF.Exp, bias=m20),
                        waits=[("tensor", t_n), ("tensor", t_nv_prev), ("vector", v_dn_prev)], inc=True)
                    psr_guard = ("scalar", a_en)
                    v_dn_prev = p.op("vector", lambda e, h=h: e.tensor_tensor(
                        out=dacc[0:1, h * 512:(h + 1) * 512], in0=dacc[0:1, h * 512:(h + 1) * 512],
                        in1=enull[0:1, 0:512], op=ALU.add), waits=[("scalar", a_en)], inc=True)
                    t_nv_prev = t_nv = p.op("tensor", lambda e, g=g, h=h: e.matmul(
                        ps_u[0][0:64, 0:512], nvss[0:1, g * 1024 + h * 64: g * 1024 + (h + 1) * 64],
                        enull[0:1, 0:512], start=True, stop=True),
                        waits=[("scalar", a_en), pu_g[0], ("dma_lnv", 16)], inc=True)
                    a_nc = p.op("scalar", lambda e, b=b: e.activation(
                        out=stg2[b:b + 64, :], in_=ps_u[0][0:64, 0:512], func=AF.Identity),
                        waits=[("tensor", t_nv), ("vector", vfin_g[h % 2])], inc=True)
                    pu_g[0] = ("scalar", a_nc)
                    p.op("vector", lambda e, b=b, hk=hk: e.tensor_tensor(
                        out=oacc[b:b + 64, hk * 512:(hk + 1) * 512],
                        in0=oacc[b:b + 64, hk * 512:(hk + 1) * 512],
                        in1=stg2[b:b + 64, :], op=ALU.add), waits=[("scalar", a_nc)], inc=True)
                    p.op("vector", lambda e, h=h: e.reciprocal(
                        out=denr[0:1, 0:512], in_=dacc[0:1, h * 512:(h + 1) * 512]),
                        waits=[("tensor", t_br_prev)], inc=True)
                    t_br_prev = p.op("tensor", lambda e: e.matmul(
                        ps_s[:, 0:512], fones[0:1, 0:128], denr[0:1, 0:512], start=True, stop=True),
                        waits=[("vector", p.val("vector"))], inc=True)
                    v_rb_prev = p.op("vector", lambda e: e.tensor_copy(out=recb[:], in_=ps_s[:, 0:512]),
                                     waits=[("tensor", t_br_prev)], inc=True)
                    v_fin = p.op("vector", lambda e, b=b, hk=hk: e.tensor_tensor(
                        out=stg2[b:b + 64, :], in0=oacc[b:b + 64, hk * 512:(hk + 1) * 512],
                        in1=recb[b:b + 64, :], op=ALU.mult), inc=True)
                    vfin_g[h % 2] = v_fin
                    a_fin_prev = p.op("scalar", lambda e, b=b, hk=hk: e.activation(
                        out=OT[b:b + 64, hk * 512:(hk + 1) * 512], in_=stg2[b:b + 64, :], func=AF.Identity),
                        waits=[("vector", v_fin)] + ([("tensor", ot_read)] if h == 0 else []), inc=True)
                wld = 0
                wpr_guard = p.val("tensor")
                for j in range(8):
                    wld = p.op("gpsimd", lambda e, g=g, j=j: e.dma_start(
                        out=WPR[:, j * 1024:(j + 1) * 1024], in_=wout_in[g, j * 128:(j + 1) * 128, :]),
                        dma="wst", waits=([("tensor", wpr_guard)] if j == 0 else []))
                for qt in range(4):
                    qcol = qselT[:, qt * 16 + r: qt * 16 + r + 1]
                    for df in range(2):
                        bi = df
                        bank = ps_t[bi]
                        for j in range(8):
                            wl = []
                            if j == 0:
                                wl = [("vector", pt_g[bi])]
                                if qt == 0 and df == 0:
                                    wl += [("scalar", a_fin_prev), ("dma_wst", wld)]
                            p.op("tensor", lambda e, qt=qt, df=df, j=j, bank=bank: e.matmul(
                                bank[:], OT[:, j * 512 + qt * 128: j * 512 + (qt + 1) * 128],
                                WPR[:, j * 1024 + df * 512: j * 1024 + (df + 1) * 512],
                                start=(j == 0), stop=(j == 7)), waits=wl, inc=(j == 7))
                        t_pj = p.val("tensor")
                        p.op("vector", lambda e, qt=qt, df=df, bank=bank, qcol=qcol: e.scalar_tensor_tensor(
                            out=accw[:, qt * 1024 + df * 512: qt * 1024 + (df + 1) * 512],
                            in0=bank[:], scalar=qcol,
                            in1=accw[:, qt * 1024 + df * 512: qt * 1024 + (df + 1) * 512],
                            op0=ALU.mult, op1=ALU.add), waits=[("tensor", t_pj)], inc=True)
                        pt_g[bi] = p.val("vector")
                qwg_read = p.val("tensor")
                ot_read = p.val("tensor")
                wpr_read = p.val("tensor")

        for qt in range(4):
            tile = w * 4 + qt
            ldx = p.op("sync", lambda e, tile=tile: e.dma_start(
                out=xt[:], in_=xf[tile * 128:(tile + 1) * 128, :]), dma="ldx",
                waits=[("vector", v_comb)])
            if w == 0:
                p.op("vector", lambda e: e.memset(rc05[:], 0.5), inc=True)
            else:
                p.op("vector", lambda e, qt=qt, w=w: e.tensor_tensor(
                    out=cntc[:], in0=qselT[:, qt * 16 + (w - 1): qt * 16 + w],
                    in1=qselT[:, qt * 16 + NWIN + (w - 1): qt * 16 + NWIN + w], op=ALU.add), inc=True)
                p.op("vector", lambda e: e.tensor_scalar_add(out=cntc[:], in0=cntc[:], scalar1=1.0), inc=True)
                p.op("vector", lambda e: e.reciprocal(out=rc05[:], in_=cntc[:]), inc=True)
                p.op("vector", lambda e: e.tensor_scalar_mul(out=rc05[:], in0=rc05[:], scalar1=0.5), inc=True)
            v_comb = p.op("vector", lambda e, qt=qt: e.scalar_tensor_tensor(
                out=yr[:], in0=accw[:, qt * 1024:(qt + 1) * 1024], scalar=rc05[:], in1=xt[:],
                op0=ALU.mult, op1=ALU.add),
                waits=[("dma_ldx", ldx), ("dma_stout", st_out)], inc=True)
            st_out = p.op("sync", lambda e, tile=tile: e.dma_start(
                out=xo[tile * 128:(tile + 1) * 128, :], in_=yr[:]), dma="stout",
                waits=[("vector", v_comb)])


def prep_l1ffn_weights(I, tf=T):
    f32 = np.float32
    ln_g = I["L1_ffn_ln_g"].astype(f32); ln_b = I["L1_ffn_ln_b"].astype(f32)
    lncol = np.zeros((128, 16), f32)
    for hk in range(8):
        lncol[:, hk] = ln_g[hk * 128:(hk + 1) * 128]
        lncol[:, 8 + hk] = ln_b[hk * 128:(hk + 1) * 128]
    rt_q = I["L1_ffn_rt_q"].astype(f32); rt_kv = I["L1_ffn_rt_kv"].astype(f32)
    rt4 = np.zeros((128, 32), f32)
    for hk in range(8):
        for g in range(2):
            rt4[:, hk * 4 + g] = rt_q[g, hk * 128:(hk + 1) * 128]
            rt4[:, hk * 4 + 2 + g] = rt_kv[g, hk * 128:(hk + 1) * 128]
    qn = I["L1_ffn_qnorm_g"].astype(f32); cn = I["L1_ffn_cnorm_g"].astype(f32)
    qcn = np.zeros((128, 32), f32)
    for hk in range(8):
        for g in range(2):
            qcn[:, hk * 4 + g] = qn[g, hk * 128:(hk + 1) * 128]
            qcn[:, hk * 4 + 2 + g] = cn[g, hk * 128:(hk + 1) * 128]
    nullkv = I["L1_ffn_null_kv"].astype(f32)
    nks = np.zeros((128, 32), f32)
    nvs = np.zeros((1, 2048), f32)
    for g in range(2):
        for h in range(16):
            b = (h % 2) * 64
            nks[b:b + 64, 2 * h + g] = nullkv[0, g, h, 0]
            nvs[0, g * 1024 + h * 64:(g * 1024 + (h + 1) * 64)] = nullkv[1, g, h, 0]
    R = np.zeros((64, 64), f32)
    for i in range(32):
        R[i, 32 + i] = -1.0
        R[32 + i, i] = 1.0
    R2 = np.zeros((128, 128), f32)
    R2[0:64, 0:64] = R
    R2[64:128, 64:128] = R
    rotm = np.ascontiguousarray(R2.T)
    inv = (1.0 / 10000.0 ** (np.arange(0, 64, 2, dtype=f32) / 64)).astype(f32)
    t = np.arange(tf, dtype=f32)
    fr = t[:, None] * inv[None, :]
    fr = np.concatenate([fr, fr], -1)
    cosv = np.cos(fr).astype(f32); sinv = np.sin(fr).astype(f32)
    cossin = np.zeros((128, 2 * tf), f32)
    for rr in range(128):
        cossin[rr, 0:tf] = cosv[:, rr % 64]
        cossin[rr, tf:2 * tf] = sinv[:, rr % 64]
    NCTX = tf - 512
    NWIN = tf // 512 - 1
    wmaskin = np.zeros((16, NCTX), f32)
    for rr in range(16):
        if rr < 2 * NWIN:
            wd = rr % NWIN
            wmaskin[rr, (wd + 1) * 512:] = -1e9
    ii = np.arange(128)[:, None]
    jj = np.arange(128)[None, :]
    pats = []
    for dlt in range(-3, 8):
        dq = dlt * 128 + (ii - jj)
        pats.append(np.where((dq >= 0) & (dq <= 512), 0.0, -1e9).astype(f32))
    mlocQ = np.zeros((128, 11 * 128), f32)
    mlocKT = np.zeros((128, 11 * 128), f32)
    for m in range(11):
        mlocQ[:, m * 128:(m + 1) * 128] = pats[(7 - m) + 3]
        mlocKT[:, m * 128:(m + 1) * 128] = pats[m].T
    out = {
        "wqkv": np.ascontiguousarray(I["L1_ffn_w_qkv"].astype(f32).T),
        "lncol": lncol, "rt4": rt4, "qcn": qcn,
        "wq": np.ascontiguousarray(np.stack([I["L1_ffn_w_q"][g].astype(f32).T for g in range(2)])),
        "wkv": np.ascontiguousarray(np.stack([I["L1_ffn_w_kv"][g].astype(f32).T for g in range(2)])),
        "wout": np.ascontiguousarray(np.stack([I["L1_ffn_w_out"][g].astype(f32).T for g in range(2)])),
        "wlo": np.ascontiguousarray(I["L1_ffn_w_lo"].astype(f32).T),
        "nks": nks, "nvs": nvs, "rotm": rotm, "cossin": cossin,
        "wmaskin": wmaskin, "mlocQ": mlocQ, "mlocKT": mlocKT,
    }
    return out


_L1FFN_NC = None


def run_l1ffn(x, I):
    global _L1FFN_NC
    if _L1FFN_NC is None:
        _L1FFN_NC = build_l1ffn()
    nc = _L1FFN_NC
    w = prep_l1ffn_weights(I)
    maps = []
    for cix in range(NCORE):
        m = dict(w)
        m["xf"] = np.ascontiguousarray(x[cix // 2].astype(np.float32))
        maps.append(m)
    res = run_bass_kernel_spmd(nc, maps, core_ids=list(range(NCORE)))
    return np.stack([res.results[2 * b]["xo"] for b in range(B)])



ARENA_B = 211200


def mega_barrier(p):
    engines = ("sync", "tensor", "vector", "scalar", "gpsimd")
    vals = [(en, p.val(en)) for en in engines]
    dvals = [(k, p.esem[k].n) for k in list(p.esem) if k.startswith("dma_")]
    for en in engines:
        wl = [(e2, v) for e2, v in vals if e2 != en] + dvals
        p.op(en, lambda e: e.nop(), waits=wl, inc=True, chain=True)


def build_mega():
    nc = bass.Bass("TRN2", target_bir_lowering=False, debug=False, name="mega")
    AI = lambda n, s: nc.dram_tensor(n, s, F32, kind="ExternalInput").ap()
    x0s = AI("x0s", [TPC, D])
    gate_w = AI("gate_w", [D, 8]); gate_b = AI("gate_b", [1, 8])
    up_w = AI("up_w", [9, 128, 8 * 32 * 128]); up_bT = AI("up_bT", [9, 128, 32])
    down_w = AI("down_w", [9, 128, 32 * D]); down_b = AI("down_b", [9, 1, D])
    mq = AI("mq", [128, 256]); mqT = AI("mqT", [128, 256])
    w_attn = AI("w_attn", [D, 3072]); b_qk = AI("b_qk", [128, 16])
    b_v = AI("b_v", [1, D]); w_proj = AI("w_proj", [D, D]); b_proj = AI("b_proj", [1, D])
    filt = AI("filt", [128, 128]); gamma = AI("gamma", [1, D]); beta = AI("beta", [1, D])
    wqkv = AI("wqkv", [D, 3072]); lncol = AI("lncol", [128, 16]); rt4 = AI("rt4", [128, 32])
    qcn = AI("qcn", [128, 32]); wq = AI("wq", [2, D, 1024]); wkv = AI("wkv", [2, D, 2048])
    wout = AI("wout", [2, D, 1024]); wlo = AI("wlo", [D, 1024]); nks = AI("nks", [128, 32])
    nvs = AI("nvs", [1, 2048]); rotm = AI("rotm", [128, 128]); cossin = AI("cossin", [128, 2 * T])
    wmaskin = AI("wmaskin", [16, T - 512])
    mlocQ = AI("mlocQ", [128, 11 * 128]); mlocKT = AI("mlocKT", [128, 11 * 128])
    xob = nc.dram_tensor("xob", [T, D], F32, kind="ExternalOutput").ap()
    x1s = nc.dram_tensor("x1s", [TPC, D], F32)
    x1g = nc.dram_tensor("x1g", [2 * TPC, D], F32)
    x1f = nc.dram_tensor("x1f", [T, D], F32)
    x2s = nc.dram_tensor("x2s", [TPC, D], F32)
    x3s = nc.dram_tensor("x3s", [TPC, D], F32)
    x3g = nc.dram_tensor("x3g", [2 * TPC, D], F32)
    x3f = nc.dram_tensor("x3f", [T, D], F32)
    x4f = nc.dram_tensor("x4f", [T, D], F32)
    x5f = nc.dram_tensor("x5f", [T, D], F32)
    s_hbm = nc.dram_tensor("s_hbm", [4, T], F32)
    yf_hbm = nc.dram_tensor("yf_hbm", [128, 8 * T], BF16)
    yr_hbm = nc.dram_tensor("yr_hbm", [128, 8 * T], BF16)
    ql_hbm = nc.dram_tensor("ql_hbm", [128, 8 * T], BF16)
    kl_hbm = nc.dram_tensor("kl_hbm", [128, 8 * T], BF16)
    vl_hbm = nc.dram_tensor("vl_hbm", [T, D], BF16)
    qg_hbm = nc.dram_tensor("qg_hbm", [2, 128, 8 * T], BF16)
    kg_hbm = nc.dram_tensor("kg_hbm", [2, 128, 8 * T], BF16)
    vg_hbm = nc.dram_tensor("vg_hbm", [2, T, D], BF16)

    p = Prog(nc)
    pool = [p.psum(f"bank{i}", [128, 512], F32) for i in range(8)]
    arena = Arena(nc, ARENA_B)
    arena.allocate(p.stack)
    p.arena = arena
    p.psum_pool = pool

    def gather(src, dst_g, dst_f):
        mega_barrier(p)
        g_cc = p.op("gpsimd", lambda e: e.collective_compute(
            "AllGather", ALU.bypass, replica_groups=[[0, 1], [2, 3], [4, 5], [6, 7]],
            ins=[src.ap()[:, :]], outs=[dst_g.ap()[:, :]]), inc=True)
        for q in range(2):
            p.op("sync", lambda e, q=q: e.dma_start(
                out=dst_f.ap().rearrange("(i q c) d -> q i (c d)", i=T // 256, q=2, c=128)[q],
                in_=dst_g.ap()[q * TPC:(q + 1) * TPC, :].rearrange("(i c) d -> i (c d)", c=128)),
                dma="gat", waits=[("gpsimd", g_cc)])
        mega_barrier(p)

    p.begin_stage()
    emit_l0ffn(p, x0s, gate_w, gate_b, up_w, up_bT, down_w, down_b, x1s.ap(), TPC)
    gather(x1s, x1g, x1f)
    p.begin_stage()
    emit_l0attn(p, x1f.ap(), x1s.ap(), mq, mqT, w_attn, b_qk, b_v, w_proj, b_proj,
                x2s.ap(), TPC, T, 7)
    mega_barrier(p)
    p.begin_stage()
    emit_l0ffn(p, x2s.ap(), gate_w, gate_b, up_w, up_bT, down_w, down_b, x3s.ap(), TPC)
    gather(x3s, x3g, x3f)
    p.begin_stage()
    emit_l1ffn(p, x3f.ap(), wqkv, lncol, rt4, qcn, wq, wkv, wout, wlo, nks, nvs,
               rotm, cossin, wmaskin, mlocQ, mlocKT, x4f.ap(),
               s_hbm, yf_hbm, yr_hbm, ql_hbm, kl_hbm, vl_hbm, qg_hbm, kg_hbm, vg_hbm, T)
    mega_barrier(p)
    p.begin_stage()
    emit_l1attn(p, x4f.ap(), filt, gamma, beta, x5f.ap(), T)
    mega_barrier(p)
    p.begin_stage()
    emit_l1ffn(p, x5f.ap(), wqkv, lncol, rt4, qcn, wq, wkv, wout, wlo, nks, nvs,
               rotm, cossin, wmaskin, mlocQ, mlocKT, xob,
               s_hbm, yf_hbm, yr_hbm, ql_hbm, kl_hbm, vl_hbm, qg_hbm, kg_hbm, vg_hbm, T)
    p.finish()
    return nc


_MEGA_NC = None


def kernel_mega(**inputs):
    global _MEGA_NC
    if _MEGA_NC is None:
        _MEGA_NC = build_mega()
    w = {}
    w.update(prep_l0ffn_weights(inputs))
    w.update(prep_l0attn_weights(inputs))
    w.update(prep_l1attn_weights(inputs))
    w.update(prep_l1ffn_weights(inputs))
    x0 = np.asarray(inputs["L0_attn_x"], np.float32)
    shards = shard_tokens(x0)
    maps = []
    for c in range(NCORE):
        mqa, mqTa = l0attn_masks(c % 2)
        maps.append(dict(w, x0s=shards[c], mq=mqa, mqT=mqTa))
    res = run_bass_kernel_spmd(_MEGA_NC, maps, core_ids=list(range(NCORE)))
    return np.stack([res.results[2 * b]["xob"] for b in range(B)])


# version 04
# speedup vs baseline: 1.0381x; 1.0381x over previous
import numpy as np
from contextlib import ExitStack

import concourse.bass as bass
import concourse.mybir as mybir
from concourse.bass_utils import run_bass_kernel_spmd

F32 = mybir.dt.float32
BF16 = mybir.dt.bfloat16
AF = mybir.ActivationFunctionType
ALU = mybir.AluOpType
AX = mybir.AxisListType

D = 1024
B = 4
T = 4096
NCORE = 8
TPC = 2048
LN_EPS = 1e-5


class Sem:
    def __init__(self, handle):
        self.h = handle
        self.n = 0

    def reserve(self, k=1):
        self.n += k
        return self.n


class Arena:

    def __init__(self, nc, size_b):
        self.nc = nc
        self.size_b = size_b
        self.cursor = 0
        self.peak = 0
        self.handle = None
        self.base = 0

    def allocate(self, stack):
        import concourse.mybir as _mb
        self.handle = stack.enter_context(
            self.nc.sbuf_tensor("mega_arena", [128, self.size_b], _mb.dt.uint8))
        self.base = self.nc.lookup_mloc(self.handle).addr

    def alloc(self, name, shape, dt):
        per = 1
        for s in shape[1:]:
            per *= int(s)
        per *= mybir.dt.size(dt)
        addr = self.cursor
        self.cursor = (addr + per + 31) // 32 * 32
        self.peak = max(self.peak, self.cursor)
        assert self.cursor <= self.size_b, f"arena overflow at {name}: {self.cursor} > {self.size_b}"
        h = self.nc.virtual_sbuf_tensor(name, shape, dt)
        self.nc._tensor(h.name, h.shape, h.dtype, type="SB")
        mloc = self.nc.lookup_mloc(h)
        mloc.allocated = True
        mloc.addr = self.base + addr
        return h


class Prog:

    def __init__(self, nc):
        self.nc = nc
        self.stack = ExitStack()
        self.queues = {"sync": [], "tensor": [], "vector": [], "scalar": [], "gpsimd": []}
        self.esem = {k: Sem(self.stack.enter_context(nc.semaphore("sem_" + k)))
                     for k in self.queues}
        self.arena = None
        self.psum_pool = None
        self.psum_ix = 0
        self.waited = {k: {} for k in self.queues}

    def begin_stage(self):
        if self.arena is not None:
            self.arena.cursor = 0
        self.psum_ix = 0

    def dsem(self, name):
        key = "dma_" + name
        if key not in self.esem:
            self.esem[key] = Sem(self.stack.enter_context(self.nc.semaphore("sem_" + key)))
        return key

    def sbuf(self, name, shape, dt):
        if self.arena is not None:
            return self.arena.alloc(name, shape, dt)
        return self.stack.enter_context(self.nc.sbuf_tensor(name, shape, dt))

    def psum(self, name, shape, dt=F32):
        if self.psum_pool is not None:
            assert dt == F32, name
            bank = self.psum_pool[self.psum_ix]
            self.psum_ix += 1
            return bank[0:shape[0], 0:shape[1]]
        return self.stack.enter_context(self.nc.psum_tensor(name, shape, dt))

    def op(self, eng, makeop, inc=False, inck=1, waits=(), chain=None, dma=None):
        if chain is None:
            chain = eng != "tensor"
        if dma is not None:
            key = self.dsem(dma)
            inc = True
            inck = 16
        else:
            key = eng
        seen = self.waited[eng]
        wl = []
        for (en, v) in waits:
            if v > 0 and seen.get(en, 0) < v:
                seen[en] = v
                wl.append((en, v))
        if chain:
            prev_c = self.esem[eng].n
            if dma is None:
                inc = True
            if prev_c > 0 and seen.get(eng, 0) < prev_c:
                seen[eng] = prev_c
                wl.append((eng, prev_c))
        tgt = self.esem[key].reserve(inck) if inc else None
        wl = [(self.esem[en], v) for (en, v) in wl]

        def f(e, makeop=makeop, inc=inc, inck=inck, wl=wl, key=key):
            for s, v in wl:
                e.wait_ge(s.h, v)
            o = makeop(e)
            if inc:
                o.then_inc(self.esem[key].h, inck)

        self.queues[eng].append(f)
        return tgt

    def val(self, eng):
        return self.esem[eng].n

    def finish(self):
        block = self.stack.enter_context(self.nc.Block())
        for eng_name in ("sync", "tensor", "vector", "scalar", "gpsimd"):
            fns = self.queues[eng_name]
            if not fns:
                continue

            def section(eng, fns=fns):
                for f in fns:
                    f(eng)

            getattr(block, eng_name)(section)
        self.stack.close()


def emit_ln_tile(p, bufs, xt_ap, yt_ap, pre_a=0, pre_v=0, scr=None):
    ln_sum, ln_ss = bufs["ln_sum"], bufs["ln_ss"]
    ln_scr = scr if scr is not None else bufs["ln_scr"]
    ln_t1, ln_t2 = bufs["ln_t1"], bufs["ln_t2"]

    p.op("vector", lambda e: e.tensor_reduce(out=ln_sum[:], in_=xt_ap, axis=AX.X, op=ALU.add),
         inc=True, waits=[("scalar", pre_a), ("vector", pre_v)])
    a0 = p.op("scalar", lambda e: e.activation(out=ln_scr[:], in_=xt_ap, func=AF.Square, accum_out=ln_ss[:]),
              inc=True, waits=[("vector", pre_v)])
    p.op("vector", lambda e: e.tensor_scalar(out=ln_t1[:], in0=ln_ss[:], scalar1=1.0 / D, scalar2=LN_EPS,
                                             op0=ALU.mult, op1=ALU.add),
         waits=[("scalar", a0)], inc=True)
    p.op("vector", lambda e: e.tensor_scalar_mul(out=ln_sum[:], in0=ln_sum[:], scalar1=1.0 / D), inc=True)
    p.op("vector", lambda e: e.tensor_mul(out=ln_t2[:], in0=ln_sum[:], in1=ln_sum[:]), inc=True)
    v0 = p.op("vector", lambda e: e.tensor_sub(out=ln_t1[:], in0=ln_t1[:], in1=ln_t2[:]), inc=True)
    a1 = p.op("scalar", lambda e: e.activation(out=ln_t2[:], in_=ln_t1[:], func=AF.Sqrt),
              waits=[("vector", v0)], inc=True)
    v1 = p.op("vector", lambda e: e.reciprocal(out=ln_t1[:], in_=ln_t2[:]), waits=[("scalar", a1)], inc=True)
    p.op("vector", lambda e: e.tensor_mul(out=ln_t2[:], in0=ln_sum[:], in1=ln_t1[:]),
         waits=[("vector", v1)], inc=True)
    v2 = p.op("vector", lambda e: e.tensor_scalar_mul(out=ln_t2[:], in0=ln_t2[:], scalar1=-1.0), inc=True)
    a2 = p.op("scalar", lambda e: e.activation(out=yt_ap, in_=xt_ap, func=AF.Identity, bias=ln_t2[:], scale=ln_t1[:]),
              waits=[("vector", v2)], inc=True)
    return a2



def emit_l0ffn(p, xs, gate_w, gate_b, up_w, up_bT, down_w, down_b, xo, tpc=TPC):
    NE, HCH, ICH, HALF = 9, 8, 32, tpc // 2
    ntile = tpc // 128
    NGR = max(1, HALF // 512)
    NDS = HALF // 256
    NSUPA = 8
    NSUPB = 4

    xt = p.sbuf("xt", [128, D], F32)
    yt = p.sbuf("yt", [128, D], F32)
    ln_bufs = {
        "ln_sum": p.sbuf("ln_sum", [128, 1], F32),
        "ln_ss": p.sbuf("ln_ss", [128, 1], F32),
        "ln_scr": p.sbuf("ln_scr", [128, D], F32),
        "ln_t1": p.sbuf("ln_t1", [128, 1], F32),
        "ln_t2": p.sbuf("ln_t2", [128, 1], F32),
    }
    ident = p.sbuf("ident", [128, 128], F32)
    fones = p.sbuf("fones", [1, 128], F32)
    bones = p.sbuf("bones", [1, 128], BF16)
    yf32 = p.sbuf("yf32", [128, HCH * 128], F32)
    ybf = p.sbuf("ybf", [128, HCH * tpc], BF16)
    gbias = p.sbuf("gbias", [1, 8], F32)
    gwt = p.sbuf("gwt", [128, HCH * 8], F32)
    gatel = p.sbuf("gatel", [128, 8 * ntile], F32)
    w01 = p.sbuf("w01", [128, 8 * ntile], F32)
    top8 = p.sbuf("top8", [128, 8], F32)
    gsc = p.sbuf("gsc", [128, 8], F32)
    mcol = [p.sbuf(f"mcol{i}", [128, 1], F32) for i in range(3)]
    upblk = [p.sbuf(f"upblk{i}", [128, HCH * 4 * 128], BF16) for i in range(2)]
    dwblk = [p.sbuf(f"dwblk{i}", [128, 8 * D], BF16) for i in range(2)]
    dbias = [p.sbuf(f"dbias{i}", [1, D], BF16) for i in range(2)]
    upb = p.sbuf("upb", [128, NE * 32], F32)
    hact = p.sbuf("hact", [128, ICH * HALF], BF16)
    acc = p.sbuf("acc", [128, (HALF // 128) * D], F32)
    xt2 = xt
    outt = yt

    ps_up = [p.psum(f"ps_up{i}", [128, 512], F32) for i in range(2)]
    ps_dn = [p.psum(f"ps_dn{i}", [128, 512], F32) for i in range(4)]
    ps_tr = [ps_up[0][:, 0:128], ps_up[1][:, 0:128]]
    ps_g = ps_dn[0][:, 0:8]

    g0 = p.op("gpsimd", lambda e: e.memset(ident[:], 0.0), inc=True)
    p.op("gpsimd", lambda e: e.affine_select(out=ident[:], in_=ident[:], compare_op=ALU.not_equal,
                                             fill=1.0, base=0, pattern=[[-1, 128]], channel_multiplier=1),
         inc=True, waits=[("gpsimd", g0)])
    p.op("gpsimd", lambda e: e.memset(fones[:], 1.0), inc=True)
    g_const = p.op("gpsimd", lambda e: e.memset(bones[:], 1.0), inc=True)
    for eix in range(NE):
        p.op("sync", lambda e, eix=eix: e.dma_start(out=upb[:, eix * 32:(eix + 1) * 32], in_=up_bT[eix]),
             dma="const")
    p.op("sync", lambda e: e.dma_start(out=gbias[:], in_=gate_b), dma="const")
    for hk in range(HCH):
        p.op("sync", lambda e, hk=hk: e.dma_start(out=gwt[:, hk * 8:(hk + 1) * 8],
                                                  in_=gate_w[hk * 128:(hk + 1) * 128, :]), dma="const")
    const_lds = p.esem[p.dsem("const")].n

    prev_ln = 0
    prev_lnv = 0
    prev_w01 = 0
    prev_gate = 0
    prev_sig = 0
    tr_evac = [0, 0]
    for t in range(ntile):
        ld = p.op("sync", lambda e, t=t: e.dma_start(out=xt[:], in_=xs[t * 128:(t + 1) * 128, :]),
                  dma="ldx", waits=[("scalar", prev_ln), ("vector", prev_lnv)])
        p.op("vector", lambda e: e.nop(), waits=[("dma_ldx", ld)])
        p.op("scalar", lambda e: e.nop(), waits=[("dma_ldx", ld)])
        prev_ln = emit_ln_tile(p, ln_bufs, xt[:], yt[:])
        prev_lnv = p.val("vector")
        for hk in range(HCH):
            pt = ps_tr[hk % 2]
            t_tr = p.op("tensor", lambda e, hk=hk, pt=pt: e.transpose(pt, yt[:, hk * 128:(hk + 1) * 128], ident[:]),
                        waits=[("scalar", prev_ln), ("vector", tr_evac[hk % 2]), ("gpsimd", g_const)], inc=True)
            p.op("vector", lambda e, hk=hk, pt=pt: e.tensor_copy(out=yf32[:, hk * 128:(hk + 1) * 128], in_=pt),
                 waits=[("tensor", t_tr), ("tensor", prev_gate)], inc=True)
            tr_evac[hk % 2] = p.op("vector", lambda e, hk=hk, t=t: e.tensor_copy(
                out=ybf[:, hk * tpc + t * 128: hk * tpc + (t + 1) * 128],
                in_=yf32[:, hk * 128:(hk + 1) * 128]), inc=True)
        v_tr = p.val("vector")
        for hk in range(HCH):
            p.op("tensor", lambda e, hk=hk: e.matmul(ps_g, yf32[:, hk * 128:(hk + 1) * 128],
                                                     gwt[:, hk * 8:(hk + 1) * 8], start=(hk == 0), stop=False),
                 waits=([("vector", v_tr), ("dma_const", const_lds), ("scalar", prev_sig)] if hk == 0 else []))
        t_g = p.op("tensor", lambda e: e.matmul(ps_g, fones[:], gbias[:], start=False, stop=True), inc=True)
        prev_gate = t_g
        prev_sig = p.op("scalar", lambda e, t=t: e.activation(out=gatel[:, t * 8:(t + 1) * 8], in_=ps_g, func=AF.Sigmoid),
                        waits=[("tensor", t_g)], inc=True)
        v0 = p.op("vector", lambda e, t=t: e.max(out=top8[:], in_=gatel[:, t * 8:(t + 1) * 8]),
                  waits=[("scalar", prev_sig)], inc=True)
        v1 = p.op("vector", lambda e, t=t: e.tensor_scalar(out=gsc[:], in0=gatel[:, t * 8:(t + 1) * 8],
                                                           scalar1=top8[:, 1:2], scalar2=None, op0=ALU.is_ge),
                  waits=[("vector", v0)], inc=True)
        p.op("vector", lambda e: e.tensor_tensor(out=mcol[0][:], in0=top8[:, 0:1], in1=top8[:, 1:2], op=ALU.add),
             inc=True)
        v2 = p.op("vector", lambda e: e.tensor_scalar_add(out=mcol[0][:], in0=mcol[0][:], scalar1=1e-6), inc=True)
        v3 = p.op("vector", lambda e: e.reciprocal(out=mcol[1][:], in_=mcol[0][:]), waits=[("vector", v2)], inc=True)
        v4 = p.op("vector", lambda e: e.tensor_scalar_mul(out=mcol[1][:], in0=mcol[1][:], scalar1=0.1),
                  waits=[("vector", v3)], inc=True)
        v5 = p.op("vector", lambda e, t=t: e.tensor_mul(out=gsc[:], in0=gsc[:], in1=gatel[:, t * 8:(t + 1) * 8]),
                  inc=True)
        prev_w01 = p.op("vector", lambda e, t=t: e.tensor_scalar(out=w01[:, t * 8:(t + 1) * 8], in0=gsc[:],
                                                                 scalar1=mcol[1][:], scalar2=None, op0=ALU.mult),
                        waits=[("vector", v4), ("vector", v5)], inc=True)
        prev_lnv = max(prev_lnv, prev_w01)
    pass1_v = p.val("vector")
    pass1_a = p.val("scalar")

    up_last_use = [0, 0]
    dw_last_use = [0, 0]
    gelu_done = [0, 0]
    fma_done = [0, 0, 0, 0]
    hact_free = 0
    dbias_free = [0, 0]
    store_guard = 0
    prev_fin = 0

    for half in range(2):
        tok0 = half * HALF
        for eix in range(NE):
            dbpar = (half * NE + eix) % 2
            wdb = p.op("gpsimd", lambda e, eix=eix, dbpar=dbpar: e.dma_start(out=dbias[dbpar][:], in_=down_b[eix]),
                       dma=f"wdb{dbpar}", waits=[("tensor", dbias_free[dbpar])])
            for sup in range(NSUPA):
                blk = upblk[sup % 2]
                src4 = up_w[eix].rearrange("p (hc ii i) -> p hc ii i", hc=8, ii=32, i=128)
                wv = p.op("gpsimd", lambda e, src4=src4, sup=sup, blk=blk: e.dma_start(
                    out=blk[:].rearrange("p (hc ii i) -> p hc ii i", hc=8, ii=4, i=128),
                    in_=src4[:, :, sup * 4:(sup + 1) * 4, :]),
                    dma=f"wup{sup % 2}", waits=[("tensor", up_last_use[sup % 2])])
                t_up = 0
                for iil in range(4):
                    ii = sup * 4 + iil
                    for gr in range(NGR):
                        bank = ps_up[gr % 2]
                        for hk in range(HCH):
                            wl = []
                            if hk == 0:
                                wl = [(f"dma_wup{sup % 2}", wv), ("scalar", gelu_done[gr % 2]), ("vector", pass1_v)]
                            t_up = p.op("tensor", lambda e, hk=hk, iil=iil, gr=gr, bank=bank, blk=blk, tok0=tok0:
                                        e.matmul(bank[:], blk[:, (hk * 4 + iil) * 128:(hk * 4 + iil + 1) * 128],
                                                 ybf[:, hk * tpc + tok0 + gr * 512: hk * tpc + tok0 + (gr + 1) * 512],
                                                 start=(hk == 0), stop=(hk == HCH - 1)), waits=wl,
                                        inc=(hk == HCH - 1))
                        gelu_done[gr % 2] = p.op("scalar", lambda e, ii=ii, gr=gr, bank=bank, eix=eix:
                                                 e.activation(out=hact[:, ii * HALF + gr * 512: ii * HALF + (gr + 1) * 512],
                                                              in_=bank[:], func=AF.Gelu,
                                                              bias=upb[:, eix * 32 + ii: eix * 32 + ii + 1]),
                                                 waits=[("tensor", t_up), ("tensor", hact_free)], inc=True)
                up_last_use[sup % 2] = t_up
            a_phaseA = p.val("scalar")
            for sup in range(NSUPB):
                blk = dwblk[sup % 2]
                src3 = down_w[eix].rearrange("p (ii f) -> p ii f", ii=32, f=D)
                wv = p.op("gpsimd", lambda e, src3=src3, sup=sup, blk=blk: e.dma_start(
                    out=blk[:].rearrange("p (ii f) -> p ii f", ii=8, f=D),
                    in_=src3[:, sup * 8:(sup + 1) * 8, :]),
                    dma=f"wdn{sup % 2}", waits=[("tensor", dw_last_use[sup % 2])])
                for ds in range(NDS):
                    grp_t = [0, 0, 0, 0]
                    for t2 in range(2):
                        for f2 in range(2):
                            bk = t2 * 2 + f2
                            bank = ps_dn[bk]
                            tt_local = ds * 2 + t2
                            first = True
                            if sup == 0:
                                p.op("tensor", lambda e, bank=bank, dbpar=dbpar, f2=f2:
                                     e.matmul(bank[:], bones[:], dbias[dbpar][0:1, f2 * 512:(f2 + 1) * 512],
                                              start=True, stop=False),
                                     waits=[(f"dma_wdn{sup % 2}", wv), (f"dma_wdb{dbpar}", wdb), ("vector", fma_done[bk]),
                                            ("scalar", a_phaseA)])
                                first = False
                            for iil in range(8):
                                ii = sup * 8 + iil
                                wl = []
                                if first:
                                    wl = [(f"dma_wdn{sup % 2}", wv), ("vector", fma_done[bk]), ("scalar", a_phaseA)]
                                    first = False
                                grp_t[bk] = p.op("tensor", lambda e, bank=bank, ii=ii, iil=iil, tt_local=tt_local, f2=f2,
                                                 blk=blk, st=(sup != 0 and iil == 0):
                                                 e.matmul(bank[:], hact[:, ii * HALF + tt_local * 128: ii * HALF + (tt_local + 1) * 128],
                                                          blk[:, iil * D + f2 * 512: iil * D + (f2 + 1) * 512],
                                                          start=st, stop=(iil == 7)), waits=wl, inc=(iil == 7))
                    for t2 in range(2):
                        for f2 in range(2):
                            bk = t2 * 2 + f2
                            bank = ps_dn[bk]
                            tt_local = ds * 2 + t2
                            tglob = (tok0 + tt_local * 128) // 128
                            acc_ap = acc[:, tt_local * D + f2 * 512: tt_local * D + (f2 + 1) * 512]
                            if eix == 0 and sup == 0:
                                fma_done[bk] = p.op("vector", lambda e, bank=bank, acc_ap=acc_ap, tglob=tglob:
                                                    e.tensor_scalar(out=acc_ap, in0=bank[:],
                                                                    scalar1=w01[:, tglob * 8: tglob * 8 + 1],
                                                                    scalar2=None, op0=ALU.mult),
                                                    waits=[("tensor", grp_t[bk])], inc=True)
                            elif eix == 8:
                                fma_done[bk] = p.op("vector", lambda e, bank=bank, acc_ap=acc_ap:
                                                    e.scalar_tensor_tensor(out=acc_ap, in0=bank[:], scalar=0.1,
                                                                           in1=acc_ap, op0=ALU.mult, op1=ALU.add),
                                                    waits=[("tensor", grp_t[bk])], inc=True)
                            else:
                                fma_done[bk] = p.op("vector", lambda e, bank=bank, acc_ap=acc_ap, tglob=tglob, eix=eix:
                                                    e.scalar_tensor_tensor(out=acc_ap, in0=bank[:],
                                                                           scalar=w01[:, tglob * 8 + eix: tglob * 8 + eix + 1],
                                                                           in1=acc_ap, op0=ALU.mult, op1=ALU.add),
                                                    waits=[("tensor", grp_t[bk])], inc=True)
                dw_last_use[sup % 2] = max(grp_t)
            t_phaseB = dw_last_use[(NSUPB - 1) % 2]
            hact_free = t_phaseB
            dbias_free[dbpar] = t_phaseB
        v_acc = p.val("vector")
        for tt_local in range(HALF // 128):
            tglob = tok0 // 128 + tt_local
            acc_ap = acc[:, tt_local * D: (tt_local + 1) * D]
            ld = p.op("sync", lambda e, tglob=tglob: e.dma_start(out=xt2[:], in_=xs[tglob * 128:(tglob + 1) * 128, :]),
                      dma="ldx", waits=[("vector", max(prev_fin, v_acc)), ("scalar", pass1_a)])
            v0 = p.op("vector", lambda e, acc_ap=acc_ap: e.tensor_reduce(out=mcol[2][:], in_=acc_ap, axis=AX.X,
                                                                         op=ALU.max, apply_absolute_value=True),
                      waits=[("vector", v_acc)], inc=True)
            v1 = p.op("vector", lambda e: e.tensor_scalar_add(out=mcol[2][:], in0=mcol[2][:], scalar1=1e-6),
                      waits=[("vector", v0)], inc=True)
            v2 = p.op("vector", lambda e: e.reciprocal(out=mcol[2][:], in_=mcol[2][:]), waits=[("vector", v1)], inc=True)
            v3 = p.op("vector", lambda e: e.tensor_scalar_mul(out=mcol[2][:], in0=mcol[2][:], scalar1=0.5),
                      waits=[("vector", v2)], inc=True)
            v4 = p.op("vector", lambda e, acc_ap=acc_ap: e.scalar_tensor_tensor(out=outt[:], in0=acc_ap,
                                                                                scalar=mcol[2][:], in1=xt2[:],
                                                                                op0=ALU.mult, op1=ALU.add),
                      waits=[("vector", v3), ("dma_ldx", ld), ("dma_stout", store_guard)], inc=True)
            store_guard = p.op("sync", lambda e, tglob=tglob: e.dma_start(out=xo[tglob * 128:(tglob + 1) * 128, :],
                                                                          in_=outt[:]),
                               dma="stout", waits=[("vector", v4)])
            prev_fin = v4
    p.op("sync", lambda e: e.nop(), waits=[("dma_stout", p.esem[p.dsem("stout")].n)])


def build_l0ffn(tpc=TPC):
    nc = bass.Bass("TRN2", target_bir_lowering=False, debug=False, name="l0ffn")
    xs = nc.dram_tensor("xs", [tpc, D], F32, kind="ExternalInput").ap()
    gate_w = nc.dram_tensor("gate_w", [D, 8], F32, kind="ExternalInput").ap()
    gate_b = nc.dram_tensor("gate_b", [1, 8], F32, kind="ExternalInput").ap()
    up_w = nc.dram_tensor("up_w", [9, 128, 8 * 32 * 128], F32, kind="ExternalInput").ap()
    up_bT = nc.dram_tensor("up_bT", [9, 128, 32], F32, kind="ExternalInput").ap()
    down_w = nc.dram_tensor("down_w", [9, 128, 32 * D], F32, kind="ExternalInput").ap()
    down_b = nc.dram_tensor("down_b", [9, 1, D], F32, kind="ExternalInput").ap()
    xo = nc.dram_tensor("xo", [tpc, D], F32, kind="ExternalOutput").ap()

    p = Prog(nc)
    emit_l0ffn(p, xs, gate_w, gate_b, up_w, up_bT, down_w, down_b, xo, tpc)
    p.finish()
    return nc



def prep_l0ffn_weights(I):
    up_all = np.concatenate([I["L0_ffn_up_w"], I["L0_ffn_sw_up"][None]], 0)
    down_all = np.concatenate([I["L0_ffn_down_w"], I["L0_ffn_sw_down"][None]], 0)
    upb_all = np.concatenate([I["L0_ffn_up_b"], I["L0_ffn_sb_up"][None]], 0)
    dnb_all = np.concatenate([I["L0_ffn_down_b"], I["L0_ffn_sb_down"][None]], 0)
    up_dev = np.ascontiguousarray(
        up_all.reshape(9, 8, 128, 32, 128).transpose(0, 2, 1, 3, 4).reshape(9, 128, 8 * 32 * 128))
    down_dev = np.ascontiguousarray(
        down_all.reshape(9, 32, 128, 1024).transpose(0, 2, 1, 3).reshape(9, 128, 32 * 1024))
    up_bT = np.ascontiguousarray(upb_all.reshape(9, 32, 128).transpose(0, 2, 1))
    down_b = np.ascontiguousarray(dnb_all.reshape(9, 1, 1024))
    return {
        "gate_w": np.ascontiguousarray(I["L0_ffn_gate_w"]),
        "gate_b": np.ascontiguousarray(I["L0_ffn_bias"].reshape(1, 8)),
        "up_w": up_dev, "up_bT": up_bT, "down_w": down_dev, "down_b": down_b,
    }


def shard_tokens(x):
    xb = x.reshape(B, T // 128, 128, D)
    return [np.ascontiguousarray(xb[c // 2, c % 2::2].reshape(TPC, D)) for c in range(NCORE)]


def unshard_tokens(shards):
    out = np.zeros((B, T // 128, 128, D), np.float32)
    for c in range(NCORE):
        out[c // 2, c % 2::2] = np.asarray(shards[c]).reshape(TPC // 128, 128, D)
    return out.reshape(B, T, D)


_L0FFN_NC = None


def run_l0ffn(x, I):
    global _L0FFN_NC
    if _L0FFN_NC is None:
        _L0FFN_NC = build_l0ffn()
    w = prep_l0ffn_weights(I)
    shards = shard_tokens(x)
    in_maps = [dict(w, xs=s) for s in shards]
    res = run_bass_kernel_spmd(_L0FFN_NC, in_maps, core_ids=list(range(NCORE)))
    return unshard_tokens([res.results[c]["xo"] for c in range(NCORE)])


def kernel(**inputs):
    return kernel_mega(**inputs)


if __name__ == "__main__":
    print("kernel module")



def emit_l0attn(p, xf, xq, mq, mqT, w_attn, b_qk, b_v, w_proj, b_proj, xo, tq=TPC, tf=T, newton=7):
    HCH = 8
    nqt = tq // 128
    nft = tf // 128
    NH = 16

    xt = p.sbuf("xt", [128, D], F32)
    yt = p.sbuf("yt", [128, D], F32)
    cols = p.sbuf("cols", [128, 32], F32)
    ln_bufs = {
        "ln_sum": cols[:, 0:1],
        "ln_ss": cols[:, 1:2],
        "ln_t1": cols[:, 2:3],
        "ln_t2": cols[:, 3:4],
    }
    ident = p.sbuf("ident", [128, 128], F32)
    fones = p.sbuf("fones", [1, 128], F32)
    bones = p.sbuf("bones", [1, 128], BF16)
    ybg = p.sbuf("ybg", [128, HCH * 512], BF16)
    watq = p.sbuf("watq", [128, HCH * D], BF16)
    watk = p.sbuf("watk", [128, HCH * D], BF16)
    watv = p.sbuf("watv", [128, HCH * D], BF16)
    bqk = cols[:, 16:32]
    bvrow = p.sbuf("bvrow", [1, D], BF16)
    bprow = bvrow
    kbf = p.sbuf("kbf", [128, HCH * tf], BF16)
    vbf = p.sbuf("vbf", [128, (tf // 128) * D], BF16)
    qtile = p.sbuf("qtile", [128, HCH * 128], BF16)
    mqs = p.sbuf("mqs", [128, 256], BF16)
    mqTs = p.sbuf("mqTs", [128, 256], BF16)
    zbuf = watv
    abuf = [p.sbuf(f"abuf{i}", [128, 128], BF16) for i in range(2)]
    ascr = p.sbuf("ascr", [128, 256], BF16)
    outf = p.sbuf("outf", [128, HCH * 128], BF16)
    ntrow = p.sbuf("ntrow", [1, 128], F32)
    m_run = cols[:, 4:5]
    tau = cols[:, 5:6]
    ntau = cols[:, 6:7]
    fcol = cols[:, 7:8]
    ccol = cols[:, 8:9]
    fcol2 = cols[:, 9:10]
    ccol2 = cols[:, 10:11]
    rcol = cols[:, 11:12]
    nscr = p.sbuf("nscr", [128, max(tf // 2, D)], BF16)
    nscr2 = ybg
    outt = yt

    ps_kv = [p.psum(f"ps_kv{i}", [128, 512], F32) for i in range(2)]
    ps_z = [p.psum(f"ps_z{i}", [128, 512], F32) for i in range(2)]
    ps_zT = [p.psum(f"ps_zT{i}", [128, 128], F32) for i in range(2)]
    ps_o = p.psum("ps_o", [64, 128], F32)
    ps_r = p.psum("ps_r", [1, 128], F32)

    g0 = p.op("gpsimd", lambda e: e.memset(ident[:], 0.0), inc=True)
    p.op("gpsimd", lambda e: e.affine_select(out=ident[:], in_=ident[:], compare_op=ALU.not_equal,
                                             fill=1.0, base=0, pattern=[[-1, 128]], channel_multiplier=1),
         inc=True, waits=[("gpsimd", g0)])
    p.op("gpsimd", lambda e: e.memset(fones[:], 1.0), inc=True)
    g_const = p.op("gpsimd", lambda e: e.memset(bones[:], 1.0), inc=True)
    p.op("sync", lambda e: e.dma_start(out=bqk[:], in_=b_qk), dma="const")
    p.op("gpsimd", lambda e: e.dma_start(out=bvrow[:], in_=b_v), dma="gconst")
    p.op("gpsimd", lambda e: e.dma_start(out=mqs[:], in_=mq), dma="gconst")
    p.op("gpsimd", lambda e: e.dma_start(out=mqTs[:], in_=mqT), dma="gconst")
    for hk in range(HCH):
        p.op("gpsimd", lambda e, hk=hk: e.dma_start(out=watq[:, hk * D:(hk + 1) * D],
                                                    in_=w_attn[hk * 128:(hk + 1) * 128, 0:D]), dma="gconst")
        p.op("gpsimd", lambda e, hk=hk: e.dma_start(out=watk[:, hk * D:(hk + 1) * D],
                                                    in_=w_attn[hk * 128:(hk + 1) * 128, D:2 * D]), dma="gconst")
        p.op("gpsimd", lambda e, hk=hk: e.dma_start(out=watv[:, hk * D:(hk + 1) * D],
                                                    in_=w_attn[hk * 128:(hk + 1) * 128, 2 * D:3 * D]), dma="gconst")
    const_lds = p.esem[p.dsem("const")].n
    gconst_lds = p.esem[p.dsem("gconst")].n
    bq_scaled = p.op("vector", lambda e: e.tensor_scalar_mul(out=bqk[:, 0:8], in0=bqk[:, 0:8], scalar1=0.125),
                     waits=[("dma_const", const_lds)], inc=True)

    def wslice(hk, f0, f1):
        if f1 <= D:
            return watq[:, hk * D + f0: hk * D + f1]
        if f1 <= 2 * D:
            return watk[:, hk * D + f0 - D: hk * D + f1 - D]
        return watv[:, hk * D + f0 - 2 * D: hk * D + f1 - 2 * D]

    prev_ln = 0
    prev_lnv = 0
    tr_evac = [0, 0]
    prev_kv = 0
    kv_evac_a = 0
    kv_evac_v = 0
    for grp in range(nft // 4):
        for tl in range(4):
            t = grp * 4 + tl
            ld = p.op("sync", lambda e, t=t: e.dma_start(out=xt[:], in_=xf[t * 128:(t + 1) * 128, :]),
                      dma="ldx", waits=[("scalar", prev_ln), ("vector", prev_lnv)])
            p.op("vector", lambda e: e.nop(), waits=[("dma_ldx", ld)])
            p.op("scalar", lambda e: e.nop(), waits=[("dma_ldx", ld)])
            prev_ln = emit_ln_tile(p, ln_bufs, xt[:], yt[:], scr=nscr[:, 0:D])
            for hk in range(HCH):
                pt = ps_z[hk % 2][:, 0:128]
                t_tr = p.op("tensor", lambda e, hk=hk, pt=pt: e.transpose(pt, yt[:, hk * 128:(hk + 1) * 128], ident[:]),
                            waits=[("scalar", prev_ln), ("vector", tr_evac[hk % 2]), ("gpsimd", g_const),
                                   ("tensor", prev_kv)], inc=True)
                tr_evac[hk % 2] = p.op("vector", lambda e, hk=hk, tl=tl, pt=pt: e.tensor_copy(
                    out=ybg[:, hk * 512 + tl * 128: hk * 512 + (tl + 1) * 128], in_=pt),
                    waits=[("tensor", t_tr)], inc=True)
            prev_lnv = p.val("vector")
        v_grp = p.val("vector")
        for fc in range(HCH):
            bank = ps_kv[fc % 2]
            for hk in range(HCH):
                wl = [("vector", max(v_grp, kv_evac_v)), ("dma_const", const_lds),
                      ("dma_gconst", gconst_lds)] if hk == 0 else []
                if hk == 0 and fc >= 2:
                    wl.append(("scalar", kv_evac_a))
                p.op("tensor", lambda e, fc=fc, hk=hk, bank=bank: e.matmul(
                    bank[:], wslice(hk, D + fc * 128, D + (fc + 1) * 128),
                    ybg[:, hk * 512:(hk + 1) * 512], start=(hk == 0), stop=(hk == HCH - 1)),
                     waits=wl, inc=(hk == HCH - 1))
            t_k = p.val("tensor")
            kv_evac_a = p.op("scalar", lambda e, fc=fc, grp=grp, bank=bank: e.activation(
                out=kbf[:, fc * tf + grp * 512: fc * tf + (grp + 1) * 512], in_=bank[:],
                func=AF.Identity, bias=bqk[:, 8 + fc: 9 + fc], scale=1.0),
                waits=[("tensor", t_k)], inc=True)
        for tl in range(4):
            for ft in range(2):
                bank = ps_kv[(tl * 2 + ft) % 2]
                for hk in range(HCH):
                    wl = []
                    if hk == 0:
                        wl = [("scalar", kv_evac_a), ("vector", kv_evac_v)]
                    p.op("tensor", lambda e, tl=tl, ft=ft, hk=hk, bank=bank: e.matmul(
                        bank[:], ybg[:, hk * 512 + tl * 128: hk * 512 + (tl + 1) * 128],
                        wslice(hk, 2048 + ft * 512, 2048 + (ft + 1) * 512),
                        start=(hk == 0), stop=False), waits=wl)
                t_v = p.op("tensor", lambda e, ft=ft, bank=bank: e.matmul(
                    bank[:], bones[0:1, :], bvrow[0:1, ft * 512:(ft + 1) * 512],
                    start=False, stop=True), inc=True)
                kv_evac_v = p.op("vector", lambda e, tl=tl, ft=ft, grp=grp, bank=bank: e.tensor_copy(
                    out=vbf[:, (grp * 4 + tl) * D + ft * 512: (grp * 4 + tl) * D + (ft + 1) * 512], in_=bank[:]),
                    waits=[("tensor", t_v)], inc=True)
        prev_kv = p.val("tensor")
    pass1_a = p.val("scalar")
    pass1_v = p.val("vector")

    wv_p = p.op("gpsimd", lambda e: e.dma_start(out=bprow[:], in_=b_proj),
                dma="wproj", waits=[("tensor", prev_kv)])
    for hk in range(HCH):
        wv_p = p.op("gpsimd", lambda e, hk=hk: e.dma_start(out=watk[:, hk * D:(hk + 1) * D],
                                                           in_=w_proj[hk * 128:(hk + 1) * 128, :]),
                    dma="wproj", waits=[("tensor", prev_kv)])

    ctx = dict(locals())
    emit_l0attn_tiles(p, ctx)
    p.op("sync", lambda e: e.nop(), waits=[("dma_stout", p.esem[p.dsem("stout")].n)])


def build_l0attn(tq=TPC, tf=T, newton=7):
    nc = bass.Bass("TRN2", target_bir_lowering=False, debug=False, name="l0attn")
    xf = nc.dram_tensor("xf", [tf, D], F32, kind="ExternalInput").ap()
    xq = nc.dram_tensor("xq", [tq, D], F32, kind="ExternalInput").ap()
    mq = nc.dram_tensor("mq", [128, 256], F32, kind="ExternalInput").ap()
    mqT = nc.dram_tensor("mqT", [128, 256], F32, kind="ExternalInput").ap()
    w_attn = nc.dram_tensor("w_attn", [D, 3072], F32, kind="ExternalInput").ap()
    b_qk = nc.dram_tensor("b_qk", [128, 16], F32, kind="ExternalInput").ap()
    b_v = nc.dram_tensor("b_v", [1, D], F32, kind="ExternalInput").ap()
    w_proj = nc.dram_tensor("w_proj", [D, D], F32, kind="ExternalInput").ap()
    b_proj = nc.dram_tensor("b_proj", [1, D], F32, kind="ExternalInput").ap()
    xo = nc.dram_tensor("xo", [tq, D], F32, kind="ExternalOutput").ap()

    p = Prog(nc)
    emit_l0attn(p, xf, xq, mq, mqT, w_attn, b_qk, b_v, w_proj, b_proj, xo, tq, tf, newton)
    p.finish()
    return nc


GP_DIV = True


def emit_l0attn_tiles(p, c):
    nqt = c["nqt"]; tf = c["tf"]; NH = c["NH"]; HCH = c["HCH"]
    xq = c["xq"]; xo = c["xo"]
    xt = c["xt"]; yt = c["yt"]; ln_bufs = c["ln_bufs"]
    ident = c["ident"]; fones = c["fones"]; bones = c["bones"]
    watq = c["watq"]; watk = c["watk"]; bqk = c["bqk"]; bprow = c["bprow"]
    kbf = c["kbf"]; vbf = c["vbf"]; qtile = c["qtile"]
    mqs = c["mqs"]; mqTs = c["mqTs"]
    zbuf = c["zbuf"]; abuf = c["abuf"]; ascr = c["ascr"]; outf = c["outf"]
    ntrow = c["ntrow"]; m_run = c["m_run"]; nscr = c["nscr"]; nscr2 = c["nscr2"]; outt = c["outt"]
    tau = c["tau"]; ntau = c["ntau"]; fcol = c["fcol"]; ccol = c["ccol"]
    fcol2 = c["fcol2"]; ccol2 = c["ccol2"]; rcol = c["rcol"]
    ps_kv = c["ps_kv"]; ps_z = c["ps_z"]; ps_zT = c["ps_zT"]; ps_o = c["ps_o"]; ps_r = c["ps_r"]
    wv_p = c["wv_p"]; pass1_a = c["pass1_a"]; pass1_v = c["pass1_v"]
    prev_kv = c["prev_kv"]

    def wslice(hk, f0, f1):
        if f1 <= D:
            return watq[:, hk * D + f0: hk * D + f1]
        return watk[:, hk * D + f0 - D: hk * D + f1 - D]

    prev_q = 0
    qev = [0, 0]
    m_guard = 0
    prev_lna = pass1_a
    prev_outp = pass1_v
    zevac_guard = 0
    ab_relu = [0, 0]
    ntrow_ev = 0
    outf_ev = 0
    prev_store = 0
    tr2_evac = [0, 0]
    last_o_ev = 0

    for i in range(nqt):
        nkb = 2 * i + 2
        W = nkb * 128
        ld = p.op("sync", lambda e, i=i: e.dma_start(out=xt[:], in_=xq[i * 128:(i + 1) * 128, :]),
                  dma="ldq", waits=[("vector", prev_outp), ("scalar", prev_lna), ("dma_stout", prev_store)])
        p.op("vector", lambda e: e.nop(), waits=[("dma_ldq", ld)])
        p.op("scalar", lambda e: e.nop(), waits=[("dma_ldq", ld)])
        ln_a = emit_ln_tile(p, ln_bufs, xt[:], yt[:], scr=nscr[:, 0:D])
        prev_lna = ln_a
        ybq = outf
        for hk in range(HCH):
            pt = ps_z[hk % 2][:, 0:128]
            t_tr = p.op("tensor", lambda e, hk=hk, pt=pt: e.transpose(pt, yt[:, hk * 128:(hk + 1) * 128], ident[:]),
                        waits=[("scalar", ln_a), ("vector", tr2_evac[hk % 2]), ("tensor", prev_kv)], inc=True)
            tr2_evac[hk % 2] = p.op("vector", lambda e, hk=hk, pt=pt: e.tensor_copy(
                out=ybq[:, hk * 128:(hk + 1) * 128], in_=pt), waits=[("tensor", t_tr)], inc=True)
        v_trq = p.val("vector")
        for fc in range(HCH):
            bank = ps_zT[fc % 2]
            for hk in range(HCH):
                wl = []
                if hk == 0:
                    wl = [("vector", v_trq), ("scalar", max(qev[fc % 2], ab_relu[0], ab_relu[1]))]
                p.op("tensor", lambda e, fc=fc, hk=hk, bank=bank, ybq=ybq: e.matmul(
                    bank[:], wslice(hk, fc * 128, (fc + 1) * 128), ybq[:, hk * 128:(hk + 1) * 128],
                    start=(hk == 0), stop=(hk == HCH - 1)), waits=wl, inc=(hk == HCH - 1))
            t_q = p.val("tensor")
            qev[fc % 2] = prev_q = p.op("scalar", lambda e, fc=fc, bank=bank: e.activation(
                out=qtile[:, fc * 128:(fc + 1) * 128], in_=bank[:], func=AF.Identity,
                bias=bqk[:, fc: fc + 1], scale=0.125),
                waits=[("tensor", t_q), ("vector", c["bq_scaled"])], inc=True)
        for h in range(NH):
            qs = qtile[(h % 2) * 64:(h % 2) * 64 + 64, (h // 2) * 128:(h // 2) * 128 + 128]
            nchunk = nkb // 2
            for ch in range(nchunk):
                bank = ps_z[ch % 2]
                t_z = p.op("tensor", lambda e, ch=ch, h=h, bank=bank, qs=qs: e.matmul(
                    bank[:, 0:256], qs,
                    kbf[(h % 2) * 64:(h % 2) * 64 + 64, (h // 2) * tf + ch * 256:(h // 2) * tf + (ch + 1) * 256],
                    start=True, stop=True),
                    waits=[("vector", tr2_evac[ch % 2]), ("scalar", max(prev_q, zevac_guard))], inc=True)
                if ch == nchunk - 1:
                    v1 = p.op("vector", lambda e, bank=bank: e.tensor_tensor(out=ascr[:, 0:256], in0=bank[:, 0:256], in1=mqs[:], op=ALU.add),
                              waits=[("tensor", t_z), ("scalar", zevac_guard)], inc=True)
                    tr2_evac[ch % 2] = p.op("vector", lambda e, ch=ch: e.tensor_scalar(
                        out=zbuf[:, ch * 256:(ch + 1) * 256], in0=ascr[:, 0:256], scalar1=0.0,
                        scalar2=(None if ch == 0 else m_run[:]), op0=ALU.add, op1=ALU.max, accum_out=m_run[:]),
                        waits=[("gpsimd", m_guard)], inc=True)
                else:
                    tr2_evac[ch % 2] = p.op("vector", lambda e, ch=ch, bank=bank: e.tensor_scalar(
                        out=zbuf[:, ch * 256:(ch + 1) * 256], in0=bank[:, 0:256], scalar1=0.0,
                        scalar2=(None if ch == 0 else m_run[:]), op0=ALU.add, op1=ALU.max, accum_out=m_run[:]),
                        waits=[("tensor", t_z)] + ([("gpsimd", m_guard)] if ch == 0 else []), inc=True)
            v_z = p.val("vector")
            g0 = p.op("gpsimd", lambda e: e.tensor_scalar(out=ntau[:], in0=m_run[:], scalar1=-1.0, scalar2=1.1,
                                                          op0=ALU.mult, op1=ALU.add),
                      waits=[("vector", v_z)], inc=True)
            g_nt = p.op("gpsimd", lambda e: e.tensor_scalar_mul(out=tau[:], in0=ntau[:], scalar1=-1.0), inc=True)
            m_guard = g_nt
            a_last = 0
            half_w = min(W, tf // 2) if tf >= 2 * D else W
            pieces = [(0, min(W, half_w))]
            if W > half_w:
                pieces.append((half_w, W))
            for it in range(5):
                for pi, (w0, w1) in enumerate(pieces):
                    a_f = p.op("scalar", lambda e, w0=w0, w1=w1, pi=pi: e.activation(
                        out=nscr[:, 0:w1 - w0], in_=zbuf[:, w0:w1], func=AF.Relu,
                        bias=ntau[:], accum_out=(fcol[:] if pi == 0 else fcol2[:])),
                        waits=[("gpsimd", g_nt)], inc=True)
                    v_c = p.op("vector", lambda e, w0=w0, w1=w1, pi=pi: e.tensor_scalar(
                        out=nscr2[:, w0:w1], in0=zbuf[:, w0:w1],
                        scalar1=tau[:], scalar2=0.0,
                        op0=ALU.is_gt, op1=ALU.add, accum_out=(ccol[:] if pi == 0 else ccol2[:])),
                        waits=[("gpsimd", g_nt)], inc=True)
                g_cmb = 0
                if len(pieces) > 1:
                    p.op("gpsimd", lambda e: e.tensor_tensor(out=fcol[:], in0=fcol[:], in1=fcol2[:], op=ALU.add),
                         waits=[("scalar", a_f), ("vector", v_c)], inc=True)
                    p.op("gpsimd", lambda e: e.tensor_tensor(out=ccol[:], in0=ccol[:], in1=ccol2[:], op=ALU.add),
                         inc=True)
                g_mx = p.op("gpsimd", lambda e: e.tensor_scalar_max(out=ccol[:], in0=ccol[:], scalar1=1.0),
                            waits=[("scalar", a_f), ("vector", v_c)], inc=True)
                v_r = p.op("vector", lambda e: e.reciprocal(out=rcol[:], in_=ccol[:]),
                           waits=[("gpsimd", g_mx)], inc=True)
                p.op("gpsimd", lambda e: e.tensor_scalar_add(out=fcol[:], in0=fcol[:], scalar1=-1.0),
                     waits=[("vector", v_r)], inc=True)
                p.op("gpsimd", lambda e: e.tensor_tensor(out=fcol[:], in0=fcol[:], in1=rcol[:], op=ALU.mult), inc=True)
                p.op("gpsimd", lambda e: e.tensor_tensor(out=tau[:], in0=tau[:], in1=fcol[:], op=ALU.add), inc=True)
                g_nt = p.op("gpsimd", lambda e: e.tensor_scalar_mul(out=ntau[:], in0=tau[:], scalar1=-1.0), inc=True)
                a_last = a_f
            g_fin = p.val("gpsimd")
            zevac_guard = a_last
            zevac_guard_v = p.val("vector")
            t_r = p.op("tensor", lambda e: e.matmul(ps_r[:], ntau[:], ident[:], start=True, stop=True),
                       waits=[("gpsimd", g_fin), ("scalar", ntrow_ev)], inc=True)
            ntrow_ev = p.op("scalar", lambda e: e.activation(out=ntrow[:], in_=ps_r[:], func=AF.Identity),
                            waits=[("tensor", t_r)], inc=True)
            for kb in range(nkb):
                bank = ps_zT[kb % 2]
                wl = [("scalar", max(ab_relu[kb % 2], ntrow_ev))]
                p.op("tensor", lambda e, kb=kb, h=h, bank=bank, qs=qs: e.matmul(
                    bank[:], kbf[(h % 2) * 64:(h % 2) * 64 + 64, (h // 2) * tf + kb * 128:(h // 2) * tf + (kb + 1) * 128],
                    qs, start=True, stop=False), waits=wl)
                t_zt = p.op("tensor", lambda e, bank=bank: e.matmul(
                    bank[:], fones[0:1, :], ntrow[0:1, :], start=False, stop=True), inc=True)
                if kb >= nkb - 2:
                    moff = (kb - (nkb - 2)) * 128
                    v_m = p.op("vector", lambda e, bank=bank, moff=moff: e.tensor_tensor(
                        out=ascr[:, 0:128], in0=bank[:], in1=mqTs[:, moff:moff + 128], op=ALU.add),
                        waits=[("tensor", t_zt)], inc=True)
                    ab_relu[kb % 2] = p.op("scalar", lambda e, kb=kb: e.activation(
                        out=abuf[kb % 2][:], in_=ascr[:, 0:128], func=AF.Relu), waits=[("vector", v_m)], inc=True)
                else:
                    ab_relu[kb % 2] = p.op("scalar", lambda e, kb=kb, bank=bank: e.activation(
                        out=abuf[kb % 2][:], in_=bank[:], func=AF.Relu), waits=[("tensor", t_zt)], inc=True)
                p.op("tensor", lambda e, kb=kb, h=h, nkb=nkb: e.matmul(
                    ps_o[:], vbf[:, kb * D + h * 64: kb * D + (h + 1) * 64], abuf[kb % 2][:],
                    start=(kb == 0), stop=(kb == nkb - 1)),
                     waits=[("scalar", ab_relu[kb % 2])] + ([("scalar", last_o_ev)] if kb == 0 else []),
                     inc=(kb == nkb - 1))
            t_av = p.val("tensor")
            last_o_ev = p.op("scalar", lambda e, h=h: e.activation(
                out=outf[(h % 2) * 64:(h % 2) * 64 + 64, (h // 2) * 128:(h // 2) * 128 + 128],
                in_=ps_o[:], func=AF.Identity), waits=[("tensor", t_av)], inc=True)
        outf_ev = last_o_ev
        for ft in range(2):
            bank = ps_z[ft]
            for hk in range(HCH):
                wl = []
                if hk == 0:
                    wl = [("scalar", outf_ev), ("vector", tr2_evac[0]), ("vector", tr2_evac[1]),
                          ("dma_wproj", wv_p)]
                p.op("tensor", lambda e, ft=ft, hk=hk, bank=bank: e.matmul(
                    bank[:], outf[:, hk * 128:(hk + 1) * 128],
                    wslice(hk, D + ft * 512, D + (ft + 1) * 512), start=(hk == 0), stop=False), waits=wl)
            t_op = p.op("tensor", lambda e, ft=ft, bank=bank: e.matmul(
                bank[:], bones[0:1, :], bprow[0:1, ft * 512:(ft + 1) * 512], start=False, stop=True), inc=True)
            prev_outp = p.op("vector", lambda e, ft=ft, bank=bank: e.tensor_tensor(
                out=outt[:, ft * 512:(ft + 1) * 512], in0=bank[:], in1=xt[:, ft * 512:(ft + 1) * 512], op=ALU.add),
                waits=[("tensor", t_op), ("dma_stout", prev_store)], inc=True)
        prev_store = p.op("sync", lambda e, i=i: e.dma_start(out=xo[i * 128:(i + 1) * 128, :], in_=outt[:]),
                          dma="stout", waits=[("vector", prev_outp)])


def prep_l0attn_weights(I):
    b_attn = I["L0_attn_b_attn"].astype(np.float32)
    b_qk = np.concatenate([
        b_attn[0:1024].reshape(8, 128).T,
        b_attn[1024:2048].reshape(8, 128).T], axis=1)
    return {
        "w_attn": np.ascontiguousarray(I["L0_attn_w_attn"]),
        "b_qk": np.ascontiguousarray(b_qk),
        "b_v": np.ascontiguousarray(b_attn[2048:3072].reshape(1, 1024)),
        "w_proj": np.ascontiguousarray(I["L0_attn_w_proj"]),
        "b_proj": np.ascontiguousarray(I["L0_attn_b_proj"].reshape(1, 1024)),
    }


def l0attn_masks(par):
    q = np.arange(128)
    tri = np.where(q[:, None] >= np.arange(128)[None, :], 0.0, -1e9).astype(np.float32)
    full = np.full((128, 128), -1e9, np.float32)
    zero = np.zeros((128, 128), np.float32)
    if par == 0:
        mq = np.concatenate([tri, full], axis=1)
        mqT = np.concatenate([tri.T, full], axis=1)
    else:
        mq = np.concatenate([zero, tri], axis=1)
        mqT = np.concatenate([zero, tri.T], axis=1)
    return np.ascontiguousarray(mq), np.ascontiguousarray(mqT)


_L0ATTN_NC = None


def run_l0attn(x, I):
    global _L0ATTN_NC
    if _L0ATTN_NC is None:
        _L0ATTN_NC = build_l0attn()
    w = prep_l0attn_weights(I)
    shards = shard_tokens(x)
    in_maps = []
    for c in range(NCORE):
        b, par = c // 2, c % 2
        mqa, mqTa = l0attn_masks(par)
        in_maps.append(dict(w, xf=np.ascontiguousarray(x[b].astype(np.float32)),
                            xq=shards[c], mq=mqa, mqT=mqTa))
    res = run_bass_kernel_spmd(_L0ATTN_NC, in_maps, core_ids=list(range(NCORE)))
    return unshard_tokens([res.results[c]["xo"] for c in range(NCORE)])



def emit_l1attn(p, xf, filt, gamma, beta, xo, tf=T):
    HCH = 8
    nft = tf // 128
    n1, n2 = tf // 2, tf // 4

    xt = p.sbuf("xt", [128, D], F32)
    yt = p.sbuf("yt", [128, D], F32)
    cols = p.sbuf("cols", [128, 16], F32)
    ln_bufs = {"ln_sum": cols[:, 0:1], "ln_ss": cols[:, 1:2], "ln_t1": cols[:, 2:3], "ln_t2": cols[:, 3:4]}
    ln_scr = p.sbuf("ln_scr", [128, D], F32)
    ident = p.sbuf("ident", [128, 128], F32)
    identb = p.sbuf("identb", [128, 128], BF16)
    fones = p.sbuf("fones", [1, 128], F32)
    filts = p.sbuf("filts", [128, 128], F32)
    grow = p.sbuf("grow", [1, D], F32)
    brow = p.sbuf("brow", [1, D], F32)
    gbc = p.sbuf("gbc", [128, D], F32)
    bbc = p.sbuf("bbc", [128, D], F32)
    ybf = p.sbuf("ybf", [128, HCH * tf], BF16)
    ypad = p.sbuf("ypad", [128, tf + 4], F32)
    l1pad = p.sbuf("l1pad", [128, n1 + 4], F32)
    h1 = p.sbuf("h1", [128, n1], F32)
    l2 = p.sbuf("l2", [128, n2], F32)
    h2 = p.sbuf("h2", [128, n2], F32)
    qef = p.sbuf("qef", [128, n2], F32)
    qet = p.sbuf("qet", [128, n2], F32)
    vt = p.sbuf("vt", [128, n2], F32)
    kvz = p.sbuf("kvz", [128, 65], F32)
    lowg = p.sbuf("lowg", [128, n2], F32)
    reppad = p.sbuf("reppad", [128, tf + 4], F32)
    low1pad = p.sbuf("low1pad", [128, n1 + 4], F32)
    low0 = p.sbuf("low0", [128, tf], F32)
    den = p.sbuf("den", [1, n2], F32)
    lowt = p.sbuf("lowt", [128, D], F32)
    ps_a = [p.psum(f"ps_a{i}", [128, 512], F32) for i in range(2)]
    ps_kv = p.psum("ps_kv", [64, 64], F32)
    ps_z = p.psum("ps_z", [64, 1], F32)
    ps_n = p.psum("ps_n", [65, 512], F32)
    ps_t = p.psum("ps_t", [128, 128], F32)
    ocol = cols[:, 15:16]

    g0 = p.op("gpsimd", lambda e: e.memset(ident[:], 0.0), inc=True)
    p.op("gpsimd", lambda e: e.affine_select(out=ident[:], in_=ident[:], compare_op=ALU.not_equal,
                                             fill=1.0, base=0, pattern=[[-1, 128]], channel_multiplier=1),
         inc=True, waits=[("gpsimd", g0)])
    p.op("gpsimd", lambda e: e.memset(fones[:], 1.0), inc=True)
    gi = p.op("gpsimd", lambda e: e.tensor_copy(out=identb[:], in_=ident[:]), inc=True)
    p.op("gpsimd", lambda e: e.memset(ocol[:], 1.0), inc=True)
    p.op("gpsimd", lambda e: e.memset(ypad[:], 0.0), inc=True)
    p.op("gpsimd", lambda e: e.memset(l1pad[:], 0.0), inc=True)
    p.op("gpsimd", lambda e: e.memset(reppad[:], 0.0), inc=True)
    g_const = p.op("gpsimd", lambda e: e.memset(low1pad[:], 0.0), inc=True)
    ld_c = p.op("sync", lambda e: e.dma_start(out=filts[:], in_=filt), dma="const")
    p.op("sync", lambda e: e.dma_start(out=grow[:], in_=gamma), dma="const")
    ld_c = p.op("sync", lambda e: e.dma_start(out=brow[:], in_=beta), dma="const")
    for half in range(2):
        bank = ps_a[half]
        t_g = p.op("tensor", lambda e, half=half, bank=bank: e.matmul(
            bank[:], fones[:], grow[0:1, half * 512:(half + 1) * 512], start=True, stop=True),
            waits=[("dma_const", ld_c), ("gpsimd", g_const)], inc=True)
        v_g = p.op("vector", lambda e, half=half, bank=bank: e.tensor_copy(out=gbc[:, half * 512:(half + 1) * 512], in_=bank[:]),
                   waits=[("tensor", t_g)], inc=True)
        t_b = p.op("tensor", lambda e, half=half, bank=bank: e.matmul(
            bank[:], fones[:], brow[0:1, half * 512:(half + 1) * 512], start=True, stop=True),
            waits=[("vector", v_g)], inc=True)
        v_b = p.op("vector", lambda e, half=half, bank=bank: e.tensor_copy(out=bbc[:, half * 512:(half + 1) * 512], in_=bank[:]),
                   waits=[("tensor", t_b)], inc=True)
    const_v = p.val("vector")

    prev_ln = 0
    tr_evac = [0, 0]
    for t in range(nft):
        ld = p.op("sync", lambda e, t=t: e.dma_start(out=xt[:], in_=xf[t * 128:(t + 1) * 128, :]),
                  dma="ldx", waits=[("scalar", prev_ln)])
        p.op("vector", lambda e: e.nop(), waits=[("dma_ldx", ld)])
        p.op("scalar", lambda e: e.nop(), waits=[("dma_ldx", ld)])
        prev_ln = emit_ln_tile(p, ln_bufs, xt[:], yt[:], scr=ln_scr[:])
        for hk in range(HCH):
            pt = ps_a[hk % 2][:, 0:128]
            t_tr = p.op("tensor", lambda e, pt=pt, hk=hk: e.transpose(pt, yt[:, hk * 128:(hk + 1) * 128], ident[:]),
                        waits=[("scalar", prev_ln), ("vector", max(tr_evac[hk % 2], const_v))], inc=True)
            tr_evac[hk % 2] = p.op("vector", lambda e, pt=pt, hk=hk, t=t: e.tensor_copy(
                out=ybf[:, hk * tf + t * 128: hk * tf + (t + 1) * 128], in_=pt), waits=[("tensor", t_tr)], inc=True)
    passA_v = p.val("vector")

    def fcol_ap(f, g, k):
        return filts[:, f * 32 + g * 4 + k: f * 32 + g * 4 + k + 1]

    def conv_strided(dst_ap, src, n_out, f, g, first_waits=()):
        for k in range(4):
            if k == 0:
                p.op("vector", lambda e, k=k: e.tensor_scalar(
                    out=dst_ap, in0=src[:, k: k + 2 * n_out: 2], scalar1=fcol_ap(f, g, k), scalar2=None,
                    op0=ALU.mult), waits=list(first_waits), inc=True)
            else:
                p.op("vector", lambda e, k=k: e.scalar_tensor_tensor(
                    out=dst_ap, in0=src[:, k: k + 2 * n_out: 2], scalar=fcol_ap(f, g, k), in1=dst_ap,
                    op0=ALU.mult, op1=ALU.add), inc=True)
        return p.val("vector")

    def conv_unit(dst_ap, src, n_out, f, g, first_waits=(), accumulate=False):
        for k in range(4):
            if k == 0 and not accumulate:
                p.op("vector", lambda e, k=k: e.tensor_scalar(
                    out=dst_ap, in0=src[:, k: k + n_out], scalar1=fcol_ap(f, g, k), scalar2=None,
                    op0=ALU.mult), waits=list(first_waits), inc=True)
            else:
                p.op("vector", lambda e, k=k: e.scalar_tensor_tensor(
                    out=dst_ap, in0=src[:, k: k + n_out], scalar=fcol_ap(f, g, k), in1=dst_ap,
                    op0=ALU.mult, op1=ALU.add), inc=True)
        return p.val("vector")

    low0_done = []
    ps_n_free = 0
    for g in range(HCH):
        v_y = p.op("vector", lambda e, g=g: e.tensor_copy(out=ypad[:, 1: 1 + tf], in_=ybf[:, g * tf:(g + 1) * tf]),
                   waits=[("vector", passA_v), ("dma_const", ld_c)], inc=True)
        conv_strided(l1pad[:, 1: 1 + n1], ypad, n1, 0, g)
        v_c0 = conv_strided(h1[:, 0:n1], ypad, n1, 1, g)
        conv_strided(l2[:, 0:n2], l1pad, n2, 0, g)
        v_c1 = conv_strided(h2[:, 0:n2], l1pad, n2, 1, g)
        v_m = p.op("vector", lambda e: e.tensor_scalar_min(out=qef[:], in0=l2[:], scalar1=0.0), inc=True)
        a_e = p.op("scalar", lambda e: e.activation(out=qef[:], in_=qef[:], func=AF.Exp),
                   waits=[("vector", v_m)], inc=True)
        v_r = p.op("vector", lambda e: e.tensor_scalar_max(out=ln_scr[:, 0:n2], in0=l2[:], scalar1=0.0),
                   waits=[("scalar", a_e)], inc=True)
        v_q = p.op("vector", lambda e: e.tensor_add(out=qef[:], in0=qef[:], in1=ln_scr[:, 0:n2]), inc=True)
        tr_ev = 0
        for chk in range(n2 // 128):
            pt = ps_a[chk % 2][:, 0:128]
            t_t1 = p.op("tensor", lambda e, chk=chk, pt=pt: e.transpose(pt, qef[:, chk * 128:(chk + 1) * 128], ident[:]),
                        waits=[("vector", v_q), ("vector", tr_ev)], inc=True)
            tr_ev = p.op("vector", lambda e, chk=chk, pt=pt: e.tensor_copy(out=qet[:, chk * 128:(chk + 1) * 128], in_=pt),
                         waits=[("tensor", t_t1)], inc=True)
            t_t2 = p.op("tensor", lambda e, chk=chk, pt=pt: e.transpose(pt, l2[:, chk * 128:(chk + 1) * 128], ident[:]),
                        waits=[("vector", tr_ev)], inc=True)
            tr_ev = p.op("vector", lambda e, chk=chk, pt=pt: e.tensor_copy(out=vt[:, chk * 128:(chk + 1) * 128], in_=pt),
                         waits=[("tensor", t_t2)], inc=True)
        for h in range(2):
            t_kv = t_z = 0
            for chk in range(n2 // 128):
                wl = [("vector", tr_ev)] if chk == 0 else []
                last = chk == n2 // 128 - 1
                t_kv = p.op("tensor", lambda e, chk=chk, h=h, last=last: e.matmul(
                    ps_kv[:], qet[0:128, chk * 128 + h * 64: chk * 128 + h * 64 + 64],
                    vt[0:128, chk * 128 + h * 64: chk * 128 + h * 64 + 64],
                    start=(chk == 0), stop=last), waits=wl, inc=last)
                t_z = p.op("tensor", lambda e, chk=chk, h=h, last=last: e.matmul(
                    ps_z[:], qet[0:128, chk * 128 + h * 64: chk * 128 + h * 64 + 64], ocol[:],
                    start=(chk == 0), stop=last), inc=last)
            v_kv = p.op("vector", lambda e, h=h: e.tensor_copy(out=kvz[h * 64:h * 64 + 64, 0:64], in_=ps_kv[:]),
                        waits=[("tensor", t_kv)], inc=True)
            v_kz = p.op("vector", lambda e, h=h: e.tensor_copy(out=kvz[h * 64:h * 64 + 64, 64:65], in_=ps_z[:]),
                        waits=[("tensor", t_z)], inc=True)
            for nh in range(n2 // 512):
                t_n = p.op("tensor", lambda e, nh=nh, h=h: e.matmul(
                    ps_n[:], kvz[h * 64:h * 64 + 64, 0:65], qef[h * 64:(h + 1) * 64, nh * 512:(nh + 1) * 512],
                    start=True, stop=True), waits=[("vector", v_kz), ("vector", ps_n_free)], inc=True)
                v_d = p.op("vector", lambda e, nh=nh: e.tensor_scalar_add(out=den[0:1, nh * 512:(nh + 1) * 512],
                                                                          in0=ps_n[64:65, :], scalar1=1e-6),
                           waits=[("tensor", t_n)], inc=True)
                v_dr = p.op("vector", lambda e, nh=nh: e.reciprocal(out=den[0:1, nh * 512:(nh + 1) * 512],
                                                                    in_=den[0:1, nh * 512:(nh + 1) * 512]),
                            waits=[("vector", v_d)], inc=True)
                t_bc = p.op("tensor", lambda e, nh=nh: e.matmul(ps_a[0][:, 0:512], fones[:],
                                                                den[0:1, nh * 512:(nh + 1) * 512], start=True, stop=True),
                            waits=[("vector", v_dr)], inc=True)
                v_rcb = p.op("vector", lambda e: e.tensor_copy(out=ln_scr[0:64, 0:512], in_=ps_a[0][0:64, 0:512]),
                             waits=[("tensor", t_bc)], inc=True)
                ps_n_free = v_low = p.op("vector", lambda e, nh=nh, h=h: e.tensor_tensor(
                    out=lowg[h * 64:(h + 1) * 64, nh * 512:(nh + 1) * 512], in0=ps_n[0:64, :],
                    in1=ln_scr[0:64, 0:512], op=ALU.mult), waits=[("vector", v_rcb)], inc=True)
        v_attn = p.val("vector")
        p.op("vector", lambda e: e.memset(reppad[:, 1 + n1: 4 + n1], 0.0), waits=[("vector", v_attn)], inc=True)
        v_rp1 = p.op("vector", lambda e: e.tensor_copy(out=reppad[:, 1: 1 + n1: 2], in_=lowg[:, 0:n2]),
                     inc=True)
        v_rp2 = p.op("vector", lambda e: e.tensor_copy(out=reppad[:, 2: 2 + n1: 2], in_=lowg[:, 0:n2]), inc=True)
        conv_unit(low1pad[:, 1: 1 + n1], reppad, n1, 2, g, first_waits=[("vector", v_rp2)])
        v_rp3 = p.op("vector", lambda e: e.tensor_copy(out=reppad[:, 1: 1 + n1: 2], in_=h2[:, 0:n2]), inc=True)
        v_rp4 = p.op("vector", lambda e: e.tensor_copy(out=reppad[:, 2: 2 + n1: 2], in_=h2[:, 0:n2]), inc=True)
        v_l1 = conv_unit(low1pad[:, 1: 1 + n1], reppad, n1, 3, g, first_waits=[("vector", v_rp4)], accumulate=True)
        v_rp5 = p.op("vector", lambda e: e.tensor_copy(out=reppad[:, 1: 1 + tf: 2], in_=low1pad[:, 1: 1 + n1]),
                     waits=[("vector", v_l1)], inc=True)
        v_rp6 = p.op("vector", lambda e: e.tensor_copy(out=reppad[:, 2: 2 + tf: 2], in_=low1pad[:, 1: 1 + n1]), inc=True)
        conv_unit(low0[:, 0:tf], reppad, tf, 2, g, first_waits=[("vector", v_rp6)])
        v_rp7 = p.op("vector", lambda e: e.tensor_copy(out=reppad[:, 1: 1 + tf: 2], in_=h1[:, 0:n1]), inc=True)
        v_rp8 = p.op("vector", lambda e: e.tensor_copy(out=reppad[:, 2: 2 + tf: 2], in_=h1[:, 0:n1]), inc=True)
        v_l0 = conv_unit(low0[:, 0:tf], reppad, tf, 3, g, first_waits=[("vector", v_rp8)], accumulate=True)
        v_ov = p.op("vector", lambda e, g=g: e.tensor_copy(out=ybf[:, g * tf:(g + 1) * tf], in_=low0[:, 0:tf]),
                    waits=[("vector", v_l0)], inc=True)
        low0_done.append(v_ov)
    passB_v = p.val("vector")

    prev_st = 0
    tr_ev2 = [0, 0]
    prev_fin = passB_v
    for t in range(nft):
        ld = p.op("sync", lambda e, t=t: e.dma_start(out=xt[:], in_=xf[t * 128:(t + 1) * 128, :]),
                  dma="ldx", waits=[("scalar", prev_ln), ("vector", prev_fin), ("dma_stout", prev_st)])
        p.op("vector", lambda e: e.nop(), waits=[("dma_ldx", ld)])
        p.op("scalar", lambda e: e.nop(), waits=[("dma_ldx", ld)])
        prev_ln = emit_ln_tile(p, ln_bufs, xt[:], yt[:], scr=ln_scr[:])
        for hk in range(HCH):
            t_tr = p.op("tensor", lambda e, hk=hk, t=t: e.matmul(
                ps_t[:], ybf[:, hk * tf + t * 128: hk * tf + (t + 1) * 128], identb[:],
                start=True, stop=True),
                waits=[("vector", max(tr_ev2[hk % 2], passB_v)), ("scalar", prev_ln),
                       ("vector", tr_ev2[(hk + 1) % 2])], inc=True)
            tr_ev2[hk % 2] = p.op("vector", lambda e, hk=hk: e.tensor_copy(
                out=lowt[:, hk * 128:(hk + 1) * 128], in_=ps_t[:]), waits=[("tensor", t_tr)], inc=True)
        v_xl = p.op("vector", lambda e: e.tensor_add(out=yt[:], in0=yt[:], in1=lowt[:]),
                    waits=[("vector", tr_ev2[0]), ("vector", tr_ev2[1])], inc=True)
        ln2 = emit_ln_tile(p, ln_bufs, yt[:], lowt[:], scr=ln_scr[:], pre_v=v_xl)
        v_f1 = p.op("vector", lambda e: e.tensor_mul(out=lowt[:], in0=lowt[:], in1=gbc[:]),
                    waits=[("scalar", ln2)], inc=True)
        v_f2 = p.op("vector", lambda e: e.tensor_add(out=lowt[:], in0=lowt[:], in1=bbc[:]), inc=True)
        prev_fin = p.op("vector", lambda e: e.tensor_add(out=yt[:], in0=lowt[:], in1=xt[:]), inc=True)
        prev_st = p.op("sync", lambda e, t=t: e.dma_start(out=xo[t * 128:(t + 1) * 128, :], in_=yt[:]),
                       dma="stout", waits=[("vector", prev_fin)])
    p.op("sync", lambda e: e.nop(), waits=[("dma_stout", p.esem[p.dsem("stout")].n)])


def build_l1attn(tf=T):
    nc = bass.Bass("TRN2", target_bir_lowering=False, debug=False, name="l1attn")
    xf = nc.dram_tensor("xf", [tf, D], F32, kind="ExternalInput").ap()
    filt = nc.dram_tensor("filt", [128, 128], F32, kind="ExternalInput").ap()
    gamma = nc.dram_tensor("gamma", [1, D], F32, kind="ExternalInput").ap()
    beta = nc.dram_tensor("beta", [1, D], F32, kind="ExternalInput").ap()
    xo = nc.dram_tensor("xo", [tf, D], F32, kind="ExternalOutput").ap()

    p = Prog(nc)
    emit_l1attn(p, xf, filt, gamma, beta, xo, tf)
    p.finish()
    return nc


def prep_l1attn_weights(I):
    fs = []
    for nm in ("L1_attn_h_filter", "L1_attn_g_filter", "L1_attn_h_synth", "L1_attn_g_synth"):
        f = I[nm].astype(np.float32)
        fs.append(f.reshape(4, 8, 128).transpose(2, 1, 0).reshape(128, 32))
    filt = np.ascontiguousarray(np.concatenate(fs, axis=1))
    return {
        "filt": filt,
        "gamma": np.ascontiguousarray(I["L1_attn_norm_gamma"].reshape(1, D)),
        "beta": np.ascontiguousarray(I["L1_attn_norm_beta"].reshape(1, D)),
    }


_L1ATTN_NC = None


def run_l1attn(x, I):
    global _L1ATTN_NC
    if _L1ATTN_NC is None:
        _L1ATTN_NC = build_l1attn()
    w = prep_l1attn_weights(I)
    in_maps = [dict(w, xf=np.ascontiguousarray(x[c // 2].astype(np.float32))) for c in range(NCORE)]
    res = run_bass_kernel_spmd(_L1ATTN_NC, in_maps, core_ids=list(range(NCORE)))
    out = np.empty((B, T, D), np.float32)
    for b in range(B):
        out[b] = res.results[2 * b]["xo"]
    return out


def emit_l1ffn(p, xf, wqkv, lncol, rt4, qcn, wq_in, wkv_in, wout_in, wlo_in, nks, nvs,
               rotm, cossin, wmaskin, mlocQ_in, mlocKT_in, xo,
               s_hbm, yf_hbm, yr_hbm, ql_hbm, kl_hbm, vl_hbm, qg_hbm, kg_hbm, vg_hbm, tf=T):
    import math as _math
    HCH = 8
    nft = tf // 128
    NW = tf // 512
    NWIN = NW - 1
    NCTX = tf - 512
    NKB = NCTX // 128
    K_Q, K_KV = 256, 1024
    logk_q = float(_math.log(min(256 * 9.0 / 8.0, 512.0)))
    logk_kv = float(_math.log(min(1024 * 9.0 / 8.0, float(NCTX))))
    EPSL = [max(4.0 * 0.7 ** i, 0.03) for i in range(20)]
    FE = 0.03

    xt = p.sbuf("xt", [128, D], F32)
    yt = p.sbuf("yt", [128, D], F32)
    yr = p.sbuf("yr", [128, D], F32)
    cols = p.sbuf("cols", [128, 32], F32)
    ln_bufs = {"ln_sum": cols[:, 0:1], "ln_ss": cols[:, 1:2], "ln_t1": cols[:, 2:3], "ln_t2": cols[:, 3:4]}
    fac = cols[:, 4:5]
    ifac = cols[:, 5:6]
    ccol = cols[:, 6:7]
    ocol = cols[:, 8:9]
    ln_scr = p.sbuf("ln_scr", [128, D], F32)
    ident = p.sbuf("ident", [128, 128], F32)
    identb = p.sbuf("identb", [128, 128], BF16)
    fones = p.sbuf("fones", [1, 128], F32)
    bones = p.sbuf("bones", [1, 128], BF16)
    yf32 = p.sbuf("yf32", [128, HCH * 128], F32)
    lnc = p.sbuf("lnc", [128, 16], F32)
    lnbb = p.sbuf("lnbb", [128, 8], BF16)
    rts = p.sbuf("rts", [128, 32], F32)
    qcns = p.sbuf("qcns", [128, 32], F32)
    nkss = p.sbuf("nkss", [128, 32], BF16)
    rotms = p.sbuf("rotms", [128, 128], BF16)
    selbT = p.sbuf("selbT", [128, NKB * 16], F32)
    qselT = p.sbuf("qselT", [128, 4 * 16], F32)
    accw = p.sbuf("accw", [128, 4 * 1024], F32)
    qkvb = p.sbuf("qkvb", [128, 24], F32)
    nmrow = p.sbuf("nmrow", [1, 512], F32)
    enull = p.sbuf("enull", [1, 512], F32)
    denr = p.sbuf("denr", [1, 512], F32)
    mvals = p.sbuf("mvals", [128, 16 * 4], F32)
    mlocQs = p.sbuf("mlocQs", [128, 11 * 128], F32)
    mlocKTs = p.sbuf("mlocKTs", [128, 11 * 128], F32)
    bocol = p.sbuf("bocol", [128, 1], BF16)
    AF32 = p.sbuf("AF32", [128, 14 * 1024], F32)
    AB16 = p.sbuf("AB16", [128, 46 * 1024], BF16)

    ps_t = [p.psum(f"ps_t{i}", [128, 512], F32) for i in range(2)]
    ps_u = [p.psum(f"ps_u{i}", [128, 512], F32) for i in range(2)]
    ps_o = p.psum("ps_o", [64, 512], F32)
    ps_d = p.psum("ps_d", [1, 512], F32)
    ps_r = p.psum("ps_r", [16, 512], F32)
    ps_s = p.psum("ps_s", [128, 512], F32)

    def barrier(extra_dma=()):
        vals = [(en, p.val(en)) for en in ("sync", "tensor", "vector", "scalar", "gpsimd")]
        dvals = [(p.dsem(nm), p.esem[p.dsem(nm)].n) for nm in extra_dma]
        dvals = [(k, v) for k, v in dvals]
        for en in ("sync", "tensor", "vector", "scalar", "gpsimd"):
            wl = [(e2, v) for e2, v in vals if e2 != en] + [(k, v) for k, v in dvals]
            p.op(en, lambda e: e.nop(), waits=wl, inc=True, chain=True)

    g0 = p.op("gpsimd", lambda e: e.memset(ident[:], 0.0), inc=True)
    p.op("gpsimd", lambda e: e.affine_select(out=ident[:], in_=ident[:], compare_op=ALU.not_equal,
                                             fill=1.0, base=0, pattern=[[-1, 128]], channel_multiplier=1),
         inc=True, waits=[("gpsimd", g0)])
    p.op("gpsimd", lambda e: e.tensor_copy(out=identb[:], in_=ident[:]), inc=True)
    p.op("gpsimd", lambda e: e.memset(fones[:], 1.0), inc=True)
    p.op("gpsimd", lambda e: e.memset(bones[:], 1.0), inc=True)
    p.op("gpsimd", lambda e: e.memset(ocol[:], 1.0), inc=True)
    p.op("gpsimd", lambda e: e.memset(bocol[:], 1.0), inc=True)
    g_const = p.op("gpsimd", lambda e: e.memset(cols[:, 20:21], -20.0), inc=True)
    ldc = 0
    for nm, dst in ((lncol, lnc), (rt4, rts), (qcn, qcns),
                    (mlocQ_in, mlocQs), (mlocKT_in, mlocKTs)):
        ldc = p.op("sync", lambda e, nm=nm, dst=dst: e.dma_start(out=dst[:], in_=nm), dma="const")
    p.op("gpsimd", lambda e: e.dma_start(out=rotms[:], in_=rotm), dma="gconst")
    gldc = p.op("gpsimd", lambda e: e.dma_start(out=nkss[:], in_=nks), dma="gconst")
    p.op("gpsimd", lambda e: e.tensor_copy(out=lnbb[:], in_=lnc[:, 8:16]),
         waits=[("dma_const", ldc)], inc=True)

    pbuf = AB16[:, 0:2048]
    srow = AF32[:, 0:4096]
    prev_ln = 0
    tr_evac = [0, 0]
    st_g = 0
    st_g2 = 0
    prev_srow = 0
    for t in range(nft):
        ld = p.op("sync", lambda e, t=t: e.dma_start(out=xt[:], in_=xf[t * 128:(t + 1) * 128, :]),
                  dma="ldx", waits=[("scalar", prev_ln)])
        p.op("vector", lambda e: e.nop(), waits=[("dma_ldx", ld)])
        p.op("scalar", lambda e: e.nop(), waits=[("dma_ldx", ld)])
        prev_ln = emit_ln_tile(p, ln_bufs, xt[:], yt[:], scr=ln_scr[:])
        v1 = p.op("vector", lambda e: e.tensor_mul(out=fac[:], in0=ln_bufs["ln_t1"][:], in1=ln_bufs["ln_t1"][:]),
                  waits=[("scalar", prev_ln)], inc=True)
        v2 = p.op("vector", lambda e: e.tensor_scalar(out=fac[:], in0=fac[:], scalar1=-LN_EPS * D, scalar2=float(D),
                                                      op0=ALU.mult, op1=ALU.add), inc=True)
        a_s = p.op("scalar", lambda e: e.activation(out=fac[:], in_=fac[:], func=AF.Sqrt), waits=[("vector", v2)], inc=True)
        p.op("vector", lambda e: e.tensor_scalar_max(out=fac[:], in0=fac[:], scalar1=1e-12),
             waits=[("scalar", a_s)], inc=True)
        p.op("vector", lambda e: e.tensor_scalar_mul(out=ifac[:], in0=fac[:], scalar1=1.0 / 32.0), inc=True)
        p.op("vector", lambda e: e.reciprocal(out=fac[:], in_=fac[:]), inc=True)
        v6 = p.op("vector", lambda e: e.tensor_scalar_mul(out=fac[:], in0=fac[:], scalar1=32.0), inc=True)
        a_yr = p.op("scalar", lambda e: e.activation(out=yr[:], in_=yt[:], func=AF.Identity, scale=fac[:]),
                    waits=[("vector", v6)], inc=True)
        for hk in range(HCH):
            pt = ps_t[hk % 2][:, 0:128]
            t_tr = p.op("tensor", lambda e, hk=hk, pt=pt: e.transpose(pt, yr[:, hk * 128:(hk + 1) * 128], ident[:]),
                        waits=[("scalar", a_yr), ("vector", tr_evac[hk % 2]), ("gpsimd", g_const)], inc=True)
            p.op("vector", lambda e, hk=hk, pt=pt: e.tensor_copy(out=yf32[:, hk * 128:(hk + 1) * 128], in_=pt),
                 waits=[("tensor", t_tr), ("dma_styr", st_g)], inc=True)
            tr_evac[hk % 2] = p.op("vector", lambda e, hk=hk: e.tensor_copy(
                out=pbuf[:, hk * 128:(hk + 1) * 128], in_=yf32[:, hk * 128:(hk + 1) * 128]), inc=True)
        v_tr = p.val("vector")
        st_g = p.op("gpsimd", lambda e, t=t: e.dma_start(
            out=yr_hbm.ap().rearrange("p (hc tt) -> p hc tt", hc=8, tt=tf)[:, :, t * 128:(t + 1) * 128],
            in_=pbuf[:, 0:1024].rearrange("p (hc c) -> p hc c", hc=8, c=128)), dma="styr",
            waits=[("vector", v_tr)])
        for hk in range(HCH):
            p.op("tensor", lambda e, hk=hk: e.matmul(ps_s[:, 0:4],
                                                     yf32[:, hk * 128:(hk + 1) * 128], rts[:, hk * 4:(hk + 1) * 4],
                                                     start=(hk == 0), stop=(hk == HCH - 1)),
                 waits=([("vector", v_tr), ("dma_const", ldc), ("vector", prev_srow)] if hk == 0 else []),
                 inc=(hk == HCH - 1))
        t_s = p.val("tensor")
        v_s = p.op("vector", lambda e: e.tensor_scalar(out=cols[:, 12:16], in0=ps_s[:, 0:4], scalar1=ifac[:],
                                                       scalar2=None, op0=ALU.mult), waits=[("tensor", t_s)], inc=True)
        t_st = p.op("tensor", lambda e: e.matmul(ps_r[0:4, 0:128], cols[:, 12:16], ident[:], start=True, stop=True),
                    waits=[("vector", v_s)], inc=True)
        prev_srow = p.op("vector", lambda e, t=t: e.tensor_copy(out=srow[0:4, t * 128:(t + 1) * 128], in_=ps_r[0:4, 0:128]),
                         waits=[("tensor", t_st)], inc=True)
        for hk in range(HCH):
            pt = ps_t[hk % 2][:, 0:128]
            t_tr = p.op("tensor", lambda e, hk=hk, pt=pt: e.transpose(pt, yt[:, hk * 128:(hk + 1) * 128], ident[:]),
                        waits=[("vector", tr_evac[hk % 2])], inc=True)
            tr_evac[hk % 2] = p.op("vector", lambda e, hk=hk, pt=pt: e.tensor_copy(
                out=pbuf[:, 1024 + hk * 128: 1024 + (hk + 1) * 128], in_=pt),
                waits=[("tensor", t_tr), ("dma_styf", st_g2)], inc=True)
        v_tr2 = p.val("vector")
        st_g2 = p.op("gpsimd", lambda e, t=t: e.dma_start(
            out=yf_hbm.ap().rearrange("p (hc tt) -> p hc tt", hc=8, tt=tf)[:, :, t * 128:(t + 1) * 128],
            in_=pbuf[:, 1024:2048].rearrange("p (hc c) -> p hc c", hc=8, c=128)), dma="styf",
            waits=[("vector", v_tr2)])
    st_s = p.op("sync", lambda e: e.dma_start(out=s_hbm[:, :], in_=srow[0:4, 0:tf]), dma="sts",
                waits=[("vector", p.val("vector"))])
    barrier(extra_dma=("styr", "styf", "sts", "const", "gconst"))

    emit_l1ffn_routers(p, locals())
    barrier()
    emit_l1ffn_proj(p, locals())
    barrier(extra_dma=("stql", "stkl", "stvl", "stqg", "stkg", "stvg", "ldyf0", "ldyf1", "ldcs"))
    emit_l1ffn_attn(p, locals())
    p.op("sync", lambda e: e.nop(), waits=[("dma_stout", p.esem[p.dsem("stout")].n)])


def build_l1ffn(tf=T):
    import math as _math
    nc = bass.Bass("TRN2", target_bir_lowering=False, debug=False, name="l1ffn")
    API = lambda n, s: nc.dram_tensor(n, s, F32, kind="ExternalInput").ap()
    xf = API("xf", [tf, D])
    wqkv = API("wqkv", [D, 3072])
    lncol = API("lncol", [128, 16])
    rt4 = API("rt4", [128, 32])
    qcn = API("qcn", [128, 32])
    wq_in = API("wq", [2, D, 1024])
    wkv_in = API("wkv", [2, D, 2048])
    wout_in = API("wout", [2, D, 1024])
    wlo_in = API("wlo", [D, 1024])
    nks = API("nks", [128, 32])
    nvs = API("nvs", [1, 2048])
    rotm = API("rotm", [128, 128])
    cossin = API("cossin", [128, 2 * tf])
    wmaskin = API("wmaskin", [16, tf - 512])
    mlocQ_in = API("mlocQ", [128, 11 * 128])
    mlocKT_in = API("mlocKT", [128, 11 * 128])
    xo = nc.dram_tensor("xo", [tf, D], F32, kind="ExternalOutput").ap()
    s_hbm = nc.dram_tensor("s_hbm", [4, tf], F32)
    yf_hbm = nc.dram_tensor("yf_hbm", [128, 8 * tf], BF16)
    yr_hbm = nc.dram_tensor("yr_hbm", [128, 8 * tf], BF16)
    ql_hbm = nc.dram_tensor("ql_hbm", [128, 8 * tf], BF16)
    kl_hbm = nc.dram_tensor("kl_hbm", [128, 8 * tf], BF16)
    vl_hbm = nc.dram_tensor("vl_hbm", [tf, D], BF16)
    qg_hbm = nc.dram_tensor("qg_hbm", [2, 128, 8 * tf], BF16)
    kg_hbm = nc.dram_tensor("kg_hbm", [2, 128, 8 * tf], BF16)
    vg_hbm = nc.dram_tensor("vg_hbm", [2, tf, D], BF16)

    p = Prog(nc)
    emit_l1ffn(p, xf, wqkv, lncol, rt4, qcn, wq_in, wkv_in, wout_in, wlo_in, nks, nvs,
               rotm, cossin, wmaskin, mlocQ_in, mlocKT_in, xo,
               s_hbm, yf_hbm, yr_hbm, ql_hbm, kl_hbm, vl_hbm, qg_hbm, kg_hbm, vg_hbm, tf)
    p.finish()
    return nc


def emit_l1ffn_routers(p, c):
    NCTX = c["NCTX"]; NKB = c["NKB"]; NWIN = c["NWIN"]
    AF32 = c["AF32"]; r16 = None
    cols = c["cols"]; ident = c["ident"]; s_hbm = c["s_hbm"]
    wmaskin = c["wmaskin"]; selbT = c["selbT"]; qselT = c["qselT"]
    ps_s = c["ps_s"]
    K_Q, K_KV = c["K_Q"], c["K_KV"]
    logk_q, logk_kv = c["logk_q"], c["logk_kv"]
    EPSL = c["EPSL"]
    ALUx = ALU

    r16b = p.sbuf("r16b", [16, 16], F32)

    def run_router(N, K, logk, src_rows, masked, sel_out_fn, start_t=0, start_v=0):
        S = AF32[0:16, 0:N]
        BP = AF32[0:16, N:2 * N]
        CU = AF32[0:16, 2 * N:3 * N]
        ONE = AF32[0:16, 3 * N:4 * N]
        a_col = r16b[:, 0:1]; m_col = r16b[:, 1:2]; nm_col = r16b[:, 2:3]
        ssum = r16b[:, 3:4]; lo = r16b[:, 4:5]; hi = r16b[:, 5:6]
        mid = r16b[:, 6:7]; cnt = r16b[:, 7:8]; cge = r16b[:, 8:9]
        dcol = r16b[:, 9:10]; room = r16b[:, 10:11]
        ldr = 0
        if masked:
            ldr = p.op("sync", lambda e: e.dma_start(out=BP[:], in_=wmaskin), dma="ldr",
                       waits=[("tensor", start_t), ("vector", start_v)])
        for r in range(16):
            ldr = p.op("sync", lambda e, r=r, ap=src_rows[r]: e.dma_start(out=S[r:r + 1, :], in_=ap),
                       dma="ldr", waits=[("tensor", start_t), ("vector", start_v)])
        v0 = p.op("vector", lambda e: e.memset(ONE[:], 1.0), waits=[("dma_ldr", ldr)], inc=True)
        if masked:
            v0 = p.op("vector", lambda e: e.tensor_add(out=S[:], in0=S[:], in1=BP[:]), inc=True)
        p.op("vector", lambda e: e.tensor_copy(out=BP[:], in_=S[:]), inc=True)
        a_last = 0
        for i, ce in enumerate(EPSL):
            p.op("vector", lambda e: e.tensor_sub(out=CU[:], in0=S[:], in1=BP[:]),
                 waits=[("scalar", a_last)], inc=True)
            p.op("vector", lambda e, ce=ce: e.tensor_scalar_mul(out=CU[:], in0=CU[:], scalar1=1.0 / ce), inc=True)
            p.op("vector", lambda e: e.tensor_reduce(out=m_col[:], in_=CU[:], axis=AX.X, op=ALUx.max), inc=True)
            v_m = p.op("vector", lambda e: e.tensor_scalar_mul(out=nm_col[:], in0=m_col[:], scalar1=-1.0), inc=True)
            a_e = p.op("scalar", lambda e: e.activation(out=CU[:], in_=CU[:], func=AF.Exp, bias=nm_col[:],
                                                        accum_out=ssum[:]), waits=[("vector", v_m)], inc=True)
            a_l = p.op("scalar", lambda e: e.activation(out=ssum[:], in_=ssum[:], func=AF.Ln), inc=True)
            v_a = p.op("vector", lambda e: e.tensor_add(out=a_col[:], in0=ssum[:], in1=m_col[:]),
                       waits=[("scalar", a_l)], inc=True)
            v_a2 = p.op("vector", lambda e, ce=ce, logk=logk: e.tensor_scalar(
                out=a_col[:], in0=a_col[:], scalar1=-ce, scalar2=ce * logk, op0=ALUx.mult, op1=ALUx.add), inc=True)
            a_last = p.op("scalar", lambda e: e.activation(out=BP[:], in_=S[:], func=AF.Relu, bias=a_col[:]),
                          waits=[("vector", v_a2)], inc=True)
        v_u = p.op("vector", lambda e: e.tensor_scalar(out=CU[:], in0=S[:], scalar1=a_col[:], scalar2=0.0,
                                                       op0=ALUx.add, op1=ALUx.min),
                   waits=[("scalar", a_last)], inc=True)
        p.op("vector", lambda e: e.tensor_scalar(out=BP[:], in0=CU[:], scalar1=-3.1192, scalar2=0.0,
                                                 op0=ALUx.is_gt, op1=ALUx.add), inc=True)
        p.op("vector", lambda e: e.tensor_scalar_add(out=CU[:], in0=CU[:], scalar1=10.0), inc=True)
        p.op("vector", lambda e: e.tensor_mul(out=CU[:], in0=CU[:], in1=BP[:]), inc=True)
        p.op("vector", lambda e: e.tensor_scalar_add(out=CU[:], in0=CU[:], scalar1=-10.0), inc=True)
        p.op("vector", lambda e: e.memset(lo[:], -11.0), inc=True)
        p.op("vector", lambda e: e.memset(hi[:], 1.0), inc=True)
        for it in range(50):
            p.op("vector", lambda e: e.tensor_add(out=mid[:], in0=lo[:], in1=hi[:]), inc=True)
            vm = p.op("vector", lambda e: e.tensor_scalar_mul(out=mid[:], in0=mid[:], scalar1=0.5), inc=True)
            vc = p.op("vector", lambda e: e.tensor_scalar(out=BP[:], in0=CU[:], scalar1=mid[:], scalar2=0.0,
                                                          op0=ALUx.is_gt, op1=ALUx.add, accum_out=cnt[:]),
                      waits=[("vector", vm)], inc=True)
            p.op("vector", lambda e, K=K: e.tensor_scalar(out=cge[:], in0=cnt[:], scalar1=float(K), scalar2=None,
                                                          op0=ALUx.is_ge), waits=[("vector", vc)], inc=True)
            p.op("vector", lambda e: e.tensor_sub(out=dcol[:], in0=mid[:], in1=lo[:]), inc=True)
            p.op("vector", lambda e: e.tensor_mul(out=dcol[:], in0=dcol[:], in1=cge[:]), inc=True)
            p.op("vector", lambda e: e.tensor_add(out=lo[:], in0=lo[:], in1=dcol[:]), inc=True)
            p.op("vector", lambda e: e.tensor_sub(out=dcol[:], in0=mid[:], in1=hi[:]), inc=True)
            p.op("vector", lambda e: e.tensor_add(out=hi[:], in0=hi[:], in1=dcol[:]), inc=True)
            p.op("vector", lambda e: e.tensor_mul(out=dcol[:], in0=dcol[:], in1=cge[:]), inc=True)
            p.op("vector", lambda e: e.tensor_sub(out=hi[:], in0=hi[:], in1=dcol[:]), inc=True)
        vd = p.op("vector", lambda e: e.tensor_scalar(out=S[:], in0=CU[:], scalar1=hi[:], scalar2=0.0,
                                                      op0=ALUx.is_gt, op1=ALUx.add, accum_out=cnt[:]), inc=True)
        p.op("vector", lambda e, K=K: e.tensor_scalar(out=room[:], in0=cnt[:], scalar1=-1.0, scalar2=float(K),
                                                      op0=ALUx.mult, op1=ALUx.add), inc=True)
        p.op("vector", lambda e: e.tensor_scalar(out=BP[:], in0=CU[:], scalar1=lo[:], scalar2=None,
                                                 op0=ALUx.is_gt), inc=True)
        p.op("vector", lambda e: e.tensor_sub(out=BP[:], in0=BP[:], in1=S[:]), inc=True)
        p.op("vector", lambda e: e.tensor_tensor_scan(out=CU[:], data0=ONE[:], data1=BP[:], initial=0.0,
                                                      op0=ALUx.mult, op1=ALUx.add), inc=True)
        p.op("vector", lambda e: e.tensor_scalar(out=CU[:], in0=CU[:], scalar1=room[:], scalar2=None,
                                                 op0=ALUx.is_le), inc=True)
        p.op("vector", lambda e: e.tensor_mul(out=CU[:], in0=CU[:], in1=BP[:]), inc=True)
        v_sel = p.op("vector", lambda e: e.tensor_add(out=S[:], in0=S[:], in1=CU[:]), inc=True)
        sel_out_fn(S, v_sel)

    kv_rows = []
    for r in range(16):
        g = r // NWIN if r < 2 * NWIN else 0
        kv_rows.append(s_hbm[2 + g: 3 + g, 0:NCTX])

    def kv_out(S, v_sel):
        p.op("vector", lambda e: e.tensor_scalar_add(out=S[:], in0=S[:], scalar1=-1.0), inc=True)
        p.op("vector", lambda e: e.tensor_scalar_mul(out=S[:], in0=S[:], scalar1=1e9), inc=True)
        vs = p.op("vector", lambda e: e.tensor_scalar_add(out=S[:], in0=S[:], scalar1=-20.0), inc=True)
        ev = 0
        for kb in range(NKB):
            t_t = p.op("tensor", lambda e, kb=kb, S=S: e.matmul(ps_s[:, 0:16], S[0:16, kb * 128:(kb + 1) * 128],
                                                                ident[0:16, 0:16], start=True, stop=True),
                       waits=[("vector", vs if kb == 0 else ev)], inc=True)
            ev = p.op("vector", lambda e, kb=kb: e.tensor_copy(out=selbT[:, kb * 16:(kb + 1) * 16], in_=ps_s[:, 0:16]),
                      waits=[("tensor", t_t)], inc=True)

    run_router(NCTX, K_KV, logk_kv, kv_rows, True, kv_out)

    q_rows = []
    for r in range(16):
        if r < 2 * NWIN:
            g, wd = r // NWIN, r % NWIN
            q_rows.append(s_hbm[g: g + 1, (wd + 1) * 512: (wd + 2) * 512])
        else:
            q_rows.append(s_hbm[0:1, 0:512])

    def q_out(S, v_sel):
        ev = 0
        for qb in range(4):
            t_t = p.op("tensor", lambda e, qb=qb, S=S: e.matmul(ps_s[:, 0:16], S[0:16, qb * 128:(qb + 1) * 128],
                                                                ident[0:16, 0:16], start=True, stop=True),
                       waits=[("vector", v_sel if qb == 0 else ev)], inc=True)
            ev = p.op("vector", lambda e, qb=qb: e.tensor_copy(out=qselT[:, qb * 16:(qb + 1) * 16], in_=ps_s[:, 0:16]),
                      waits=[("tensor", t_t)], inc=True)

    run_router(512, K_Q, logk_q, q_rows, False, q_out,
               start_t=p.val("tensor"), start_v=p.val("vector"))


def emit_l1ffn_proj(p, c):
    tf = c["tf"]; HCH = c["HCH"]; NW = c["NW"]
    AB16 = c["AB16"]; AF32 = c["AF32"]
    wqkv = c["wqkv"]; wq_in = c["wq_in"]; wkv_in = c["wkv_in"]
    lnc = c["lnc"]; lnbb = c["lnbb"]; qcns = c["qcns"]; qkvb = c["qkvb"]
    yf_hbm = c["yf_hbm"]; yr_hbm = c["yr_hbm"]
    ql_hbm = c["ql_hbm"]; kl_hbm = c["kl_hbm"]; vl_hbm = c["vl_hbm"]
    qg_hbm = c["qg_hbm"]; kg_hbm = c["kg_hbm"]; vg_hbm = c["vg_hbm"]
    rotms = c["rotms"]; cossin = c["cossin"] if "cossin" in c else None
    fones = c["fones"]; bones = c["bones"]
    ps_t = c["ps_t"]; ps_u = c["ps_u"]; ps_d = c["ps_d"]
    D_ = D

    WQKV = AB16[:, 0:24576]
    YB = [AB16[:, 24576 + i * 4096: 24576 + (i + 1) * 4096] for i in range(2)]
    PB = AB16[:, 32768 + 0: 32768 + 4096]
    RB = AB16[:, 36864: 36864 + 4096]
    WKVG = AB16[:, 0: 16384]
    WQG = AB16[:, 16384: 24576]
    CSB = AB16[:, 40960: 40960 + 1024]
    RSC = AB16[:, 41984: 41984 + 512]
    qb_row = AF32[0:1, 0:3072]

    wld = 0
    for hk in range(HCH):
        wld = p.op("gpsimd", lambda e, hk=hk: e.dma_start(out=WQKV[:, hk * 3072:(hk + 1) * 3072],
                                                          in_=wqkv[hk * 128:(hk + 1) * 128, :]), dma="wld")
    t_b = 0
    v_bev = 0
    for ftile in range(6):
        for hk in range(HCH):
            wl = []
            if hk == 0:
                wl = ([("dma_wld", wld)] if ftile == 0 else []) + [("vector", v_bev)]
            p.op("tensor", lambda e, ftile=ftile, hk=hk: e.matmul(
                ps_d[:], lnbb[:, hk: hk + 1], WQKV[:, hk * 3072 + ftile * 512: hk * 3072 + (ftile + 1) * 512],
                start=(hk == 0), stop=(hk == HCH - 1)),
                 waits=wl, inc=(hk == HCH - 1))
        t_b = p.val("tensor")
        v_bev = p.op("vector", lambda e, ftile=ftile: e.tensor_copy(out=qb_row[0:1, ftile * 512:(ftile + 1) * 512], in_=ps_d[:]),
                     waits=[("tensor", t_b)], inc=True)
    ev = 0
    for j in range(24):
        t_t = p.op("tensor", lambda e, j=j: e.matmul(ps_u[0][:, 0:1], qb_row[0:1, j * 128:(j + 1) * 128],
                                                     fones[0:1, 0:1], start=True, stop=True),
                   waits=[("vector", ev if j > 0 else p.val("vector"))], inc=True)
        ev = p.op("vector", lambda e, j=j: e.tensor_copy(out=qkvb[:, j:j + 1], in_=ps_u[0][:, 0:1]),
                  waits=[("tensor", t_t)], inc=True)
    v_qb = p.op("vector", lambda e: e.tensor_scalar_mul(out=qkvb[:, 0:8], in0=qkvb[:, 0:8], scalar1=0.125), inc=True)
    v_w = 0
    for hk in range(HCH):
        v_w = p.op("vector", lambda e, hk=hk: e.tensor_scalar(out=WQKV[:, hk * 3072:(hk + 1) * 3072],
                                                              in0=WQKV[:, hk * 3072:(hk + 1) * 3072],
                                                              scalar1=lnc[:, hk:hk + 1], scalar2=None, op0=ALU.mult),
                   waits=[("tensor", t_b)], inc=True)

    st_q = st_k = st_v = 0
    ld_y = 0
    prev_rope_v = 0
    prev_ev = [0, 0]
    for w in range(NW):
        yb = YB[w % 2]
        ld_y = p.op("gpsimd", lambda e, w=w, yb=yb: e.dma_start(
            out=yb.rearrange("p (hc t) -> p hc t", hc=8, t=512),
            in_=yf_hbm.ap().rearrange("p (hc tt) -> p hc tt", hc=8, tt=tf)[:, :, w * 512:(w + 1) * 512]),
            dma="ldyf%d" % (w % 2), waits=[("tensor", prev_ev[w % 2])])
        ldy_nm = "dma_ldyf%d" % (w % 2)
        ld_cs = p.op("gpsimd", lambda e, w=w: e.dma_start(
            out=CSB.rearrange("p (a t) -> p a t", a=2, t=512),
            in_=cossin.rearrange("p (a tt) -> p a tt", a=2, tt=tf)[:, :, w * 512:(w + 1) * 512]),
            dma="ldcs", waits=[("vector", prev_rope_v)])
        for fc in range(16):
            bank = ps_t[fc % 2]
            for hk in range(HCH):
                wl = []
                if hk == 0:
                    wl = [(ldy_nm, ld_y), ("vector", v_w if fc < 2 else p.val("vector")),
                          ("scalar", p.val("scalar"))]
                p.op("tensor", lambda e, fc=fc, hk=hk, bank=bank, yb=yb: e.matmul(
                    bank[:], WQKV[:, hk * 3072 + fc * 128: hk * 3072 + (fc + 1) * 128],
                    yb[:, hk * 512:(hk + 1) * 512], start=(hk == 0), stop=(hk == HCH - 1)), waits=wl,
                    inc=(hk == HCH - 1))
            t_p = p.val("tensor")
            if fc < 8:
                a_e = p.op("scalar", lambda e, fc=fc, bank=bank: e.activation(
                    out=PB[:, fc * 512:(fc + 1) * 512], in_=bank[:], func=AF.Identity,
                    bias=qkvb[:, fc:fc + 1], scale=0.125),
                    waits=[("tensor", t_p)] + ([("dma_stql", st_q), ("dma_stvl", st_v)] if fc == 0 else []), inc=True)
            else:
                a_e = p.op("scalar", lambda e, fc=fc, bank=bank: e.activation(
                    out=RB[:, (fc - 8) * 512:(fc - 7) * 512], in_=bank[:], func=AF.Identity,
                    bias=qkvb[:, fc:fc + 1], scale=1.0),
                    waits=[("tensor", t_p)] + ([("dma_stkl", st_k)] if fc == 8 else []), inc=True)
        a_qk = p.val("scalar")
        for src_buf, dst_hbm, st_prev, nm in ((PB, ql_hbm, st_q, "stql"), (RB, kl_hbm, st_k, "stkl")):
            for fc in range(HCH):
                bank = ps_u[fc % 2]
                t_r = p.op("tensor", lambda e, fc=fc, bank=bank, src_buf=src_buf: e.matmul(
                    bank[:], rotms[:], src_buf[:, fc * 512:(fc + 1) * 512], start=True, stop=True),
                    waits=[("scalar", a_qk), ("vector", p.val("vector"))], inc=True)
                v_0 = p.op("vector", lambda e, bank=bank: e.tensor_copy(out=RSC[:], in_=bank[:]),
                           waits=[("tensor", t_r)], inc=True)
                v_1 = p.op("vector", lambda e, fc=fc, src_buf=src_buf: e.tensor_tensor(
                    out=src_buf[:, fc * 512:(fc + 1) * 512], in0=src_buf[:, fc * 512:(fc + 1) * 512],
                    in1=CSB[:, 0:512], op=ALU.mult), waits=[("dma_ldcs", ld_cs)], inc=True)
                v_2 = p.op("vector", lambda e: e.tensor_tensor(
                    out=RSC[:], in0=RSC[:], in1=CSB[:, 512:1024], op=ALU.mult), inc=True)
                v_3 = p.op("vector", lambda e, fc=fc, src_buf=src_buf: e.tensor_add(
                    out=src_buf[:, fc * 512:(fc + 1) * 512], in0=src_buf[:, fc * 512:(fc + 1) * 512],
                    in1=RSC[:]), inc=True)
            v_rope = prev_rope_v = p.val("vector")
            stv = p.op("gpsimd", lambda e, w=w, src_buf=src_buf, dst_hbm=dst_hbm: e.dma_start(
                out=dst_hbm.ap().rearrange("p (hc tt) -> p hc tt", hc=8, tt=tf)[:, :, w * 512:(w + 1) * 512],
                in_=src_buf.rearrange("p (hc t) -> p hc t", hc=8, t=512)), dma=nm,
                waits=[("vector", v_rope)])
            if nm == "stql":
                st_q = stv
            else:
                st_k = stv
        for sub in range(4):
            for ft in range(2):
                bank = ps_t[(sub * 2 + ft) % 2]
                for hk in range(HCH):
                    wl = [("vector", p.val("vector"))] if hk == 0 else []
                    p.op("tensor", lambda e, sub=sub, ft=ft, hk=hk, bank=bank, yb=yb: e.matmul(
                        bank[:], yb[:, hk * 512 + sub * 128: hk * 512 + (sub + 1) * 128],
                        WQKV[:, hk * 3072 + 2048 + ft * 512: hk * 3072 + 2048 + (ft + 1) * 512],
                        start=(hk == 0), stop=False), waits=wl)
                t_v = p.op("tensor", lambda e, ft=ft, bank=bank: e.matmul(
                    bank[:], fones[0:1, :], qb_row[0:1, 2048 + ft * 512: 2048 + (ft + 1) * 512],
                    start=False, stop=True), inc=True)
                v_e = p.op("vector", lambda e, sub=sub, ft=ft, bank=bank: e.tensor_copy(
                    out=PB[:, sub * 1024 + ft * 512: sub * 1024 + (ft + 1) * 512], in_=bank[:]),
                    waits=[("tensor", t_v), ("dma_stvl", st_v), ("dma_stql", st_q)], inc=True)
        v_vv = p.val("vector")
        st_v = p.op("gpsimd", lambda e, w=w: e.dma_start(
            out=vl_hbm[w * 512:(w + 1) * 512, :].rearrange("(s q) d -> q s d", q=128, s=4),
            in_=PB[:, 0:4096].rearrange("p (s d) -> p s d", s=4, d=1024)), dma="stvl",
            waits=[("vector", v_vv)])
        prev_ev[w % 2] = p.val("tensor")

    c["barrier"](extra_dma=("stql", "stkl", "stvl", "ldcs", "ldyf0", "ldyf1"))
    st_qg = st_kg = st_vg = 0
    for g in range(2):
        wld2 = 0
        for hk in range(HCH):
            wld2 = p.op("gpsimd", lambda e, g=g, hk=hk: e.dma_start(
                out=WKVG[:, hk * 2048:(hk + 1) * 2048], in_=wkv_in[g, hk * 128:(hk + 1) * 128, :]),
                dma="wld2", waits=([("tensor", p.val("tensor"))] if (g > 0 and hk == 0) else []))
        for hk in range(HCH):
            wld2 = p.op("gpsimd", lambda e, g=g, hk=hk: e.dma_start(
                out=WQG[:, hk * 1024:(hk + 1) * 1024], in_=wq_in[g, hk * 128:(hk + 1) * 128, :]), dma="wld2")
        v_w2 = 0
        for hk in range(HCH):
            p.op("vector", lambda e, g=g, hk=hk: e.tensor_scalar(
                out=WKVG[:, hk * 2048:(hk + 1) * 2048], in0=WKVG[:, hk * 2048:(hk + 1) * 2048],
                scalar1=qcns[:, hk * 4 + 2 + g: hk * 4 + 3 + g], scalar2=None, op0=ALU.mult),
                 waits=[("dma_wld2", wld2)], inc=True)
            v_w2 = p.op("vector", lambda e, g=g, hk=hk: e.tensor_scalar(
                out=WQG[:, hk * 1024:(hk + 1) * 1024], in0=WQG[:, hk * 1024:(hk + 1) * 1024],
                scalar1=qcns[:, hk * 4 + g: hk * 4 + 1 + g], scalar2=None, op0=ALU.mult), inc=True)
        for w in range(NW):
            yb = YB[w % 2]
            ld_y = p.op("gpsimd", lambda e, w=w, g=g, yb=yb: e.dma_start(
                out=yb.rearrange("p (hc t) -> p hc t", hc=8, t=512),
                in_=yr_hbm.ap().rearrange("p (hc tt) -> p hc tt", hc=8, tt=tf)[:, :, w * 512:(w + 1) * 512]),
                dma="ldyf%d" % (w % 2), waits=[("tensor", prev_ev[w % 2])])
            ldy_nm = "dma_ldyf%d" % (w % 2)
            for fc in range(16):
                bank = ps_t[fc % 2]
                wsrc = WQG if fc < 8 else WKVG
                fcc = fc if fc < 8 else fc - 8
                wslc = (lambda hk, fcc=fcc, wsrc=wsrc: wsrc[:, hk * (1024 if wsrc is WQG else 2048) + fcc * 128:
                                                           hk * (1024 if wsrc is WQG else 2048) + (fcc + 1) * 128])
                for hk in range(HCH):
                    wl = []
                    if hk == 0:
                        wl = [(ldy_nm, ld_y), ("vector", v_w2 if fc < 2 else p.val("vector")),
                              ("scalar", p.val("scalar"))]
                    p.op("tensor", lambda e, fc=fc, hk=hk, bank=bank, yb=yb, wslc=wslc: e.matmul(
                        bank[:], wslc(hk), yb[:, hk * 512:(hk + 1) * 512],
                        start=(hk == 0), stop=(hk == HCH - 1)), waits=wl, inc=(hk == HCH - 1))
                t_p = p.val("tensor")
                wl2 = [("tensor", t_p)]
                if fc == 0:
                    wl2.append(("dma_stqg", st_qg))
                    wl2.append(("dma_stkg", st_kg))
                if fc == 8:
                    wl2.append(("dma_stkg", st_kg))
                    wl2.append(("dma_stqg", st_qg))
                a_e = p.op("scalar", lambda e, fc=fc, bank=bank: e.activation(
                    out=PB[:, (fc % 8) * 512:((fc % 8) + 1) * 512], in_=bank[:], func=AF.Identity,
                    scale=(0.125 if fc < 8 else 1.0)), waits=wl2, inc=True)
                if fc == 7 or fc == 15:
                    a_d = p.val("scalar")
                    hbm = qg_hbm if fc == 7 else kg_hbm
                    nm = "stqg" if fc == 7 else "stkg"
                    stx = p.op("gpsimd", lambda e, w=w, g=g, hbm=hbm: e.dma_start(
                        out=hbm.ap()[g].rearrange("p (hc tt) -> p hc tt", hc=8, tt=tf)[:, :, w * 512:(w + 1) * 512],
                        in_=PB[:, 0:4096].rearrange("p (hc t) -> p hc t", hc=8, t=512)), dma=nm,
                        waits=[("scalar", a_d)])
                    if fc == 7:
                        st_qg = stx
                    else:
                        st_kg = stx
            for sub in range(4):
                for ft in range(2):
                    bank = ps_t[(sub * 2 + ft) % 2]
                    for hk in range(HCH):
                        wl = [("vector", p.val("vector")), ("scalar", p.val("scalar"))] if hk == 0 else []
                        p.op("tensor", lambda e, sub=sub, ft=ft, hk=hk, bank=bank, yb=yb: e.matmul(
                            bank[:], yb[:, hk * 512 + sub * 128: hk * 512 + (sub + 1) * 128],
                            WKVG[:, hk * 2048 + 1024 + ft * 512: hk * 2048 + 1024 + (ft + 1) * 512],
                            start=(hk == 0), stop=(hk == HCH - 1)), waits=wl, inc=(hk == HCH - 1))
                    t_v = p.val("tensor")
                    v_e = p.op("vector", lambda e, sub=sub, ft=ft, bank=bank: e.tensor_copy(
                        out=RB[:, sub * 1024 + ft * 512: sub * 1024 + (ft + 1) * 512], in_=bank[:]),
                        waits=[("tensor", t_v), ("dma_stvg", st_vg)], inc=True)
            v_vv = p.val("vector")
            st_vg = p.op("gpsimd", lambda e, w=w, g=g: e.dma_start(
                out=vg_hbm.ap()[g][w * 512:(w + 1) * 512, :].rearrange("(s q) d -> q s d", q=128, s=4),
                in_=RB[:, 0:4096].rearrange("p (s d) -> p s d", s=4, d=1024)), dma="stvg",
                waits=[("vector", v_vv)])
            prev_ev[w % 2] = p.val("tensor")


def emit_l1ffn_attn(p, c):
    tf = c["tf"]; NW = c["NW"]; NWIN = c["NWIN"]
    AB16 = c["AB16"]; AF32 = c["AF32"]
    xf = c["xf"]; xo = c["xo"]
    ql_hbm = c["ql_hbm"]; kl_hbm = c["kl_hbm"]; vl_hbm = c["vl_hbm"]
    qg_hbm = c["qg_hbm"]; kg_hbm = c["kg_hbm"]; vg_hbm = c["vg_hbm"]
    wlo_in = c["wlo_in"]; wout_in = c["wout_in"]
    fones = c["fones"]; ident = c["ident"]; bocol = c["bocol"]
    mlocQs = c["mlocQs"]; mlocKTs = c["mlocKTs"]
    selbT = c["selbT"]; qselT = c["qselT"]; accw = c["accw"]
    nkss = c["nkss"]; nvs_in = c["nvs"]
    nmrow = c["nmrow"]; enull = c["enull"]; denr = c["denr"]; mvals = c["mvals"]
    cols = c["cols"]; xt = c["xt"]; yr = c["yr"]; yf32 = c["yf32"]; ln_scr = c["ln_scr"]
    ps_t = c["ps_t"]; ps_u = c["ps_u"]; ps_o = c["ps_o"]; ps_d = c["ps_d"]
    ps_r = c["ps_r"]; ps_s = c["ps_s"]

    KW = [AB16[:, 0:4096], AB16[:, 4096:8192]]
    VW = [AB16[:, 8192:12288], AB16[:, 12288:16384]]
    QWL = AB16[:, 16384:20480]
    QWG = AB16[:, 20480:24576]
    WPR = AB16[:, 24576:32768]
    KCH = [AB16[:, 24576:28672], AB16[:, 28672:32768]]
    VCH = [AB16[:, 32768:36864], AB16[:, 36864:40960]]
    OT = AB16[:, 40960:45056]
    ABF = [AB16[:, 45056 + i * 512: 45056 + (i + 1) * 512] for i in range(4)]
    oacc = AF32[:, 0:4096]
    dacc = AF32[0:1, 4096:12288]
    nvss = AF32[0:1, 12288:14336]
    ascr = [yf32[:, 0:512], ln_scr[:, 0:512]]
    recb = yf32[:, 512:1024]
    stg2 = ln_scr[:, 512:1024]
    cntc = cols[:, 18:19]
    rc05 = cols[:, 19:20]
    m20 = cols[0:1, 20:21]

    def fm_load(dst, hbm_ap, wdx):
        return lambda e: e.dma_start(
            out=dst.rearrange("p (hc t) -> p hc t", hc=8, t=512),
            in_=hbm_ap.rearrange("p (hc tt) -> p hc tt", hc=8, tt=tf)[:, :, wdx * 512:(wdx + 1) * 512])

    def tm_load(dst, hbm_rows):
        return lambda e: e.dma_start(
            out=dst.rearrange("p (s d) -> p s d", s=4, d=1024),
            in_=hbm_rows.rearrange("(s q) d -> q s d", q=128, s=4))

    p.op("sync", lambda e: e.dma_start(out=nvss, in_=nvs_in), dma="lnv")
    st_out = 0
    wpr_read = 0
    ot_read = 0
    qwl_read = 0
    qwg_read = 0
    kwread = [0, 0]
    vwread = [0, 0]
    kchread = [0, 0]
    vchread = [0, 0]
    mval_tread = 0
    abf_g = [0, 0, 0, 0]
    SB4 = [ps_u[0], ps_u[1], ps_t[0], ps_t[1]]
    pg = [("vector", 0)] * 4
    asc_g = [0, 0]
    a_oc_prev = 0
    v_de_prev = 0
    v_rb_prev = 0
    t_br_prev = 0
    t_nv_prev = 0
    v_dn_prev = 0
    a_fin_prev = 0
    psr_guard = ("vector", 0)
    vfin_g = [0, 0]
    v_comb = 0

    for w in range(NW):
        cur = w % 2
        prv = 1 - cur
        lq_t = p.op("sync", fm_load(QWL, ql_hbm.ap(), w), dma="lq", waits=[("tensor", qwl_read)])
        lk_t = p.op("sync", fm_load(KW[cur], kl_hbm.ap(), w), dma="lk%d" % cur,
                    waits=[("tensor", kwread[cur])])
        lv_t = p.op("sync", tm_load(VW[cur], vl_hbm[w * 512:(w + 1) * 512, :]), dma="lv%d" % cur,
                    waits=[("tensor", vwread[cur])])
        ld_waits = [("dma_lq", lq_t), ("dma_lk%d" % cur, lk_t), ("dma_lv%d" % cur, lv_t)]
        wld = 0
        for j in range(8):
            wld = p.op("gpsimd", lambda e, j=j: e.dma_start(
                out=WPR[:, j * 1024:(j + 1) * 1024], in_=wlo_in[j * 128:(j + 1) * 128, :]),
                dma="wst", waits=([("tensor", wpr_read)] if j == 0 else []))
        wld_local = wld

        nch = 2 if w > 0 else 1
        zi = 0
        first_acc = True
        for h in range(16):
            b = (h % 2) * 64
            hk = h // 2
            for qt in range(4):
                lhs = QWL[b:b + 64, hk * 512 + qt * 128: hk * 512 + (qt + 1) * 128]
                mcol = mvals[:, h * 4 + qt: h * 4 + qt + 1]
                for ci in range(nch):
                    iscur = 1 if (w == 0 or ci == 1) else 0
                    kbuf = KW[cur] if iscur else KW[prv]
                    bank = ps_t[zi % 2]
                    t_z = p.op("tensor", lambda e, lhs=lhs, kbuf=kbuf, bank=bank, b=b, hk=hk: e.matmul(
                        bank[:], lhs, kbuf[b:b + 64, hk * 512:(hk + 1) * 512], start=True, stop=True),
                        waits=[pg[2 + zi % 2]] + (ld_waits if zi < 2 else []), inc=True)
                    m0 = 3 + 4 * iscur - qt
                    p.op("vector", lambda e, bank=bank, m0=m0: e.tensor_tensor(
                        out=stg2[:], in0=bank[:], in1=mlocQs[:, m0 * 128: m0 * 128 + 512], op=ALU.add),
                        waits=[("tensor", t_z)] + ([("scalar", p.val("scalar"))] if first_acc else []),
                        inc=True)
                    p.op("vector", lambda e, mcol=mcol, ci=ci: e.tensor_scalar(
                        out=stg2[:], in0=stg2[:], scalar1=0.0,
                        scalar2=(None if ci == 0 else mcol), op0=ALU.add, op1=ALU.max, accum_out=mcol),
                        waits=([("tensor", mval_tread)] if first_acc else []), inc=True)
                    first_acc = False
                    pg[2 + zi % 2] = ("vector", p.val("vector"))
                    zi += 1
        v_neg = p.op("vector", lambda e: e.tensor_scalar_mul(out=mvals[:, 0:64], in0=mvals[:, 0:64],
                                                             scalar1=-1.0), inc=True)
        vfin_g = [v_neg, v_neg]

        kbs = list(range(4, 8)) if w == 0 else list(range(8))
        fkb, lkb = kbs[0], kbs[-1]
        k1_last = 0
        nm_ev = 0
        for h in range(16):
            b = (h % 2) * 64
            hk = h // 2
            qsl = QWL[b:b + 64, hk * 512:(hk + 1) * 512]
            for qt in range(4):
                wl = [("vector", v_neg if (h == 0 and qt == 0) else nm_ev)]
                if h == 0 and qt == 0:
                    wl.append(psr_guard)
                t_t = p.op("tensor", lambda e, h=h, qt=qt: e.matmul(
                    ps_r[0:1, 0:128], mvals[:, h * 4 + qt: h * 4 + qt + 1], ident[:],
                    start=True, stop=True), waits=wl, inc=True)
                nm_ev = p.op("vector", lambda e, qt=qt: e.tensor_copy(
                    out=nmrow[0:1, qt * 128:(qt + 1) * 128], in_=ps_r[0:1, 0:128]),
                    waits=[("tensor", t_t)] + ([("tensor", k1_last)] if qt == 0 else []), inc=True)
            mval_tread = p.val("tensor")
            psr_guard = ("vector", nm_ev)
            for kb in kbs:
                iscur = kb >= 4
                kbuf = KW[cur] if iscur else KW[prv]
                vbuf = VW[cur] if iscur else VW[prv]
                kbr = kb % 4
                bank = ps_u[kb % 2]
                wl = [pg[kb % 2]]
                if kb == fkb:
                    wl.append(("vector", nm_ev))
                p.op("tensor", lambda e, kbuf=kbuf, b=b, hk=hk, kbr=kbr, bank=bank, qsl=qsl: e.matmul(
                    bank[:], kbuf[b:b + 64, hk * 512 + kbr * 128: hk * 512 + (kbr + 1) * 128],
                    qsl, start=True, stop=False), waits=wl)
                k1_last = p.op("tensor", lambda e, bank=bank: e.matmul(
                    bank[:], fones[0:1, 0:128], nmrow[0:1, 0:512], start=False, stop=True), inc=True)
                n0 = 7 - kb
                v_mk = p.op("vector", lambda e, bank=bank, n0=n0, kb=kb: e.tensor_tensor(
                    out=ascr[kb % 2], in0=bank[:], in1=mlocKTs[:, n0 * 128: n0 * 128 + 512], op=ALU.add),
                    waits=[("tensor", k1_last), ("scalar", asc_g[kb % 2])], inc=True)
                pg[kb % 2] = ("vector", v_mk)
                a_ex = p.op("scalar", lambda e, kb=kb: e.activation(
                    out=ABF[kb % 4][:], in_=ascr[kb % 2], func=AF.Exp),
                    waits=[("vector", v_mk), ("tensor", abf_g[kb % 4])], inc=True)
                asc_g[kb % 2] = a_ex
                p.op("tensor", lambda e, vbuf=vbuf, kbr=kbr, h=h, kb=kb, fkb=fkb, lkb=lkb: e.matmul(
                    ps_o[:], vbuf[:, kbr * 1024 + h * 64: kbr * 1024 + (h + 1) * 64], ABF[kb % 4][:],
                    start=(kb == fkb), stop=(kb == lkb)),
                    waits=[("scalar", a_ex)] + ([("scalar", a_oc_prev)] if kb == fkb else []), inc=True)
                abf_g[kb % 4] = p.op("tensor", lambda e, kb=kb, fkb=fkb, lkb=lkb: e.matmul(
                    ps_d[:], bocol[:, 0:1], ABF[kb % 4][:], start=(kb == fkb), stop=(kb == lkb)),
                    waits=([("vector", v_de_prev)] if kb == fkb else []), inc=True)
            t_avd = p.val("tensor")
            v_de_prev = p.op("vector", lambda e: e.tensor_copy(out=denr[0:1, 0:512], in_=ps_d[0:1, 0:512]),
                             waits=[("tensor", t_avd), ("tensor", t_br_prev)], inc=True)
            p.op("vector", lambda e: e.reciprocal(out=denr[0:1, 0:512], in_=denr[0:1, 0:512]), inc=True)
            t_br_prev = p.op("tensor", lambda e: e.matmul(
                ps_s[:, 0:512], fones[0:1, 0:128], denr[0:1, 0:512], start=True, stop=True),
                waits=[("vector", p.val("vector"))], inc=True)
            v_rb_prev = p.op("vector", lambda e: e.tensor_copy(out=recb[:], in_=ps_s[:, 0:512]),
                             waits=[("tensor", t_br_prev)], inc=True)
            a_oc_prev = a_oc = p.op("scalar", lambda e, b=b: e.activation(
                out=stg2[b:b + 64, :], in_=ps_o[:], func=AF.Identity),
                waits=[("tensor", t_avd), ("vector", vfin_g[h % 2])], inc=True)
            v_fin = p.op("vector", lambda e, b=b: e.tensor_tensor(
                out=stg2[b:b + 64, :], in0=stg2[b:b + 64, :], in1=recb[b:b + 64, :], op=ALU.mult),
                waits=[("scalar", a_oc)], inc=True)
            vfin_g[h % 2] = v_fin
            a_fin_prev = p.op("scalar", lambda e, b=b, hk=hk: e.activation(
                out=OT[b:b + 64, hk * 512:(hk + 1) * 512], in_=stg2[b:b + 64, :], func=AF.Identity),
                waits=[("vector", v_fin)] + ([("tensor", ot_read)] if h == 0 else []), inc=True)
        qwl_read = p.val("tensor")
        kwread[0] = kwread[1] = p.val("tensor")
        vwread[0] = vwread[1] = p.val("tensor")

        for qt in range(4):
            for df in range(2):
                bank = ps_t[df]
                for j in range(8):
                    wl = []
                    if j == 0:
                        wl = [pg[2 + df]]
                        if qt == 0 and df == 0:
                            wl += [("scalar", a_fin_prev), ("dma_wst", wld_local)]
                    p.op("tensor", lambda e, qt=qt, df=df, j=j, bank=bank: e.matmul(
                        bank[:], OT[:, j * 512 + qt * 128: j * 512 + (qt + 1) * 128],
                        WPR[:, j * 1024 + df * 512: j * 1024 + (df + 1) * 512],
                        start=(j == 0), stop=(j == 7)), waits=wl, inc=(j == 7))
                t_pj = p.val("tensor")
                p.op("vector", lambda e, qt=qt, df=df, bank=bank: e.tensor_copy(
                    out=accw[:, qt * 1024 + df * 512: qt * 1024 + (df + 1) * 512], in_=bank[:]),
                    waits=[("tensor", t_pj)], inc=True)
                pg[2 + df] = ("vector", p.val("vector"))
        ot_read = p.val("tensor")
        wpr_read = p.val("tensor")

        if w > 0:
            for g in range(2):
                r = g * NWIN + (w - 1)
                gq_t = p.op("sync", fm_load(QWG, qg_hbm.ap()[g], w), dma="gq",
                            waits=[("tensor", qwg_read)])
                for ch in range(w):
                    cb = ch % 2
                    gk_t = p.op("sync", fm_load(KCH[cb], kg_hbm.ap()[g], ch), dma="gk%d" % cb,
                                waits=[("tensor", max(wpr_read, kchread[cb]))])
                    gv_t = p.op("sync", tm_load(VCH[cb], vg_hbm.ap()[g][ch * 512:(ch + 1) * 512, :]),
                                dma="gv%d" % cb, waits=[("tensor", vchread[cb])])
                    gld_waits = [("dma_gk%d" % cb, gk_t), ("dma_gv%d" % cb, gv_t)] + \
                                ([("dma_gq", gq_t)] if ch == 0 else [])
                    for h in range(16):
                        b = (h % 2) * 64
                        hk = h // 2
                        for kb in range(4):
                            bank = SB4[kb]
                            kbglob = ch * 4 + kb
                            t_s = p.op("tensor", lambda e, cb=cb, b=b, hk=hk, kb=kb, bank=bank: e.matmul(
                                bank[:], KCH[cb][b:b + 64, hk * 512 + kb * 128: hk * 512 + (kb + 1) * 128],
                                QWG[b:b + 64, hk * 512:(hk + 1) * 512], start=True, stop=True),
                                waits=[pg[kb]] + (gld_waits if (h == 0 and kb == 0) else []),
                                inc=True)
                            a_ex = p.op("scalar", lambda e, kb=kb, kbglob=kbglob, r=r, bank=bank: e.activation(
                                out=ABF[kb][:], in_=bank[:], func=AF.Exp,
                                bias=selbT[:, kbglob * 16 + r: kbglob * 16 + r + 1]),
                                waits=[("tensor", t_s), ("tensor", abf_g[kb])], inc=True)
                            pg[kb] = ("scalar", a_ex)
                            p.op("tensor", lambda e, cb=cb, kb=kb, h=h: e.matmul(
                                ps_o[:], VCH[cb][:, kb * 1024 + h * 64: kb * 1024 + (h + 1) * 64],
                                ABF[kb][:], start=(kb == 0), stop=(kb == 3)),
                                waits=[("scalar", a_ex)] + ([("scalar", a_oc_prev)] if kb == 0 else []),
                                inc=True)
                            abf_g[kb] = p.op("tensor", lambda e, kb=kb: e.matmul(
                                ps_d[:], bocol[:, 0:1], ABF[kb][:], start=(kb == 0), stop=(kb == 3)),
                                waits=([("vector", v_de_prev)] if kb == 0 else []), inc=True)
                        t_avd = p.val("tensor")
                        a_oc_prev = a_oc = p.op("scalar", lambda e, b=b: e.activation(
                            out=stg2[b:b + 64, :], in_=ps_o[:], func=AF.Identity),
                            waits=[("tensor", t_avd), ("vector", vfin_g[h % 2])], inc=True)
                        if ch == 0:
                            p.op("vector", lambda e, b=b, hk=hk: e.tensor_copy(
                                out=oacc[b:b + 64, hk * 512:(hk + 1) * 512], in_=stg2[b:b + 64, :]),
                                waits=[("scalar", a_oc)], inc=True)
                        else:
                            p.op("vector", lambda e, b=b, hk=hk: e.tensor_tensor(
                                out=oacc[b:b + 64, hk * 512:(hk + 1) * 512],
                                in0=oacc[b:b + 64, hk * 512:(hk + 1) * 512], in1=stg2[b:b + 64, :],
                                op=ALU.add), waits=[("scalar", a_oc)], inc=True)
                        vfin_g[h % 2] = p.val("vector")
                        if ch == 0:
                            v_de_prev = p.op("vector", lambda e, h=h: e.tensor_copy(
                                out=dacc[0:1, h * 512:(h + 1) * 512], in_=ps_d[0:1, 0:512]),
                                waits=[("tensor", t_avd), ("tensor", t_br_prev)], inc=True)
                        else:
                            v_de_prev = p.op("vector", lambda e, h=h: e.tensor_tensor(
                                out=dacc[0:1, h * 512:(h + 1) * 512],
                                in0=dacc[0:1, h * 512:(h + 1) * 512], in1=ps_d[0:1, 0:512], op=ALU.add),
                                waits=[("tensor", t_avd), ("tensor", t_br_prev)], inc=True)
                    kchread[cb] = p.val("tensor")
                    vchread[cb] = p.val("tensor")
                for h in range(16):
                    b = (h % 2) * 64
                    hk = h // 2
                    t_n = p.op("tensor", lambda e, b=b, hk=hk, h=h, g=g: e.matmul(
                        ps_r[0:1, 0:512], nkss[b:b + 64, 2 * h + g: 2 * h + g + 1],
                        QWG[b:b + 64, hk * 512:(hk + 1) * 512], start=True, stop=True),
                        waits=[psr_guard], inc=True)
                    a_en = p.op("scalar", lambda e: e.activation(
                        out=enull[0:1, 0:512], in_=ps_r[0:1, 0:512], func=AF.Exp, bias=m20),
                        waits=[("tensor", t_n), ("tensor", t_nv_prev), ("vector", v_dn_prev)], inc=True)
                    psr_guard = ("scalar", a_en)
                    v_dn_prev = p.op("vector", lambda e, h=h: e.tensor_tensor(
                        out=dacc[0:1, h * 512:(h + 1) * 512], in0=dacc[0:1, h * 512:(h + 1) * 512],
                        in1=enull[0:1, 0:512], op=ALU.add), waits=[("scalar", a_en)], inc=True)
                    t_nv_prev = t_nv = p.op("tensor", lambda e, g=g, h=h: e.matmul(
                        ps_u[0][0:64, 0:512], nvss[0:1, g * 1024 + h * 64: g * 1024 + (h + 1) * 64],
                        enull[0:1, 0:512], start=True, stop=True),
                        waits=[("scalar", a_en), pg[0], ("dma_lnv", 16)], inc=True)
                    a_nc = p.op("scalar", lambda e, b=b: e.activation(
                        out=stg2[b:b + 64, :], in_=ps_u[0][0:64, 0:512], func=AF.Identity),
                        waits=[("tensor", t_nv), ("vector", vfin_g[h % 2])], inc=True)
                    pg[0] = ("scalar", a_nc)
                    p.op("vector", lambda e, b=b, hk=hk: e.tensor_tensor(
                        out=oacc[b:b + 64, hk * 512:(hk + 1) * 512],
                        in0=oacc[b:b + 64, hk * 512:(hk + 1) * 512],
                        in1=stg2[b:b + 64, :], op=ALU.add), waits=[("scalar", a_nc)], inc=True)
                    p.op("vector", lambda e, h=h: e.reciprocal(
                        out=denr[0:1, 0:512], in_=dacc[0:1, h * 512:(h + 1) * 512]),
                        waits=[("tensor", t_br_prev)], inc=True)
                    t_br_prev = p.op("tensor", lambda e: e.matmul(
                        ps_s[:, 0:512], fones[0:1, 0:128], denr[0:1, 0:512], start=True, stop=True),
                        waits=[("vector", p.val("vector"))], inc=True)
                    v_rb_prev = p.op("vector", lambda e: e.tensor_copy(out=recb[:], in_=ps_s[:, 0:512]),
                                     waits=[("tensor", t_br_prev)], inc=True)
                    v_fin = p.op("vector", lambda e, b=b, hk=hk: e.tensor_tensor(
                        out=stg2[b:b + 64, :], in0=oacc[b:b + 64, hk * 512:(hk + 1) * 512],
                        in1=recb[b:b + 64, :], op=ALU.mult), inc=True)
                    vfin_g[h % 2] = v_fin
                    a_fin_prev = p.op("scalar", lambda e, b=b, hk=hk: e.activation(
                        out=OT[b:b + 64, hk * 512:(hk + 1) * 512], in_=stg2[b:b + 64, :], func=AF.Identity),
                        waits=[("vector", v_fin)] + ([("tensor", ot_read)] if h == 0 else []), inc=True)
                wld = 0
                wpr_guard = p.val("tensor")
                for j in range(8):
                    wld = p.op("gpsimd", lambda e, g=g, j=j: e.dma_start(
                        out=WPR[:, j * 1024:(j + 1) * 1024], in_=wout_in[g, j * 128:(j + 1) * 128, :]),
                        dma="wst", waits=([("tensor", wpr_guard)] if j == 0 else []))
                for qt in range(4):
                    qcol = qselT[:, qt * 16 + r: qt * 16 + r + 1]
                    for df in range(2):
                        bank = ps_t[df]
                        for j in range(8):
                            wl = []
                            if j == 0:
                                wl = [pg[2 + df]]
                                if qt == 0 and df == 0:
                                    wl += [("scalar", a_fin_prev), ("dma_wst", wld)]
                            p.op("tensor", lambda e, qt=qt, df=df, j=j, bank=bank: e.matmul(
                                bank[:], OT[:, j * 512 + qt * 128: j * 512 + (qt + 1) * 128],
                                WPR[:, j * 1024 + df * 512: j * 1024 + (df + 1) * 512],
                                start=(j == 0), stop=(j == 7)), waits=wl, inc=(j == 7))
                        t_pj = p.val("tensor")
                        p.op("vector", lambda e, qt=qt, df=df, bank=bank, qcol=qcol: e.scalar_tensor_tensor(
                            out=accw[:, qt * 1024 + df * 512: qt * 1024 + (df + 1) * 512],
                            in0=bank[:], scalar=qcol,
                            in1=accw[:, qt * 1024 + df * 512: qt * 1024 + (df + 1) * 512],
                            op0=ALU.mult, op1=ALU.add), waits=[("tensor", t_pj)], inc=True)
                        pg[2 + df] = ("vector", p.val("vector"))
                qwg_read = p.val("tensor")
                ot_read = p.val("tensor")
                wpr_read = p.val("tensor")

        for qt in range(4):
            tile = w * 4 + qt
            ldx = p.op("sync", lambda e, tile=tile: e.dma_start(
                out=xt[:], in_=xf[tile * 128:(tile + 1) * 128, :]), dma="ldx",
                waits=[("vector", v_comb)])
            if w == 0:
                p.op("vector", lambda e: e.memset(rc05[:], 0.5), inc=True)
            else:
                p.op("vector", lambda e, qt=qt, w=w: e.tensor_tensor(
                    out=cntc[:], in0=qselT[:, qt * 16 + (w - 1): qt * 16 + w],
                    in1=qselT[:, qt * 16 + NWIN + (w - 1): qt * 16 + NWIN + w], op=ALU.add), inc=True)
                p.op("vector", lambda e: e.tensor_scalar_add(out=cntc[:], in0=cntc[:], scalar1=1.0), inc=True)
                p.op("vector", lambda e: e.reciprocal(out=rc05[:], in_=cntc[:]), inc=True)
                p.op("vector", lambda e: e.tensor_scalar_mul(out=rc05[:], in0=rc05[:], scalar1=0.5), inc=True)
            v_comb = p.op("vector", lambda e, qt=qt: e.scalar_tensor_tensor(
                out=yr[:], in0=accw[:, qt * 1024:(qt + 1) * 1024], scalar=rc05[:], in1=xt[:],
                op0=ALU.mult, op1=ALU.add),
                waits=[("dma_ldx", ldx), ("dma_stout", st_out)], inc=True)
            st_out = p.op("sync", lambda e, tile=tile: e.dma_start(
                out=xo[tile * 128:(tile + 1) * 128, :], in_=yr[:]), dma="stout",
                waits=[("vector", v_comb)])


def prep_l1ffn_weights(I, tf=T):
    f32 = np.float32
    ln_g = I["L1_ffn_ln_g"].astype(f32); ln_b = I["L1_ffn_ln_b"].astype(f32)
    lncol = np.zeros((128, 16), f32)
    for hk in range(8):
        lncol[:, hk] = ln_g[hk * 128:(hk + 1) * 128]
        lncol[:, 8 + hk] = ln_b[hk * 128:(hk + 1) * 128]
    rt_q = I["L1_ffn_rt_q"].astype(f32); rt_kv = I["L1_ffn_rt_kv"].astype(f32)
    rt4 = np.zeros((128, 32), f32)
    for hk in range(8):
        for g in range(2):
            rt4[:, hk * 4 + g] = rt_q[g, hk * 128:(hk + 1) * 128]
            rt4[:, hk * 4 + 2 + g] = rt_kv[g, hk * 128:(hk + 1) * 128]
    qn = I["L1_ffn_qnorm_g"].astype(f32); cn = I["L1_ffn_cnorm_g"].astype(f32)
    qcn = np.zeros((128, 32), f32)
    for hk in range(8):
        for g in range(2):
            qcn[:, hk * 4 + g] = qn[g, hk * 128:(hk + 1) * 128]
            qcn[:, hk * 4 + 2 + g] = cn[g, hk * 128:(hk + 1) * 128]
    nullkv = I["L1_ffn_null_kv"].astype(f32)
    nks = np.zeros((128, 32), f32)
    nvs = np.zeros((1, 2048), f32)
    for g in range(2):
        for h in range(16):
            b = (h % 2) * 64
            nks[b:b + 64, 2 * h + g] = nullkv[0, g, h, 0]
            nvs[0, g * 1024 + h * 64:(g * 1024 + (h + 1) * 64)] = nullkv[1, g, h, 0]
    R = np.zeros((64, 64), f32)
    for i in range(32):
        R[i, 32 + i] = -1.0
        R[32 + i, i] = 1.0
    R2 = np.zeros((128, 128), f32)
    R2[0:64, 0:64] = R
    R2[64:128, 64:128] = R
    rotm = np.ascontiguousarray(R2.T)
    inv = (1.0 / 10000.0 ** (np.arange(0, 64, 2, dtype=f32) / 64)).astype(f32)
    t = np.arange(tf, dtype=f32)
    fr = t[:, None] * inv[None, :]
    fr = np.concatenate([fr, fr], -1)
    cosv = np.cos(fr).astype(f32); sinv = np.sin(fr).astype(f32)
    cossin = np.zeros((128, 2 * tf), f32)
    for rr in range(128):
        cossin[rr, 0:tf] = cosv[:, rr % 64]
        cossin[rr, tf:2 * tf] = sinv[:, rr % 64]
    NCTX = tf - 512
    NWIN = tf // 512 - 1
    wmaskin = np.zeros((16, NCTX), f32)
    for rr in range(16):
        if rr < 2 * NWIN:
            wd = rr % NWIN
            wmaskin[rr, (wd + 1) * 512:] = -1e9
    ii = np.arange(128)[:, None]
    jj = np.arange(128)[None, :]
    pats = []
    for dlt in range(-3, 8):
        dq = dlt * 128 + (ii - jj)
        pats.append(np.where((dq >= 0) & (dq <= 512), 0.0, -1e9).astype(f32))
    mlocQ = np.zeros((128, 11 * 128), f32)
    mlocKT = np.zeros((128, 11 * 128), f32)
    for m in range(11):
        mlocQ[:, m * 128:(m + 1) * 128] = pats[(7 - m) + 3]
        mlocKT[:, m * 128:(m + 1) * 128] = pats[m].T
    out = {
        "wqkv": np.ascontiguousarray(I["L1_ffn_w_qkv"].astype(f32).T),
        "lncol": lncol, "rt4": rt4, "qcn": qcn,
        "wq": np.ascontiguousarray(np.stack([I["L1_ffn_w_q"][g].astype(f32).T for g in range(2)])),
        "wkv": np.ascontiguousarray(np.stack([I["L1_ffn_w_kv"][g].astype(f32).T for g in range(2)])),
        "wout": np.ascontiguousarray(np.stack([I["L1_ffn_w_out"][g].astype(f32).T for g in range(2)])),
        "wlo": np.ascontiguousarray(I["L1_ffn_w_lo"].astype(f32).T),
        "nks": nks, "nvs": nvs, "rotm": rotm, "cossin": cossin,
        "wmaskin": wmaskin, "mlocQ": mlocQ, "mlocKT": mlocKT,
    }
    return out


_L1FFN_NC = None


def run_l1ffn(x, I):
    global _L1FFN_NC
    if _L1FFN_NC is None:
        _L1FFN_NC = build_l1ffn()
    nc = _L1FFN_NC
    w = prep_l1ffn_weights(I)
    maps = []
    for cix in range(NCORE):
        m = dict(w)
        m["xf"] = np.ascontiguousarray(x[cix // 2].astype(np.float32))
        maps.append(m)
    res = run_bass_kernel_spmd(nc, maps, core_ids=list(range(NCORE)))
    return np.stack([res.results[2 * b]["xo"] for b in range(B)])



ARENA_B = 211200


def mega_barrier(p):
    engines = ("sync", "tensor", "vector", "scalar", "gpsimd")
    vals = [(en, p.val(en)) for en in engines]
    dvals = [(k, p.esem[k].n) for k in list(p.esem) if k.startswith("dma_")]
    for en in engines:
        wl = [(e2, v) for e2, v in vals if e2 != en] + dvals
        p.op(en, lambda e: e.nop(), waits=wl, inc=True, chain=True)


def build_mega():
    nc = bass.Bass("TRN2", target_bir_lowering=False, debug=False, name="mega")
    AI = lambda n, s: nc.dram_tensor(n, s, F32, kind="ExternalInput").ap()
    x0s = AI("x0s", [TPC, D])
    gate_w = AI("gate_w", [D, 8]); gate_b = AI("gate_b", [1, 8])
    up_w = AI("up_w", [9, 128, 8 * 32 * 128]); up_bT = AI("up_bT", [9, 128, 32])
    down_w = AI("down_w", [9, 128, 32 * D]); down_b = AI("down_b", [9, 1, D])
    mq = AI("mq", [128, 256]); mqT = AI("mqT", [128, 256])
    w_attn = AI("w_attn", [D, 3072]); b_qk = AI("b_qk", [128, 16])
    b_v = AI("b_v", [1, D]); w_proj = AI("w_proj", [D, D]); b_proj = AI("b_proj", [1, D])
    filt = AI("filt", [128, 128]); gamma = AI("gamma", [1, D]); beta = AI("beta", [1, D])
    wqkv = AI("wqkv", [D, 3072]); lncol = AI("lncol", [128, 16]); rt4 = AI("rt4", [128, 32])
    qcn = AI("qcn", [128, 32]); wq = AI("wq", [2, D, 1024]); wkv = AI("wkv", [2, D, 2048])
    wout = AI("wout", [2, D, 1024]); wlo = AI("wlo", [D, 1024]); nks = AI("nks", [128, 32])
    nvs = AI("nvs", [1, 2048]); rotm = AI("rotm", [128, 128]); cossin = AI("cossin", [128, 2 * T])
    wmaskin = AI("wmaskin", [16, T - 512])
    mlocQ = AI("mlocQ", [128, 11 * 128]); mlocKT = AI("mlocKT", [128, 11 * 128])
    xob = nc.dram_tensor("xob", [T, D], F32, kind="ExternalOutput").ap()
    x1s = nc.dram_tensor("x1s", [TPC, D], F32)
    x1g = nc.dram_tensor("x1g", [2 * TPC, D], F32)
    x1f = nc.dram_tensor("x1f", [T, D], F32)
    x2s = nc.dram_tensor("x2s", [TPC, D], F32)
    x3s = nc.dram_tensor("x3s", [TPC, D], F32)
    x3g = nc.dram_tensor("x3g", [2 * TPC, D], F32)
    x3f = nc.dram_tensor("x3f", [T, D], F32)
    x4f = nc.dram_tensor("x4f", [T, D], F32)
    x5f = nc.dram_tensor("x5f", [T, D], F32)
    s_hbm = nc.dram_tensor("s_hbm", [4, T], F32)
    yf_hbm = nc.dram_tensor("yf_hbm", [128, 8 * T], BF16)
    yr_hbm = nc.dram_tensor("yr_hbm", [128, 8 * T], BF16)
    ql_hbm = nc.dram_tensor("ql_hbm", [128, 8 * T], BF16)
    kl_hbm = nc.dram_tensor("kl_hbm", [128, 8 * T], BF16)
    vl_hbm = nc.dram_tensor("vl_hbm", [T, D], BF16)
    qg_hbm = nc.dram_tensor("qg_hbm", [2, 128, 8 * T], BF16)
    kg_hbm = nc.dram_tensor("kg_hbm", [2, 128, 8 * T], BF16)
    vg_hbm = nc.dram_tensor("vg_hbm", [2, T, D], BF16)

    p = Prog(nc)
    pool = [p.psum(f"bank{i}", [128, 512], F32) for i in range(8)]
    arena = Arena(nc, ARENA_B)
    arena.allocate(p.stack)
    p.arena = arena
    p.psum_pool = pool

    def gather(src, dst_g, dst_f):
        mega_barrier(p)
        g_cc = p.op("gpsimd", lambda e: e.collective_compute(
            "AllGather", ALU.bypass, replica_groups=[[0, 1], [2, 3], [4, 5], [6, 7]],
            ins=[src.ap()[:, :]], outs=[dst_g.ap()[:, :]]), inc=True)
        for q in range(2):
            p.op("sync", lambda e, q=q: e.dma_start(
                out=dst_f.ap().rearrange("(i q c) d -> q i (c d)", i=T // 256, q=2, c=128)[q],
                in_=dst_g.ap()[q * TPC:(q + 1) * TPC, :].rearrange("(i c) d -> i (c d)", c=128)),
                dma="gat", waits=[("gpsimd", g_cc)])
        mega_barrier(p)

    p.begin_stage()
    emit_l0ffn(p, x0s, gate_w, gate_b, up_w, up_bT, down_w, down_b, x1s.ap(), TPC)
    gather(x1s, x1g, x1f)
    p.begin_stage()
    emit_l0attn(p, x1f.ap(), x1s.ap(), mq, mqT, w_attn, b_qk, b_v, w_proj, b_proj,
                x2s.ap(), TPC, T, 7)
    mega_barrier(p)
    p.begin_stage()
    emit_l0ffn(p, x2s.ap(), gate_w, gate_b, up_w, up_bT, down_w, down_b, x3s.ap(), TPC)
    gather(x3s, x3g, x3f)
    p.begin_stage()
    emit_l1ffn(p, x3f.ap(), wqkv, lncol, rt4, qcn, wq, wkv, wout, wlo, nks, nvs,
               rotm, cossin, wmaskin, mlocQ, mlocKT, x4f.ap(),
               s_hbm, yf_hbm, yr_hbm, ql_hbm, kl_hbm, vl_hbm, qg_hbm, kg_hbm, vg_hbm, T)
    mega_barrier(p)
    p.begin_stage()
    emit_l1attn(p, x4f.ap(), filt, gamma, beta, x5f.ap(), T)
    mega_barrier(p)
    p.begin_stage()
    emit_l1ffn(p, x5f.ap(), wqkv, lncol, rt4, qcn, wq, wkv, wout, wlo, nks, nvs,
               rotm, cossin, wmaskin, mlocQ, mlocKT, xob,
               s_hbm, yf_hbm, yr_hbm, ql_hbm, kl_hbm, vl_hbm, qg_hbm, kg_hbm, vg_hbm, T)
    p.finish()
    return nc


_MEGA_NC = None


def kernel_mega(**inputs):
    global _MEGA_NC
    if _MEGA_NC is None:
        _MEGA_NC = build_mega()
    w = {}
    w.update(prep_l0ffn_weights(inputs))
    w.update(prep_l0attn_weights(inputs))
    w.update(prep_l1attn_weights(inputs))
    w.update(prep_l1ffn_weights(inputs))
    x0 = np.asarray(inputs["L0_attn_x"], np.float32)
    shards = shard_tokens(x0)
    maps = []
    for c in range(NCORE):
        mqa, mqTa = l0attn_masks(c % 2)
        maps.append(dict(w, x0s=shards[c], mq=mqa, mqT=mqTa))
    res = run_bass_kernel_spmd(_MEGA_NC, maps, core_ids=list(range(NCORE)))
    return np.stack([res.results[2 * b]["xob"] for b in range(B)])


# version 05
# speedup vs baseline: 1.0580x; 1.0191x over previous
import numpy as np
from contextlib import ExitStack

import concourse.bass as bass
import concourse.mybir as mybir
from concourse.bass_utils import run_bass_kernel_spmd

F32 = mybir.dt.float32
BF16 = mybir.dt.bfloat16
AF = mybir.ActivationFunctionType
ALU = mybir.AluOpType
AX = mybir.AxisListType

D = 1024
B = 4
T = 4096
NCORE = 8
TPC = 2048
LN_EPS = 1e-5


class Sem:
    def __init__(self, handle):
        self.h = handle
        self.n = 0

    def reserve(self, k=1):
        self.n += k
        return self.n


class Arena:

    def __init__(self, nc, size_b):
        self.nc = nc
        self.size_b = size_b
        self.cursor = 0
        self.peak = 0
        self.handle = None
        self.base = 0

    def allocate(self, stack):
        import concourse.mybir as _mb
        self.handle = stack.enter_context(
            self.nc.sbuf_tensor("mega_arena", [128, self.size_b], _mb.dt.uint8))
        self.base = self.nc.lookup_mloc(self.handle).addr

    def alloc(self, name, shape, dt):
        per = 1
        for s in shape[1:]:
            per *= int(s)
        per *= mybir.dt.size(dt)
        addr = self.cursor
        self.cursor = (addr + per + 31) // 32 * 32
        self.peak = max(self.peak, self.cursor)
        assert self.cursor <= self.size_b, f"arena overflow at {name}: {self.cursor} > {self.size_b}"
        h = self.nc.virtual_sbuf_tensor(name, shape, dt)
        self.nc._tensor(h.name, h.shape, h.dtype, type="SB")
        mloc = self.nc.lookup_mloc(h)
        mloc.allocated = True
        mloc.addr = self.base + addr
        return h


class Prog:

    def __init__(self, nc):
        self.nc = nc
        self.stack = ExitStack()
        self.queues = {"sync": [], "tensor": [], "vector": [], "scalar": [], "gpsimd": []}
        self.esem = {k: Sem(self.stack.enter_context(nc.semaphore("sem_" + k)))
                     for k in self.queues}
        self.arena = None
        self.psum_pool = None
        self.psum_ix = 0
        self.waited = {k: {} for k in self.queues}

    def begin_stage(self):
        if self.arena is not None:
            self.arena.cursor = 0
        self.psum_ix = 0

    def dsem(self, name):
        key = "dma_" + name
        if key not in self.esem:
            self.esem[key] = Sem(self.stack.enter_context(self.nc.semaphore("sem_" + key)))
        return key

    def sbuf(self, name, shape, dt):
        if self.arena is not None:
            return self.arena.alloc(name, shape, dt)
        return self.stack.enter_context(self.nc.sbuf_tensor(name, shape, dt))

    def psum(self, name, shape, dt=F32):
        if self.psum_pool is not None:
            assert dt == F32, name
            bank = self.psum_pool[self.psum_ix]
            self.psum_ix += 1
            return bank[0:shape[0], 0:shape[1]]
        return self.stack.enter_context(self.nc.psum_tensor(name, shape, dt))

    def op(self, eng, makeop, inc=False, inck=1, waits=(), chain=None, dma=None):
        if chain is None:
            chain = eng != "tensor"
        if dma is not None:
            key = self.dsem(dma)
            inc = True
            inck = 16
        else:
            key = eng
        seen = self.waited[eng]
        wl = []
        for (en, v) in waits:
            if v > 0 and seen.get(en, 0) < v:
                seen[en] = v
                wl.append((en, v))
        if chain:
            prev_c = self.esem[eng].n
            if dma is None:
                inc = True
            if prev_c > 0 and seen.get(eng, 0) < prev_c:
                seen[eng] = prev_c
                wl.append((eng, prev_c))
        tgt = self.esem[key].reserve(inck) if inc else None
        wl = [(self.esem[en], v) for (en, v) in wl]

        def f(e, makeop=makeop, inc=inc, inck=inck, wl=wl, key=key):
            for s, v in wl:
                e.wait_ge(s.h, v)
            o = makeop(e)
            if inc:
                o.then_inc(self.esem[key].h, inck)

        self.queues[eng].append(f)
        return tgt

    def val(self, eng):
        return self.esem[eng].n

    def finish(self):
        block = self.stack.enter_context(self.nc.Block())
        for eng_name in ("sync", "tensor", "vector", "scalar", "gpsimd"):
            fns = self.queues[eng_name]
            if not fns:
                continue

            def section(eng, fns=fns):
                for f in fns:
                    f(eng)

            getattr(block, eng_name)(section)
        self.stack.close()


def emit_ln_tile(p, bufs, xt_ap, yt_ap, pre_a=0, pre_v=0, scr=None):
    ln_sum, ln_ss = bufs["ln_sum"], bufs["ln_ss"]
    ln_scr = scr if scr is not None else bufs["ln_scr"]
    ln_t1, ln_t2 = bufs["ln_t1"], bufs["ln_t2"]

    p.op("vector", lambda e: e.tensor_reduce(out=ln_sum[:], in_=xt_ap, axis=AX.X, op=ALU.add),
         inc=True, waits=[("scalar", pre_a), ("vector", pre_v)])
    a0 = p.op("scalar", lambda e: e.activation(out=ln_scr[:], in_=xt_ap, func=AF.Square, accum_out=ln_ss[:]),
              inc=True, waits=[("vector", pre_v)])
    p.op("vector", lambda e: e.tensor_scalar(out=ln_t1[:], in0=ln_ss[:], scalar1=1.0 / D, scalar2=LN_EPS,
                                             op0=ALU.mult, op1=ALU.add),
         waits=[("scalar", a0)], inc=True)
    p.op("vector", lambda e: e.tensor_scalar_mul(out=ln_sum[:], in0=ln_sum[:], scalar1=1.0 / D), inc=True)
    p.op("vector", lambda e: e.tensor_mul(out=ln_t2[:], in0=ln_sum[:], in1=ln_sum[:]), inc=True)
    v0 = p.op("vector", lambda e: e.tensor_sub(out=ln_t1[:], in0=ln_t1[:], in1=ln_t2[:]), inc=True)
    a1 = p.op("scalar", lambda e: e.activation(out=ln_t2[:], in_=ln_t1[:], func=AF.Sqrt),
              waits=[("vector", v0)], inc=True)
    v1 = p.op("vector", lambda e: e.reciprocal(out=ln_t1[:], in_=ln_t2[:]), waits=[("scalar", a1)], inc=True)
    p.op("vector", lambda e: e.tensor_mul(out=ln_t2[:], in0=ln_sum[:], in1=ln_t1[:]),
         waits=[("vector", v1)], inc=True)
    v2 = p.op("vector", lambda e: e.tensor_scalar_mul(out=ln_t2[:], in0=ln_t2[:], scalar1=-1.0), inc=True)
    a2 = p.op("scalar", lambda e: e.activation(out=yt_ap, in_=xt_ap, func=AF.Identity, bias=ln_t2[:], scale=ln_t1[:]),
              waits=[("vector", v2)], inc=True)
    return a2



def emit_l0ffn(p, xs, gate_w, gate_b, up_w, up_bT, down_w, down_b, xo, tpc=TPC):
    NE, HCH, ICH, HALF = 9, 8, 32, tpc // 2
    ntile = tpc // 128
    NGR = max(1, HALF // 512)
    NDS = HALF // 256
    NSUPA = 8
    NSUPB = 4

    xt = p.sbuf("xt", [128, D], F32)
    yt = p.sbuf("yt", [128, D], F32)
    ln_bufs = {
        "ln_sum": p.sbuf("ln_sum", [128, 1], F32),
        "ln_ss": p.sbuf("ln_ss", [128, 1], F32),
        "ln_scr": p.sbuf("ln_scr", [128, D], F32),
        "ln_t1": p.sbuf("ln_t1", [128, 1], F32),
        "ln_t2": p.sbuf("ln_t2", [128, 1], F32),
    }
    ident = p.sbuf("ident", [128, 128], F32)
    fones = p.sbuf("fones", [1, 128], F32)
    bones = p.sbuf("bones", [1, 128], BF16)
    yf32 = p.sbuf("yf32", [128, HCH * 128], F32)
    ybf = p.sbuf("ybf", [128, HCH * tpc], BF16)
    gbias = p.sbuf("gbias", [1, 8], F32)
    gwt = p.sbuf("gwt", [128, HCH * 8], F32)
    gatel = p.sbuf("gatel", [128, 8 * ntile], F32)
    w01 = p.sbuf("w01", [128, 8 * ntile], F32)
    top8 = p.sbuf("top8", [128, 8], F32)
    gsc = p.sbuf("gsc", [128, 8], F32)
    mcol = [p.sbuf(f"mcol{i}", [128, 1], F32) for i in range(3)]
    upblk = [p.sbuf(f"upblk{i}", [128, HCH * 4 * 128], BF16) for i in range(2)]
    dwblk = [p.sbuf(f"dwblk{i}", [128, 8 * D], BF16) for i in range(2)]
    dbias = [p.sbuf(f"dbias{i}", [1, D], BF16) for i in range(2)]
    upb = p.sbuf("upb", [128, NE * 32], F32)
    hact = p.sbuf("hact", [128, ICH * HALF], BF16)
    acc = p.sbuf("acc", [128, (HALF // 128) * D], F32)
    xt2 = xt
    outt = yt

    ps_up = [p.psum(f"ps_up{i}", [128, 512], F32) for i in range(2)]
    ps_dn = [p.psum(f"ps_dn{i}", [128, 512], F32) for i in range(4)]
    ps_tr = [ps_up[0][:, 0:128], ps_up[1][:, 0:128]]
    ps_g = ps_dn[0][:, 0:8]

    g0 = p.op("gpsimd", lambda e: e.memset(ident[:], 0.0), inc=True)
    p.op("gpsimd", lambda e: e.affine_select(out=ident[:], in_=ident[:], compare_op=ALU.not_equal,
                                             fill=1.0, base=0, pattern=[[-1, 128]], channel_multiplier=1),
         inc=True, waits=[("gpsimd", g0)])
    p.op("gpsimd", lambda e: e.memset(fones[:], 1.0), inc=True)
    g_const = p.op("gpsimd", lambda e: e.memset(bones[:], 1.0), inc=True)
    for eix in range(NE):
        p.op("sync", lambda e, eix=eix: e.dma_start(out=upb[:, eix * 32:(eix + 1) * 32], in_=up_bT[eix]),
             dma="const")
    p.op("sync", lambda e: e.dma_start(out=gbias[:], in_=gate_b), dma="const")
    for hk in range(HCH):
        p.op("sync", lambda e, hk=hk: e.dma_start(out=gwt[:, hk * 8:(hk + 1) * 8],
                                                  in_=gate_w[hk * 128:(hk + 1) * 128, :]), dma="const")
    const_lds = p.esem[p.dsem("const")].n

    prev_ln = 0
    prev_lnv = 0
    prev_w01 = 0
    prev_gate = 0
    prev_sig = 0
    tr_evac = [0, 0]
    for t in range(ntile):
        ld = p.op("sync", lambda e, t=t: e.dma_start(out=xt[:], in_=xs[t * 128:(t + 1) * 128, :]),
                  dma="ldx", waits=[("scalar", prev_ln), ("vector", prev_lnv)])
        p.op("vector", lambda e: e.nop(), waits=[("dma_ldx", ld)])
        p.op("scalar", lambda e: e.nop(), waits=[("dma_ldx", ld)])
        prev_ln = emit_ln_tile(p, ln_bufs, xt[:], yt[:])
        prev_lnv = p.val("vector")
        for hk in range(HCH):
            pt = ps_tr[hk % 2]
            t_tr = p.op("tensor", lambda e, hk=hk, pt=pt: e.transpose(pt, yt[:, hk * 128:(hk + 1) * 128], ident[:]),
                        waits=[("scalar", prev_ln), ("vector", tr_evac[hk % 2]), ("gpsimd", g_const)], inc=True)
            p.op("vector", lambda e, hk=hk, pt=pt: e.tensor_copy(out=yf32[:, hk * 128:(hk + 1) * 128], in_=pt),
                 waits=[("tensor", t_tr), ("tensor", prev_gate)], inc=True)
            tr_evac[hk % 2] = p.op("vector", lambda e, hk=hk, t=t: e.tensor_copy(
                out=ybf[:, hk * tpc + t * 128: hk * tpc + (t + 1) * 128],
                in_=yf32[:, hk * 128:(hk + 1) * 128]), inc=True)
        v_tr = p.val("vector")
        for hk in range(HCH):
            p.op("tensor", lambda e, hk=hk: e.matmul(ps_g, yf32[:, hk * 128:(hk + 1) * 128],
                                                     gwt[:, hk * 8:(hk + 1) * 8], start=(hk == 0), stop=False),
                 waits=([("vector", v_tr), ("dma_const", const_lds), ("scalar", prev_sig)] if hk == 0 else []))
        t_g = p.op("tensor", lambda e: e.matmul(ps_g, fones[:], gbias[:], start=False, stop=True), inc=True)
        prev_gate = t_g
        prev_sig = p.op("scalar", lambda e, t=t: e.activation(out=gatel[:, t * 8:(t + 1) * 8], in_=ps_g, func=AF.Sigmoid),
                        waits=[("tensor", t_g)], inc=True)
        v0 = p.op("vector", lambda e, t=t: e.max(out=top8[:], in_=gatel[:, t * 8:(t + 1) * 8]),
                  waits=[("scalar", prev_sig)], inc=True)
        v1 = p.op("vector", lambda e, t=t: e.tensor_scalar(out=gsc[:], in0=gatel[:, t * 8:(t + 1) * 8],
                                                           scalar1=top8[:, 1:2], scalar2=None, op0=ALU.is_ge),
                  waits=[("vector", v0)], inc=True)
        p.op("vector", lambda e: e.tensor_tensor(out=mcol[0][:], in0=top8[:, 0:1], in1=top8[:, 1:2], op=ALU.add),
             inc=True)
        v2 = p.op("vector", lambda e: e.tensor_scalar_add(out=mcol[0][:], in0=mcol[0][:], scalar1=1e-6), inc=True)
        v3 = p.op("vector", lambda e: e.reciprocal(out=mcol[1][:], in_=mcol[0][:]), waits=[("vector", v2)], inc=True)
        v4 = p.op("vector", lambda e: e.tensor_scalar_mul(out=mcol[1][:], in0=mcol[1][:], scalar1=0.1),
                  waits=[("vector", v3)], inc=True)
        v5 = p.op("vector", lambda e, t=t: e.tensor_mul(out=gsc[:], in0=gsc[:], in1=gatel[:, t * 8:(t + 1) * 8]),
                  inc=True)
        prev_w01 = p.op("vector", lambda e, t=t: e.tensor_scalar(out=w01[:, t * 8:(t + 1) * 8], in0=gsc[:],
                                                                 scalar1=mcol[1][:], scalar2=None, op0=ALU.mult),
                        waits=[("vector", v4), ("vector", v5)], inc=True)
        prev_lnv = max(prev_lnv, prev_w01)
    pass1_v = p.val("vector")
    pass1_a = p.val("scalar")

    up_last_use = [0, 0]
    dw_last_use = [0, 0]
    gelu_done = [0, 0]
    fma_done = [0, 0, 0, 0]
    hact_free = 0
    dbias_free = [0, 0]
    store_guard = 0
    prev_fin = 0

    for half in range(2):
        tok0 = half * HALF
        for eix in range(NE):
            dbpar = (half * NE + eix) % 2
            wdb = p.op("gpsimd", lambda e, eix=eix, dbpar=dbpar: e.dma_start(out=dbias[dbpar][:], in_=down_b[eix]),
                       dma=f"wdb{dbpar}", waits=[("tensor", dbias_free[dbpar])])
            for sup in range(NSUPA):
                blk = upblk[sup % 2]
                src4 = up_w[eix].rearrange("p (hc ii i) -> p hc ii i", hc=8, ii=32, i=128)
                wv = p.op("gpsimd", lambda e, src4=src4, sup=sup, blk=blk: e.dma_start(
                    out=blk[:].rearrange("p (hc ii i) -> p hc ii i", hc=8, ii=4, i=128),
                    in_=src4[:, :, sup * 4:(sup + 1) * 4, :]),
                    dma=f"wup{sup % 2}", waits=[("tensor", up_last_use[sup % 2])])
                t_up = 0
                for iil in range(4):
                    ii = sup * 4 + iil
                    for gr in range(NGR):
                        bank = ps_up[gr % 2]
                        for hk in range(HCH):
                            wl = []
                            if hk == 0:
                                wl = [(f"dma_wup{sup % 2}", wv), ("scalar", gelu_done[gr % 2]), ("vector", pass1_v)]
                            t_up = p.op("tensor", lambda e, hk=hk, iil=iil, gr=gr, bank=bank, blk=blk, tok0=tok0:
                                        e.matmul(bank[:], blk[:, (hk * 4 + iil) * 128:(hk * 4 + iil + 1) * 128],
                                                 ybf[:, hk * tpc + tok0 + gr * 512: hk * tpc + tok0 + (gr + 1) * 512],
                                                 start=(hk == 0), stop=(hk == HCH - 1)), waits=wl,
                                        inc=(hk == HCH - 1))
                        gelu_done[gr % 2] = p.op("scalar", lambda e, ii=ii, gr=gr, bank=bank, eix=eix:
                                                 e.activation(out=hact[:, ii * HALF + gr * 512: ii * HALF + (gr + 1) * 512],
                                                              in_=bank[:], func=AF.Gelu,
                                                              bias=upb[:, eix * 32 + ii: eix * 32 + ii + 1]),
                                                 waits=[("tensor", t_up), ("tensor", hact_free)], inc=True)
                up_last_use[sup % 2] = t_up
            a_phaseA = p.val("scalar")
            for sup in range(NSUPB):
                blk = dwblk[sup % 2]
                src3 = down_w[eix].rearrange("p (ii f) -> p ii f", ii=32, f=D)
                wv = p.op("gpsimd", lambda e, src3=src3, sup=sup, blk=blk: e.dma_start(
                    out=blk[:].rearrange("p (ii f) -> p ii f", ii=8, f=D),
                    in_=src3[:, sup * 8:(sup + 1) * 8, :]),
                    dma=f"wdn{sup % 2}", waits=[("tensor", dw_last_use[sup % 2])])
                for ds in range(NDS):
                    grp_t = [0, 0, 0, 0]
                    for t2 in range(2):
                        for f2 in range(2):
                            bk = t2 * 2 + f2
                            bank = ps_dn[bk]
                            tt_local = ds * 2 + t2
                            first = True
                            if sup == 0:
                                p.op("tensor", lambda e, bank=bank, dbpar=dbpar, f2=f2:
                                     e.matmul(bank[:], bones[:], dbias[dbpar][0:1, f2 * 512:(f2 + 1) * 512],
                                              start=True, stop=False),
                                     waits=[(f"dma_wdn{sup % 2}", wv), (f"dma_wdb{dbpar}", wdb), ("vector", fma_done[bk]),
                                            ("scalar", a_phaseA)])
                                first = False
                            for iil in range(8):
                                ii = sup * 8 + iil
                                wl = []
                                if first:
                                    wl = [(f"dma_wdn{sup % 2}", wv), ("vector", fma_done[bk]), ("scalar", a_phaseA)]
                                    first = False
                                grp_t[bk] = p.op("tensor", lambda e, bank=bank, ii=ii, iil=iil, tt_local=tt_local, f2=f2,
                                                 blk=blk, st=(sup != 0 and iil == 0):
                                                 e.matmul(bank[:], hact[:, ii * HALF + tt_local * 128: ii * HALF + (tt_local + 1) * 128],
                                                          blk[:, iil * D + f2 * 512: iil * D + (f2 + 1) * 512],
                                                          start=st, stop=(iil == 7)), waits=wl, inc=(iil == 7))
                    for t2 in range(2):
                        for f2 in range(2):
                            bk = t2 * 2 + f2
                            bank = ps_dn[bk]
                            tt_local = ds * 2 + t2
                            tglob = (tok0 + tt_local * 128) // 128
                            acc_ap = acc[:, tt_local * D + f2 * 512: tt_local * D + (f2 + 1) * 512]
                            if eix == 0 and sup == 0:
                                fma_done[bk] = p.op("vector", lambda e, bank=bank, acc_ap=acc_ap, tglob=tglob:
                                                    e.tensor_scalar(out=acc_ap, in0=bank[:],
                                                                    scalar1=w01[:, tglob * 8: tglob * 8 + 1],
                                                                    scalar2=None, op0=ALU.mult),
                                                    waits=[("tensor", grp_t[bk])], inc=True)
                            elif eix == 8:
                                fma_done[bk] = p.op("vector", lambda e, bank=bank, acc_ap=acc_ap:
                                                    e.scalar_tensor_tensor(out=acc_ap, in0=bank[:], scalar=0.1,
                                                                           in1=acc_ap, op0=ALU.mult, op1=ALU.add),
                                                    waits=[("tensor", grp_t[bk])], inc=True)
                            else:
                                fma_done[bk] = p.op("vector", lambda e, bank=bank, acc_ap=acc_ap, tglob=tglob, eix=eix:
                                                    e.scalar_tensor_tensor(out=acc_ap, in0=bank[:],
                                                                           scalar=w01[:, tglob * 8 + eix: tglob * 8 + eix + 1],
                                                                           in1=acc_ap, op0=ALU.mult, op1=ALU.add),
                                                    waits=[("tensor", grp_t[bk])], inc=True)
                dw_last_use[sup % 2] = max(grp_t)
            t_phaseB = dw_last_use[(NSUPB - 1) % 2]
            hact_free = t_phaseB
            dbias_free[dbpar] = t_phaseB
        v_acc = p.val("vector")
        for tt_local in range(HALF // 128):
            tglob = tok0 // 128 + tt_local
            acc_ap = acc[:, tt_local * D: (tt_local + 1) * D]
            ld = p.op("sync", lambda e, tglob=tglob: e.dma_start(out=xt2[:], in_=xs[tglob * 128:(tglob + 1) * 128, :]),
                      dma="ldx", waits=[("vector", max(prev_fin, v_acc)), ("scalar", pass1_a)])
            v0 = p.op("vector", lambda e, acc_ap=acc_ap: e.tensor_reduce(out=mcol[2][:], in_=acc_ap, axis=AX.X,
                                                                         op=ALU.max, apply_absolute_value=True),
                      waits=[("vector", v_acc)], inc=True)
            v1 = p.op("vector", lambda e: e.tensor_scalar_add(out=mcol[2][:], in0=mcol[2][:], scalar1=1e-6),
                      waits=[("vector", v0)], inc=True)
            v2 = p.op("vector", lambda e: e.reciprocal(out=mcol[2][:], in_=mcol[2][:]), waits=[("vector", v1)], inc=True)
            v3 = p.op("vector", lambda e: e.tensor_scalar_mul(out=mcol[2][:], in0=mcol[2][:], scalar1=0.5),
                      waits=[("vector", v2)], inc=True)
            v4 = p.op("vector", lambda e, acc_ap=acc_ap: e.scalar_tensor_tensor(out=outt[:], in0=acc_ap,
                                                                                scalar=mcol[2][:], in1=xt2[:],
                                                                                op0=ALU.mult, op1=ALU.add),
                      waits=[("vector", v3), ("dma_ldx", ld), ("dma_stout", store_guard)], inc=True)
            store_guard = p.op("sync", lambda e, tglob=tglob: e.dma_start(out=xo[tglob * 128:(tglob + 1) * 128, :],
                                                                          in_=outt[:]),
                               dma="stout", waits=[("vector", v4)])
            prev_fin = v4
    p.op("sync", lambda e: e.nop(), waits=[("dma_stout", p.esem[p.dsem("stout")].n)])


def build_l0ffn(tpc=TPC):
    nc = bass.Bass("TRN2", target_bir_lowering=False, debug=False, name="l0ffn")
    xs = nc.dram_tensor("xs", [tpc, D], F32, kind="ExternalInput").ap()
    gate_w = nc.dram_tensor("gate_w", [D, 8], F32, kind="ExternalInput").ap()
    gate_b = nc.dram_tensor("gate_b", [1, 8], F32, kind="ExternalInput").ap()
    up_w = nc.dram_tensor("up_w", [9, 128, 8 * 32 * 128], F32, kind="ExternalInput").ap()
    up_bT = nc.dram_tensor("up_bT", [9, 128, 32], F32, kind="ExternalInput").ap()
    down_w = nc.dram_tensor("down_w", [9, 128, 32 * D], F32, kind="ExternalInput").ap()
    down_b = nc.dram_tensor("down_b", [9, 1, D], F32, kind="ExternalInput").ap()
    xo = nc.dram_tensor("xo", [tpc, D], F32, kind="ExternalOutput").ap()

    p = Prog(nc)
    emit_l0ffn(p, xs, gate_w, gate_b, up_w, up_bT, down_w, down_b, xo, tpc)
    p.finish()
    return nc



def prep_l0ffn_weights(I):
    up_all = np.concatenate([I["L0_ffn_up_w"], I["L0_ffn_sw_up"][None]], 0)
    down_all = np.concatenate([I["L0_ffn_down_w"], I["L0_ffn_sw_down"][None]], 0)
    upb_all = np.concatenate([I["L0_ffn_up_b"], I["L0_ffn_sb_up"][None]], 0)
    dnb_all = np.concatenate([I["L0_ffn_down_b"], I["L0_ffn_sb_down"][None]], 0)
    up_dev = np.ascontiguousarray(
        up_all.reshape(9, 8, 128, 32, 128).transpose(0, 2, 1, 3, 4).reshape(9, 128, 8 * 32 * 128))
    down_dev = np.ascontiguousarray(
        down_all.reshape(9, 32, 128, 1024).transpose(0, 2, 1, 3).reshape(9, 128, 32 * 1024))
    up_bT = np.ascontiguousarray(upb_all.reshape(9, 32, 128).transpose(0, 2, 1))
    down_b = np.ascontiguousarray(dnb_all.reshape(9, 1, 1024))
    return {
        "gate_w": np.ascontiguousarray(I["L0_ffn_gate_w"]),
        "gate_b": np.ascontiguousarray(I["L0_ffn_bias"].reshape(1, 8)),
        "up_w": up_dev, "up_bT": up_bT, "down_w": down_dev, "down_b": down_b,
    }


def shard_tokens(x):
    xb = x.reshape(B, T // 128, 128, D)
    return [np.ascontiguousarray(xb[c // 2, c % 2::2].reshape(TPC, D)) for c in range(NCORE)]


def unshard_tokens(shards):
    out = np.zeros((B, T // 128, 128, D), np.float32)
    for c in range(NCORE):
        out[c // 2, c % 2::2] = np.asarray(shards[c]).reshape(TPC // 128, 128, D)
    return out.reshape(B, T, D)


_L0FFN_NC = None


def run_l0ffn(x, I):
    global _L0FFN_NC
    if _L0FFN_NC is None:
        _L0FFN_NC = build_l0ffn()
    w = prep_l0ffn_weights(I)
    shards = shard_tokens(x)
    in_maps = [dict(w, xs=s) for s in shards]
    res = run_bass_kernel_spmd(_L0FFN_NC, in_maps, core_ids=list(range(NCORE)))
    return unshard_tokens([res.results[c]["xo"] for c in range(NCORE)])


def kernel(**inputs):
    return kernel_mega(**inputs)


if __name__ == "__main__":
    print("kernel module")



def emit_l0attn(p, xf, xq, mq, mqT, w_attn, b_qk, b_v, w_proj, b_proj, xo, tq=TPC, tf=T, newton=7):
    HCH = 8
    nqt = tq // 128
    nft = tf // 128
    NH = 16

    xt = p.sbuf("xt", [128, D], F32)
    yt = p.sbuf("yt", [128, D], F32)
    cols = p.sbuf("cols", [128, 32], F32)
    ln_bufs = {
        "ln_sum": cols[:, 0:1],
        "ln_ss": cols[:, 1:2],
        "ln_t1": cols[:, 2:3],
        "ln_t2": cols[:, 3:4],
    }
    ident = p.sbuf("ident", [128, 128], F32)
    fones = p.sbuf("fones", [1, 128], F32)
    bones = p.sbuf("bones", [1, 128], BF16)
    ybg = p.sbuf("ybg", [128, HCH * 512], BF16)
    watq = p.sbuf("watq", [128, HCH * D], BF16)
    watk = p.sbuf("watk", [128, HCH * D], BF16)
    watv = p.sbuf("watv", [128, HCH * D], BF16)
    bqk = cols[:, 16:32]
    bvrow = p.sbuf("bvrow", [1, D], BF16)
    bprow = bvrow
    kbf = p.sbuf("kbf", [128, HCH * tf], BF16)
    vbf = p.sbuf("vbf", [128, (tf // 128) * D], BF16)
    qtile = p.sbuf("qtile", [128, HCH * 128], BF16)
    mqs = p.sbuf("mqs", [128, 256], BF16)
    mqTs = p.sbuf("mqTs", [128, 256], BF16)
    zbuf = watv
    abuf = [p.sbuf(f"abuf{i}", [128, 128], BF16) for i in range(2)]
    ascr = p.sbuf("ascr", [128, 256], BF16)
    outf = p.sbuf("outf", [128, HCH * 128], BF16)
    ntrow = p.sbuf("ntrow", [1, 128], F32)
    m_run = cols[:, 4:5]
    tau = cols[:, 5:6]
    ntau = cols[:, 6:7]
    fcol = cols[:, 7:8]
    ccol = cols[:, 8:9]
    fcol2 = cols[:, 9:10]
    ccol2 = cols[:, 10:11]
    rcol = cols[:, 11:12]
    nscr = p.sbuf("nscr", [128, max(tf // 2, D)], BF16)
    nscr2 = ybg
    outt = yt

    ps_kv = [p.psum(f"ps_kv{i}", [128, 512], F32) for i in range(2)]
    ps_z = [p.psum(f"ps_z{i}", [128, 512], F32) for i in range(2)]
    _zt = [p.psum(f"ps_zT{i}", [128, 256], F32) for i in range(2)]
    ps_zT = [_zt[0][:, 0:128], _zt[1][:, 0:128]]
    ps_o = p.psum("ps_o", [64, 128], F32)
    ps_r = p.psum("ps_r", [1, 128], F32)

    g0 = p.op("gpsimd", lambda e: e.memset(ident[:], 0.0), inc=True)
    p.op("gpsimd", lambda e: e.affine_select(out=ident[:], in_=ident[:], compare_op=ALU.not_equal,
                                             fill=1.0, base=0, pattern=[[-1, 128]], channel_multiplier=1),
         inc=True, waits=[("gpsimd", g0)])
    p.op("gpsimd", lambda e: e.memset(fones[:], 1.0), inc=True)
    g_const = p.op("gpsimd", lambda e: e.memset(bones[:], 1.0), inc=True)
    p.op("sync", lambda e: e.dma_start(out=bqk[:], in_=b_qk), dma="const")
    p.op("gpsimd", lambda e: e.dma_start(out=bvrow[:], in_=b_v), dma="gconst")
    p.op("gpsimd", lambda e: e.dma_start(out=mqs[:], in_=mq), dma="gconst")
    p.op("gpsimd", lambda e: e.dma_start(out=mqTs[:], in_=mqT), dma="gconst")
    for hk in range(HCH):
        p.op("gpsimd", lambda e, hk=hk: e.dma_start(out=watq[:, hk * D:(hk + 1) * D],
                                                    in_=w_attn[hk * 128:(hk + 1) * 128, 0:D]), dma="gconst")
        p.op("gpsimd", lambda e, hk=hk: e.dma_start(out=watk[:, hk * D:(hk + 1) * D],
                                                    in_=w_attn[hk * 128:(hk + 1) * 128, D:2 * D]), dma="gconst")
        p.op("gpsimd", lambda e, hk=hk: e.dma_start(out=watv[:, hk * D:(hk + 1) * D],
                                                    in_=w_attn[hk * 128:(hk + 1) * 128, 2 * D:3 * D]), dma="gconst")
    const_lds = p.esem[p.dsem("const")].n
    gconst_lds = p.esem[p.dsem("gconst")].n
    bq_scaled = p.op("vector", lambda e: e.tensor_scalar_mul(out=bqk[:, 0:8], in0=bqk[:, 0:8], scalar1=0.125),
                     waits=[("dma_const", const_lds)], inc=True)

    def wslice(hk, f0, f1):
        if f1 <= D:
            return watq[:, hk * D + f0: hk * D + f1]
        if f1 <= 2 * D:
            return watk[:, hk * D + f0 - D: hk * D + f1 - D]
        return watv[:, hk * D + f0 - 2 * D: hk * D + f1 - 2 * D]

    prev_ln = 0
    prev_lnv = 0
    tr_evac = [0, 0]
    prev_kv = 0
    kv_evac_a = 0
    kv_evac_v = 0
    for grp in range(nft // 4):
        for tl in range(4):
            t = grp * 4 + tl
            ld = p.op("sync", lambda e, t=t: e.dma_start(out=xt[:], in_=xf[t * 128:(t + 1) * 128, :]),
                      dma="ldx", waits=[("scalar", prev_ln), ("vector", prev_lnv)])
            p.op("vector", lambda e: e.nop(), waits=[("dma_ldx", ld)])
            p.op("scalar", lambda e: e.nop(), waits=[("dma_ldx", ld)])
            prev_ln = emit_ln_tile(p, ln_bufs, xt[:], yt[:], scr=nscr[:, 0:D])
            for hk in range(HCH):
                pt = ps_z[hk % 2][:, 0:128]
                t_tr = p.op("tensor", lambda e, hk=hk, pt=pt: e.transpose(pt, yt[:, hk * 128:(hk + 1) * 128], ident[:]),
                            waits=[("scalar", prev_ln), ("vector", tr_evac[hk % 2]), ("gpsimd", g_const),
                                   ("tensor", prev_kv)], inc=True)
                tr_evac[hk % 2] = p.op("vector", lambda e, hk=hk, tl=tl, pt=pt: e.tensor_copy(
                    out=ybg[:, hk * 512 + tl * 128: hk * 512 + (tl + 1) * 128], in_=pt),
                    waits=[("tensor", t_tr)], inc=True)
            prev_lnv = p.val("vector")
        v_grp = p.val("vector")
        for fc in range(HCH):
            bank = ps_kv[fc % 2]
            for hk in range(HCH):
                wl = [("vector", max(v_grp, kv_evac_v)), ("dma_const", const_lds),
                      ("dma_gconst", gconst_lds)] if hk == 0 else []
                if hk == 0 and fc >= 2:
                    wl.append(("scalar", kv_evac_a))
                p.op("tensor", lambda e, fc=fc, hk=hk, bank=bank: e.matmul(
                    bank[:], wslice(hk, D + fc * 128, D + (fc + 1) * 128),
                    ybg[:, hk * 512:(hk + 1) * 512], start=(hk == 0), stop=(hk == HCH - 1)),
                     waits=wl, inc=(hk == HCH - 1))
            t_k = p.val("tensor")
            kv_evac_a = p.op("scalar", lambda e, fc=fc, grp=grp, bank=bank: e.activation(
                out=kbf[:, fc * tf + grp * 512: fc * tf + (grp + 1) * 512], in_=bank[:],
                func=AF.Identity, bias=bqk[:, 8 + fc: 9 + fc], scale=1.0),
                waits=[("tensor", t_k)], inc=True)
        for tl in range(4):
            for ft in range(2):
                bank = ps_kv[(tl * 2 + ft) % 2]
                for hk in range(HCH):
                    wl = []
                    if hk == 0:
                        wl = [("scalar", kv_evac_a), ("vector", kv_evac_v)]
                    p.op("tensor", lambda e, tl=tl, ft=ft, hk=hk, bank=bank: e.matmul(
                        bank[:], ybg[:, hk * 512 + tl * 128: hk * 512 + (tl + 1) * 128],
                        wslice(hk, 2048 + ft * 512, 2048 + (ft + 1) * 512),
                        start=(hk == 0), stop=False), waits=wl)
                t_v = p.op("tensor", lambda e, ft=ft, bank=bank: e.matmul(
                    bank[:], bones[0:1, :], bvrow[0:1, ft * 512:(ft + 1) * 512],
                    start=False, stop=True), inc=True)
                kv_evac_v = p.op("vector", lambda e, tl=tl, ft=ft, grp=grp, bank=bank: e.tensor_copy(
                    out=vbf[:, (grp * 4 + tl) * D + ft * 512: (grp * 4 + tl) * D + (ft + 1) * 512], in_=bank[:]),
                    waits=[("tensor", t_v)], inc=True)
        prev_kv = p.val("tensor")
    pass1_a = p.val("scalar")
    pass1_v = p.val("vector")

    wv_p = p.op("gpsimd", lambda e: e.dma_start(out=bprow[:], in_=b_proj),
                dma="wproj", waits=[("tensor", prev_kv)])
    for hk in range(HCH):
        wv_p = p.op("gpsimd", lambda e, hk=hk: e.dma_start(out=watk[:, hk * D:(hk + 1) * D],
                                                           in_=w_proj[hk * 128:(hk + 1) * 128, :]),
                    dma="wproj", waits=[("tensor", prev_kv)])

    ctx = dict(locals())
    emit_l0attn_tiles(p, ctx)
    p.op("sync", lambda e: e.nop(), waits=[("dma_stout", p.esem[p.dsem("stout")].n)])


def build_l0attn(tq=TPC, tf=T, newton=7):
    nc = bass.Bass("TRN2", target_bir_lowering=False, debug=False, name="l0attn")
    xf = nc.dram_tensor("xf", [tf, D], F32, kind="ExternalInput").ap()
    xq = nc.dram_tensor("xq", [tq, D], F32, kind="ExternalInput").ap()
    mq = nc.dram_tensor("mq", [128, 256], F32, kind="ExternalInput").ap()
    mqT = nc.dram_tensor("mqT", [128, 256], F32, kind="ExternalInput").ap()
    w_attn = nc.dram_tensor("w_attn", [D, 3072], F32, kind="ExternalInput").ap()
    b_qk = nc.dram_tensor("b_qk", [128, 16], F32, kind="ExternalInput").ap()
    b_v = nc.dram_tensor("b_v", [1, D], F32, kind="ExternalInput").ap()
    w_proj = nc.dram_tensor("w_proj", [D, D], F32, kind="ExternalInput").ap()
    b_proj = nc.dram_tensor("b_proj", [1, D], F32, kind="ExternalInput").ap()
    xo = nc.dram_tensor("xo", [tq, D], F32, kind="ExternalOutput").ap()

    p = Prog(nc)
    emit_l0attn(p, xf, xq, mq, mqT, w_attn, b_qk, b_v, w_proj, b_proj, xo, tq, tf, newton)
    p.finish()
    return nc


GP_DIV = True


def emit_l0attn_tiles(p, c):
    nqt = c["nqt"]; tf = c["tf"]; NH = c["NH"]; HCH = c["HCH"]
    xq = c["xq"]; xo = c["xo"]
    xt = c["xt"]; yt = c["yt"]; ln_bufs = c["ln_bufs"]
    ident = c["ident"]; fones = c["fones"]; bones = c["bones"]
    watq = c["watq"]; watk = c["watk"]; bqk = c["bqk"]; bprow = c["bprow"]
    kbf = c["kbf"]; vbf = c["vbf"]; qtile = c["qtile"]
    mqs = c["mqs"]; mqTs = c["mqTs"]
    zbuf = c["zbuf"]; abuf = c["abuf"]; ascr = c["ascr"]; outf = c["outf"]
    ntrow = c["ntrow"]; m_run = c["m_run"]; nscr = c["nscr"]; nscr2 = c["nscr2"]; outt = c["outt"]
    tau = c["tau"]; ntau = c["ntau"]; fcol = c["fcol"]; ccol = c["ccol"]
    fcol2 = c["fcol2"]; ccol2 = c["ccol2"]; rcol = c["rcol"]
    ps_kv = c["ps_kv"]; ps_z = c["ps_z"]; ps_zT = c["ps_zT"]; ps_o = c["ps_o"]; ps_r = c["ps_r"]
    wv_p = c["wv_p"]; pass1_a = c["pass1_a"]; pass1_v = c["pass1_v"]
    prev_kv = c["prev_kv"]

    def wslice(hk, f0, f1):
        if f1 <= D:
            return watq[:, hk * D + f0: hk * D + f1]
        return watk[:, hk * D + f0 - D: hk * D + f1 - D]

    prev_q = 0
    qev = [0, 0, 0, 0]
    m_guard = 0
    prev_lna = pass1_a
    prev_outp = pass1_v
    zevac_guard = 0
    ab_relu = [0, 0, 0, 0]
    ntrow_ev = 0
    outf_ev = 0
    prev_store = 0
    tr2_evac = [0, 0]
    last_o_ev = 0

    for i in range(nqt):
        nkb = 2 * i + 2
        W = nkb * 128
        ld = p.op("sync", lambda e, i=i: e.dma_start(out=xt[:], in_=xq[i * 128:(i + 1) * 128, :]),
                  dma="ldq", waits=[("vector", prev_outp), ("scalar", prev_lna), ("dma_stout", prev_store)])
        p.op("vector", lambda e: e.nop(), waits=[("dma_ldq", ld)])
        p.op("scalar", lambda e: e.nop(), waits=[("dma_ldq", ld)])
        ln_a = emit_ln_tile(p, ln_bufs, xt[:], yt[:], scr=nscr[:, 0:D])
        prev_lna = ln_a
        ybq = outf
        for hk in range(HCH):
            pt = ps_z[hk % 2][:, 0:128]
            t_tr = p.op("tensor", lambda e, hk=hk, pt=pt: e.transpose(pt, yt[:, hk * 128:(hk + 1) * 128], ident[:]),
                        waits=[("scalar", ln_a), ("vector", tr2_evac[hk % 2]), ("tensor", prev_kv)], inc=True)
            tr2_evac[hk % 2] = p.op("vector", lambda e, hk=hk, pt=pt: e.tensor_copy(
                out=ybq[:, hk * 128:(hk + 1) * 128], in_=pt), waits=[("tensor", t_tr)], inc=True)
        v_trq = p.val("vector")
        for fc in range(HCH):
            bank = ps_zT[fc % 2]
            for hk in range(HCH):
                wl = []
                if hk == 0:
                    wl = [("vector", v_trq), ("scalar", max(qev[fc % 2], *ab_relu))]
                p.op("tensor", lambda e, fc=fc, hk=hk, bank=bank, ybq=ybq: e.matmul(
                    bank[:], wslice(hk, fc * 128, (fc + 1) * 128), ybq[:, hk * 128:(hk + 1) * 128],
                    start=(hk == 0), stop=(hk == HCH - 1)), waits=wl, inc=(hk == HCH - 1))
            t_q = p.val("tensor")
            qev[fc % 2] = prev_q = p.op("scalar", lambda e, fc=fc, bank=bank: e.activation(
                out=qtile[:, fc * 128:(fc + 1) * 128], in_=bank[:], func=AF.Identity,
                bias=bqk[:, fc: fc + 1], scale=0.125),
                waits=[("tensor", t_q), ("vector", c["bq_scaled"])], inc=True)
        for h in range(NH):
            qs = qtile[(h % 2) * 64:(h % 2) * 64 + 64, (h // 2) * 128:(h // 2) * 128 + 128]
            nchunk = nkb // 2
            for ch in range(nchunk):
                bank = ps_z[ch % 2]
                t_z = p.op("tensor", lambda e, ch=ch, h=h, bank=bank, qs=qs: e.matmul(
                    bank[:, 0:256], qs,
                    kbf[(h % 2) * 64:(h % 2) * 64 + 64, (h // 2) * tf + ch * 256:(h // 2) * tf + (ch + 1) * 256],
                    start=True, stop=True),
                    waits=[("vector", tr2_evac[ch % 2]), ("scalar", max(prev_q, zevac_guard))], inc=True)
                if ch == nchunk - 1:
                    v1 = p.op("vector", lambda e, bank=bank: e.tensor_tensor(out=ascr[:, 0:256], in0=bank[:, 0:256], in1=mqs[:], op=ALU.add),
                              waits=[("tensor", t_z), ("scalar", zevac_guard)], inc=True)
                    tr2_evac[ch % 2] = p.op("vector", lambda e, ch=ch: e.tensor_scalar(
                        out=zbuf[:, ch * 256:(ch + 1) * 256], in0=ascr[:, 0:256], scalar1=0.0,
                        scalar2=(None if ch == 0 else m_run[:]), op0=ALU.add, op1=ALU.max, accum_out=m_run[:]),
                        waits=[("gpsimd", m_guard)], inc=True)
                else:
                    tr2_evac[ch % 2] = p.op("vector", lambda e, ch=ch, bank=bank: e.tensor_scalar(
                        out=zbuf[:, ch * 256:(ch + 1) * 256], in0=bank[:, 0:256], scalar1=0.0,
                        scalar2=(None if ch == 0 else m_run[:]), op0=ALU.add, op1=ALU.max, accum_out=m_run[:]),
                        waits=[("tensor", t_z)] + ([("gpsimd", m_guard)] if ch == 0 else []), inc=True)
            v_z = p.val("vector")
            g0 = p.op("gpsimd", lambda e: e.tensor_scalar(out=ntau[:], in0=m_run[:], scalar1=-1.0, scalar2=1.1,
                                                          op0=ALU.mult, op1=ALU.add),
                      waits=[("vector", v_z)], inc=True)
            g_nt = p.op("gpsimd", lambda e: e.tensor_scalar_mul(out=tau[:], in0=ntau[:], scalar1=-1.0), inc=True)
            m_guard = g_nt
            a_last = 0
            half_w = min(W, tf // 2) if tf >= 2 * D else W
            pieces = [(0, min(W, half_w))]
            if W > half_w:
                pieces.append((half_w, W))
            for it in range(5):
                for pi, (w0, w1) in enumerate(pieces):
                    a_f = p.op("scalar", lambda e, w0=w0, w1=w1, pi=pi: e.activation(
                        out=nscr[:, 0:w1 - w0], in_=zbuf[:, w0:w1], func=AF.Relu,
                        bias=ntau[:], accum_out=(fcol[:] if pi == 0 else fcol2[:])),
                        waits=[("gpsimd", g_nt)], inc=True)
                    v_c = p.op("vector", lambda e, w0=w0, w1=w1, pi=pi: e.tensor_scalar(
                        out=nscr2[:, w0:w1], in0=zbuf[:, w0:w1],
                        scalar1=tau[:], scalar2=0.0,
                        op0=ALU.is_gt, op1=ALU.add, accum_out=(ccol[:] if pi == 0 else ccol2[:])),
                        waits=[("gpsimd", g_nt)], inc=True)
                g_cmb = 0
                if len(pieces) > 1:
                    p.op("gpsimd", lambda e: e.tensor_tensor(out=fcol[:], in0=fcol[:], in1=fcol2[:], op=ALU.add),
                         waits=[("scalar", a_f), ("vector", v_c)], inc=True)
                    p.op("gpsimd", lambda e: e.tensor_tensor(out=ccol[:], in0=ccol[:], in1=ccol2[:], op=ALU.add),
                         inc=True)
                g_mx = p.op("gpsimd", lambda e: e.tensor_scalar_max(out=ccol[:], in0=ccol[:], scalar1=1.0),
                            waits=[("scalar", a_f), ("vector", v_c)], inc=True)
                v_r = p.op("vector", lambda e: e.reciprocal(out=rcol[:], in_=ccol[:]),
                           waits=[("gpsimd", g_mx)], inc=True)
                p.op("gpsimd", lambda e: e.tensor_scalar_add(out=fcol[:], in0=fcol[:], scalar1=-1.0),
                     waits=[("vector", v_r)], inc=True)
                p.op("gpsimd", lambda e: e.tensor_tensor(out=fcol[:], in0=fcol[:], in1=rcol[:], op=ALU.mult), inc=True)
                p.op("gpsimd", lambda e: e.tensor_tensor(out=tau[:], in0=tau[:], in1=fcol[:], op=ALU.add), inc=True)
                g_nt = p.op("gpsimd", lambda e: e.tensor_scalar_mul(out=ntau[:], in0=tau[:], scalar1=-1.0), inc=True)
                a_last = a_f
            g_fin = p.val("gpsimd")
            zevac_guard = a_last
            zevac_guard_v = p.val("vector")
            t_r = p.op("tensor", lambda e: e.matmul(ps_r[:], ntau[:], ident[:], start=True, stop=True),
                       waits=[("gpsimd", g_fin), ("scalar", ntrow_ev)], inc=True)
            ntrow_ev = p.op("scalar", lambda e: e.activation(out=ntrow[:], in_=ps_r[:], func=AF.Identity),
                            waits=[("tensor", t_r)], inc=True)
            for kb in range(nkb):
                bank = ps_zT[kb % 2]
                wl = [("scalar", max(ab_relu[kb % 2], ntrow_ev, qev[kb % 2]))]
                p.op("tensor", lambda e, kb=kb, h=h, bank=bank, qs=qs: e.matmul(
                    bank[:], kbf[(h % 2) * 64:(h % 2) * 64 + 64, (h // 2) * tf + kb * 128:(h // 2) * tf + (kb + 1) * 128],
                    qs, start=True, stop=False), waits=wl)
                t_zt = p.op("tensor", lambda e, bank=bank: e.matmul(
                    bank[:], fones[0:1, :], ntrow[0:1, :], start=False, stop=True), inc=True)
                if kb >= nkb - 2:
                    moff = (kb - (nkb - 2)) * 128
                    v_m = p.op("vector", lambda e, bank=bank, moff=moff: e.tensor_tensor(
                        out=ascr[:, 0:128], in0=bank[:], in1=mqTs[:, moff:moff + 128], op=ALU.add),
                        waits=[("tensor", t_zt)], inc=True)
                    ab_relu[kb % 2] = p.op("scalar", lambda e, kb=kb: e.activation(
                        out=abuf[kb % 2][:], in_=ascr[:, 0:128], func=AF.Relu), waits=[("vector", v_m)], inc=True)
                else:
                    ab_relu[kb % 2] = p.op("scalar", lambda e, kb=kb, bank=bank: e.activation(
                        out=abuf[kb % 2][:], in_=bank[:], func=AF.Relu), waits=[("tensor", t_zt)], inc=True)
                p.op("tensor", lambda e, kb=kb, h=h, nkb=nkb: e.matmul(
                    ps_o[:], vbf[:, kb * D + h * 64: kb * D + (h + 1) * 64], abuf[kb % 2][:],
                    start=(kb == 0), stop=(kb == nkb - 1)),
                     waits=[("scalar", ab_relu[kb % 2])] + ([("scalar", last_o_ev)] if kb == 0 else []),
                     inc=(kb == nkb - 1))
            t_av = p.val("tensor")
            last_o_ev = p.op("scalar", lambda e, h=h: e.activation(
                out=outf[(h % 2) * 64:(h % 2) * 64 + 64, (h // 2) * 128:(h // 2) * 128 + 128],
                in_=ps_o[:], func=AF.Identity), waits=[("tensor", t_av)], inc=True)
        outf_ev = last_o_ev
        for ft in range(2):
            bank = ps_z[ft]
            for hk in range(HCH):
                wl = []
                if hk == 0:
                    wl = [("scalar", outf_ev), ("vector", tr2_evac[0]), ("vector", tr2_evac[1]),
                          ("dma_wproj", wv_p)]
                p.op("tensor", lambda e, ft=ft, hk=hk, bank=bank: e.matmul(
                    bank[:], outf[:, hk * 128:(hk + 1) * 128],
                    wslice(hk, D + ft * 512, D + (ft + 1) * 512), start=(hk == 0), stop=False), waits=wl)
            t_op = p.op("tensor", lambda e, ft=ft, bank=bank: e.matmul(
                bank[:], bones[0:1, :], bprow[0:1, ft * 512:(ft + 1) * 512], start=False, stop=True), inc=True)
            prev_outp = p.op("vector", lambda e, ft=ft, bank=bank: e.tensor_tensor(
                out=outt[:, ft * 512:(ft + 1) * 512], in0=bank[:], in1=xt[:, ft * 512:(ft + 1) * 512], op=ALU.add),
                waits=[("tensor", t_op), ("dma_stout", prev_store)], inc=True)
        prev_store = p.op("sync", lambda e, i=i: e.dma_start(out=xo[i * 128:(i + 1) * 128, :], in_=outt[:]),
                          dma="stout", waits=[("vector", prev_outp)])


def prep_l0attn_weights(I):
    b_attn = I["L0_attn_b_attn"].astype(np.float32)
    b_qk = np.concatenate([
        b_attn[0:1024].reshape(8, 128).T,
        b_attn[1024:2048].reshape(8, 128).T], axis=1)
    return {
        "w_attn": np.ascontiguousarray(I["L0_attn_w_attn"]),
        "b_qk": np.ascontiguousarray(b_qk),
        "b_v": np.ascontiguousarray(b_attn[2048:3072].reshape(1, 1024)),
        "w_proj": np.ascontiguousarray(I["L0_attn_w_proj"]),
        "b_proj": np.ascontiguousarray(I["L0_attn_b_proj"].reshape(1, 1024)),
    }


def l0attn_masks(par):
    q = np.arange(128)
    tri = np.where(q[:, None] >= np.arange(128)[None, :], 0.0, -1e9).astype(np.float32)
    full = np.full((128, 128), -1e9, np.float32)
    zero = np.zeros((128, 128), np.float32)
    if par == 0:
        mq = np.concatenate([tri, full], axis=1)
        mqT = np.concatenate([tri.T, full], axis=1)
    else:
        mq = np.concatenate([zero, tri], axis=1)
        mqT = np.concatenate([zero, tri.T], axis=1)
    return np.ascontiguousarray(mq), np.ascontiguousarray(mqT)


_L0ATTN_NC = None


def run_l0attn(x, I):
    global _L0ATTN_NC
    if _L0ATTN_NC is None:
        _L0ATTN_NC = build_l0attn()
    w = prep_l0attn_weights(I)
    shards = shard_tokens(x)
    in_maps = []
    for c in range(NCORE):
        b, par = c // 2, c % 2
        mqa, mqTa = l0attn_masks(par)
        in_maps.append(dict(w, xf=np.ascontiguousarray(x[b].astype(np.float32)),
                            xq=shards[c], mq=mqa, mqT=mqTa))
    res = run_bass_kernel_spmd(_L0ATTN_NC, in_maps, core_ids=list(range(NCORE)))
    return unshard_tokens([res.results[c]["xo"] for c in range(NCORE)])



def emit_l1attn(p, xf, filt, gamma, beta, xo, tf=T):
    HCH = 8
    nft = tf // 128
    n1, n2 = tf // 2, tf // 4

    xt = p.sbuf("xt", [128, D], F32)
    yt = p.sbuf("yt", [128, D], F32)
    cols = p.sbuf("cols", [128, 16], F32)
    ln_bufs = {"ln_sum": cols[:, 0:1], "ln_ss": cols[:, 1:2], "ln_t1": cols[:, 2:3], "ln_t2": cols[:, 3:4]}
    ln_scr = p.sbuf("ln_scr", [128, D], F32)
    ident = p.sbuf("ident", [128, 128], F32)
    identb = p.sbuf("identb", [128, 128], BF16)
    fones = p.sbuf("fones", [1, 128], F32)
    filts = p.sbuf("filts", [128, 128], F32)
    grow = p.sbuf("grow", [1, D], F32)
    brow = p.sbuf("brow", [1, D], F32)
    gbc = p.sbuf("gbc", [128, D], F32)
    bbc = p.sbuf("bbc", [128, D], F32)
    ybf = p.sbuf("ybf", [128, HCH * tf], BF16)
    ypad = p.sbuf("ypad", [128, tf + 4], F32)
    l1pad = p.sbuf("l1pad", [128, n1 + 4], F32)
    h1 = p.sbuf("h1", [128, n1], F32)
    l2 = p.sbuf("l2", [128, n2], F32)
    h2 = p.sbuf("h2", [128, n2], F32)
    qef = p.sbuf("qef", [128, n2], F32)
    qet = p.sbuf("qet", [128, n2], F32)
    vt = p.sbuf("vt", [128, n2], F32)
    kvz = p.sbuf("kvz", [128, 65], F32)
    lowg = p.sbuf("lowg", [128, n2], F32)
    reppad = p.sbuf("reppad", [128, tf + 4], F32)
    low1pad = p.sbuf("low1pad", [128, n1 + 4], F32)
    low0 = p.sbuf("low0", [128, tf], F32)
    den = p.sbuf("den", [1, n2], F32)
    lowt = p.sbuf("lowt", [128, D], F32)
    ps_a = [p.psum(f"ps_a{i}", [128, 512], F32) for i in range(2)]
    ps_kv = p.psum("ps_kv", [64, 64], F32)
    ps_z = p.psum("ps_z", [64, 1], F32)
    ps_n = p.psum("ps_n", [65, 512], F32)
    ps_t = p.psum("ps_t", [128, 128], F32)
    ocol = cols[:, 15:16]

    g0 = p.op("gpsimd", lambda e: e.memset(ident[:], 0.0), inc=True)
    p.op("gpsimd", lambda e: e.affine_select(out=ident[:], in_=ident[:], compare_op=ALU.not_equal,
                                             fill=1.0, base=0, pattern=[[-1, 128]], channel_multiplier=1),
         inc=True, waits=[("gpsimd", g0)])
    p.op("gpsimd", lambda e: e.memset(fones[:], 1.0), inc=True)
    gi = p.op("gpsimd", lambda e: e.tensor_copy(out=identb[:], in_=ident[:]), inc=True)
    p.op("gpsimd", lambda e: e.memset(ocol[:], 1.0), inc=True)
    p.op("gpsimd", lambda e: e.memset(ypad[:], 0.0), inc=True)
    p.op("gpsimd", lambda e: e.memset(l1pad[:], 0.0), inc=True)
    p.op("gpsimd", lambda e: e.memset(reppad[:], 0.0), inc=True)
    g_const = p.op("gpsimd", lambda e: e.memset(low1pad[:], 0.0), inc=True)
    ld_c = p.op("sync", lambda e: e.dma_start(out=filts[:], in_=filt), dma="const")
    p.op("sync", lambda e: e.dma_start(out=grow[:], in_=gamma), dma="const")
    ld_c = p.op("sync", lambda e: e.dma_start(out=brow[:], in_=beta), dma="const")
    for half in range(2):
        bank = ps_a[half]
        t_g = p.op("tensor", lambda e, half=half, bank=bank: e.matmul(
            bank[:], fones[:], grow[0:1, half * 512:(half + 1) * 512], start=True, stop=True),
            waits=[("dma_const", ld_c), ("gpsimd", g_const)], inc=True)
        v_g = p.op("vector", lambda e, half=half, bank=bank: e.tensor_copy(out=gbc[:, half * 512:(half + 1) * 512], in_=bank[:]),
                   waits=[("tensor", t_g)], inc=True)
        t_b = p.op("tensor", lambda e, half=half, bank=bank: e.matmul(
            bank[:], fones[:], brow[0:1, half * 512:(half + 1) * 512], start=True, stop=True),
            waits=[("vector", v_g)], inc=True)
        v_b = p.op("vector", lambda e, half=half, bank=bank: e.tensor_copy(out=bbc[:, half * 512:(half + 1) * 512], in_=bank[:]),
                   waits=[("tensor", t_b)], inc=True)
    const_v = p.val("vector")

    prev_ln = 0
    tr_evac = [0, 0]
    for t in range(nft):
        ld = p.op("sync", lambda e, t=t: e.dma_start(out=xt[:], in_=xf[t * 128:(t + 1) * 128, :]),
                  dma="ldx", waits=[("scalar", prev_ln)])
        p.op("vector", lambda e: e.nop(), waits=[("dma_ldx", ld)])
        p.op("scalar", lambda e: e.nop(), waits=[("dma_ldx", ld)])
        prev_ln = emit_ln_tile(p, ln_bufs, xt[:], yt[:], scr=ln_scr[:])
        for hk in range(HCH):
            pt = ps_a[hk % 2][:, 0:128]
            t_tr = p.op("tensor", lambda e, pt=pt, hk=hk: e.transpose(pt, yt[:, hk * 128:(hk + 1) * 128], ident[:]),
                        waits=[("scalar", prev_ln), ("vector", max(tr_evac[hk % 2], const_v))], inc=True)
            tr_evac[hk % 2] = p.op("vector", lambda e, pt=pt, hk=hk, t=t: e.tensor_copy(
                out=ybf[:, hk * tf + t * 128: hk * tf + (t + 1) * 128], in_=pt), waits=[("tensor", t_tr)], inc=True)
    passA_v = p.val("vector")

    def fcol_ap(f, g, k):
        return filts[:, f * 32 + g * 4 + k: f * 32 + g * 4 + k + 1]

    def conv_strided(dst_ap, src, n_out, f, g, first_waits=()):
        for k in range(4):
            if k == 0:
                p.op("vector", lambda e, k=k: e.tensor_scalar(
                    out=dst_ap, in0=src[:, k: k + 2 * n_out: 2], scalar1=fcol_ap(f, g, k), scalar2=None,
                    op0=ALU.mult), waits=list(first_waits), inc=True)
            else:
                p.op("vector", lambda e, k=k: e.scalar_tensor_tensor(
                    out=dst_ap, in0=src[:, k: k + 2 * n_out: 2], scalar=fcol_ap(f, g, k), in1=dst_ap,
                    op0=ALU.mult, op1=ALU.add), inc=True)
        return p.val("vector")

    def conv_unit(dst_ap, src, n_out, f, g, first_waits=(), accumulate=False):
        for k in range(4):
            if k == 0 and not accumulate:
                p.op("vector", lambda e, k=k: e.tensor_scalar(
                    out=dst_ap, in0=src[:, k: k + n_out], scalar1=fcol_ap(f, g, k), scalar2=None,
                    op0=ALU.mult), waits=list(first_waits), inc=True)
            else:
                p.op("vector", lambda e, k=k: e.scalar_tensor_tensor(
                    out=dst_ap, in0=src[:, k: k + n_out], scalar=fcol_ap(f, g, k), in1=dst_ap,
                    op0=ALU.mult, op1=ALU.add), inc=True)
        return p.val("vector")

    low0_done = []
    ps_n_free = 0
    for g in range(HCH):
        v_y = p.op("vector", lambda e, g=g: e.tensor_copy(out=ypad[:, 1: 1 + tf], in_=ybf[:, g * tf:(g + 1) * tf]),
                   waits=[("vector", passA_v), ("dma_const", ld_c)], inc=True)
        conv_strided(l1pad[:, 1: 1 + n1], ypad, n1, 0, g)
        v_c0 = conv_strided(h1[:, 0:n1], ypad, n1, 1, g)
        conv_strided(l2[:, 0:n2], l1pad, n2, 0, g)
        v_c1 = conv_strided(h2[:, 0:n2], l1pad, n2, 1, g)
        v_m = p.op("vector", lambda e: e.tensor_scalar_min(out=qef[:], in0=l2[:], scalar1=0.0), inc=True)
        a_e = p.op("scalar", lambda e: e.activation(out=qef[:], in_=qef[:], func=AF.Exp),
                   waits=[("vector", v_m)], inc=True)
        v_r = p.op("vector", lambda e: e.tensor_scalar_max(out=ln_scr[:, 0:n2], in0=l2[:], scalar1=0.0),
                   waits=[("scalar", a_e)], inc=True)
        v_q = p.op("vector", lambda e: e.tensor_add(out=qef[:], in0=qef[:], in1=ln_scr[:, 0:n2]), inc=True)
        tr_ev = 0
        for chk in range(n2 // 128):
            pt = ps_a[chk % 2][:, 0:128]
            t_t1 = p.op("tensor", lambda e, chk=chk, pt=pt: e.transpose(pt, qef[:, chk * 128:(chk + 1) * 128], ident[:]),
                        waits=[("vector", v_q), ("vector", tr_ev)], inc=True)
            tr_ev = p.op("vector", lambda e, chk=chk, pt=pt: e.tensor_copy(out=qet[:, chk * 128:(chk + 1) * 128], in_=pt),
                         waits=[("tensor", t_t1)], inc=True)
            t_t2 = p.op("tensor", lambda e, chk=chk, pt=pt: e.transpose(pt, l2[:, chk * 128:(chk + 1) * 128], ident[:]),
                        waits=[("vector", tr_ev)], inc=True)
            tr_ev = p.op("vector", lambda e, chk=chk, pt=pt: e.tensor_copy(out=vt[:, chk * 128:(chk + 1) * 128], in_=pt),
                         waits=[("tensor", t_t2)], inc=True)
        for h in range(2):
            t_kv = t_z = 0
            for chk in range(n2 // 128):
                wl = [("vector", tr_ev)] if chk == 0 else []
                last = chk == n2 // 128 - 1
                t_kv = p.op("tensor", lambda e, chk=chk, h=h, last=last: e.matmul(
                    ps_kv[:], qet[0:128, chk * 128 + h * 64: chk * 128 + h * 64 + 64],
                    vt[0:128, chk * 128 + h * 64: chk * 128 + h * 64 + 64],
                    start=(chk == 0), stop=last), waits=wl, inc=last)
                t_z = p.op("tensor", lambda e, chk=chk, h=h, last=last: e.matmul(
                    ps_z[:], qet[0:128, chk * 128 + h * 64: chk * 128 + h * 64 + 64], ocol[:],
                    start=(chk == 0), stop=last), inc=last)
            v_kv = p.op("vector", lambda e, h=h: e.tensor_copy(out=kvz[h * 64:h * 64 + 64, 0:64], in_=ps_kv[:]),
                        waits=[("tensor", t_kv)], inc=True)
            v_kz = p.op("vector", lambda e, h=h: e.tensor_copy(out=kvz[h * 64:h * 64 + 64, 64:65], in_=ps_z[:]),
                        waits=[("tensor", t_z)], inc=True)
            for nh in range(n2 // 512):
                t_n = p.op("tensor", lambda e, nh=nh, h=h: e.matmul(
                    ps_n[:], kvz[h * 64:h * 64 + 64, 0:65], qef[h * 64:(h + 1) * 64, nh * 512:(nh + 1) * 512],
                    start=True, stop=True), waits=[("vector", v_kz), ("vector", ps_n_free)], inc=True)
                v_d = p.op("vector", lambda e, nh=nh: e.tensor_scalar_add(out=den[0:1, nh * 512:(nh + 1) * 512],
                                                                          in0=ps_n[64:65, :], scalar1=1e-6),
                           waits=[("tensor", t_n)], inc=True)
                v_dr = p.op("vector", lambda e, nh=nh: e.reciprocal(out=den[0:1, nh * 512:(nh + 1) * 512],
                                                                    in_=den[0:1, nh * 512:(nh + 1) * 512]),
                            waits=[("vector", v_d)], inc=True)
                t_bc = p.op("tensor", lambda e, nh=nh: e.matmul(ps_a[0][:, 0:512], fones[:],
                                                                den[0:1, nh * 512:(nh + 1) * 512], start=True, stop=True),
                            waits=[("vector", v_dr)], inc=True)
                v_rcb = p.op("vector", lambda e: e.tensor_copy(out=ln_scr[0:64, 0:512], in_=ps_a[0][0:64, 0:512]),
                             waits=[("tensor", t_bc)], inc=True)
                ps_n_free = v_low = p.op("vector", lambda e, nh=nh, h=h: e.tensor_tensor(
                    out=lowg[h * 64:(h + 1) * 64, nh * 512:(nh + 1) * 512], in0=ps_n[0:64, :],
                    in1=ln_scr[0:64, 0:512], op=ALU.mult), waits=[("vector", v_rcb)], inc=True)
        v_attn = p.val("vector")
        p.op("vector", lambda e: e.memset(reppad[:, 1 + n1: 4 + n1], 0.0), waits=[("vector", v_attn)], inc=True)
        v_rp1 = p.op("vector", lambda e: e.tensor_copy(out=reppad[:, 1: 1 + n1: 2], in_=lowg[:, 0:n2]),
                     inc=True)
        v_rp2 = p.op("vector", lambda e: e.tensor_copy(out=reppad[:, 2: 2 + n1: 2], in_=lowg[:, 0:n2]), inc=True)
        conv_unit(low1pad[:, 1: 1 + n1], reppad, n1, 2, g, first_waits=[("vector", v_rp2)])
        v_rp3 = p.op("vector", lambda e: e.tensor_copy(out=reppad[:, 1: 1 + n1: 2], in_=h2[:, 0:n2]), inc=True)
        v_rp4 = p.op("vector", lambda e: e.tensor_copy(out=reppad[:, 2: 2 + n1: 2], in_=h2[:, 0:n2]), inc=True)
        v_l1 = conv_unit(low1pad[:, 1: 1 + n1], reppad, n1, 3, g, first_waits=[("vector", v_rp4)], accumulate=True)
        v_rp5 = p.op("vector", lambda e: e.tensor_copy(out=reppad[:, 1: 1 + tf: 2], in_=low1pad[:, 1: 1 + n1]),
                     waits=[("vector", v_l1)], inc=True)
        v_rp6 = p.op("vector", lambda e: e.tensor_copy(out=reppad[:, 2: 2 + tf: 2], in_=low1pad[:, 1: 1 + n1]), inc=True)
        conv_unit(low0[:, 0:tf], reppad, tf, 2, g, first_waits=[("vector", v_rp6)])
        v_rp7 = p.op("vector", lambda e: e.tensor_copy(out=reppad[:, 1: 1 + tf: 2], in_=h1[:, 0:n1]), inc=True)
        v_rp8 = p.op("vector", lambda e: e.tensor_copy(out=reppad[:, 2: 2 + tf: 2], in_=h1[:, 0:n1]), inc=True)
        v_l0 = conv_unit(low0[:, 0:tf], reppad, tf, 3, g, first_waits=[("vector", v_rp8)], accumulate=True)
        v_ov = p.op("vector", lambda e, g=g: e.tensor_copy(out=ybf[:, g * tf:(g + 1) * tf], in_=low0[:, 0:tf]),
                    waits=[("vector", v_l0)], inc=True)
        low0_done.append(v_ov)
    passB_v = p.val("vector")

    prev_st = 0
    tr_ev2 = [0, 0]
    prev_fin = passB_v
    for t in range(nft):
        ld = p.op("sync", lambda e, t=t: e.dma_start(out=xt[:], in_=xf[t * 128:(t + 1) * 128, :]),
                  dma="ldx", waits=[("scalar", prev_ln), ("vector", prev_fin), ("dma_stout", prev_st)])
        p.op("vector", lambda e: e.nop(), waits=[("dma_ldx", ld)])
        p.op("scalar", lambda e: e.nop(), waits=[("dma_ldx", ld)])
        prev_ln = emit_ln_tile(p, ln_bufs, xt[:], yt[:], scr=ln_scr[:])
        for hk in range(HCH):
            t_tr = p.op("tensor", lambda e, hk=hk, t=t: e.matmul(
                ps_t[:], ybf[:, hk * tf + t * 128: hk * tf + (t + 1) * 128], identb[:],
                start=True, stop=True),
                waits=[("vector", max(tr_ev2[hk % 2], passB_v)), ("scalar", prev_ln),
                       ("vector", tr_ev2[(hk + 1) % 2])], inc=True)
            tr_ev2[hk % 2] = p.op("vector", lambda e, hk=hk: e.tensor_copy(
                out=lowt[:, hk * 128:(hk + 1) * 128], in_=ps_t[:]), waits=[("tensor", t_tr)], inc=True)
        v_xl = p.op("vector", lambda e: e.tensor_add(out=yt[:], in0=yt[:], in1=lowt[:]),
                    waits=[("vector", tr_ev2[0]), ("vector", tr_ev2[1])], inc=True)
        ln2 = emit_ln_tile(p, ln_bufs, yt[:], lowt[:], scr=ln_scr[:], pre_v=v_xl)
        v_f1 = p.op("vector", lambda e: e.tensor_mul(out=lowt[:], in0=lowt[:], in1=gbc[:]),
                    waits=[("scalar", ln2)], inc=True)
        v_f2 = p.op("vector", lambda e: e.tensor_add(out=lowt[:], in0=lowt[:], in1=bbc[:]), inc=True)
        prev_fin = p.op("vector", lambda e: e.tensor_add(out=yt[:], in0=lowt[:], in1=xt[:]), inc=True)
        prev_st = p.op("sync", lambda e, t=t: e.dma_start(out=xo[t * 128:(t + 1) * 128, :], in_=yt[:]),
                       dma="stout", waits=[("vector", prev_fin)])
    p.op("sync", lambda e: e.nop(), waits=[("dma_stout", p.esem[p.dsem("stout")].n)])


def build_l1attn(tf=T):
    nc = bass.Bass("TRN2", target_bir_lowering=False, debug=False, name="l1attn")
    xf = nc.dram_tensor("xf", [tf, D], F32, kind="ExternalInput").ap()
    filt = nc.dram_tensor("filt", [128, 128], F32, kind="ExternalInput").ap()
    gamma = nc.dram_tensor("gamma", [1, D], F32, kind="ExternalInput").ap()
    beta = nc.dram_tensor("beta", [1, D], F32, kind="ExternalInput").ap()
    xo = nc.dram_tensor("xo", [tf, D], F32, kind="ExternalOutput").ap()

    p = Prog(nc)
    emit_l1attn(p, xf, filt, gamma, beta, xo, tf)
    p.finish()
    return nc


def prep_l1attn_weights(I):
    fs = []
    for nm in ("L1_attn_h_filter", "L1_attn_g_filter", "L1_attn_h_synth", "L1_attn_g_synth"):
        f = I[nm].astype(np.float32)
        fs.append(f.reshape(4, 8, 128).transpose(2, 1, 0).reshape(128, 32))
    filt = np.ascontiguousarray(np.concatenate(fs, axis=1))
    return {
        "filt": filt,
        "gamma": np.ascontiguousarray(I["L1_attn_norm_gamma"].reshape(1, D)),
        "beta": np.ascontiguousarray(I["L1_attn_norm_beta"].reshape(1, D)),
    }


_L1ATTN_NC = None


def run_l1attn(x, I):
    global _L1ATTN_NC
    if _L1ATTN_NC is None:
        _L1ATTN_NC = build_l1attn()
    w = prep_l1attn_weights(I)
    in_maps = [dict(w, xf=np.ascontiguousarray(x[c // 2].astype(np.float32))) for c in range(NCORE)]
    res = run_bass_kernel_spmd(_L1ATTN_NC, in_maps, core_ids=list(range(NCORE)))
    out = np.empty((B, T, D), np.float32)
    for b in range(B):
        out[b] = res.results[2 * b]["xo"]
    return out


def emit_l1ffn(p, xf, wqkv, lncol, rt4, qcn, wq_in, wkv_in, wout_in, wlo_in, nks, nvs,
               rotm, cossin, wmaskin, mlocQ_in, mlocKT_in, xo,
               s_hbm, yf_hbm, yr_hbm, ql_hbm, kl_hbm, vl_hbm, qg_hbm, kg_hbm, vg_hbm, tf=T):
    import math as _math
    HCH = 8
    nft = tf // 128
    NW = tf // 512
    NWIN = NW - 1
    NCTX = tf - 512
    NKB = NCTX // 128
    K_Q, K_KV = 256, 1024
    logk_q = float(_math.log(min(256 * 9.0 / 8.0, 512.0)))
    logk_kv = float(_math.log(min(1024 * 9.0 / 8.0, float(NCTX))))
    EPSL = [max(4.0 * 0.7 ** i, 0.03) for i in range(20)]
    FE = 0.03

    xt = p.sbuf("xt", [128, D], F32)
    yt = p.sbuf("yt", [128, D], F32)
    yr = p.sbuf("yr", [128, D], F32)
    cols = p.sbuf("cols", [128, 32], F32)
    ln_bufs = {"ln_sum": cols[:, 0:1], "ln_ss": cols[:, 1:2], "ln_t1": cols[:, 2:3], "ln_t2": cols[:, 3:4]}
    fac = cols[:, 4:5]
    ifac = cols[:, 5:6]
    ccol = cols[:, 6:7]
    ocol = cols[:, 8:9]
    ln_scr = p.sbuf("ln_scr", [128, D], F32)
    ident = p.sbuf("ident", [128, 128], F32)
    identb = p.sbuf("identb", [128, 128], BF16)
    fones = p.sbuf("fones", [1, 128], F32)
    bones = p.sbuf("bones", [1, 128], BF16)
    yf32 = p.sbuf("yf32", [128, HCH * 128], F32)
    lnc = p.sbuf("lnc", [128, 16], F32)
    lnbb = p.sbuf("lnbb", [128, 8], BF16)
    rts = p.sbuf("rts", [128, 32], F32)
    qcns = p.sbuf("qcns", [128, 32], F32)
    nkss = p.sbuf("nkss", [128, 32], BF16)
    rotms = p.sbuf("rotms", [128, 128], BF16)
    selbT = p.sbuf("selbT", [128, NKB * 16], F32)
    qselT = p.sbuf("qselT", [128, 4 * 16], F32)
    accw = p.sbuf("accw", [128, 4 * 1024], F32)
    qkvb = p.sbuf("qkvb", [128, 24], F32)
    nmrow = p.sbuf("nmrow", [1, 512], F32)
    enull = p.sbuf("enull", [1, 512], F32)
    denr = p.sbuf("denr", [1, 512], F32)
    mvals = p.sbuf("mvals", [128, 16 * 4], F32)
    mlocQs = p.sbuf("mlocQs", [128, 11 * 128], F32)
    mlocKTs = p.sbuf("mlocKTs", [128, 11 * 128], F32)
    bocol = p.sbuf("bocol", [128, 1], BF16)
    AF32 = p.sbuf("AF32", [128, 14 * 1024], F32)
    AB16 = p.sbuf("AB16", [128, 46 * 1024], BF16)

    ps_t = [p.psum(f"ps_t{i}", [128, 512], F32) for i in range(2)]
    ps_u = [p.psum(f"ps_u{i}", [128, 512], F32) for i in range(2)]
    ps_o = p.psum("ps_o", [64, 512], F32)
    ps_d = p.psum("ps_d", [1, 512], F32)
    ps_r = p.psum("ps_r", [16, 512], F32)
    ps_s = p.psum("ps_s", [128, 512], F32)

    def barrier(extra_dma=()):
        vals = [(en, p.val(en)) for en in ("sync", "tensor", "vector", "scalar", "gpsimd")]
        dvals = [(p.dsem(nm), p.esem[p.dsem(nm)].n) for nm in extra_dma]
        dvals = [(k, v) for k, v in dvals]
        for en in ("sync", "tensor", "vector", "scalar", "gpsimd"):
            wl = [(e2, v) for e2, v in vals if e2 != en] + [(k, v) for k, v in dvals]
            p.op(en, lambda e: e.nop(), waits=wl, inc=True, chain=True)

    g0 = p.op("gpsimd", lambda e: e.memset(ident[:], 0.0), inc=True)
    p.op("gpsimd", lambda e: e.affine_select(out=ident[:], in_=ident[:], compare_op=ALU.not_equal,
                                             fill=1.0, base=0, pattern=[[-1, 128]], channel_multiplier=1),
         inc=True, waits=[("gpsimd", g0)])
    p.op("gpsimd", lambda e: e.tensor_copy(out=identb[:], in_=ident[:]), inc=True)
    p.op("gpsimd", lambda e: e.memset(fones[:], 1.0), inc=True)
    p.op("gpsimd", lambda e: e.memset(bones[:], 1.0), inc=True)
    p.op("gpsimd", lambda e: e.memset(ocol[:], 1.0), inc=True)
    p.op("gpsimd", lambda e: e.memset(bocol[:], 1.0), inc=True)
    g_const = p.op("gpsimd", lambda e: e.memset(cols[:, 20:21], -20.0), inc=True)
    ldc = 0
    for nm, dst in ((lncol, lnc), (rt4, rts), (qcn, qcns),
                    (mlocQ_in, mlocQs), (mlocKT_in, mlocKTs)):
        ldc = p.op("sync", lambda e, nm=nm, dst=dst: e.dma_start(out=dst[:], in_=nm), dma="const")
    p.op("gpsimd", lambda e: e.dma_start(out=rotms[:], in_=rotm), dma="gconst")
    gldc = p.op("gpsimd", lambda e: e.dma_start(out=nkss[:], in_=nks), dma="gconst")
    p.op("gpsimd", lambda e: e.tensor_copy(out=lnbb[:], in_=lnc[:, 8:16]),
         waits=[("dma_const", ldc)], inc=True)

    pbuf = AB16[:, 0:2048]
    srow = AF32[:, 0:4096]
    prev_ln = 0
    tr_evac = [0, 0]
    st_g = 0
    st_g2 = 0
    prev_srow = 0
    for t in range(nft):
        ld = p.op("sync", lambda e, t=t: e.dma_start(out=xt[:], in_=xf[t * 128:(t + 1) * 128, :]),
                  dma="ldx", waits=[("scalar", prev_ln)])
        p.op("vector", lambda e: e.nop(), waits=[("dma_ldx", ld)])
        p.op("scalar", lambda e: e.nop(), waits=[("dma_ldx", ld)])
        prev_ln = emit_ln_tile(p, ln_bufs, xt[:], yt[:], scr=ln_scr[:])
        v1 = p.op("vector", lambda e: e.tensor_mul(out=fac[:], in0=ln_bufs["ln_t1"][:], in1=ln_bufs["ln_t1"][:]),
                  waits=[("scalar", prev_ln)], inc=True)
        v2 = p.op("vector", lambda e: e.tensor_scalar(out=fac[:], in0=fac[:], scalar1=-LN_EPS * D, scalar2=float(D),
                                                      op0=ALU.mult, op1=ALU.add), inc=True)
        a_s = p.op("scalar", lambda e: e.activation(out=fac[:], in_=fac[:], func=AF.Sqrt), waits=[("vector", v2)], inc=True)
        p.op("vector", lambda e: e.tensor_scalar_max(out=fac[:], in0=fac[:], scalar1=1e-12),
             waits=[("scalar", a_s)], inc=True)
        p.op("vector", lambda e: e.tensor_scalar_mul(out=ifac[:], in0=fac[:], scalar1=1.0 / 32.0), inc=True)
        p.op("vector", lambda e: e.reciprocal(out=fac[:], in_=fac[:]), inc=True)
        v6 = p.op("vector", lambda e: e.tensor_scalar_mul(out=fac[:], in0=fac[:], scalar1=32.0), inc=True)
        a_yr = p.op("scalar", lambda e: e.activation(out=yr[:], in_=yt[:], func=AF.Identity, scale=fac[:]),
                    waits=[("vector", v6)], inc=True)
        for hk in range(HCH):
            pt = ps_t[hk % 2][:, 0:128]
            t_tr = p.op("tensor", lambda e, hk=hk, pt=pt: e.transpose(pt, yr[:, hk * 128:(hk + 1) * 128], ident[:]),
                        waits=[("scalar", a_yr), ("vector", tr_evac[hk % 2]), ("gpsimd", g_const)], inc=True)
            p.op("vector", lambda e, hk=hk, pt=pt: e.tensor_copy(out=yf32[:, hk * 128:(hk + 1) * 128], in_=pt),
                 waits=[("tensor", t_tr), ("dma_styr", st_g)], inc=True)
            tr_evac[hk % 2] = p.op("vector", lambda e, hk=hk: e.tensor_copy(
                out=pbuf[:, hk * 128:(hk + 1) * 128], in_=yf32[:, hk * 128:(hk + 1) * 128]), inc=True)
        v_tr = p.val("vector")
        st_g = p.op("gpsimd", lambda e, t=t: e.dma_start(
            out=yr_hbm.ap().rearrange("p (hc tt) -> p hc tt", hc=8, tt=tf)[:, :, t * 128:(t + 1) * 128],
            in_=pbuf[:, 0:1024].rearrange("p (hc c) -> p hc c", hc=8, c=128)), dma="styr",
            waits=[("vector", v_tr)])
        for hk in range(HCH):
            p.op("tensor", lambda e, hk=hk: e.matmul(ps_s[:, 0:4],
                                                     yf32[:, hk * 128:(hk + 1) * 128], rts[:, hk * 4:(hk + 1) * 4],
                                                     start=(hk == 0), stop=(hk == HCH - 1)),
                 waits=([("vector", v_tr), ("dma_const", ldc), ("vector", prev_srow)] if hk == 0 else []),
                 inc=(hk == HCH - 1))
        t_s = p.val("tensor")
        v_s = p.op("vector", lambda e: e.tensor_scalar(out=cols[:, 12:16], in0=ps_s[:, 0:4], scalar1=ifac[:],
                                                       scalar2=None, op0=ALU.mult), waits=[("tensor", t_s)], inc=True)
        t_st = p.op("tensor", lambda e: e.matmul(ps_r[0:4, 0:128], cols[:, 12:16], ident[:], start=True, stop=True),
                    waits=[("vector", v_s)], inc=True)
        prev_srow = p.op("vector", lambda e, t=t: e.tensor_copy(out=srow[0:4, t * 128:(t + 1) * 128], in_=ps_r[0:4, 0:128]),
                         waits=[("tensor", t_st)], inc=True)
        for hk in range(HCH):
            pt = ps_t[hk % 2][:, 0:128]
            t_tr = p.op("tensor", lambda e, hk=hk, pt=pt: e.transpose(pt, yt[:, hk * 128:(hk + 1) * 128], ident[:]),
                        waits=[("vector", tr_evac[hk % 2])], inc=True)
            tr_evac[hk % 2] = p.op("vector", lambda e, hk=hk, pt=pt: e.tensor_copy(
                out=pbuf[:, 1024 + hk * 128: 1024 + (hk + 1) * 128], in_=pt),
                waits=[("tensor", t_tr), ("dma_styf", st_g2)], inc=True)
        v_tr2 = p.val("vector")
        st_g2 = p.op("gpsimd", lambda e, t=t: e.dma_start(
            out=yf_hbm.ap().rearrange("p (hc tt) -> p hc tt", hc=8, tt=tf)[:, :, t * 128:(t + 1) * 128],
            in_=pbuf[:, 1024:2048].rearrange("p (hc c) -> p hc c", hc=8, c=128)), dma="styf",
            waits=[("vector", v_tr2)])
    st_s = p.op("sync", lambda e: e.dma_start(out=s_hbm[:, :], in_=srow[0:4, 0:tf]), dma="sts",
                waits=[("vector", p.val("vector"))])
    barrier(extra_dma=("styr", "styf", "sts", "const", "gconst"))

    emit_l1ffn_routers(p, locals())
    barrier()
    emit_l1ffn_proj(p, locals())
    barrier(extra_dma=("stql", "stkl", "stvl", "stqg", "stkg", "stvg", "ldyf0", "ldyf1", "ldcs"))
    emit_l1ffn_attn(p, locals())
    p.op("sync", lambda e: e.nop(), waits=[("dma_stout", p.esem[p.dsem("stout")].n)])


def build_l1ffn(tf=T):
    import math as _math
    nc = bass.Bass("TRN2", target_bir_lowering=False, debug=False, name="l1ffn")
    API = lambda n, s: nc.dram_tensor(n, s, F32, kind="ExternalInput").ap()
    xf = API("xf", [tf, D])
    wqkv = API("wqkv", [D, 3072])
    lncol = API("lncol", [128, 16])
    rt4 = API("rt4", [128, 32])
    qcn = API("qcn", [128, 32])
    wq_in = API("wq", [2, D, 1024])
    wkv_in = API("wkv", [2, D, 2048])
    wout_in = API("wout", [2, D, 1024])
    wlo_in = API("wlo", [D, 1024])
    nks = API("nks", [128, 32])
    nvs = API("nvs", [1, 2048])
    rotm = API("rotm", [128, 128])
    cossin = API("cossin", [128, 2 * tf])
    wmaskin = API("wmaskin", [16, tf - 512])
    mlocQ_in = API("mlocQ", [128, 11 * 128])
    mlocKT_in = API("mlocKT", [128, 11 * 128])
    xo = nc.dram_tensor("xo", [tf, D], F32, kind="ExternalOutput").ap()
    s_hbm = nc.dram_tensor("s_hbm", [4, tf], F32)
    yf_hbm = nc.dram_tensor("yf_hbm", [128, 8 * tf], BF16)
    yr_hbm = nc.dram_tensor("yr_hbm", [128, 8 * tf], BF16)
    ql_hbm = nc.dram_tensor("ql_hbm", [128, 8 * tf], BF16)
    kl_hbm = nc.dram_tensor("kl_hbm", [128, 8 * tf], BF16)
    vl_hbm = nc.dram_tensor("vl_hbm", [tf, D], BF16)
    qg_hbm = nc.dram_tensor("qg_hbm", [2, 128, 8 * tf], BF16)
    kg_hbm = nc.dram_tensor("kg_hbm", [2, 128, 8 * tf], BF16)
    vg_hbm = nc.dram_tensor("vg_hbm", [2, tf, D], BF16)

    p = Prog(nc)
    emit_l1ffn(p, xf, wqkv, lncol, rt4, qcn, wq_in, wkv_in, wout_in, wlo_in, nks, nvs,
               rotm, cossin, wmaskin, mlocQ_in, mlocKT_in, xo,
               s_hbm, yf_hbm, yr_hbm, ql_hbm, kl_hbm, vl_hbm, qg_hbm, kg_hbm, vg_hbm, tf)
    p.finish()
    return nc


def emit_l1ffn_routers(p, c):
    NCTX = c["NCTX"]; NKB = c["NKB"]; NWIN = c["NWIN"]
    AF32 = c["AF32"]; r16 = None
    cols = c["cols"]; ident = c["ident"]; s_hbm = c["s_hbm"]
    wmaskin = c["wmaskin"]; selbT = c["selbT"]; qselT = c["qselT"]
    ps_s = c["ps_s"]
    K_Q, K_KV = c["K_Q"], c["K_KV"]
    logk_q, logk_kv = c["logk_q"], c["logk_kv"]
    EPSL = c["EPSL"]
    ALUx = ALU

    r16b = p.sbuf("r16b", [16, 16], F32)

    def run_router(N, K, logk, src_rows, masked, sel_out_fn, start_t=0, start_v=0):
        S = AF32[0:16, 0:N]
        BP = AF32[0:16, N:2 * N]
        CU = AF32[0:16, 2 * N:3 * N]
        ONE = AF32[0:16, 3 * N:4 * N]
        a_col = r16b[:, 0:1]; m_col = r16b[:, 1:2]; nm_col = r16b[:, 2:3]
        ssum = r16b[:, 3:4]; lo = r16b[:, 4:5]; hi = r16b[:, 5:6]
        mid = r16b[:, 6:7]; cnt = r16b[:, 7:8]; cge = r16b[:, 8:9]
        dcol = r16b[:, 9:10]; room = r16b[:, 10:11]
        ldr = 0
        if masked:
            ldr = p.op("sync", lambda e: e.dma_start(out=BP[:], in_=wmaskin), dma="ldr",
                       waits=[("tensor", start_t), ("vector", start_v)])
        for r in range(16):
            ldr = p.op("sync", lambda e, r=r, ap=src_rows[r]: e.dma_start(out=S[r:r + 1, :], in_=ap),
                       dma="ldr", waits=[("tensor", start_t), ("vector", start_v)])
        v0 = p.op("vector", lambda e: e.memset(ONE[:], 1.0), waits=[("dma_ldr", ldr)], inc=True)
        if masked:
            v0 = p.op("vector", lambda e: e.tensor_add(out=S[:], in0=S[:], in1=BP[:]), inc=True)
        p.op("vector", lambda e: e.tensor_copy(out=BP[:], in_=S[:]), inc=True)
        a_last = 0
        for i, ce in enumerate(EPSL):
            p.op("vector", lambda e: e.tensor_sub(out=CU[:], in0=S[:], in1=BP[:]),
                 waits=[("scalar", a_last)], inc=True)
            p.op("vector", lambda e, ce=ce: e.tensor_scalar_mul(out=CU[:], in0=CU[:], scalar1=1.0 / ce), inc=True)
            p.op("vector", lambda e: e.tensor_reduce(out=m_col[:], in_=CU[:], axis=AX.X, op=ALUx.max), inc=True)
            v_m = p.op("vector", lambda e: e.tensor_scalar_mul(out=nm_col[:], in0=m_col[:], scalar1=-1.0), inc=True)
            a_e = p.op("scalar", lambda e: e.activation(out=CU[:], in_=CU[:], func=AF.Exp, bias=nm_col[:],
                                                        accum_out=ssum[:]), waits=[("vector", v_m)], inc=True)
            a_l = p.op("scalar", lambda e: e.activation(out=ssum[:], in_=ssum[:], func=AF.Ln), inc=True)
            v_a = p.op("vector", lambda e: e.tensor_add(out=a_col[:], in0=ssum[:], in1=m_col[:]),
                       waits=[("scalar", a_l)], inc=True)
            v_a2 = p.op("vector", lambda e, ce=ce, logk=logk: e.tensor_scalar(
                out=a_col[:], in0=a_col[:], scalar1=-ce, scalar2=ce * logk, op0=ALUx.mult, op1=ALUx.add), inc=True)
            a_last = p.op("scalar", lambda e: e.activation(out=BP[:], in_=S[:], func=AF.Relu, bias=a_col[:]),
                          waits=[("vector", v_a2)], inc=True)
        v_u = p.op("vector", lambda e: e.tensor_scalar(out=CU[:], in0=S[:], scalar1=a_col[:], scalar2=0.0,
                                                       op0=ALUx.add, op1=ALUx.min),
                   waits=[("scalar", a_last)], inc=True)
        p.op("vector", lambda e: e.tensor_scalar(out=BP[:], in0=CU[:], scalar1=-3.1192, scalar2=0.0,
                                                 op0=ALUx.is_gt, op1=ALUx.add), inc=True)
        p.op("vector", lambda e: e.tensor_scalar_add(out=CU[:], in0=CU[:], scalar1=10.0), inc=True)
        p.op("vector", lambda e: e.tensor_mul(out=CU[:], in0=CU[:], in1=BP[:]), inc=True)
        p.op("vector", lambda e: e.tensor_scalar_add(out=CU[:], in0=CU[:], scalar1=-10.0), inc=True)
        p.op("vector", lambda e: e.memset(lo[:], -11.0), inc=True)
        p.op("vector", lambda e: e.memset(hi[:], 1.0), inc=True)
        for it in range(50):
            p.op("vector", lambda e: e.tensor_add(out=mid[:], in0=lo[:], in1=hi[:]), inc=True)
            vm = p.op("vector", lambda e: e.tensor_scalar_mul(out=mid[:], in0=mid[:], scalar1=0.5), inc=True)
            vc = p.op("vector", lambda e: e.tensor_scalar(out=BP[:], in0=CU[:], scalar1=mid[:], scalar2=0.0,
                                                          op0=ALUx.is_gt, op1=ALUx.add, accum_out=cnt[:]),
                      waits=[("vector", vm)], inc=True)
            p.op("vector", lambda e, K=K: e.tensor_scalar(out=cge[:], in0=cnt[:], scalar1=float(K), scalar2=None,
                                                          op0=ALUx.is_ge), waits=[("vector", vc)], inc=True)
            p.op("vector", lambda e: e.tensor_sub(out=dcol[:], in0=mid[:], in1=lo[:]), inc=True)
            p.op("vector", lambda e: e.tensor_mul(out=dcol[:], in0=dcol[:], in1=cge[:]), inc=True)
            p.op("vector", lambda e: e.tensor_add(out=lo[:], in0=lo[:], in1=dcol[:]), inc=True)
            p.op("vector", lambda e: e.tensor_sub(out=dcol[:], in0=mid[:], in1=hi[:]), inc=True)
            p.op("vector", lambda e: e.tensor_add(out=hi[:], in0=hi[:], in1=dcol[:]), inc=True)
            p.op("vector", lambda e: e.tensor_mul(out=dcol[:], in0=dcol[:], in1=cge[:]), inc=True)
            p.op("vector", lambda e: e.tensor_sub(out=hi[:], in0=hi[:], in1=dcol[:]), inc=True)
        vd = p.op("vector", lambda e: e.tensor_scalar(out=S[:], in0=CU[:], scalar1=hi[:], scalar2=0.0,
                                                      op0=ALUx.is_gt, op1=ALUx.add, accum_out=cnt[:]), inc=True)
        p.op("vector", lambda e, K=K: e.tensor_scalar(out=room[:], in0=cnt[:], scalar1=-1.0, scalar2=float(K),
                                                      op0=ALUx.mult, op1=ALUx.add), inc=True)
        p.op("vector", lambda e: e.tensor_scalar(out=BP[:], in0=CU[:], scalar1=lo[:], scalar2=None,
                                                 op0=ALUx.is_gt), inc=True)
        p.op("vector", lambda e: e.tensor_sub(out=BP[:], in0=BP[:], in1=S[:]), inc=True)
        p.op("vector", lambda e: e.tensor_tensor_scan(out=CU[:], data0=ONE[:], data1=BP[:], initial=0.0,
                                                      op0=ALUx.mult, op1=ALUx.add), inc=True)
        p.op("vector", lambda e: e.tensor_scalar(out=CU[:], in0=CU[:], scalar1=room[:], scalar2=None,
                                                 op0=ALUx.is_le), inc=True)
        p.op("vector", lambda e: e.tensor_mul(out=CU[:], in0=CU[:], in1=BP[:]), inc=True)
        v_sel = p.op("vector", lambda e: e.tensor_add(out=S[:], in0=S[:], in1=CU[:]), inc=True)
        sel_out_fn(S, v_sel)

    kv_rows = []
    for r in range(16):
        g = r // NWIN if r < 2 * NWIN else 0
        kv_rows.append(s_hbm[2 + g: 3 + g, 0:NCTX])

    def kv_out(S, v_sel):
        p.op("vector", lambda e: e.tensor_scalar_add(out=S[:], in0=S[:], scalar1=-1.0), inc=True)
        p.op("vector", lambda e: e.tensor_scalar_mul(out=S[:], in0=S[:], scalar1=1e9), inc=True)
        vs = p.op("vector", lambda e: e.tensor_scalar_add(out=S[:], in0=S[:], scalar1=-20.0), inc=True)
        ev = 0
        for kb in range(NKB):
            t_t = p.op("tensor", lambda e, kb=kb, S=S: e.matmul(ps_s[:, 0:16], S[0:16, kb * 128:(kb + 1) * 128],
                                                                ident[0:16, 0:16], start=True, stop=True),
                       waits=[("vector", vs if kb == 0 else ev)], inc=True)
            ev = p.op("vector", lambda e, kb=kb: e.tensor_copy(out=selbT[:, kb * 16:(kb + 1) * 16], in_=ps_s[:, 0:16]),
                      waits=[("tensor", t_t)], inc=True)

    run_router(NCTX, K_KV, logk_kv, kv_rows, True, kv_out)

    q_rows = []
    for r in range(16):
        if r < 2 * NWIN:
            g, wd = r // NWIN, r % NWIN
            q_rows.append(s_hbm[g: g + 1, (wd + 1) * 512: (wd + 2) * 512])
        else:
            q_rows.append(s_hbm[0:1, 0:512])

    def q_out(S, v_sel):
        ev = 0
        for qb in range(4):
            t_t = p.op("tensor", lambda e, qb=qb, S=S: e.matmul(ps_s[:, 0:16], S[0:16, qb * 128:(qb + 1) * 128],
                                                                ident[0:16, 0:16], start=True, stop=True),
                       waits=[("vector", v_sel if qb == 0 else ev)], inc=True)
            ev = p.op("vector", lambda e, qb=qb: e.tensor_copy(out=qselT[:, qb * 16:(qb + 1) * 16], in_=ps_s[:, 0:16]),
                      waits=[("tensor", t_t)], inc=True)

    run_router(512, K_Q, logk_q, q_rows, False, q_out,
               start_t=p.val("tensor"), start_v=p.val("vector"))


def emit_l1ffn_proj(p, c):
    tf = c["tf"]; HCH = c["HCH"]; NW = c["NW"]
    AB16 = c["AB16"]; AF32 = c["AF32"]
    wqkv = c["wqkv"]; wq_in = c["wq_in"]; wkv_in = c["wkv_in"]
    lnc = c["lnc"]; lnbb = c["lnbb"]; qcns = c["qcns"]; qkvb = c["qkvb"]
    yf_hbm = c["yf_hbm"]; yr_hbm = c["yr_hbm"]
    ql_hbm = c["ql_hbm"]; kl_hbm = c["kl_hbm"]; vl_hbm = c["vl_hbm"]
    qg_hbm = c["qg_hbm"]; kg_hbm = c["kg_hbm"]; vg_hbm = c["vg_hbm"]
    rotms = c["rotms"]; cossin = c["cossin"] if "cossin" in c else None
    fones = c["fones"]; bones = c["bones"]
    ps_t = c["ps_t"]; ps_u = c["ps_u"]; ps_d = c["ps_d"]
    D_ = D

    WQKV = AB16[:, 0:24576]
    YB = [AB16[:, 24576 + i * 4096: 24576 + (i + 1) * 4096] for i in range(2)]
    PB = AB16[:, 32768 + 0: 32768 + 4096]
    RB = AB16[:, 36864: 36864 + 4096]
    WKVG = AB16[:, 0: 16384]
    WQG = AB16[:, 16384: 24576]
    CSB = AB16[:, 40960: 40960 + 1024]
    RSC = AB16[:, 41984: 41984 + 512]
    qb_row = AF32[0:1, 0:3072]

    wld = 0
    for hk in range(HCH):
        wld = p.op("gpsimd", lambda e, hk=hk: e.dma_start(out=WQKV[:, hk * 3072:(hk + 1) * 3072],
                                                          in_=wqkv[hk * 128:(hk + 1) * 128, :]), dma="wld")
    t_b = 0
    v_bev = 0
    for ftile in range(6):
        for hk in range(HCH):
            wl = []
            if hk == 0:
                wl = ([("dma_wld", wld)] if ftile == 0 else []) + [("vector", v_bev)]
            p.op("tensor", lambda e, ftile=ftile, hk=hk: e.matmul(
                ps_d[:], lnbb[:, hk: hk + 1], WQKV[:, hk * 3072 + ftile * 512: hk * 3072 + (ftile + 1) * 512],
                start=(hk == 0), stop=(hk == HCH - 1)),
                 waits=wl, inc=(hk == HCH - 1))
        t_b = p.val("tensor")
        v_bev = p.op("vector", lambda e, ftile=ftile: e.tensor_copy(out=qb_row[0:1, ftile * 512:(ftile + 1) * 512], in_=ps_d[:]),
                     waits=[("tensor", t_b)], inc=True)
    ev = 0
    for j in range(24):
        t_t = p.op("tensor", lambda e, j=j: e.matmul(ps_u[0][:, 0:1], qb_row[0:1, j * 128:(j + 1) * 128],
                                                     fones[0:1, 0:1], start=True, stop=True),
                   waits=[("vector", ev if j > 0 else p.val("vector"))], inc=True)
        ev = p.op("vector", lambda e, j=j: e.tensor_copy(out=qkvb[:, j:j + 1], in_=ps_u[0][:, 0:1]),
                  waits=[("tensor", t_t)], inc=True)
    v_qb = p.op("vector", lambda e: e.tensor_scalar_mul(out=qkvb[:, 0:8], in0=qkvb[:, 0:8], scalar1=0.125), inc=True)
    v_w = 0
    for hk in range(HCH):
        v_w = p.op("vector", lambda e, hk=hk: e.tensor_scalar(out=WQKV[:, hk * 3072:(hk + 1) * 3072],
                                                              in0=WQKV[:, hk * 3072:(hk + 1) * 3072],
                                                              scalar1=lnc[:, hk:hk + 1], scalar2=None, op0=ALU.mult),
                   waits=[("tensor", t_b)], inc=True)

    st_q = st_k = st_v = 0
    ld_y = 0
    prev_rope_v = 0
    prev_ev = [0, 0]
    for w in range(NW):
        yb = YB[w % 2]
        ld_y = p.op("gpsimd", lambda e, w=w, yb=yb: e.dma_start(
            out=yb.rearrange("p (hc t) -> p hc t", hc=8, t=512),
            in_=yf_hbm.ap().rearrange("p (hc tt) -> p hc tt", hc=8, tt=tf)[:, :, w * 512:(w + 1) * 512]),
            dma="ldyf%d" % (w % 2), waits=[("tensor", prev_ev[w % 2])])
        ldy_nm = "dma_ldyf%d" % (w % 2)
        ld_cs = p.op("gpsimd", lambda e, w=w: e.dma_start(
            out=CSB.rearrange("p (a t) -> p a t", a=2, t=512),
            in_=cossin.rearrange("p (a tt) -> p a tt", a=2, tt=tf)[:, :, w * 512:(w + 1) * 512]),
            dma="ldcs", waits=[("vector", prev_rope_v)])
        for fc in range(16):
            bank = ps_t[fc % 2]
            for hk in range(HCH):
                wl = []
                if hk == 0:
                    wl = [(ldy_nm, ld_y), ("vector", v_w if fc < 2 else p.val("vector")),
                          ("scalar", p.val("scalar"))]
                p.op("tensor", lambda e, fc=fc, hk=hk, bank=bank, yb=yb: e.matmul(
                    bank[:], WQKV[:, hk * 3072 + fc * 128: hk * 3072 + (fc + 1) * 128],
                    yb[:, hk * 512:(hk + 1) * 512], start=(hk == 0), stop=(hk == HCH - 1)), waits=wl,
                    inc=(hk == HCH - 1))
            t_p = p.val("tensor")
            if fc < 8:
                a_e = p.op("scalar", lambda e, fc=fc, bank=bank: e.activation(
                    out=PB[:, fc * 512:(fc + 1) * 512], in_=bank[:], func=AF.Identity,
                    bias=qkvb[:, fc:fc + 1], scale=0.125),
                    waits=[("tensor", t_p)] + ([("dma_stql", st_q), ("dma_stvl", st_v)] if fc == 0 else []), inc=True)
            else:
                a_e = p.op("scalar", lambda e, fc=fc, bank=bank: e.activation(
                    out=RB[:, (fc - 8) * 512:(fc - 7) * 512], in_=bank[:], func=AF.Identity,
                    bias=qkvb[:, fc:fc + 1], scale=1.0),
                    waits=[("tensor", t_p)] + ([("dma_stkl", st_k)] if fc == 8 else []), inc=True)
        a_qk = p.val("scalar")
        for src_buf, dst_hbm, st_prev, nm in ((PB, ql_hbm, st_q, "stql"), (RB, kl_hbm, st_k, "stkl")):
            for fc in range(HCH):
                bank = ps_u[fc % 2]
                t_r = p.op("tensor", lambda e, fc=fc, bank=bank, src_buf=src_buf: e.matmul(
                    bank[:], rotms[:], src_buf[:, fc * 512:(fc + 1) * 512], start=True, stop=True),
                    waits=[("scalar", a_qk), ("vector", p.val("vector"))], inc=True)
                v_0 = p.op("vector", lambda e, bank=bank: e.tensor_copy(out=RSC[:], in_=bank[:]),
                           waits=[("tensor", t_r)], inc=True)
                v_1 = p.op("vector", lambda e, fc=fc, src_buf=src_buf: e.tensor_tensor(
                    out=src_buf[:, fc * 512:(fc + 1) * 512], in0=src_buf[:, fc * 512:(fc + 1) * 512],
                    in1=CSB[:, 0:512], op=ALU.mult), waits=[("dma_ldcs", ld_cs)], inc=True)
                v_2 = p.op("vector", lambda e: e.tensor_tensor(
                    out=RSC[:], in0=RSC[:], in1=CSB[:, 512:1024], op=ALU.mult), inc=True)
                v_3 = p.op("vector", lambda e, fc=fc, src_buf=src_buf: e.tensor_add(
                    out=src_buf[:, fc * 512:(fc + 1) * 512], in0=src_buf[:, fc * 512:(fc + 1) * 512],
                    in1=RSC[:]), inc=True)
            v_rope = prev_rope_v = p.val("vector")
            stv = p.op("gpsimd", lambda e, w=w, src_buf=src_buf, dst_hbm=dst_hbm: e.dma_start(
                out=dst_hbm.ap().rearrange("p (hc tt) -> p hc tt", hc=8, tt=tf)[:, :, w * 512:(w + 1) * 512],
                in_=src_buf.rearrange("p (hc t) -> p hc t", hc=8, t=512)), dma=nm,
                waits=[("vector", v_rope)])
            if nm == "stql":
                st_q = stv
            else:
                st_k = stv
        for sub in range(4):
            for ft in range(2):
                bank = ps_t[(sub * 2 + ft) % 2]
                for hk in range(HCH):
                    wl = [("vector", p.val("vector"))] if hk == 0 else []
                    p.op("tensor", lambda e, sub=sub, ft=ft, hk=hk, bank=bank, yb=yb: e.matmul(
                        bank[:], yb[:, hk * 512 + sub * 128: hk * 512 + (sub + 1) * 128],
                        WQKV[:, hk * 3072 + 2048 + ft * 512: hk * 3072 + 2048 + (ft + 1) * 512],
                        start=(hk == 0), stop=False), waits=wl)
                t_v = p.op("tensor", lambda e, ft=ft, bank=bank: e.matmul(
                    bank[:], fones[0:1, :], qb_row[0:1, 2048 + ft * 512: 2048 + (ft + 1) * 512],
                    start=False, stop=True), inc=True)
                v_e = p.op("vector", lambda e, sub=sub, ft=ft, bank=bank: e.tensor_copy(
                    out=PB[:, sub * 1024 + ft * 512: sub * 1024 + (ft + 1) * 512], in_=bank[:]),
                    waits=[("tensor", t_v), ("dma_stvl", st_v), ("dma_stql", st_q)], inc=True)
        v_vv = p.val("vector")
        st_v = p.op("gpsimd", lambda e, w=w: e.dma_start(
            out=vl_hbm[w * 512:(w + 1) * 512, :].rearrange("(s q) d -> q s d", q=128, s=4),
            in_=PB[:, 0:4096].rearrange("p (s d) -> p s d", s=4, d=1024)), dma="stvl",
            waits=[("vector", v_vv)])
        prev_ev[w % 2] = p.val("tensor")

    c["barrier"](extra_dma=("stql", "stkl", "stvl", "ldcs", "ldyf0", "ldyf1"))
    st_qg = st_kg = st_vg = 0
    for g in range(2):
        wld2 = 0
        for hk in range(HCH):
            wld2 = p.op("gpsimd", lambda e, g=g, hk=hk: e.dma_start(
                out=WKVG[:, hk * 2048:(hk + 1) * 2048], in_=wkv_in[g, hk * 128:(hk + 1) * 128, :]),
                dma="wld2", waits=([("tensor", p.val("tensor"))] if (g > 0 and hk == 0) else []))
        for hk in range(HCH):
            wld2 = p.op("gpsimd", lambda e, g=g, hk=hk: e.dma_start(
                out=WQG[:, hk * 1024:(hk + 1) * 1024], in_=wq_in[g, hk * 128:(hk + 1) * 128, :]), dma="wld2")
        v_w2 = 0
        for hk in range(HCH):
            p.op("vector", lambda e, g=g, hk=hk: e.tensor_scalar(
                out=WKVG[:, hk * 2048:(hk + 1) * 2048], in0=WKVG[:, hk * 2048:(hk + 1) * 2048],
                scalar1=qcns[:, hk * 4 + 2 + g: hk * 4 + 3 + g], scalar2=None, op0=ALU.mult),
                 waits=[("dma_wld2", wld2)], inc=True)
            v_w2 = p.op("vector", lambda e, g=g, hk=hk: e.tensor_scalar(
                out=WQG[:, hk * 1024:(hk + 1) * 1024], in0=WQG[:, hk * 1024:(hk + 1) * 1024],
                scalar1=qcns[:, hk * 4 + g: hk * 4 + 1 + g], scalar2=None, op0=ALU.mult), inc=True)
        for w in range(NW):
            yb = YB[w % 2]
            ld_y = p.op("gpsimd", lambda e, w=w, g=g, yb=yb: e.dma_start(
                out=yb.rearrange("p (hc t) -> p hc t", hc=8, t=512),
                in_=yr_hbm.ap().rearrange("p (hc tt) -> p hc tt", hc=8, tt=tf)[:, :, w * 512:(w + 1) * 512]),
                dma="ldyf%d" % (w % 2), waits=[("tensor", prev_ev[w % 2])])
            ldy_nm = "dma_ldyf%d" % (w % 2)
            for fc in range(16):
                bank = ps_t[fc % 2]
                wsrc = WQG if fc < 8 else WKVG
                fcc = fc if fc < 8 else fc - 8
                wslc = (lambda hk, fcc=fcc, wsrc=wsrc: wsrc[:, hk * (1024 if wsrc is WQG else 2048) + fcc * 128:
                                                           hk * (1024 if wsrc is WQG else 2048) + (fcc + 1) * 128])
                for hk in range(HCH):
                    wl = []
                    if hk == 0:
                        wl = [(ldy_nm, ld_y), ("vector", v_w2 if fc < 2 else p.val("vector")),
                              ("scalar", p.val("scalar"))]
                    p.op("tensor", lambda e, fc=fc, hk=hk, bank=bank, yb=yb, wslc=wslc: e.matmul(
                        bank[:], wslc(hk), yb[:, hk * 512:(hk + 1) * 512],
                        start=(hk == 0), stop=(hk == HCH - 1)), waits=wl, inc=(hk == HCH - 1))
                t_p = p.val("tensor")
                wl2 = [("tensor", t_p)]
                if fc == 0:
                    wl2.append(("dma_stqg", st_qg))
                    wl2.append(("dma_stkg", st_kg))
                if fc == 8:
                    wl2.append(("dma_stkg", st_kg))
                    wl2.append(("dma_stqg", st_qg))
                a_e = p.op("scalar", lambda e, fc=fc, bank=bank: e.activation(
                    out=PB[:, (fc % 8) * 512:((fc % 8) + 1) * 512], in_=bank[:], func=AF.Identity,
                    scale=(0.125 if fc < 8 else 1.0)), waits=wl2, inc=True)
                if fc == 7 or fc == 15:
                    a_d = p.val("scalar")
                    hbm = qg_hbm if fc == 7 else kg_hbm
                    nm = "stqg" if fc == 7 else "stkg"
                    stx = p.op("gpsimd", lambda e, w=w, g=g, hbm=hbm: e.dma_start(
                        out=hbm.ap()[g].rearrange("p (hc tt) -> p hc tt", hc=8, tt=tf)[:, :, w * 512:(w + 1) * 512],
                        in_=PB[:, 0:4096].rearrange("p (hc t) -> p hc t", hc=8, t=512)), dma=nm,
                        waits=[("scalar", a_d)])
                    if fc == 7:
                        st_qg = stx
                    else:
                        st_kg = stx
            for sub in range(4):
                for ft in range(2):
                    bank = ps_t[(sub * 2 + ft) % 2]
                    for hk in range(HCH):
                        wl = [("vector", p.val("vector")), ("scalar", p.val("scalar"))] if hk == 0 else []
                        p.op("tensor", lambda e, sub=sub, ft=ft, hk=hk, bank=bank, yb=yb: e.matmul(
                            bank[:], yb[:, hk * 512 + sub * 128: hk * 512 + (sub + 1) * 128],
                            WKVG[:, hk * 2048 + 1024 + ft * 512: hk * 2048 + 1024 + (ft + 1) * 512],
                            start=(hk == 0), stop=(hk == HCH - 1)), waits=wl, inc=(hk == HCH - 1))
                    t_v = p.val("tensor")
                    v_e = p.op("vector", lambda e, sub=sub, ft=ft, bank=bank: e.tensor_copy(
                        out=RB[:, sub * 1024 + ft * 512: sub * 1024 + (ft + 1) * 512], in_=bank[:]),
                        waits=[("tensor", t_v), ("dma_stvg", st_vg)], inc=True)
            v_vv = p.val("vector")
            st_vg = p.op("gpsimd", lambda e, w=w, g=g: e.dma_start(
                out=vg_hbm.ap()[g][w * 512:(w + 1) * 512, :].rearrange("(s q) d -> q s d", q=128, s=4),
                in_=RB[:, 0:4096].rearrange("p (s d) -> p s d", s=4, d=1024)), dma="stvg",
                waits=[("vector", v_vv)])
            prev_ev[w % 2] = p.val("tensor")


def emit_l1ffn_attn(p, c):
    tf = c["tf"]; NW = c["NW"]; NWIN = c["NWIN"]
    AB16 = c["AB16"]; AF32 = c["AF32"]
    xf = c["xf"]; xo = c["xo"]
    ql_hbm = c["ql_hbm"]; kl_hbm = c["kl_hbm"]; vl_hbm = c["vl_hbm"]
    qg_hbm = c["qg_hbm"]; kg_hbm = c["kg_hbm"]; vg_hbm = c["vg_hbm"]
    wlo_in = c["wlo_in"]; wout_in = c["wout_in"]
    fones = c["fones"]; ident = c["ident"]; bocol = c["bocol"]
    mlocQs = c["mlocQs"]; mlocKTs = c["mlocKTs"]
    selbT = c["selbT"]; qselT = c["qselT"]; accw = c["accw"]
    nkss = c["nkss"]; nvs_in = c["nvs"]
    nmrow = c["nmrow"]; enull = c["enull"]; denr = c["denr"]; mvals = c["mvals"]
    cols = c["cols"]; xt = c["xt"]; yr = c["yr"]; yf32 = c["yf32"]; ln_scr = c["ln_scr"]
    ps_t = c["ps_t"]; ps_u = c["ps_u"]; ps_o = c["ps_o"]; ps_d = c["ps_d"]
    ps_r = c["ps_r"]; ps_s = c["ps_s"]

    KW = [AB16[:, 0:4096], AB16[:, 4096:8192]]
    VW = [AB16[:, 8192:12288], AB16[:, 12288:16384]]
    QWL = AB16[:, 16384:20480]
    QWG = AB16[:, 20480:24576]
    WPR = AB16[:, 24576:32768]
    KCH = [AB16[:, 24576:28672], AB16[:, 28672:32768]]
    VCH = [AB16[:, 32768:36864], AB16[:, 36864:40960]]
    OT = AB16[:, 40960:45056]
    ABF = [AB16[:, 45056 + i * 512: 45056 + (i + 1) * 512] for i in range(4)]
    oacc = AF32[:, 0:4096]
    dacc = AF32[0:1, 4096:12288]
    nvss = AF32[0:1, 12288:14336]
    ascr = [yf32[:, 0:512], ln_scr[:, 0:512]]
    recb = yf32[:, 512:1024]
    stg2 = ln_scr[:, 512:1024]
    cntc = cols[:, 18:19]
    rc05 = cols[:, 19:20]
    m20 = cols[0:1, 20:21]

    def fm_load(dst, hbm_ap, wdx):
        return lambda e: e.dma_start(
            out=dst.rearrange("p (hc t) -> p hc t", hc=8, t=512),
            in_=hbm_ap.rearrange("p (hc tt) -> p hc tt", hc=8, tt=tf)[:, :, wdx * 512:(wdx + 1) * 512])

    def tm_load(dst, hbm_rows):
        return lambda e: e.dma_start(
            out=dst.rearrange("p (s d) -> p s d", s=4, d=1024),
            in_=hbm_rows.rearrange("(s q) d -> q s d", q=128, s=4))

    p.op("sync", lambda e: e.dma_start(out=nvss, in_=nvs_in), dma="lnv")
    global _WBCTR
    _WBCTR += 1
    wcast = p.nc.dram_tensor(f"wcast{_WBCTR}", [3, 128, 8 * 1024], BF16)
    wld0 = 0
    for si, win in enumerate((wlo_in, None, None)):
        for j in range(8):
            src_ap = (wlo_in[j * 128:(j + 1) * 128, :] if si == 0
                      else wout_in[si - 1, j * 128:(j + 1) * 128, :])
            wld0 = p.op("gpsimd", lambda e, j=j, src_ap=src_ap: e.dma_start(
                out=WPR[:, j * 1024:(j + 1) * 1024], in_=src_ap), dma="wst",
                waits=([("dma_wpc", p.esem[p.dsem("wpc")].n)] if j == 0 else []))
        p.op("sync", lambda e, si=si: e.dma_start(out=wcast.ap()[si], in_=WPR[:]),
             dma="wpc", waits=[("dma_wst", wld0)])
    wpc_done = p.esem[p.dsem("wpc")].n
    st_out = 0
    wpr_read = 0
    ot_read = 0
    qwl_read = 0
    qwg_read = 0
    kwread = [0, 0]
    vwread = [0, 0]
    kchread = [0, 0]
    vchread = [0, 0]
    mval_tread = 0
    abf_g = [0, 0, 0, 0]
    SB4 = [ps_u[0], ps_u[1], ps_t[0], ps_t[1]]
    pg = [("vector", 0)] * 4
    asc_g = [0, 0]
    a_oc_prev = 0
    v_de_prev = 0
    v_rb_prev = 0
    t_br_prev = 0
    t_nv_prev = 0
    v_dn_prev = 0
    a_fin_prev = 0
    psr_guard = ("vector", 0)
    vfin_g = [0, 0]
    v_comb = 0

    for w in range(NW):
        cur = w % 2
        prv = 1 - cur
        lq_t = p.op("sync", fm_load(QWL, ql_hbm.ap(), w), dma="lq", waits=[("tensor", qwl_read)])
        lk_t = p.op("sync", fm_load(KW[cur], kl_hbm.ap(), w), dma="lk%d" % cur,
                    waits=[("tensor", kwread[cur])])
        lv_t = p.op("sync", tm_load(VW[cur], vl_hbm[w * 512:(w + 1) * 512, :]), dma="lv%d" % cur,
                    waits=[("tensor", vwread[cur])])
        ld_waits = [("dma_lq", lq_t), ("dma_lk%d" % cur, lk_t), ("dma_lv%d" % cur, lv_t)]
        wl0 = [("tensor", wpr_read)]
        if w == 0:
            wl0.append(("dma_wpc", wpc_done))
        wld_local = p.op("gpsimd", lambda e: e.dma_start(out=WPR[:], in_=wcast.ap()[0]),
                         dma="wst", waits=wl0)

        nch = 2 if w > 0 else 1
        zi = 0
        first_acc = True
        for h in range(16):
            b = (h % 2) * 64
            hk = h // 2
            for qt in range(4):
                lhs = QWL[b:b + 64, hk * 512 + qt * 128: hk * 512 + (qt + 1) * 128]
                mcol = mvals[:, h * 4 + qt: h * 4 + qt + 1]
                for ci in range(nch):
                    iscur = 1 if (w == 0 or ci == 1) else 0
                    kbuf = KW[cur] if iscur else KW[prv]
                    bank = ps_t[zi % 2]
                    t_z = p.op("tensor", lambda e, lhs=lhs, kbuf=kbuf, bank=bank, b=b, hk=hk: e.matmul(
                        bank[:], lhs, kbuf[b:b + 64, hk * 512:(hk + 1) * 512], start=True, stop=True),
                        waits=[pg[2 + zi % 2]] + (ld_waits if zi < 2 else []), inc=True)
                    m0 = 3 + 4 * iscur - qt
                    p.op("vector", lambda e, bank=bank, m0=m0: e.tensor_tensor(
                        out=stg2[:], in0=bank[:], in1=mlocQs[:, m0 * 128: m0 * 128 + 512], op=ALU.add),
                        waits=[("tensor", t_z)] + ([("scalar", p.val("scalar"))] if first_acc else []),
                        inc=True)
                    p.op("vector", lambda e, mcol=mcol, ci=ci: e.tensor_scalar(
                        out=stg2[:], in0=stg2[:], scalar1=0.0,
                        scalar2=(None if ci == 0 else mcol), op0=ALU.add, op1=ALU.max, accum_out=mcol),
                        waits=([("tensor", mval_tread)] if first_acc else []), inc=True)
                    first_acc = False
                    pg[2 + zi % 2] = ("vector", p.val("vector"))
                    zi += 1
        v_neg = p.op("vector", lambda e: e.tensor_scalar_mul(out=mvals[:, 0:64], in0=mvals[:, 0:64],
                                                             scalar1=-1.0), inc=True)
        vfin_g = [v_neg, v_neg]

        kbs = list(range(4, 8)) if w == 0 else list(range(8))
        fkb, lkb = kbs[0], kbs[-1]
        k1_last = 0
        nm_ev = 0
        for h in range(16):
            b = (h % 2) * 64
            hk = h // 2
            qsl = QWL[b:b + 64, hk * 512:(hk + 1) * 512]
            for qt in range(4):
                wl = [("vector", v_neg if (h == 0 and qt == 0) else nm_ev)]
                if h == 0 and qt == 0:
                    wl.append(psr_guard)
                t_t = p.op("tensor", lambda e, h=h, qt=qt: e.matmul(
                    ps_r[0:1, 0:128], mvals[:, h * 4 + qt: h * 4 + qt + 1], ident[:],
                    start=True, stop=True), waits=wl, inc=True)
                nm_ev = p.op("vector", lambda e, qt=qt: e.tensor_copy(
                    out=nmrow[0:1, qt * 128:(qt + 1) * 128], in_=ps_r[0:1, 0:128]),
                    waits=[("tensor", t_t)] + ([("tensor", k1_last)] if qt == 0 else []), inc=True)
            mval_tread = p.val("tensor")
            psr_guard = ("vector", nm_ev)
            for kb in kbs:
                iscur = kb >= 4
                kbuf = KW[cur] if iscur else KW[prv]
                vbuf = VW[cur] if iscur else VW[prv]
                kbr = kb % 4
                bank = ps_u[kb % 2]
                wl = [pg[kb % 2]]
                if kb == fkb:
                    wl.append(("vector", nm_ev))
                p.op("tensor", lambda e, kbuf=kbuf, b=b, hk=hk, kbr=kbr, bank=bank, qsl=qsl: e.matmul(
                    bank[:], kbuf[b:b + 64, hk * 512 + kbr * 128: hk * 512 + (kbr + 1) * 128],
                    qsl, start=True, stop=False), waits=wl)
                k1_last = p.op("tensor", lambda e, bank=bank: e.matmul(
                    bank[:], fones[0:1, 0:128], nmrow[0:1, 0:512], start=False, stop=True), inc=True)
                n0 = 7 - kb
                v_mk = p.op("vector", lambda e, bank=bank, n0=n0, kb=kb: e.tensor_tensor(
                    out=ascr[kb % 2], in0=bank[:], in1=mlocKTs[:, n0 * 128: n0 * 128 + 512], op=ALU.add),
                    waits=[("tensor", k1_last), ("scalar", asc_g[kb % 2])], inc=True)
                pg[kb % 2] = ("vector", v_mk)
                a_ex = p.op("scalar", lambda e, kb=kb: e.activation(
                    out=ABF[kb % 4][:], in_=ascr[kb % 2], func=AF.Exp),
                    waits=[("vector", v_mk), ("tensor", abf_g[kb % 4])], inc=True)
                asc_g[kb % 2] = a_ex
                p.op("tensor", lambda e, vbuf=vbuf, kbr=kbr, h=h, kb=kb, fkb=fkb, lkb=lkb: e.matmul(
                    ps_o[:], vbuf[:, kbr * 1024 + h * 64: kbr * 1024 + (h + 1) * 64], ABF[kb % 4][:],
                    start=(kb == fkb), stop=(kb == lkb)),
                    waits=[("scalar", a_ex)] + ([("scalar", a_oc_prev)] if kb == fkb else []), inc=True)
                abf_g[kb % 4] = p.op("tensor", lambda e, kb=kb, fkb=fkb, lkb=lkb: e.matmul(
                    ps_d[:], bocol[:, 0:1], ABF[kb % 4][:], start=(kb == fkb), stop=(kb == lkb)),
                    waits=([("vector", v_de_prev)] if kb == fkb else []), inc=True)
            t_avd = p.val("tensor")
            v_de_prev = p.op("vector", lambda e: e.tensor_copy(out=denr[0:1, 0:512], in_=ps_d[0:1, 0:512]),
                             waits=[("tensor", t_avd), ("tensor", t_br_prev)], inc=True)
            p.op("vector", lambda e: e.reciprocal(out=denr[0:1, 0:512], in_=denr[0:1, 0:512]), inc=True)
            t_br_prev = p.op("tensor", lambda e: e.matmul(
                ps_s[:, 0:512], fones[0:1, 0:128], denr[0:1, 0:512], start=True, stop=True),
                waits=[("vector", p.val("vector"))], inc=True)
            v_rb_prev = p.op("vector", lambda e: e.tensor_copy(out=recb[:], in_=ps_s[:, 0:512]),
                             waits=[("tensor", t_br_prev)], inc=True)
            a_oc_prev = a_oc = p.op("scalar", lambda e, b=b: e.activation(
                out=stg2[b:b + 64, :], in_=ps_o[:], func=AF.Identity),
                waits=[("tensor", t_avd), ("vector", vfin_g[h % 2])], inc=True)
            v_fin = p.op("vector", lambda e, b=b: e.tensor_tensor(
                out=stg2[b:b + 64, :], in0=stg2[b:b + 64, :], in1=recb[b:b + 64, :], op=ALU.mult),
                waits=[("scalar", a_oc)], inc=True)
            vfin_g[h % 2] = v_fin
            a_fin_prev = p.op("scalar", lambda e, b=b, hk=hk: e.activation(
                out=OT[b:b + 64, hk * 512:(hk + 1) * 512], in_=stg2[b:b + 64, :], func=AF.Identity),
                waits=[("vector", v_fin)] + ([("tensor", ot_read)] if h == 0 else []), inc=True)
        qwl_read = p.val("tensor")
        kwread[0] = kwread[1] = p.val("tensor")
        vwread[0] = vwread[1] = p.val("tensor")

        for qt in range(4):
            for df in range(2):
                bank = ps_t[df]
                for j in range(8):
                    wl = []
                    if j == 0:
                        wl = [pg[2 + df]]
                        if qt == 0 and df == 0:
                            wl += [("scalar", a_fin_prev), ("dma_wst", wld_local)]
                    p.op("tensor", lambda e, qt=qt, df=df, j=j, bank=bank: e.matmul(
                        bank[:], OT[:, j * 512 + qt * 128: j * 512 + (qt + 1) * 128],
                        WPR[:, j * 1024 + df * 512: j * 1024 + (df + 1) * 512],
                        start=(j == 0), stop=(j == 7)), waits=wl, inc=(j == 7))
                t_pj = p.val("tensor")
                p.op("vector", lambda e, qt=qt, df=df, bank=bank: e.tensor_copy(
                    out=accw[:, qt * 1024 + df * 512: qt * 1024 + (df + 1) * 512], in_=bank[:]),
                    waits=[("tensor", t_pj)], inc=True)
                pg[2 + df] = ("vector", p.val("vector"))
        ot_read = p.val("tensor")
        wpr_read = p.val("tensor")

        if w > 0:
            for g in range(2):
                r = g * NWIN + (w - 1)
                gq_t = p.op("sync", fm_load(QWG, qg_hbm.ap()[g], w), dma="gq",
                            waits=[("tensor", qwg_read)])
                for ch in range(w):
                    cb = ch % 2
                    gk_t = p.op("sync", fm_load(KCH[cb], kg_hbm.ap()[g], ch), dma="gk%d" % cb,
                                waits=[("tensor", max(wpr_read, kchread[cb]))])
                    gv_t = p.op("sync", tm_load(VCH[cb], vg_hbm.ap()[g][ch * 512:(ch + 1) * 512, :]),
                                dma="gv%d" % cb, waits=[("tensor", vchread[cb])])
                    gld_waits = [("dma_gk%d" % cb, gk_t), ("dma_gv%d" % cb, gv_t)] + \
                                ([("dma_gq", gq_t)] if ch == 0 else [])
                    for h in range(16):
                        b = (h % 2) * 64
                        hk = h // 2
                        for kb in range(4):
                            bank = SB4[kb]
                            kbglob = ch * 4 + kb
                            t_s = p.op("tensor", lambda e, cb=cb, b=b, hk=hk, kb=kb, bank=bank: e.matmul(
                                bank[:], KCH[cb][b:b + 64, hk * 512 + kb * 128: hk * 512 + (kb + 1) * 128],
                                QWG[b:b + 64, hk * 512:(hk + 1) * 512], start=True, stop=True),
                                waits=[pg[kb]] + (gld_waits if (h == 0 and kb == 0) else []),
                                inc=True)
                            a_ex = p.op("scalar", lambda e, kb=kb, kbglob=kbglob, r=r, bank=bank: e.activation(
                                out=ABF[kb][:], in_=bank[:], func=AF.Exp,
                                bias=selbT[:, kbglob * 16 + r: kbglob * 16 + r + 1]),
                                waits=[("tensor", t_s), ("tensor", abf_g[kb])], inc=True)
                            pg[kb] = ("scalar", a_ex)
                            p.op("tensor", lambda e, cb=cb, kb=kb, h=h: e.matmul(
                                ps_o[:], VCH[cb][:, kb * 1024 + h * 64: kb * 1024 + (h + 1) * 64],
                                ABF[kb][:], start=(kb == 0), stop=(kb == 3)),
                                waits=[("scalar", a_ex)] + ([("scalar", a_oc_prev)] if kb == 0 else []),
                                inc=True)
                            abf_g[kb] = p.op("tensor", lambda e, kb=kb: e.matmul(
                                ps_d[:], bocol[:, 0:1], ABF[kb][:], start=(kb == 0), stop=(kb == 3)),
                                waits=([("vector", v_de_prev)] if kb == 0 else []), inc=True)
                        t_avd = p.val("tensor")
                        a_oc_prev = a_oc = p.op("scalar", lambda e, b=b: e.activation(
                            out=stg2[b:b + 64, :], in_=ps_o[:], func=AF.Identity),
                            waits=[("tensor", t_avd), ("vector", vfin_g[h % 2])], inc=True)
                        if ch == 0:
                            p.op("vector", lambda e, b=b, hk=hk: e.tensor_copy(
                                out=oacc[b:b + 64, hk * 512:(hk + 1) * 512], in_=stg2[b:b + 64, :]),
                                waits=[("scalar", a_oc)], inc=True)
                        else:
                            p.op("vector", lambda e, b=b, hk=hk: e.tensor_tensor(
                                out=oacc[b:b + 64, hk * 512:(hk + 1) * 512],
                                in0=oacc[b:b + 64, hk * 512:(hk + 1) * 512], in1=stg2[b:b + 64, :],
                                op=ALU.add), waits=[("scalar", a_oc)], inc=True)
                        vfin_g[h % 2] = p.val("vector")
                        if ch == 0:
                            v_de_prev = p.op("vector", lambda e, h=h: e.tensor_copy(
                                out=dacc[0:1, h * 512:(h + 1) * 512], in_=ps_d[0:1, 0:512]),
                                waits=[("tensor", t_avd), ("tensor", t_br_prev)], inc=True)
                        else:
                            v_de_prev = p.op("vector", lambda e, h=h: e.tensor_tensor(
                                out=dacc[0:1, h * 512:(h + 1) * 512],
                                in0=dacc[0:1, h * 512:(h + 1) * 512], in1=ps_d[0:1, 0:512], op=ALU.add),
                                waits=[("tensor", t_avd), ("tensor", t_br_prev)], inc=True)
                    kchread[cb] = p.val("tensor")
                    vchread[cb] = p.val("tensor")
                for h in range(16):
                    b = (h % 2) * 64
                    hk = h // 2
                    t_n = p.op("tensor", lambda e, b=b, hk=hk, h=h, g=g: e.matmul(
                        ps_r[0:1, 0:512], nkss[b:b + 64, 2 * h + g: 2 * h + g + 1],
                        QWG[b:b + 64, hk * 512:(hk + 1) * 512], start=True, stop=True),
                        waits=[psr_guard], inc=True)
                    a_en = p.op("scalar", lambda e: e.activation(
                        out=enull[0:1, 0:512], in_=ps_r[0:1, 0:512], func=AF.Exp, bias=m20),
                        waits=[("tensor", t_n), ("tensor", t_nv_prev), ("vector", v_dn_prev)], inc=True)
                    psr_guard = ("scalar", a_en)
                    v_dn_prev = p.op("vector", lambda e, h=h: e.tensor_tensor(
                        out=dacc[0:1, h * 512:(h + 1) * 512], in0=dacc[0:1, h * 512:(h + 1) * 512],
                        in1=enull[0:1, 0:512], op=ALU.add), waits=[("scalar", a_en)], inc=True)
                    t_nv_prev = t_nv = p.op("tensor", lambda e, g=g, h=h: e.matmul(
                        ps_u[0][0:64, 0:512], nvss[0:1, g * 1024 + h * 64: g * 1024 + (h + 1) * 64],
                        enull[0:1, 0:512], start=True, stop=True),
                        waits=[("scalar", a_en), pg[0], ("dma_lnv", 16)], inc=True)
                    a_nc = p.op("scalar", lambda e, b=b: e.activation(
                        out=stg2[b:b + 64, :], in_=ps_u[0][0:64, 0:512], func=AF.Identity),
                        waits=[("tensor", t_nv), ("vector", vfin_g[h % 2])], inc=True)
                    pg[0] = ("scalar", a_nc)
                    p.op("vector", lambda e, b=b, hk=hk: e.tensor_tensor(
                        out=oacc[b:b + 64, hk * 512:(hk + 1) * 512],
                        in0=oacc[b:b + 64, hk * 512:(hk + 1) * 512],
                        in1=stg2[b:b + 64, :], op=ALU.add), waits=[("scalar", a_nc)], inc=True)
                    p.op("vector", lambda e, h=h: e.reciprocal(
                        out=denr[0:1, 0:512], in_=dacc[0:1, h * 512:(h + 1) * 512]),
                        waits=[("tensor", t_br_prev)], inc=True)
                    t_br_prev = p.op("tensor", lambda e: e.matmul(
                        ps_s[:, 0:512], fones[0:1, 0:128], denr[0:1, 0:512], start=True, stop=True),
                        waits=[("vector", p.val("vector"))], inc=True)
                    v_rb_prev = p.op("vector", lambda e: e.tensor_copy(out=recb[:], in_=ps_s[:, 0:512]),
                                     waits=[("tensor", t_br_prev)], inc=True)
                    v_fin = p.op("vector", lambda e, b=b, hk=hk: e.tensor_tensor(
                        out=stg2[b:b + 64, :], in0=oacc[b:b + 64, hk * 512:(hk + 1) * 512],
                        in1=recb[b:b + 64, :], op=ALU.mult), inc=True)
                    vfin_g[h % 2] = v_fin
                    a_fin_prev = p.op("scalar", lambda e, b=b, hk=hk: e.activation(
                        out=OT[b:b + 64, hk * 512:(hk + 1) * 512], in_=stg2[b:b + 64, :], func=AF.Identity),
                        waits=[("vector", v_fin)] + ([("tensor", ot_read)] if h == 0 else []), inc=True)
                wpr_guard = p.val("tensor")
                wld = p.op("gpsimd", lambda e, g=g: e.dma_start(out=WPR[:], in_=wcast.ap()[1 + g]),
                           dma="wst", waits=[("tensor", wpr_guard)])
                for qt in range(4):
                    qcol = qselT[:, qt * 16 + r: qt * 16 + r + 1]
                    for df in range(2):
                        bank = ps_t[df]
                        for j in range(8):
                            wl = []
                            if j == 0:
                                wl = [pg[2 + df]]
                                if qt == 0 and df == 0:
                                    wl += [("scalar", a_fin_prev), ("dma_wst", wld)]
                            p.op("tensor", lambda e, qt=qt, df=df, j=j, bank=bank: e.matmul(
                                bank[:], OT[:, j * 512 + qt * 128: j * 512 + (qt + 1) * 128],
                                WPR[:, j * 1024 + df * 512: j * 1024 + (df + 1) * 512],
                                start=(j == 0), stop=(j == 7)), waits=wl, inc=(j == 7))
                        t_pj = p.val("tensor")
                        p.op("vector", lambda e, qt=qt, df=df, bank=bank, qcol=qcol: e.scalar_tensor_tensor(
                            out=accw[:, qt * 1024 + df * 512: qt * 1024 + (df + 1) * 512],
                            in0=bank[:], scalar=qcol,
                            in1=accw[:, qt * 1024 + df * 512: qt * 1024 + (df + 1) * 512],
                            op0=ALU.mult, op1=ALU.add), waits=[("tensor", t_pj)], inc=True)
                        pg[2 + df] = ("vector", p.val("vector"))
                qwg_read = p.val("tensor")
                ot_read = p.val("tensor")
                wpr_read = p.val("tensor")

        for qt in range(4):
            tile = w * 4 + qt
            ldx = p.op("sync", lambda e, tile=tile: e.dma_start(
                out=xt[:], in_=xf[tile * 128:(tile + 1) * 128, :]), dma="ldx",
                waits=[("vector", v_comb)])
            if w == 0:
                p.op("vector", lambda e: e.memset(rc05[:], 0.5), inc=True)
            else:
                p.op("vector", lambda e, qt=qt, w=w: e.tensor_tensor(
                    out=cntc[:], in0=qselT[:, qt * 16 + (w - 1): qt * 16 + w],
                    in1=qselT[:, qt * 16 + NWIN + (w - 1): qt * 16 + NWIN + w], op=ALU.add), inc=True)
                p.op("vector", lambda e: e.tensor_scalar_add(out=cntc[:], in0=cntc[:], scalar1=1.0), inc=True)
                p.op("vector", lambda e: e.reciprocal(out=rc05[:], in_=cntc[:]), inc=True)
                p.op("vector", lambda e: e.tensor_scalar_mul(out=rc05[:], in0=rc05[:], scalar1=0.5), inc=True)
            v_comb = p.op("vector", lambda e, qt=qt: e.scalar_tensor_tensor(
                out=yr[:], in0=accw[:, qt * 1024:(qt + 1) * 1024], scalar=rc05[:], in1=xt[:],
                op0=ALU.mult, op1=ALU.add),
                waits=[("dma_ldx", ldx), ("dma_stout", st_out)], inc=True)
            st_out = p.op("sync", lambda e, tile=tile: e.dma_start(
                out=xo[tile * 128:(tile + 1) * 128, :], in_=yr[:]), dma="stout",
                waits=[("vector", v_comb)])


def prep_l1ffn_weights(I, tf=T):
    f32 = np.float32
    ln_g = I["L1_ffn_ln_g"].astype(f32); ln_b = I["L1_ffn_ln_b"].astype(f32)
    lncol = np.zeros((128, 16), f32)
    for hk in range(8):
        lncol[:, hk] = ln_g[hk * 128:(hk + 1) * 128]
        lncol[:, 8 + hk] = ln_b[hk * 128:(hk + 1) * 128]
    rt_q = I["L1_ffn_rt_q"].astype(f32); rt_kv = I["L1_ffn_rt_kv"].astype(f32)
    rt4 = np.zeros((128, 32), f32)
    for hk in range(8):
        for g in range(2):
            rt4[:, hk * 4 + g] = rt_q[g, hk * 128:(hk + 1) * 128]
            rt4[:, hk * 4 + 2 + g] = rt_kv[g, hk * 128:(hk + 1) * 128]
    qn = I["L1_ffn_qnorm_g"].astype(f32); cn = I["L1_ffn_cnorm_g"].astype(f32)
    qcn = np.zeros((128, 32), f32)
    for hk in range(8):
        for g in range(2):
            qcn[:, hk * 4 + g] = qn[g, hk * 128:(hk + 1) * 128]
            qcn[:, hk * 4 + 2 + g] = cn[g, hk * 128:(hk + 1) * 128]
    nullkv = I["L1_ffn_null_kv"].astype(f32)
    nks = np.zeros((128, 32), f32)
    nvs = np.zeros((1, 2048), f32)
    for g in range(2):
        for h in range(16):
            b = (h % 2) * 64
            nks[b:b + 64, 2 * h + g] = nullkv[0, g, h, 0]
            nvs[0, g * 1024 + h * 64:(g * 1024 + (h + 1) * 64)] = nullkv[1, g, h, 0]
    R = np.zeros((64, 64), f32)
    for i in range(32):
        R[i, 32 + i] = -1.0
        R[32 + i, i] = 1.0
    R2 = np.zeros((128, 128), f32)
    R2[0:64, 0:64] = R
    R2[64:128, 64:128] = R
    rotm = np.ascontiguousarray(R2.T)
    inv = (1.0 / 10000.0 ** (np.arange(0, 64, 2, dtype=f32) / 64)).astype(f32)
    t = np.arange(tf, dtype=f32)
    fr = t[:, None] * inv[None, :]
    fr = np.concatenate([fr, fr], -1)
    cosv = np.cos(fr).astype(f32); sinv = np.sin(fr).astype(f32)
    cossin = np.zeros((128, 2 * tf), f32)
    for rr in range(128):
        cossin[rr, 0:tf] = cosv[:, rr % 64]
        cossin[rr, tf:2 * tf] = sinv[:, rr % 64]
    NCTX = tf - 512
    NWIN = tf // 512 - 1
    wmaskin = np.zeros((16, NCTX), f32)
    for rr in range(16):
        if rr < 2 * NWIN:
            wd = rr % NWIN
            wmaskin[rr, (wd + 1) * 512:] = -1e9
    ii = np.arange(128)[:, None]
    jj = np.arange(128)[None, :]
    pats = []
    for dlt in range(-3, 8):
        dq = dlt * 128 + (ii - jj)
        pats.append(np.where((dq >= 0) & (dq <= 512), 0.0, -1e9).astype(f32))
    mlocQ = np.zeros((128, 11 * 128), f32)
    mlocKT = np.zeros((128, 11 * 128), f32)
    for m in range(11):
        mlocQ[:, m * 128:(m + 1) * 128] = pats[(7 - m) + 3]
        mlocKT[:, m * 128:(m + 1) * 128] = pats[m].T
    out = {
        "wqkv": np.ascontiguousarray(I["L1_ffn_w_qkv"].astype(f32).T),
        "lncol": lncol, "rt4": rt4, "qcn": qcn,
        "wq": np.ascontiguousarray(np.stack([I["L1_ffn_w_q"][g].astype(f32).T for g in range(2)])),
        "wkv": np.ascontiguousarray(np.stack([I["L1_ffn_w_kv"][g].astype(f32).T for g in range(2)])),
        "wout": np.ascontiguousarray(np.stack([I["L1_ffn_w_out"][g].astype(f32).T for g in range(2)])),
        "wlo": np.ascontiguousarray(I["L1_ffn_w_lo"].astype(f32).T),
        "nks": nks, "nvs": nvs, "rotm": rotm, "cossin": cossin,
        "wmaskin": wmaskin, "mlocQ": mlocQ, "mlocKT": mlocKT,
    }
    return out


_L1FFN_NC = None


def run_l1ffn(x, I):
    global _L1FFN_NC
    if _L1FFN_NC is None:
        _L1FFN_NC = build_l1ffn()
    nc = _L1FFN_NC
    w = prep_l1ffn_weights(I)
    maps = []
    for cix in range(NCORE):
        m = dict(w)
        m["xf"] = np.ascontiguousarray(x[cix // 2].astype(np.float32))
        maps.append(m)
    res = run_bass_kernel_spmd(nc, maps, core_ids=list(range(NCORE)))
    return np.stack([res.results[2 * b]["xo"] for b in range(B)])



ARENA_B = 211200
_WBCTR = 0


def mega_barrier(p):
    engines = ("sync", "tensor", "vector", "scalar", "gpsimd")
    vals = [(en, p.val(en)) for en in engines]
    dvals = [(k, p.esem[k].n) for k in list(p.esem) if k.startswith("dma_")]
    for en in engines:
        wl = [(e2, v) for e2, v in vals if e2 != en] + dvals
        p.op(en, lambda e: e.nop(), waits=wl, inc=True, chain=True)


def build_mega():
    nc = bass.Bass("TRN2", target_bir_lowering=False, debug=False, name="mega")
    AI = lambda n, s: nc.dram_tensor(n, s, F32, kind="ExternalInput").ap()
    x0s = AI("x0s", [TPC, D])
    gate_w = AI("gate_w", [D, 8]); gate_b = AI("gate_b", [1, 8])
    up_w = AI("up_w", [9, 128, 8 * 32 * 128]); up_bT = AI("up_bT", [9, 128, 32])
    down_w = AI("down_w", [9, 128, 32 * D]); down_b = AI("down_b", [9, 1, D])
    mq = AI("mq", [128, 256]); mqT = AI("mqT", [128, 256])
    w_attn = AI("w_attn", [D, 3072]); b_qk = AI("b_qk", [128, 16])
    b_v = AI("b_v", [1, D]); w_proj = AI("w_proj", [D, D]); b_proj = AI("b_proj", [1, D])
    filt = AI("filt", [128, 128]); gamma = AI("gamma", [1, D]); beta = AI("beta", [1, D])
    wqkv = AI("wqkv", [D, 3072]); lncol = AI("lncol", [128, 16]); rt4 = AI("rt4", [128, 32])
    qcn = AI("qcn", [128, 32]); wq = AI("wq", [2, D, 1024]); wkv = AI("wkv", [2, D, 2048])
    wout = AI("wout", [2, D, 1024]); wlo = AI("wlo", [D, 1024]); nks = AI("nks", [128, 32])
    nvs = AI("nvs", [1, 2048]); rotm = AI("rotm", [128, 128]); cossin = AI("cossin", [128, 2 * T])
    wmaskin = AI("wmaskin", [16, T - 512])
    mlocQ = AI("mlocQ", [128, 11 * 128]); mlocKT = AI("mlocKT", [128, 11 * 128])
    xob = nc.dram_tensor("xob", [T, D], F32, kind="ExternalOutput").ap()
    x1s = nc.dram_tensor("x1s", [TPC, D], F32)
    x1g = nc.dram_tensor("x1g", [2 * TPC, D], F32)
    x1f = nc.dram_tensor("x1f", [T, D], F32)
    x2s = nc.dram_tensor("x2s", [TPC, D], F32)
    x3s = nc.dram_tensor("x3s", [TPC, D], F32)
    x3g = nc.dram_tensor("x3g", [2 * TPC, D], F32)
    x3f = nc.dram_tensor("x3f", [T, D], F32)
    x4f = nc.dram_tensor("x4f", [T, D], F32)
    x5f = nc.dram_tensor("x5f", [T, D], F32)
    s_hbm = nc.dram_tensor("s_hbm", [4, T], F32)
    yf_hbm = nc.dram_tensor("yf_hbm", [128, 8 * T], BF16)
    yr_hbm = nc.dram_tensor("yr_hbm", [128, 8 * T], BF16)
    ql_hbm = nc.dram_tensor("ql_hbm", [128, 8 * T], BF16)
    kl_hbm = nc.dram_tensor("kl_hbm", [128, 8 * T], BF16)
    vl_hbm = nc.dram_tensor("vl_hbm", [T, D], BF16)
    qg_hbm = nc.dram_tensor("qg_hbm", [2, 128, 8 * T], BF16)
    kg_hbm = nc.dram_tensor("kg_hbm", [2, 128, 8 * T], BF16)
    vg_hbm = nc.dram_tensor("vg_hbm", [2, T, D], BF16)

    p = Prog(nc)
    pool = [p.psum(f"bank{i}", [128, 512], F32) for i in range(8)]
    arena = Arena(nc, ARENA_B)
    arena.allocate(p.stack)
    p.arena = arena
    p.psum_pool = pool

    def gather(src, dst_g, dst_f):
        mega_barrier(p)
        g_cc = p.op("gpsimd", lambda e: e.collective_compute(
            "AllGather", ALU.bypass, replica_groups=[[0, 1], [2, 3], [4, 5], [6, 7]],
            ins=[src.ap()[:, :]], outs=[dst_g.ap()[:, :]]), inc=True)
        for q in range(2):
            p.op("sync", lambda e, q=q: e.dma_start(
                out=dst_f.ap().rearrange("(i q c) d -> q i (c d)", i=T // 256, q=2, c=128)[q],
                in_=dst_g.ap()[q * TPC:(q + 1) * TPC, :].rearrange("(i c) d -> i (c d)", c=128)),
                dma="gat", waits=[("gpsimd", g_cc)])
        mega_barrier(p)

    p.begin_stage()
    emit_l0ffn(p, x0s, gate_w, gate_b, up_w, up_bT, down_w, down_b, x1s.ap(), TPC)
    gather(x1s, x1g, x1f)
    p.begin_stage()
    emit_l0attn(p, x1f.ap(), x1s.ap(), mq, mqT, w_attn, b_qk, b_v, w_proj, b_proj,
                x2s.ap(), TPC, T, 7)
    mega_barrier(p)
    p.begin_stage()
    emit_l0ffn(p, x2s.ap(), gate_w, gate_b, up_w, up_bT, down_w, down_b, x3s.ap(), TPC)
    gather(x3s, x3g, x3f)
    p.begin_stage()
    emit_l1ffn(p, x3f.ap(), wqkv, lncol, rt4, qcn, wq, wkv, wout, wlo, nks, nvs,
               rotm, cossin, wmaskin, mlocQ, mlocKT, x4f.ap(),
               s_hbm, yf_hbm, yr_hbm, ql_hbm, kl_hbm, vl_hbm, qg_hbm, kg_hbm, vg_hbm, T)
    mega_barrier(p)
    p.begin_stage()
    emit_l1attn(p, x4f.ap(), filt, gamma, beta, x5f.ap(), T)
    mega_barrier(p)
    p.begin_stage()
    emit_l1ffn(p, x5f.ap(), wqkv, lncol, rt4, qcn, wq, wkv, wout, wlo, nks, nvs,
               rotm, cossin, wmaskin, mlocQ, mlocKT, xob,
               s_hbm, yf_hbm, yr_hbm, ql_hbm, kl_hbm, vl_hbm, qg_hbm, kg_hbm, vg_hbm, T)
    p.finish()
    return nc


_MEGA_NC = None


def kernel_mega(**inputs):
    global _MEGA_NC
    if _MEGA_NC is None:
        _MEGA_NC = build_mega()
    w = {}
    w.update(prep_l0ffn_weights(inputs))
    w.update(prep_l0attn_weights(inputs))
    w.update(prep_l1attn_weights(inputs))
    w.update(prep_l1ffn_weights(inputs))
    x0 = np.asarray(inputs["L0_attn_x"], np.float32)
    shards = shard_tokens(x0)
    maps = []
    for c in range(NCORE):
        mqa, mqTa = l0attn_masks(c % 2)
        maps.append(dict(w, x0s=shards[c], mq=mqa, mqT=mqTa))
    res = run_bass_kernel_spmd(_MEGA_NC, maps, core_ids=list(range(NCORE)))
    return np.stack([res.results[2 * b]["xob"] for b in range(B)])
